# version 15
# baseline (speedup 1.0000x reference)
import contextlib
import numpy as np
import ml_dtypes
import concourse.bass as bass
import concourse.mybir as mybir
from concourse.bass_utils import run_bass_kernel_spmd

F32 = mybir.dt.float32
BF16 = mybir.dt.bfloat16
AF = mybir.ActivationFunctionType
ALU = mybir.AluOpType
AX = mybir.AxisListType.X
BF = ml_dtypes.bfloat16

B, S, D = 2, 8192, 1024
H, NP = 16, 8
NBLK = 16
NEG = -1.0e30
NEGBIG = -3.0e38

STAGE = 4
DEBUG_SPILLS = False


def local_tile_order(k):
    order = []
    for t in range(NBLK):
        order.append(4 * t + k)
        order.extend(4 * t + j for j in range(4) if j != k)
    return order


def own_pos(i):
    qb, r = divmod(i, 4)
    return 512 * qb + 128 * (3 - r)


def host_inputs(full):
    xf = np.ascontiguousarray(np.asarray(full["L0_attn_residual"], np.float32))
    WQ = np.asarray(full["L0_attn_W_Q"], np.float32)
    WK = np.asarray(full["L0_attn_W_K"], np.float32)
    WV = np.asarray(full["L0_attn_W_V"], np.float32)
    bQ = np.asarray(full["L0_attn_b_Q"], np.float32)
    bK = np.asarray(full["L0_attn_b_K"], np.float32)
    bV = np.asarray(full["L0_attn_b_V"], np.float32)

    def pairmajor(W):
        return np.ascontiguousarray(W.transpose(1, 0, 2).reshape(D, D))

    bqk = np.zeros((128, 16), np.float32)
    for p in range(NP):
        bqk[:, p] = bQ[2 * p:2 * p + 2].reshape(128)
        bqk[:, 8 + p] = bK[2 * p:2 * p + 2].reshape(128)

    w1c = np.stack([np.asarray(full[f"L0_ffn_{f}_w1"], np.float32)
                    .transpose(1, 0, 2).reshape(D, D) for f in "qkv"])
    w2c = np.stack([np.asarray(full[f"L0_ffn_{f}_w2"], np.float32)
                    .reshape(D, D) for f in "qkv"])
    b1c = np.zeros((128, 3, 8), np.float32)
    for fi, f in enumerate("qkv"):
        b1c[:, fi, :] = np.asarray(full[f"L0_ffn_{f}_b1"], np.float32).reshape(8, 128).T
    b2c = np.stack([np.asarray(full[f"L0_ffn_{f}_b2"], np.float32) for f in "qkv"])
    nodew = np.concatenate([np.ascontiguousarray(
        np.asarray(full[f"L0_ffn_{f}_node_w"], np.float32).T) for f in "qkv"],
        axis=1)
    nodeb = np.concatenate([np.asarray(full[f"L0_ffn_{f}_node_b"], np.float32)
                            .reshape(1, 15) for f in "qkv"], axis=1)

    kk = np.arange(128)[:, None]
    rr = np.arange(128)[None, :]
    trimask = np.where(kk <= rr, 0.0, NEG).astype(np.float32)
    hidiota = np.zeros((128, 8), np.float32)
    for hc in range(8):
        hidiota[:, hc] = 2 * hc + np.arange(128) // 64

    common = dict(
        wq=pairmajor(WQ), wk=pairmajor(WK), wv=pairmajor(WV), bqk=bqk,
        bvrow=np.ascontiguousarray(bV.reshape(1, D)),
        wo=np.ascontiguousarray(np.asarray(full["L0_attn_W_O"], np.float32)),
        borow=np.ascontiguousarray(
            np.asarray(full["L0_attn_b_O"], np.float32).reshape(1, D)),
        w1c=w1c, w2c=w2c, b1c=b1c, b2c=b2c, nodew=nodew, nodeb=nodeb,
        idf32=np.eye(128, dtype=np.float32), trimask=trimask,
        onesrow=np.ones((1, 128), np.float32),
        onesrow_bf=np.ones((1, 128), BF),
        hidiota=hidiota,
        iota16=np.arange(16, dtype=np.float32).reshape(16, 1),
        onescol_bf=np.ones((128, 1), BF),
    )
    maps = []
    for c in range(8):
        b, k = c // 4, c % 4
        order = local_tile_order(k)
        x_all = np.concatenate([xf[b, 128 * g:128 * g + 128] for g in order], axis=0)
        others = [j for j in range(4) if j != k]
        allow = np.zeros((1, 4), np.float32)
        allow[0, 0] = 1.0
        for m in range(3):
            allow[0, m + 1] = 1.0 if others[m] < k else 0.0
        d = dict(common)
        d["x"] = np.ascontiguousarray(x_all)
        d["allow"] = allow
        maps.append(d)
    return maps


def host_output(results):
    out = np.zeros((B, S, D), np.float32)
    for c in range(8):
        b, k = c // 4, c % 4
        oc = np.asarray(results[c]["out"], np.float32)
        for i in range(NBLK):
            g = 4 * i + k
            out[b, 128 * g:128 * g + 128] = oc[own_pos(i):own_pos(i) + 128]
    return out



class Sem:
    __slots__ = ("h", "n", "name")

    def __init__(self, name):
        self.h = None
        self.n = 0
        self.name = name


class Prog:
    def __init__(self, nc):
        self.nc = nc
        self.ops = {k: [] for k in ("sync", "gpsimd", "tensor", "vector", "scalar")}
        self.sems = []

    def sem(self, name):
        s = Sem(name)
        self.sems.append(s)
        return s

    def op(self, eng, fn, waits=(), inc=None):
        incs = []
        ret = None
        if inc is not None:
            s, a = inc
            s.n += a
            incs.append((s, a))
            ret = s.n
        ws = []
        for w in waits:
            if w is None:
                continue
            s, v = w
            if v and v > 0:
                ws.append((s, v))
        self.ops[eng].append((fn, ws, incs))
        return ret

    def dma(self, q, out, in_, sem, waits=()):
        return self.op(q, lambda e: e.dma_start(out=out, in_=in_), waits, (sem, 16))

    def emit(self):
        nc = self.nc
        with contextlib.ExitStack() as st:
            for s in self.sems:
                s.h = st.enter_context(nc.semaphore(s.name))
            with nc.Block() as block:
                def runner(name):
                    def run(e):
                        for (fn, waits, incs) in self.ops[name]:
                            for (s, v) in waits:
                                e.wait_ge(s.h, v)
                            ins = fn(e)
                            for (s, a) in incs:
                                ins.then_inc(s.h, a)
                    return run

                block.sync(runner("sync"))
                block.gpsimd(runner("gpsimd"))
                block.tensor(runner("tensor"))
                block.vector(runner("vector"))
                block.scalar(runner("scalar"))


class Slots:

    def __init__(self, slots):
        self.slots = slots
        self.free = [[] for _ in slots]
        self.i = 0

    def acquire(self):
        j = self.i % len(self.slots)
        self.i += 1
        return j, self.slots[j], list(self.free[j])

    def release(self, j, wait):
        self.free[j] = wait if isinstance(wait, list) else [wait]


class Ctx:
    pass


def build_nc():
    nc = bass.Bass("TRN2", target_bir_lowering=False)
    P = Prog(nc)
    C = Ctx()
    C.P = P

    din = {}
    for name, shape, dt in [
        ("x", [S, D], F32), ("wq", [D, D], F32), ("wk", [D, D], F32),
        ("wv", [D, D], F32), ("bqk", [128, 16], F32), ("bvrow", [1, D], F32),
        ("wo", [D, D], F32), ("borow", [1, D], F32),
        ("w1c", [3, D, D], F32), ("w2c", [3, D, D], F32),
        ("b1c", [128, 3, 8], F32), ("b2c", [3, 16, D], F32),
        ("nodew", [D, 45], F32), ("nodeb", [1, 45], F32),
        ("idf32", [128, 128], F32), ("trimask", [128, 128], F32),
        ("onesrow", [1, 128], F32), ("onesrow_bf", [1, 128], BF16),
        ("hidiota", [128, 8], F32), ("iota16", [16, 1], F32),
        ("onescol_bf", [128, 1], BF16), ("allow", [1, 4], F32),
    ]:
        din[name] = nc.dram_tensor(name, shape, dt, kind="ExternalInput").ap()
    C.din = din
    C.out_d = nc.dram_tensor("out", [2048, D], F32, kind="ExternalOutput").ap()

    kd = "ExternalOutput" if DEBUG_SPILLS else "Internal"
    C.xnt_sp = nc.dram_tensor("xnt_sp", [NBLK, 8, 128, 512], BF16, kind=kd).ap()
    C.kt_sp = nc.dram_tensor("kt_sp", [NP, 128, S], BF16, kind=kd).ap()
    C.vp_sp = nc.dram_tensor("vp_sp", [NP, 128, 64, 130], BF16, kind=kd).ap()
    C.fkt_sp = nc.dram_tensor("fkt_sp", [8, 128, S], BF16, kind=kd).ap()
    C.fv_sp = nc.dram_tensor("fv_sp", [64, 128, D], BF16, kind=kd).ap()
    C.fqt_sp = nc.dram_tensor("fqt_sp", [8, 128, 2048], BF16, kind=kd).ap()
    C.s_sp3 = nc.dram_tensor("s_sp3", [16, 128, D], BF16, kind=kd).ap()
    C.mh_dbg = nc.dram_tensor("mh_dbg", [8, 128, 2048], BF16, kind=kd).ap() \
        if DEBUG_SPILLS else None
    C.wo_dbg = nc.dram_tensor("wo_dbg", [128, 8, 1024], BF16, kind=kd).ap() \
        if DEBUG_SPILLS else None
    C.bo_dbg = nc.dram_tensor("bo_dbg", [128, D], F32, kind=kd).ap() \
        if DEBUG_SPILLS else None

    ctx = contextlib.ExitStack()
    sbuf = lambda n, sh, dt: ctx.enter_context(nc.sbuf_tensor(n, sh, dt))
    psum = lambda n, sh: ctx.enter_context(nc.psum_tensor(n, sh, F32))

    with ctx:
        C.pS1 = psum("pS1", [128, 1024])
        C.pS2 = psum("pS2", [128, 1024])
        C.pA = psum("pA", [128, 1024])
        C.pX1 = psum("pX1", [128, 512])
        C.pX2 = psum("pX2", [128, 512])

        C.idf32 = sbuf("c_idf32", [128, 128], F32)
        C.trimask = sbuf("c_trimask", [128, 128], F32)
        C.onesP = sbuf("c_onesP", [128, 128], F32)
        C.onesbf = sbuf("c_onesbf", [1, 128], BF16)
        C.hidiota = sbuf("c_hidiota", [128, 8], F32)
        C.iota16 = sbuf("c_iota16", [16, 1], F32)
        C.onescolb = sbuf("c_onescolb", [128, 1], BF16)
        C.bqk = sbuf("c_bqk", [128, 16], F32)
        C.b1c = sbuf("c_b1c", [128, 3, 8], F32)
        C.allowrow = sbuf("c_allow", [1, 4], F32)
        C.nodew_s = sbuf("c_nodew", [128, 8, 45], F32)
        C.nodebB = sbuf("c_nodebB", [128, 45], F32)
        C.nodeb_s = sbuf("c_nodeb", [1, 45], F32)

        C.bvB = sbuf("bvB", [128, D], F32)
        C.reclB = C.bvB[0:64, 0:512]
        C.boB = sbuf("boB", [128, D], F32)
        C.allowB = sbuf("allowB", [128, 4], F32)

        ar1 = sbuf("ar1", [128, 16384], BF16)
        ar2 = sbuf("ar2", [128, 16384], BF16)
        ar3 = sbuf("ar3", [128, 16384], BF16)
        ar4 = sbuf("ar4", [128, 8320], BF16)
        ar5 = sbuf("ar5", [128, 4096], F32)

        C.QT = ar1[...].rearrange("p (c n) -> p c n", c=8)
        C.qTb = ar1[:, 0:4096].rearrange("p (c n) -> p c n", c=8)
        C.P2T = ar1[:, 4096:6144].rearrange("p (c n) -> p c n", c=4)
        C.gsb = ar1[:, 6144:7168]

        C.xnT = ar2[:, 0:8192].rearrange("p (a c n) -> p a c n", a=2, c=8)
        C.hidT = ar2[:, 8192:12288].rearrange("p (c n) -> p c n", c=8)
        C.evst = ar2[:, 12288:14336].rearrange("p (c n) -> p c n", c=4)
        C.vevst = ar2[:, 14336:15424].rearrange("p (a n) -> p a n", a=2)
        C.mhT = ar2[...].rearrange("p (c n) -> p c n", c=8)

        C.ktp = ar3[...].rearrange("p (a n) -> p a n", a=2)
        C.ktf = ar3[:, 0:8192].rearrange("p (a c n) -> p a c n", a=2, c=8)
        C.vf = ar3[:, 8192:16384].rearrange("p (a c n) -> p a c n", a=2, c=4)

        C.vaug = ar4[...].rearrange("p (c n) -> p c n", c=64)

        C.xnTf = ar5[...].rearrange("p (c n) -> p c n", c=8)
        C.A2 = ar5[...].rearrange("p (c n) -> p c n", c=4)

        C.xtile = sbuf("xtile", [128, 2, 1024], F32)
        C.xnf = sbuf("xnf", [128, 2, 1024], F32)
        C.outb = C.xnf
        C.stats = sbuf("stats", [128, 48], F32)
        C.scores = sbuf("scores", [128, 4, 3, 15], F32)
        C.leafall = sbuf("leafall", [128, 3, 64], F32)
        C.dscr = sbuf("dscr", [128, 136], F32)

        C.wbig = sbuf("wbig", [128, 2, 8, 1024], BF16)
        C.wstage = sbuf("wstage", [128, 2, 1024], F32)
        C.bvrow_s = C.wstage[0:1, 0:1, 0:1024].rearrange("p a n -> p (a n)")
        C.borow_s = C.wstage[0:1, 1:2, 0:1024].rearrange("p a n -> p (a n)")
        C.lnscr = C.wstage[0:128, 1:2, 0:1024].rearrange("p a n -> p (a n)")
        C.b2bf = ar4[0:16, 0:3072].rearrange("p (f n) -> p f n", f=3)
        C.b2f32 = C.wstage[0:16, 0:1, 0:1024].rearrange("p a n -> p (a n)")
        pts = sbuf("pts", [128, 2048], BF16)
        C.PT = pts[:, 0:2048].rearrange("p (a b n) -> p a b n", a=2, b=2)
        C.mxs = sbuf("mxs", [128, 2, 64], F32)
        C.selT = sbuf("selT", [16, 512], BF16)
        C.lrow_f = C.xnf[0:1, 0:1, 0:512].rearrange("p a n -> p (a n)")
        C.lrow_b = pts[0:1, 0:512]
        C.sc1 = sbuf("sc1", [1, 16], F32)
        C.biasq = sbuf("biasq", [128, 2, 2], F32)
        C.l2 = sbuf("l2", [128, 4], F32)
        C.fsc = sbuf("fsc", [1, 8], F32)
        C.fB = sbuf("fB", [128, 2], F32)
        C.m2t = sbuf("m2t", [128, 4], F32)
        C.sload = pts[:, 0:2048].rearrange("p (a n) -> p a n", a=2)

        for s in ("s_in", "s_w", "s_sp", "s_sp2", "s_ss", "s_out", "s_mv", "s_q",
                  "c_t", "c_v", "c_a", "c_g"):
            setattr(C, s, P.sem(s))
        C.s_x = [P.sem(f"s_x{i}") for i in range(2)]
        C.s_xnt = [P.sem(f"s_xnt{i}") for i in range(2)]
        C.s_xr = [P.sem(f"s_xr{i}") for i in range(2)]
        C.s_ev = [P.sem(f"s_ev{i}") for i in range(4)]
        C.s_mk = [P.sem(f"s_mk{i}") for i in range(2)]
        C.s_fk = [P.sem(f"s_fk{i}") for i in range(2)]
        C.s_fv = [P.sem(f"s_fv{i}") for i in range(2)]
        C.s_xl = [P.sem(f"s_xl{i}") for i in range(2)]
        C.s_sl = [P.sem(f"s_sl{i}") for i in range(2)]

        emit_setup(C)
        emit_ph0a(C)
        if STAGE >= 2:
            emit_ph0b(C)
            emit_ph0c(C)
        if STAGE >= 3:
            emit_ph1(C)
        if STAGE >= 4:
            emit_ph2(C)

        P.op("gpsimd", lambda e: e.engine_nop(),
             waits=[(C.s_out, C.s_out.n)], inc=(C.c_g, 1))
        P.emit()
    return nc


def emit_setup(C):
    P, din = C.P, C.din
    s_in, c_t, c_a = C.s_in, C.c_t, C.c_a
    for nm, dst in [("idf32", C.idf32[...]), ("trimask", C.trimask[...]),
                    ("onesrow", C.onesP[0:1, :]), ("onesrow", C.onesP[64:65, :]),
                    ("onesrow_bf", C.onesbf[...]), ("hidiota", C.hidiota[...]),
                    ("iota16", C.iota16[...]), ("onescol_bf", C.onescolb[...]),
                    ("bqk", C.bqk[...]), ("b1c", C.b1c[...]), ("allow", C.allowrow[...]),
                    ("nodeb", C.nodeb_s[...]), ("bvrow", C.bvrow_s),
                    ("borow", C.borow_s)]:
        P.dma("sync", dst, din[nm], s_in)
    P.dma("sync", C.nodew_s[...],
          din["nodew"].rearrange("(c p) n -> p c n", p=128), s_in)
    C.consts_ok = s_in.n

    def mk_bcast(rowsl, width, dst, scale=1.0):
        t = P.op("tensor", lambda e: e.matmul(
            C.pX1[0:128, 0:width], C.onesP[0:1, 0:128], rowsl,
            start=True, stop=True),
            waits=[(s_in, C.consts_ok), (c_a, c_a.n)], inc=(c_t, 1))
        P.op("scalar", lambda e: e.activation(
            out=dst, in_=C.pX1[0:128, 0:width], func=AF.Copy, scale=scale),
            waits=[(c_t, t)], inc=(c_a, 1))

    for hf in range(2):
        mk_bcast(C.bvrow_s[0:1, 512 * hf:512 * hf + 512], 512,
                 C.bvB[0:128, 512 * hf:512 * hf + 512])
        mk_bcast(C.borow_s[0:1, 512 * hf:512 * hf + 512], 512,
                 C.boB[0:128, 512 * hf:512 * hf + 512])
    mk_bcast(C.allowrow[0:1, 0:4], 4, C.allowB[0:128, 0:4])
    mk_bcast(C.nodeb_s[0:1, 0:45], 45, C.nodebB[0:128, 0:45])
    C.setup_t = C.c_t.n


def emit_ph0a(C):
    P = C.P
    T, V, A, G, SY = "tensor", "vector", "scalar", "gpsimd", "sync"
    s_in, s_sp, c_t, c_v, c_a = C.s_in, C.s_sp, C.c_t, C.c_v, C.c_a
    xnf, xtile, stats = C.xnf, C.xtile, C.stats
    xnT, xnTf, scores, leafall, dscr = C.xnT, C.xnTf, C.scores, C.leafall, C.dscr
    evst, dscr = C.evst, C.dscr

    w_ok = max(load_weight_slab(C, C.din["wk"], 0),
               load_weight_slab(C, C.din["wv"], 1))
    vev4 = C.vevst[:, :, 0:520].rearrange("p a (j y) -> p a j y", j=4)
    m_on = 0
    for sl in range(2):
        P.op(G, lambda e, sl=sl: e.memset(vev4[:, sl, :, 64:65], 1.0),
             waits=[(c_a, c_a.n)], inc=(C.c_g, 1))
        m_on = P.op(G, lambda e, sl=sl: e.memset(vev4[:, sl, :, 129:130], 1.0),
                    inc=(C.c_g, 1))

    x_t = C.din["x"].rearrange("(t p) d -> t p d", p=128)
    psl = Slots([(C.pS1, 0), (C.pS1, 512), (C.pS2, 0), (C.pS2, 512)])
    esl = Slots([0, 1, 2, 3])
    xn_done = {}
    tr_done = {}
    kv_t = {}

    def load_xtile(gt):
        w = []
        if gt >= 2:
            w.append((c_v, xn_done[gt - 2]))
        if gt == 0:
            w.append((s_in, C.consts_ok))
        P.dma(SY, xtile[:, gt % 2, :], x_t[gt], C.s_x[gt % 2], waits=w)

    def mmgroup(lhs_fn, rhs_fn, n_k, width, extra0):
        j, (pt, off), frees = psl.acquire()
        for dc in range(n_k):
            w0 = (extra0 + frees) if dc == 0 else []
            lhs, rhs = lhs_fn(dc), rhs_fn(dc)
            P.op(T, lambda e, lhs=lhs, rhs=rhs, dc=dc, pt=pt, off=off,
                 width=width, n_k=n_k: e.matmul(
                pt[0:128, off:off + width], lhs, rhs,
                start=(dc == 0), stop=(dc == n_k - 1)),
                waits=w0, inc=(c_t, 1) if dc == n_k - 1 else None)
        return j, pt, off, c_t.n

    load_xtile(0)
    load_xtile(1)
    kv_ready = {}

    def emit_kv_group(b, gi):
        parb = b % 2
        if gi < 8:
            p8 = gi
            e0 = [(c_a, kv_ready[b])] + ([(c_a, w_ok)] if b == 0 and p8 == 0 else [])
            j, pt, off, tmm = mmgroup(
                lambda dc, p8=p8: C.wbig[:, 0, dc, 128 * p8:128 * p8 + 128],
                lambda dc, parb=parb: xnT[:, parb, dc, 0:512], 8, 512, e0)
            je, es, efree = esl.acquire()
            a_ev = P.op(V, lambda e, p8=p8, pt=pt, off=off, es=es: e.tensor_scalar_add(
                evst[:, es, :], pt[0:128, off:off + 512],
                C.bqk[:, 8 + p8:9 + p8]),
                waits=[(c_t, tmm)] + efree, inc=(c_v, 1))
            psl.release(j, (c_v, a_ev))
            P.dma(G, C.kt_sp[p8][:, 512 * b:512 * b + 512], evst[:, es, :],
                  s_sp, waits=[(c_v, a_ev)])
            esl.release(je, (s_sp, s_sp.n))
        else:
            tt, hf = (gi - 8) // 2, (gi - 8) % 2
            j, pt, off, tmm = mmgroup(
                lambda dc, tt=tt, parb=parb: xnT[:, parb, dc, 128 * tt:128 * tt + 128],
                lambda dc, hf=hf: C.wbig[:, 1, dc, 512 * hf:512 * hf + 512],
                8, 512, [(c_a, kv_ready[b])])
            sl = hf
            for hhx in range(2):
                v_ev = P.op(V, lambda e, pt=pt, off=off, sl=sl, hf=hf, hhx=hhx: e.tensor_tensor(
                    vev4[:, sl, :, 65 * hhx:65 * hhx + 64],
                    pt[0:128, off:off + 512].rearrange("p (j h x) -> p j h x", j=4, h=2)[:, :, hhx, :],
                    C.bvB[:, 512 * hf:512 * hf + 512].rearrange("p (j h x) -> p j h x", j=4, h=2)[:, :, hhx, :],
                    ALU.add),
                    waits=[(c_t, tmm), (C.c_g, m_on), (s_sp, s_sp.n)] if hhx == 0 else [],
                    inc=(c_v, 1))
            psl.release(j, (c_v, v_ev))
            cch = 4 * b + tt
            P.dma(G, C.vp_sp[4 * hf:4 * hf + 4, :, cch, :].rearrange("j p y -> p j y"),
                  C.vevst[:, sl, 0:520].rearrange("p (j y) -> p j y", j=4),
                  s_sp, waits=[(c_v, v_ev)])
        if gi == 15:
            kv_t[b] = c_t.n

    for blk in range(NBLK):
        par = blk % 2
        rout = []
        q6h = {}
        for half in range(2):
            for tt in (2 * half, 2 * half + 1):
                gt = 4 * blk + tt
                c0 = 10 * tt
                t_x = 16 * (gt // 2 + 1)
                s_x_par = C.s_x[gt % 2]
                xa = xtile[:, gt % 2, :]
                w_ln = [(s_x_par, t_x)]
                if gt == 0:
                    w_ln.append((c_t, C.setup_t))
                if gt >= 2:
                    w_ln.append((c_t, tr_done[gt - 2]))
                P.op(V, lambda e, xa=xa, c0=c0, gt=gt: e.scalar_tensor_tensor(
                    out=xnf[:, gt % 2, :], in0=xa, scalar=1.0, in1=xa,
                    op0=ALU.mult, op1=ALU.mult, accum_out=stats[:, c0 + 1:c0 + 2]),
                    waits=w_ln, inc=(c_v, 1))
                p2 = P.op(V, lambda e, xa=xa, c0=c0: e.reduce_sum(
                    stats[:, c0:c0 + 1], xa, axis=AX), inc=(c_v, 1))
            b0 = 20 * half
            sv = lambda j, b0=b0: stats[:, b0 + j:b0 + j + 11:10].rearrange(
                "p (a b) -> p a b", b=1)
            q1 = P.op(V, lambda e, sv=sv: e.tensor_scalar_mul(sv(2), sv(0), 1.0 / D),
                      waits=[(c_v, p2)], inc=(c_v, 1))
            q2 = P.op(V, lambda e, sv=sv: e.tensor_scalar_mul(sv(3), sv(1), 1.0 / D),
                      waits=[(c_v, q1)], inc=(c_v, 1))
            q3 = P.op(V, lambda e, sv=sv: e.tensor_tensor(sv(4), sv(2), sv(2), ALU.mult),
                      waits=[(c_v, q2)], inc=(c_v, 1))
            q4 = P.op(V, lambda e, sv=sv: e.tensor_tensor(sv(5), sv(3), sv(4), ALU.subtract),
                      waits=[(c_v, q3)], inc=(c_v, 1))
            q5 = P.op(V, lambda e, sv=sv: e.tensor_scalar_add(sv(6), sv(5), 1e-5),
                      waits=[(c_v, q4)], inc=(c_v, 1))
            a_sq = P.op(A, lambda e, sv=sv: e.activation(out=sv(7), in_=sv(6), func=AF.Sqrt),
                        waits=[(c_v, q5)], inc=(c_a, 1))
            q6h[half] = P.op(V, lambda e, sv=sv: e.reciprocal(sv(8), sv(7)),
                             waits=[(c_a, a_sq)], inc=(c_v, 1))
            for tt in (2 * half, 2 * half + 1):
                gt = 4 * blk + tt
                c0 = 10 * tt
                xa = xtile[:, gt % 2, :]
                wv_ = [(c_t, tr_done[gt - 2])] if gt >= 2 else []
                v_xn = P.op(V, lambda e, xa=xa, gt=gt, c0=c0: e.tensor_scalar(
                    out=xnf[:, gt % 2, :], in0=xa,
                    scalar1=stats[:, c0 + 2:c0 + 3], scalar2=stats[:, c0 + 8:c0 + 9],
                    op0=ALU.subtract, op1=ALU.mult),
                    waits=wv_ + [(c_v, q6h[tt // 2])], inc=(c_v, 1))
                xn_done[gt] = v_xn
                if gt + 2 < 64:
                    load_xtile(gt + 2)
                for dc in range(8):
                    j, (pt, off), frees = psl.acquire()
                    t_tr = P.op(T, lambda e, gt=gt, dc=dc, pt=pt, off=off: e.transpose(
                        pt[0:128, off:off + 128],
                        xnf[:, gt % 2, 128 * dc:128 * dc + 128], C.idf32[...]),
                        waits=[(c_v, v_xn)] + frees, inc=(c_t, 1))
                    bw = [(c_t, t_tr)]
                    if blk >= 2 and dc == 0 and tt == 0:
                        bw.append((C.s_xnt[par], C.s_xnt[par].n))
                        bw.append((c_t, kv_t[blk - 2]))
                    P.op(A, lambda e, tt=tt, dc=dc, pt=pt, off=off, par=par: e.activation(
                        out=xnT[:, par, dc, 128 * tt:128 * tt + 128],
                        in_=pt[0:128, off:off + 128], func=AF.Copy),
                        waits=bw, inc=(c_a, 1))
                    a_f = P.op(A, lambda e, tt=tt, dc=dc, pt=pt, off=off: e.activation(
                        out=xnTf[:, dc, 128 * tt:128 * tt + 128],
                        in_=pt[0:128, off:off + 128], func=AF.Copy), inc=(c_a, 1))
                    psl.release(j, (c_a, a_f))
                tr_done[gt] = c_t.n
                a_f32 = c_a.n

                col0 = 45 * tt
                for dc in range(8):
                    w0 = [(c_a, a_f32), (c_v, c_v.n)] if dc == 0 else []
                    P.op(T, lambda e, dc=dc, tt=tt, col0=col0: e.matmul(
                        C.pX2[0:128, col0:col0 + 45],
                        xnTf[:, dc, 128 * tt:128 * tt + 128],
                        C.nodew_s[:, dc, :], start=(dc == 0), stop=(dc == 7)),
                        waits=w0, inc=(c_t, 1) if dc == 7 else None)
                rout.append((tt, col0, c_t.n))
                ti = 2 * half + (tt - 2 * half)
                if blk >= 1:
                    for gi in range(4 * ti, 4 * ti + 4):
                        emit_kv_group(blk - 1, gi)

        kv_ready[blk] = c_a.n
        for tt, col0, t_r in rout:
            P.op(V, lambda e, tt=tt, col0=col0: e.tensor_tensor(
                scores[:, tt, :, :].rearrange("p f n -> p (f n)"),
                C.pX2[0:128, col0:col0 + 45], C.nodebB[0:128, 0:45], ALU.add),
                waits=[(c_t, t_r)], inc=(c_v, 1))

        scl = [scores[:, :, :, j:j + 1].rearrange("p a f b -> p (a f) b")
               for j in range(15)]

        def dr(j):
            return dscr[:, 12 * j:12 * j + 12].rearrange("p (a b) -> p a b", b=1)

        def bA(j):
            return dscr[:, 12 * j:12 * j + 12].rearrange("p (a f) -> p a f", f=3)

        b0, b1, b2, b3 = dr(0), dr(1), dr(2), dr(3)
        tmp0, tmp1 = dr(4), dr(5)
        t1, tA, tB_, tC, tD = dr(6), dr(7), dr(8), dr(9), dr(10)

        def lerp(dst, a, b, bit):
            w1 = P.op(V, lambda e, a=a, b=b: e.tensor_tensor(tmp0, b, a, ALU.subtract),
                      waits=[(c_v, c_v.n)], inc=(c_v, 1))
            w2 = P.op(V, lambda e, bit=bit: e.tensor_tensor(tmp1, tmp0, bit, ALU.mult),
                      waits=[(c_v, w1)], inc=(c_v, 1))
            P.op(V, lambda e, dst=dst, a=a: e.tensor_tensor(dst, a, tmp1, ALU.add),
                 waits=[(c_v, w2)], inc=(c_v, 1))

        P.op(V, lambda e, s0=scl[0], b0=b0: e.tensor_single_scalar(
            b0, s0, 0.0, ALU.is_ge), waits=[(c_v, c_v.n)], inc=(c_v, 1))
        lerp(t1, scl[1], scl[2], b0)
        P.op(V, lambda e, t1=t1, b1=b1: e.tensor_single_scalar(
            b1, t1, 0.0, ALU.is_ge), waits=[(c_v, c_v.n)], inc=(c_v, 1))
        lerp(tA, scl[3], scl[5], b0)
        lerp(tB_, scl[4], scl[6], b0)
        lerp(t1, tA, tB_, b1)
        P.op(V, lambda e, t1=t1, b2=b2: e.tensor_single_scalar(
            b2, t1, 0.0, ALU.is_ge), waits=[(c_v, c_v.n)], inc=(c_v, 1))
        lerp(tA, scl[7], scl[11], b0)
        lerp(tB_, scl[8], scl[12], b0)
        lerp(tC, scl[9], scl[13], b0)
        lerp(tD, scl[10], scl[14], b0)
        lerp(tA, tA, tC, b1)
        lerp(tB_, tB_, tD, b1)
        lerp(t1, tA, tB_, b2)
        P.op(V, lambda e, t1=t1, b3=b3: e.tensor_single_scalar(
            b3, t1, 0.0, ALU.is_ge), waits=[(c_v, c_v.n)], inc=(c_v, 1))
        lf = leafall[:, :, 4 * blk:4 * blk + 4].rearrange("p f a -> p a f")
        P.op(V, lambda e, lf=lf: e.tensor_scalar_mul(lf, bA(0), 8.0),
             waits=[(c_v, c_v.n)], inc=(c_v, 1))
        P.op(V, lambda e, lf=lf: e.scalar_tensor_tensor(
            out=lf, in0=bA(1), scalar=4.0, in1=lf, op0=ALU.mult, op1=ALU.add),
            waits=[(c_v, c_v.n)], inc=(c_v, 1))
        P.op(V, lambda e, lf=lf: e.scalar_tensor_tensor(
            out=lf, in0=bA(2), scalar=2.0, in1=lf, op0=ALU.mult, op1=ALU.add),
            waits=[(c_v, c_v.n)], inc=(c_v, 1))
        P.op(V, lambda e, lf=lf: e.tensor_tensor(lf, lf, bA(3), ALU.add),
             waits=[(c_v, c_v.n)], inc=(c_v, 1))

        P.dma(SY, C.xnt_sp[blk].rearrange("c p n -> p c n"), xnT[:, par, :, :],
              C.s_xnt[par], waits=[(c_a, c_a.n)])

    for gi in range(16):
        emit_kv_group(NBLK - 1, gi)
    C.ph0a_v = c_v.n
    C.ph0b_sp = s_sp.n
    C.xnt_all = [(C.s_xnt[0], C.s_xnt[0].n), (C.s_xnt[1], C.s_xnt[1].n)]


def load_weight_slab(C, src2d, wi, extra_w=()):
    P = C.P
    a = 0
    for q in range(4):
        wld = P.dma("sync", C.wstage[0:128, 0:2, 0:1024],
                    src2d.rearrange("(c p) n -> p c n", p=128)[:, 2 * q:2 * q + 2, :],
                    C.s_w, waits=[(C.c_a, C.c_a.n), (C.c_t, C.c_t.n)] + list(extra_w))
        for j in range(2):
            a = P.op("scalar", lambda e, wi=wi, q=q, j=j: e.activation(
                out=C.wbig[:, wi, 2 * q + j, 0:1024],
                in_=C.wstage[:, j, :], func=AF.Copy),
                waits=[(C.s_w, wld)], inc=(C.c_a, 1))
    return a


def emit_ph0b(C):
    P = C.P
    T, V, A, G, SY = "tensor", "vector", "scalar", "gpsimd", "sync"
    s_sp, c_t, c_v, c_a = C.s_sp, C.c_t, C.c_v, C.c_a
    xnT, QT = C.xnT, C.QT

    msl = Slots([(C.pS1, 0), (C.pS1, 512), (C.pS2, 0), (C.pS2, 512)])
    xr_done = {}

    def mmgroup(lhs_fn, rhs_fn, n_k, width, extra0):
            j, (pt, off), frees = msl.acquire()
            for dc in range(n_k):
                w0 = (extra0 + frees) if dc == 0 else []
                lhs, rhs = lhs_fn(dc), rhs_fn(dc)
                P.op(T, lambda e, lhs=lhs, rhs=rhs, dc=dc, pt=pt, off=off,
                     width=width, n_k=n_k: e.matmul(
                    pt[0:128, off:off + width], lhs, rhs,
                    start=(dc == 0), stop=(dc == n_k - 1)),
                    waits=w0, inc=(c_t, 1) if dc == n_k - 1 else None)
            return j, pt, off, c_t.n

    wq_ok = load_weight_slab(C, C.din["wq"], 0)
    for blk in range(NBLK):
        par = blk % 2
        w = [(c_t, xr_done[blk - 1])] if blk >= 1 else [(c_a, wq_ok)]
        if blk >= 2:
            w.append((c_t, xr_done[blk - 2]))
        s_xr = C.s_xr[par]
        rld = P.dma(SY, xnT[:, par, :, 0:128],
                    C.xnt_sp[blk, :, :, 0:128].rearrange("c p n -> p c n"),
                    s_xr, waits=w)
        qc = own_pos(blk)
        for p8 in range(8):
            e0 = [(s_xr, rld)] + ([(c_a, wq_ok)] if blk == 0 and p8 == 0 else [])
            j, pt, off, tmm = mmgroup(
                lambda dc, p8=p8: C.wbig[:, 0, dc, 128 * p8:128 * p8 + 128],
                lambda dc, par=par: xnT[:, par, dc, 0:128], 8, 128, e0)
            a_ev = P.op(V, lambda e, p8=p8, pt=pt, off=off, qc=qc: e.tensor_scalar_add(
                QT[:, p8, qc:qc + 128], pt[0:128, off:off + 128],
                C.bqk[:, p8:p8 + 1]),
                waits=[(c_t, tmm)], inc=(c_v, 1))
            msl.release(j, (c_v, a_ev))
        xr_done[blk] = c_t.n


def emit_ph0c(C):
    P = C.P
    T, V, A, G, SY = "tensor", "vector", "scalar", "gpsimd", "sync"
    s_w, s_sp2, c_t, c_v, c_a = C.s_w, C.s_sp2, C.c_t, C.c_v, C.c_a
    xnT, evst, hidT, selT = C.xnT, C.evst, C.hidT, C.selT

    for f in range(3):
        b2ld = P.dma(SY, C.b2f32, C.din["b2c"][f], s_w, waits=[(c_a, c_a.n)])
        P.op(A, lambda e, f=f: e.activation(out=C.b2bf[:, f, :], in_=C.b2f32,
                                            func=AF.Copy),
             waits=[(s_w, b2ld)], inc=(c_a, 1))

    msl = Slots([(C.pS1, 0), (C.pS1, 512), (C.pS2, 0), (C.pS2, 512)])
    esl = Slots([0, 1, 2, 3])
    rld_hist = []
    prev_block_t = 0

    for f in range(3):
        load_weight_slab(C, C.din["w1c"][f], 0)
        w_ok = load_weight_slab(C, C.din["w2c"][f], 1)
        N = 128 if f == 0 else 512

        for blk in range(NBLK):
            par = blk % 2
            w = []
            if len(rld_hist) >= 2:
                w.append((c_t, rld_hist[-2][1]))
            if f == 0 and blk == 0:
                w.append((c_t, C.c_t.n))
            s_xr = C.s_xr[par]
            rld = P.dma(SY, xnT[:, par, :, :], C.xnt_sp[blk].rearrange("c p n -> p c n"),
                        s_xr, waits=w)

            for tt in range(4):
                w0 = [(c_a, c_a.n), (c_v, c_v.n)] if tt == 0 else []
                P.op(T, lambda e, f=f, blk=blk, tt=tt: e.transpose(
                    C.pX2[0:1, 128 * tt:128 * tt + 128],
                    C.leafall[:, f, 4 * blk + tt:4 * blk + tt + 1], C.idf32[...]),
                    waits=w0, inc=(c_t, 1))
            t_lt = c_t.n
            a_lf = P.op(A, lambda e: e.activation(out=C.lrow_f, in_=C.pX2[0:1, 0:512],
                                                  func=AF.Copy),
                        waits=[(c_t, t_lt)], inc=(c_a, 1))
            a_lb = P.op(A, lambda e: e.activation(out=C.lrow_b, in_=C.pX2[0:1, 0:512],
                                                  func=AF.Copy), inc=(c_a, 1))
            t_s16 = P.op(T, lambda e: e.matmul(C.pX1[0:16, 0:512], C.onesbf[0:1, 0:16],
                                               C.lrow_b, start=True, stop=True),
                         waits=[(c_a, a_lb), (c_v, c_v.n)], inc=(c_t, 1))
            v_sel = P.op(V, lambda e: e.tensor_scalar(
                out=selT[...], in0=C.pX1[0:16, 0:512], scalar1=C.iota16[...],
                scalar2=None, op0=ALU.is_equal),
                waits=[(c_t, t_s16)], inc=(c_v, 1))
            t_b128 = P.op(T, lambda e: e.matmul(C.pX2[0:128, 0:512], C.onesbf[0:1, 0:128],
                                                C.lrow_b, start=True, stop=True),
                          waits=[(c_a, a_lb), (c_v, c_v.n)], inc=(c_t, 1))

            hid_v = 0
            for hc in range(8):
                j, (pt, off), frees = msl.acquire()
                w0 = [(s_xr, rld)] + frees
                if hc == 0:
                    w0.append((c_a, w_ok))
                for dc in range(8):
                    P.op(T, lambda e, hc=hc, dc=dc, pt=pt, off=off, par=par, N=N: e.matmul(
                        pt[0:128, off:off + N], C.wbig[:, 0, dc, 128 * hc:128 * hc + 128],
                        xnT[:, par, dc, 0:N], start=(dc == 0), stop=(dc == 7)),
                        waits=w0 if dc == 0 else [], inc=(c_t, 1))
                tmm = c_t.n
                a_h = P.op(A, lambda e, hc=hc, pt=pt, off=off, f=f, N=N: e.activation(
                    out=hidT[:, hc, 0:N], in_=pt[0:128, off:off + N],
                    func=AF.Relu, bias=C.b1c[:, f, hc:hc + 1]),
                    waits=[(c_t, tmm), (c_t, prev_block_t)], inc=(c_a, 1))
                hid_v = P.op(V, lambda e, hc=hc, N=N: e.scalar_tensor_tensor(
                    out=hidT[:, hc, 0:N], in0=C.pX2[0:128, 0:N],
                    scalar=C.hidiota[:, hc:hc + 1], in1=hidT[:, hc, 0:N],
                    op0=ALU.is_equal, op1=ALU.mult),
                    waits=[(c_a, a_h), (c_t, t_b128)], inc=(c_v, 1))
                msl.release(j, (c_v, hid_v))

            if f == 0:
                qc = own_pos(blk)
                for dc in range(8):
                    j, (pt, off), frees = msl.acquire()
                    w0 = [(c_v, hid_v)] + frees
                    for hc in range(8):
                        P.op(T, lambda e, dc=dc, hc=hc, pt=pt, off=off: e.matmul(
                            pt[0:128, off:off + 128],
                            C.wbig[:, 1, hc, 128 * dc:128 * dc + 128],
                            hidT[:, hc, 0:128], start=(hc == 0), stop=False),
                            waits=w0 if hc == 0 else [], inc=(c_t, 1))
                    P.op(T, lambda e, dc=dc, f=f, pt=pt, off=off: e.matmul(
                        pt[0:128, off:off + 128],
                        C.b2bf[0:16, f, 128 * dc:128 * dc + 128],
                        selT[0:16, 0:128], start=False, stop=True),
                        waits=[(c_v, v_sel)], inc=(c_t, 1))
                    tmm = c_t.n
                    je, es, efree = esl.acquire()
                    a_ev = P.op(A, lambda e, pt=pt, off=off, es=es: e.activation(
                        out=evst[:, es, 0:128], in_=pt[0:128, off:off + 128],
                        func=AF.Copy), waits=[(c_t, tmm)] + efree, inc=(c_a, 1))
                    msl.release(j, (c_a, a_ev))
                    P.dma(G, C.fqt_sp[dc][:, qc:qc + 128], evst[:, es, 0:128],
                          s_sp2, waits=[(c_a, a_ev)])
                    esl.release(je, (s_sp2, s_sp2.n))
            elif f == 1:
                for dc in range(8):
                    j, (pt, off), frees = msl.acquire()
                    w0 = [(c_v, hid_v)] + frees
                    for hc in range(8):
                        P.op(T, lambda e, dc=dc, hc=hc, pt=pt, off=off: e.matmul(
                            pt[0:128, off:off + 512],
                            C.wbig[:, 1, hc, 128 * dc:128 * dc + 128],
                            hidT[:, hc, 0:512], start=(hc == 0), stop=False),
                            waits=w0 if hc == 0 else [], inc=(c_t, 1))
                    P.op(T, lambda e, dc=dc, f=f, pt=pt, off=off: e.matmul(
                        pt[0:128, off:off + 512],
                        C.b2bf[0:16, f, 128 * dc:128 * dc + 128],
                        selT[...], start=False, stop=True),
                        waits=[(c_v, v_sel)], inc=(c_t, 1))
                    tmm = c_t.n
                    je, es, efree = esl.acquire()
                    a_ev = P.op(A, lambda e, pt=pt, off=off, es=es: e.activation(
                        out=evst[:, es, :], in_=pt[0:128, off:off + 512], func=AF.Copy),
                        waits=[(c_t, tmm)] + efree, inc=(c_a, 1))
                    msl.release(j, (c_a, a_ev))
                    P.dma(G, C.fkt_sp[dc][:, 512 * blk:512 * blk + 512],
                          evst[:, es, :], s_sp2, waits=[(c_a, a_ev)])
                    esl.release(je, (s_sp2, s_sp2.n))
            else:
                for tt in range(4):
                    for hf in range(2):
                        j, (pt, off), frees = msl.acquire()
                        w0 = [(c_v, hid_v)] + frees
                        for hc in range(8):
                            P.op(T, lambda e, tt=tt, hf=hf, hc=hc, pt=pt, off=off: e.matmul(
                                pt[0:128, off:off + 512],
                                hidT[:, hc, 128 * tt:128 * tt + 128],
                                C.wbig[:, 1, hc, 512 * hf:512 * hf + 512],
                                start=(hc == 0), stop=False),
                                waits=w0 if hc == 0 else [], inc=(c_t, 1))
                        P.op(T, lambda e, tt=tt, hf=hf, f=f, pt=pt, off=off: e.matmul(
                            pt[0:128, off:off + 512],
                            selT[0:16, 128 * tt:128 * tt + 128],
                            C.b2bf[0:16, f, 512 * hf:512 * hf + 512],
                            start=False, stop=True),
                            waits=[(c_v, v_sel)], inc=(c_t, 1))
                        tmm = c_t.n
                        je, es, efree = esl.acquire()
                        a_ev = P.op(A, lambda e, pt=pt, off=off, es=es: e.activation(
                            out=evst[:, es, :], in_=pt[0:128, off:off + 512],
                            func=AF.Copy), waits=[(c_t, tmm)] + efree, inc=(c_a, 1))
                        msl.release(j, (c_a, a_ev))
                        P.dma(G, C.fv_sp[4 * blk + tt][:, 512 * hf:512 * hf + 512],
                              evst[:, es, :], s_sp2, waits=[(c_a, a_ev)])
                        esl.release(je, (s_sp2, s_sp2.n))
            prev_block_t = c_t.n
            rld_hist.append((rld, c_t.n))
    C.ph0c_t = c_t.n
    C.ph0c_sp2 = s_sp2.n


def emit_ph1(C):
    P = C.P
    T, V, A, G, SY = "tensor", "vector", "scalar", "gpsimd", "sync"
    s_sp, c_t, c_v, c_a, c_g = C.s_sp, C.c_t, C.c_v, C.c_a, C.c_g
    QT, ktp, vaug, PT, mxs, mhT = C.QT, C.ktp, C.vaug, C.PT, C.mxs, C.mhT
    stats, sc1, biasq, lnscr = C.stats, C.sc1, C.biasq, C.lnscr

    pair_done = {}
    pt_guard = {}
    st = {"pa_free": 0}
    pools = [Slots([(C.pS1, 0), (C.pS1, 512), (C.pX1, 0)]),
             Slots([(C.pS2, 0), (C.pS2, 512), (C.pX2, 0)])]

    def Nof(c, qb):
        over = c - (16 * qb + 3)
        return 512 if over <= 0 else 512 - 128 * ((over + 3) // 4)

    def emit_p1_chunk(c, qb, p8, kpar, kld, red_v):
        for hh in range(2):
            j, (pt, off), frees = pools[hh].acquire()
            N = Nof(c, qb)
            w0 = [(C.s_mk[kpar], kld)] + frees
            if c == 0 and p8 == 0 and qb == 0:
                w0.append((c_t, getattr(C, "ph0c_t", 0)))
            mm = P.op(T, lambda e, c=c, hh=hh, pt=pt, off=off, N=N,
                      kpar=kpar, p8=p8, qb=qb: e.matmul(
                pt[0:128, off:off + N],
                ktp[64 * hh:64 * hh + 64, kpar, 128 * c:128 * c + 128],
                QT[64 * hh:64 * hh + 64, p8, 512 * qb:512 * qb + N],
                start=True, stop=True, tile_position=(64 * hh, 0)),
                waits=w0, inc=(c_t, 1))
            red = P.op(V, lambda e, c=c, hh=hh, pt=pt, off=off, N=N: e.reduce_max(
                mxs[:, hh, c:c + 1], pt[0:128, off:off + N], axis=AX),
                waits=[(c_t, mm)], inc=(c_v, 1))
            red_v[(c, hh)] = red
            pools[hh].release(j, [(c_v, red)])

    def emit_chain(qb, Cn, red_v):
        qp = qb % 2
        j, (pt, off), frees = pools[0].acquire()
        v_m = P.op(V, lambda e, Cn=Cn: e.reduce_max(
            stats[:, 40:42], mxs[:, 0:2, 0:Cn], axis=AX),
            waits=[(c_v, red_v[(Cn - 1, 1)])], inc=(c_v, 1))
        t_tr = P.op(T, lambda e, pt=pt, off=off: e.transpose(
            pt[0:2, off:off + 128], stats[:, 40:42], C.idf32[...]),
            waits=[(c_v, v_m)] + frees, inc=(c_t, 1))
        v_s = P.op(V, lambda e, pt=pt, off=off: e.reduce_max(
            stats[0:2, 42:43], pt[0:2, off:off + 128], axis=AX),
            waits=[(c_t, t_tr)], inc=(c_v, 1))
        t_t2 = P.op(T, lambda e, pt=pt, off=off: e.transpose(
            pt[0:1, off + 128:off + 130], stats[0:2, 42:43], C.idf32[0:2, 0:2]),
            waits=[(c_v, v_s)], inc=(c_t, 1))
        a_cp = P.op(A, lambda e, pt=pt, off=off: e.activation(
            out=sc1[0:1, 0:2], in_=pt[0:1, off + 128:off + 130], func=AF.Copy),
            waits=[(c_t, t_t2)], inc=(c_a, 1))
        t_b = P.op(T, lambda e, pt=pt, off=off: e.matmul(
            pt[0:128, off + 256:off + 258], C.onesP[0:1, 0:128],
            sc1[0:1, 0:2], start=True, stop=True),
            waits=[(c_a, a_cp)], inc=(c_t, 1))
        a_b = P.op(A, lambda e, qp=qp, pt=pt, off=off: e.activation(
            out=biasq[:, qp, 0:2], in_=pt[0:128, off + 256:off + 258],
            func=AF.Copy, scale=-0.125, bias=70.0),
            waits=[(c_t, t_b)], inc=(c_a, 1))
        pools[0].release(j, [(c_a, a_b)])
        return {0: a_b, 1: a_b}

    def emit_p2_pv(c, qb, Cn, exp_a, pv_t, vld):
        for hh in range(2):
            N = Nof(c, qb)
            dep = [exp_a[(c, hh)]]
            if c == 0:
                dep.append((c_v, st["pa_free"]))
                dep.append((C.s_mv, vld))
            pv = P.op(T, lambda e, c=c, hh=hh, N=N, Cn=Cn: e.matmul(
                C.pA[0:65, 512 * hh:512 * hh + N], vaug[:, c, 65 * hh:65 * hh + 65],
                PT[:, hh, c % 2, 0:N], start=(c == 0), stop=(c == Cn - 1),
                skip_group_check=True),
                waits=dep, inc=(c_t, 1))
            pv_t[(c, hh)] = pv
            pt_guard[(hh, c % 2)] = pv

    def emit_p2_chunk(c, qb, p8, kpar, Cn, bias_a, exp_a, pv_t, vld):
        qp = qb % 2
        for hh in range(2):
            j, (pt, off), frees = pools[hh].acquire()
            N = Nof(c, qb)
            w0 = [(c_a, bias_a[hh])] + frees
            mm = P.op(T, lambda e, c=c, hh=hh, pt=pt, off=off, N=N,
                      kpar=kpar, p8=p8, qb=qb: e.matmul(
                pt[0:128, off:off + N],
                ktp[64 * hh:64 * hh + 64, kpar, 128 * c:128 * c + 128],
                QT[64 * hh:64 * hh + 64, p8, 512 * qb:512 * qb + N],
                start=True, stop=True, tile_position=(64 * hh, 0)),
                waits=w0, inc=(c_t, 1))
            extra = []
            if c % 4 == 0 and 4 * qb <= c // 4 <= 4 * qb + 3:
                dd = (4 * qb + 3) - c // 4
                v_msk = P.op(V, lambda e, pt=pt, off=off, dd=dd: e.tensor_tensor(
                    pt[0:128, off + 128 * dd:off + 128 * dd + 128],
                    pt[0:128, off + 128 * dd:off + 128 * dd + 128],
                    C.trimask[...], ALU.add),
                    waits=[(c_t, mm)], inc=(c_v, 1))
                extra = [(c_v, v_msk)]
            slotg = pt_guard.get((hh, c % 2))
            exp = P.op(A, lambda e, c=c, hh=hh, pt=pt, off=off, N=N, qp=qp: e.activation(
                out=PT[:, hh, c % 2, 0:N], in_=pt[0:128, off:off + N],
                func=AF.Exp, scale=0.125, bias=biasq[:, qp, hh:hh + 1]),
                waits=[(c_t, mm)] + extra +
                      ([(c_t, slotg)] if slotg else []), inc=(c_a, 1))
            pools[hh].release(j, [(c_a, exp)])
            dep = (c_a, exp)
            if c % 4 != 0 and 4 * qb <= c // 4 <= 4 * qb + 3:
                mmod = c % 4
                dd = (4 * qb + 3) - c // 4
                g_d = P.op(V, lambda e, c=c, hh=hh, dd=dd, mmod=mmod: e.tensor_scalar_mul(
                    PT[:, hh, c % 2, 128 * dd:128 * dd + 128],
                    PT[:, hh, c % 2, 128 * dd:128 * dd + 128],
                    C.allowB[:, mmod:mmod + 1]),
                    waits=[dep], inc=(c_v, 1))
                dep = (c_v, g_d)
            exp_a[(c, hh)] = dep
        if c >= 1:
            emit_p2_pv(c - 1, qb, Cn, exp_a, pv_t, vld)
        if c == Cn - 1:
            emit_p2_pv(c, qb, Cn, exp_a, pv_t, vld)

    def emit_norm(qb, p8, Cn, pv_t):
        for hh in range(2):
            h = 2 * p8 + hh
            j, (pt, off), frees = pools[1].acquire()
            v_l = P.op(V, lambda e, hh=hh: e.tensor_scalar_add(
                lnscr[64:65, 0:512], C.pA[64:65, 512 * hh:512 * hh + 512], 1e-30),
                waits=[(c_t, pv_t[(Cn - 1, 1)])], inc=(c_v, 1))
            v_r = P.op(V, lambda e: e.reciprocal(
                lnscr[64:65, 512:1024], lnscr[64:65, 0:512]),
                waits=[(c_v, v_l)], inc=(c_v, 1))
            t_b = P.op(T, lambda e, pt=pt, off=off: e.matmul(
                pt[0:64, off:off + 512], C.onesP[64:65, 0:64],
                lnscr[64:65, 512:1024],
                start=True, stop=True, tile_position=(64, 0)),
                waits=[(c_v, v_r)] + frees, inc=(c_t, 1))
            a_rb = P.op(A, lambda e, pt=pt, off=off: e.activation(
                out=C.reclB, in_=pt[0:64, off:off + 512], func=AF.Copy),
                waits=[(c_t, t_b)], inc=(c_a, 1))
            st["pa_free"] = P.op(V, lambda e, hh=hh, h=h, qb=qb: e.tensor_tensor(
                mhT[64 * (h % 2):64 * (h % 2) + 64, h // 2, 512 * qb:512 * qb + 512],
                C.pA[0:64, 512 * hh:512 * hh + 512], C.reclB, ALU.mult),
                waits=[(c_a, a_rb)], inc=(c_v, 1))
            pools[1].release(j, [(c_a, a_rb)])

    for p8 in range(8):
        kpar = p8 % 2
        w = [(s_sp, C.ph0b_sp)] if p8 <= 1 else []
        if p8 >= 2:
            w.append((c_t, pair_done[p8 - 2]))
        kld = P.dma(G, ktp[:, kpar, :], C.kt_sp[p8], C.s_mk[kpar], waits=w)
        wv = [(s_sp, C.ph0b_sp)]
        if p8 >= 1:
            wv.append((c_t, pair_done[p8 - 1]))
        vld = P.dma(G, C.vaug[:, 0:64, 0:130], C.vp_sp[p8], C.s_mv, waits=wv)

        prev = None
        for qb in range(4):
            Cn1 = 16 * qb + 16
            red_v = {}
            for c in range(Cn1):
                emit_p1_chunk(c, qb, p8, kpar, kld, red_v)
                if prev is not None and c < prev[1]:
                    emit_p2_chunk(c, prev[0], p8, kpar, prev[1], prev[2],
                                  prev[3], prev[4], vld)
            if prev is not None:
                emit_norm(prev[0], p8, prev[1], prev[4])
            bias_a = emit_chain(qb, Cn1, red_v)
            prev = (qb, Cn1, bias_a, {}, {})
        for c in range(prev[1]):
            emit_p2_chunk(c, prev[0], p8, kpar, prev[1], prev[2], prev[3],
                          prev[4], vld)
        emit_norm(prev[0], p8, prev[1], prev[4])
        pair_done[p8] = c_t.n

    if C.mh_dbg is not None:
        for dcq in range(8):
            P.dma(SY, C.mh_dbg[dcq], mhT[:, dcq, :], C.s_out,
                  waits=[(c_v, c_v.n)])
    C.ph1_v = c_v.n
    C.ph1_t = c_t.n


def emit_ph2(C):
    P = C.P
    T, V, A, G, SY = "tensor", "vector", "scalar", "gpsimd", "sync"
    s_w, s_sp2, s_out = C.s_w, C.s_sp2, C.s_out
    c_t, c_v, c_a, c_g = C.c_t, C.c_v, C.c_a, C.c_g
    mhT, qTb, ktf, vf = C.mhT, C.qTb, C.ktf, C.vf
    P2T, A2, l2, fsc, fB, m2t = C.P2T, C.A2, C.l2, C.fsc, C.fB, C.m2t
    stats, outb, xtile = C.stats, C.outb, C.xtile

    wo_ok = load_weight_slab(C, C.din["wo"], 0, extra_w=[(c_t, C.ph1_t)])
    if C.wo_dbg is not None:
        P.dma(SY, C.wo_dbg, C.wbig[:, 0, :, :], C.s_out, waits=[(c_a, wo_ok)])
        P.dma(SY, C.bo_dbg, C.boB[0:128, :], C.s_out, waits=[(c_a, wo_ok)])

    sst = C.qTb
    ev_half = {0: C.ph1_v, 1: C.ph1_v}
    ev_prev = C.ph1_v
    for ot in range(16):
        sj = ot % 4
        for hf in range(2):
            for dc in range(8):
                w0 = [(c_a, wo_ok), (c_v, ev_half[hf])] if dc == 0 else []
                P.op(T, lambda e, ot=ot, hf=hf, dc=dc: e.matmul(
                    C.pA[0:128, 512 * hf:512 * hf + 512],
                    mhT[:, dc, 128 * ot:128 * ot + 128],
                    C.wbig[:, 0, dc, 512 * hf:512 * hf + 512],
                    start=(dc == 0), stop=(dc == 7), skip_group_check=True),
                    waits=w0, inc=(c_t, 1) if dc == 7 else None)
            tmm = c_t.n
            w0 = [(c_t, tmm)]
            if ot >= 4 and hf == 0:
                w0.append((C.s_ss, C.s_ss.n))
            ev_half[hf] = P.op(V, lambda e, ot=ot, hf=hf, sj=sj: e.tensor_tensor(
                sst[:, 2 * sj + hf, :],
                C.pA[0:128, 512 * hf:512 * hf + 512],
                C.boB[:, 512 * hf:512 * hf + 512], ALU.add),
                waits=w0, inc=(c_v, 1))
            ev_prev = ev_half[hf]
        P.dma(G, C.s_sp3[ot], sst[:, 2 * sj:2 * sj + 2, :].rearrange("p a n -> p (a n)"),
              C.s_ss, waits=[(c_v, ev_prev)])
    wo_v = c_v.n

    slots2 = [(C.pS1, 0), (C.pS1, 512), (C.pS2, 0), (C.pS2, 512)]
    exp_prev = {}
    sup_hist = {}
    st2 = {"mrg": wo_v, "pv_done": 0}

    def emit_qk2_group(qb, sup, spar, kld, qld, mz, cc):
        gs = 16 * qb + sup
        pt, off = slots2[cc]
        w0 = [(C.s_fk[spar], kld), (C.s_q, qld)]
        if exp_prev.get(cc):
            w0.append((c_a, exp_prev[cc]))
        if gs == 0 and cc == 0:
            w0.append((c_g, mz))
            w0.append((c_v, c_v.n))
        for dc in range(8):
            P.op(T, lambda e, cc=cc, dc=dc, pt=pt, off=off, spar=spar: e.matmul(
                pt[0:128, off:off + 512],
                ktf[:, spar, dc, 128 * cc:128 * cc + 128],
                qTb[:, dc, 0:512], start=(dc == 0), stop=(dc == 7)),
                waits=w0 if dc == 0 else [], inc=(c_t, 1) if dc == 7 else None)
        tqk = c_t.n
        P.op(V, lambda e, cc=cc, pt=pt, off=off: e.reduce_max(
            m2t[:, cc:cc + 1], pt[0:128, off:off + 512], axis=AX),
            waits=[(c_t, tqk)], inc=(c_v, 1))

    def emit_chain2(sup, mz):
        v_m = P.op(V, lambda e: e.reduce_max(stats[:, 44:45], m2t[...], axis=AX),
                   inc=(c_v, 1))
        t_tr = P.op(T, lambda e: e.transpose(C.pX1[0:1, 0:128], stats[:, 44:45],
                                             C.idf32[...]),
                    waits=[(c_v, v_m), (c_v, c_v.n), (c_a, c_a.n)], inc=(c_t, 1))
        v_s = P.op(V, lambda e: e.reduce_max(fsc[0:1, 1:2], C.pX1[0:1, 0:128],
                                             axis=AX),
                   waits=[(c_t, t_tr), (c_g, mz)], inc=(c_v, 1))
        v_mx = P.op(V, lambda e: e.tensor_tensor(fsc[0:1, 5:6], fsc[0:1, 0:1],
                                                 fsc[0:1, 1:2], ALU.max),
                    waits=[(c_v, v_s)], inc=(c_v, 1))
        v_df = P.op(V, lambda e: e.tensor_tensor(fsc[0:1, 3:4], fsc[0:1, 0:1],
                                                 fsc[0:1, 5:6], ALU.subtract),
                    waits=[(c_v, v_mx)], inc=(c_v, 1))
        a_f = P.op(A, lambda e: e.activation(out=fsc[0:1, 4:5], in_=fsc[0:1, 3:4],
                                             func=AF.Exp, scale=1.0 / 32),
                   waits=[(c_v, v_df), (c_v, st2["mrg"])], inc=(c_a, 1))
        v_cp = P.op(V, lambda e: e.tensor_copy(fsc[0:1, 0:1], fsc[0:1, 5:6]),
                    waits=[(c_a, a_f)], inc=(c_v, 1))
        t_b = P.op(T, lambda e: e.matmul(C.pX1[0:128, 192:194], C.onesP[0:1, 0:128],
                                         fsc[0:1, 4:6], start=True, stop=True),
                   waits=[(c_a, a_f), (c_v, c_v.n)], inc=(c_t, 1))
        a_fB = P.op(A, lambda e: e.activation(out=fB[:, 0:1], in_=C.pX1[0:128, 192:193],
                                              func=AF.Copy),
                    waits=[(c_t, t_b), (c_v, st2["mrg"])], inc=(c_a, 1))
        a_b2 = P.op(A, lambda e: e.activation(out=fB[:, 1:2], in_=C.pX1[0:128, 193:194],
                                              func=AF.Copy, scale=-1.0 / 32),
                    waits=[(c_t, t_b)], inc=(c_a, 1))
        return a_b2

    def emit_exps2(sup, a_b2):
        for cc in range(4):
            pt, off = slots2[cc]
            exp_prev[cc] = P.op(A, lambda e, cc=cc, pt=pt, off=off: e.activation(
                out=P2T[:, cc, :], in_=pt[0:128, off:off + 512],
                func=AF.Exp, scale=1.0 / 32, bias=fB[:, 1:2]),
                waits=[(c_a, a_b2)] + ([(c_t, st2["pv_done"])] if cc == 0 else []),
                inc=(c_a, 1))
        return c_a.n

    def emit_pv2_group(qb, sup, spar, vld, exps, qt):
        lb = C.pX1 if qt % 2 == 0 else C.pX2
        w0 = [(c_a, exps), (C.s_fv[spar], vld), (c_v, st2["mrg"])]
        for cc in range(4):
            for hf in range(2):
                P.op(T, lambda e, qt=qt, cc=cc, hf=hf, spar=spar: e.matmul(
                    C.pA[0:128, 512 * hf:512 * hf + 512],
                    P2T[:, cc, 128 * qt:128 * qt + 128],
                    vf[:, spar, cc, 512 * hf:512 * hf + 512],
                    start=(cc == 0), stop=(cc == 3), skip_group_check=True),
                    waits=w0 if (cc == 0 and hf == 0) else [])
            P.op(T, lambda e, qt=qt, cc=cc, lb=lb: e.matmul(
                lb[0:128, 300 + qt:301 + qt],
                P2T[:, cc, 128 * qt:128 * qt + 128],
                C.onescolb[...], start=(cc == 0), stop=(cc == 3),
                skip_group_check=True), inc=(c_t, 1) if cc == 3 else None)
        tpv = c_t.n
        P.op(V, lambda e, qt=qt: e.scalar_tensor_tensor(
            out=A2[:, qt, :], in0=A2[:, qt, :], scalar=fB[:, 0:1],
            in1=C.pA[0:128, 0:1024], op0=ALU.mult, op1=ALU.add),
            waits=[(c_t, tpv)], inc=(c_v, 1))
        st2["mrg"] = P.op(V, lambda e, qt=qt, lb=lb: e.scalar_tensor_tensor(
            out=l2[:, qt:qt + 1], in0=l2[:, qt:qt + 1], scalar=fB[:, 0:1],
            in1=lb[0:128, 300 + qt:301 + qt], op0=ALU.mult, op1=ALU.add),
            inc=(c_v, 1))
        if qt == 3:
            st2["pv_done"] = c_t.n

    def emit_pv2(qb, sup, spar, vld, exps):
        for qt in range(4):
            emit_pv2_group(qb, sup, spar, vld, exps, qt)

    for qb in range(4):
        qld = P.dma(SY, qTb[0:128, 0:8, 0:512],
                    C.fqt_sp[:, :, 512 * qb:512 * qb + 512].rearrange("c p n -> p c n"),
                    C.s_q, waits=[(s_sp2, C.ph0c_sp2), (C.s_ss, C.s_ss.n),
                                  (c_t, c_t.n)])
        P.op(G, lambda e: e.memset(A2[0:128, 0:4, 0:1024], 0.0),
             waits=[(c_v, c_v.n), (c_a, c_a.n)], inc=(c_g, 1))
        P.op(G, lambda e: e.memset(l2[:, 0:4], 0.0), inc=(c_g, 1))
        mz = P.op(G, lambda e: e.memset(fsc[0:1, 0:1], NEGBIG), inc=(c_g, 1))

        prev2 = None
        for sup in range(16):
            gs = 16 * qb + sup
            spar = sup % 2
            w = [] if gs >= 2 else [(s_sp2, C.ph0c_sp2)]
            if gs >= 2:
                w.append((c_t, sup_hist[gs - 2]))
            kld = P.dma(G, ktf[:, spar, :, :],
                        C.fkt_sp[:, :, 512 * sup:512 * sup + 512].rearrange("c p n -> p c n"),
                        C.s_fk[spar], waits=w)
            wv2 = list(w)
            if gs >= 1:
                wv2.append((c_t, sup_hist[gs - 1]))
            vld = P.dma(G, vf[:, spar, :, :],
                        C.fv_sp[4 * sup:4 * sup + 4].rearrange("c p n -> p c n"),
                        C.s_fv[spar], waits=wv2)
            for i in range(4):
                if prev2 is not None:
                    emit_pv2_group(qb, prev2[0], prev2[1], prev2[2], prev2[3], i)
                emit_qk2_group(qb, sup, spar, kld, qld, mz, i)
            a_b2 = emit_chain2(sup, mz)
            exps = emit_exps2(sup, a_b2)
            prev2 = (sup, spar, vld, exps)
            sup_hist[gs] = c_t.n
        emit_pv2(qb, prev2[0], prev2[1], prev2[2], prev2[3])
        sup_hist[16 * qb + 15] = c_t.n
        mrg_prev = st2["mrg"]

        for qt in range(4):
            i_own = 4 * qb + (3 - qt)
            v_l = P.op(V, lambda e, qt=qt: e.tensor_scalar_add(
                stats[:, 46:47], l2[:, qt:qt + 1], 1e-30),
                waits=[(c_v, mrg_prev)], inc=(c_v, 1))
            v_r = P.op(V, lambda e: e.reciprocal(stats[:, 47:48], stats[:, 46:47]),
                       waits=[(c_v, v_l)], inc=(c_v, 1))
            a_g = P.op(A, lambda e, qt=qt: e.activation(
                out=C.gsb[0:128, 0:1024], in_=A2[:, qt, :], func=AF.Silu,
                scale=stats[:, 47:48]),
                waits=[(c_v, v_r), (c_v, c_v.n)], inc=(c_a, 1))
            xld = P.dma(SY, xtile[:, qt % 2, :],
                        C.din["x"][512 * i_own:512 * i_own + 128, :], C.s_xl[qt % 2],
                        waits=[(c_v, c_v.n)])
            sld = P.dma(SY, C.sload[:, qt % 2, :], C.s_sp3[4 * qb + qt],
                        C.s_sl[qt % 2], waits=[(c_v, c_v.n)])
            v_sg = P.op(V, lambda e, qb=qb, qt=qt: e.scalar_tensor_tensor(
                out=outb[:, qt % 2, :], in0=C.sload[:, qt % 2, :], scalar=1.0,
                in1=C.gsb[0:128, 0:1024], op0=ALU.mult, op1=ALU.mult),
                waits=[(c_a, a_g), (C.s_sl[qt % 2], sld)], inc=(c_v, 1))
            v_o = P.op(V, lambda e, qt=qt: e.tensor_tensor(
                outb[:, qt % 2, :], outb[:, qt % 2, :], xtile[:, qt % 2, :],
                ALU.add), waits=[(C.s_xl[qt % 2], xld), (c_v, v_sg)], inc=(c_v, 1))
            P.dma(SY, C.out_d[512 * qb + 128 * qt:512 * qb + 128 * qt + 128, :],
                  outb[:, qt % 2, :], s_out, waits=[(c_v, v_o)])



_NC_CACHE = {}


def kernel(**inputs):
    if "nc" not in _NC_CACHE:
        _NC_CACHE["nc"] = build_nc()
    nc = _NC_CACHE["nc"]
    in_maps = host_inputs(inputs)
    res = run_bass_kernel_spmd(nc, in_maps, core_ids=list(range(8)))
    return host_output(res.results)


if __name__ == "__main__":
    pass


# version 16
# speedup vs baseline: 1.0077x; 1.0077x over previous
import contextlib
import numpy as np
import ml_dtypes
import concourse.bass as bass
import concourse.mybir as mybir
from concourse.bass_utils import run_bass_kernel_spmd

F32 = mybir.dt.float32
BF16 = mybir.dt.bfloat16
AF = mybir.ActivationFunctionType
ALU = mybir.AluOpType
AX = mybir.AxisListType.X
BF = ml_dtypes.bfloat16

B, S, D = 2, 8192, 1024
H, NP = 16, 8
NBLK = 16
NEG = -1.0e30
NEGBIG = -3.0e38

STAGE = 4
DEBUG_SPILLS = False


def local_tile_order(k):
    order = []
    for t in range(NBLK):
        order.append(4 * t + k)
        order.extend(4 * t + j for j in range(4) if j != k)
    return order


def own_pos(i):
    qb, r = divmod(i, 4)
    return 512 * qb + 128 * (3 - r)


def host_inputs(full):
    xf = np.ascontiguousarray(np.asarray(full["L0_attn_residual"], np.float32))
    WQ = np.asarray(full["L0_attn_W_Q"], np.float32)
    WK = np.asarray(full["L0_attn_W_K"], np.float32)
    WV = np.asarray(full["L0_attn_W_V"], np.float32)
    bQ = np.asarray(full["L0_attn_b_Q"], np.float32)
    bK = np.asarray(full["L0_attn_b_K"], np.float32)
    bV = np.asarray(full["L0_attn_b_V"], np.float32)

    def pairmajor(W):
        return np.ascontiguousarray(W.transpose(1, 0, 2).reshape(D, D))

    bqk = np.zeros((128, 16), np.float32)
    for p in range(NP):
        bqk[:, p] = bQ[2 * p:2 * p + 2].reshape(128)
        bqk[:, 8 + p] = bK[2 * p:2 * p + 2].reshape(128)

    w1c = np.stack([np.asarray(full[f"L0_ffn_{f}_w1"], np.float32)
                    .transpose(1, 0, 2).reshape(D, D) for f in "qkv"])
    w2c = np.stack([np.asarray(full[f"L0_ffn_{f}_w2"], np.float32)
                    .reshape(D, D) for f in "qkv"])
    b1c = np.zeros((128, 3, 8), np.float32)
    for fi, f in enumerate("qkv"):
        b1c[:, fi, :] = np.asarray(full[f"L0_ffn_{f}_b1"], np.float32).reshape(8, 128).T
    b2c = np.stack([np.asarray(full[f"L0_ffn_{f}_b2"], np.float32) for f in "qkv"])
    nodew = np.concatenate([np.ascontiguousarray(
        np.asarray(full[f"L0_ffn_{f}_node_w"], np.float32).T) for f in "qkv"],
        axis=1)
    nodeb = np.concatenate([np.asarray(full[f"L0_ffn_{f}_node_b"], np.float32)
                            .reshape(1, 15) for f in "qkv"], axis=1)

    kk = np.arange(128)[:, None]
    rr = np.arange(128)[None, :]
    trimask = np.where(kk <= rr, 0.0, NEG).astype(np.float32)
    hidiota = np.zeros((128, 8), np.float32)
    for hc in range(8):
        hidiota[:, hc] = 2 * hc + np.arange(128) // 64

    common = dict(
        wq=pairmajor(WQ), wk=pairmajor(WK), wv=pairmajor(WV), bqk=bqk,
        bvrow=np.ascontiguousarray(bV.reshape(1, D)),
        wo=np.ascontiguousarray(np.asarray(full["L0_attn_W_O"], np.float32)),
        borow=np.ascontiguousarray(
            np.asarray(full["L0_attn_b_O"], np.float32).reshape(1, D)),
        w1c=w1c, w2c=w2c, b1c=b1c, b2c=b2c, nodew=nodew, nodeb=nodeb,
        idf32=np.eye(128, dtype=np.float32), trimask=trimask,
        onesrow=np.ones((1, 128), np.float32),
        onesrow_bf=np.ones((1, 128), BF),
        hidiota=hidiota,
        iota16=np.arange(16, dtype=np.float32).reshape(16, 1),
        onescol_bf=np.ones((128, 1), BF),
    )
    maps = []
    for c in range(8):
        b, k = c // 4, c % 4
        order = local_tile_order(k)
        x_all = np.concatenate([xf[b, 128 * g:128 * g + 128] for g in order], axis=0)
        others = [j for j in range(4) if j != k]
        allow = np.zeros((1, 4), np.float32)
        allow[0, 0] = 1.0
        for m in range(3):
            allow[0, m + 1] = 1.0 if others[m] < k else 0.0
        d = dict(common)
        d["x"] = np.ascontiguousarray(x_all)
        d["allow"] = allow
        maps.append(d)
    return maps


def host_output(results):
    out = np.zeros((B, S, D), np.float32)
    for c in range(8):
        b, k = c // 4, c % 4
        oc = np.asarray(results[c]["out"], np.float32)
        for i in range(NBLK):
            g = 4 * i + k
            out[b, 128 * g:128 * g + 128] = oc[own_pos(i):own_pos(i) + 128]
    return out



class Sem:
    __slots__ = ("h", "n", "name")

    def __init__(self, name):
        self.h = None
        self.n = 0
        self.name = name


class Prog:
    def __init__(self, nc):
        self.nc = nc
        self.ops = {k: [] for k in ("sync", "gpsimd", "tensor", "vector", "scalar")}
        self.sems = []

    def sem(self, name):
        s = Sem(name)
        self.sems.append(s)
        return s

    def op(self, eng, fn, waits=(), inc=None):
        incs = []
        ret = None
        if inc is not None:
            s, a = inc
            s.n += a
            incs.append((s, a))
            ret = s.n
        ws = []
        for w in waits:
            if w is None:
                continue
            s, v = w
            if v and v > 0:
                ws.append((s, v))
        self.ops[eng].append((fn, ws, incs))
        return ret

    def dma(self, q, out, in_, sem, waits=()):
        return self.op(q, lambda e: e.dma_start(out=out, in_=in_), waits, (sem, 16))

    def emit(self):
        nc = self.nc
        with contextlib.ExitStack() as st:
            for s in self.sems:
                s.h = st.enter_context(nc.semaphore(s.name))
            with nc.Block() as block:
                def runner(name):
                    def run(e):
                        for (fn, waits, incs) in self.ops[name]:
                            for (s, v) in waits:
                                e.wait_ge(s.h, v)
                            ins = fn(e)
                            for (s, a) in incs:
                                ins.then_inc(s.h, a)
                    return run

                block.sync(runner("sync"))
                block.gpsimd(runner("gpsimd"))
                block.tensor(runner("tensor"))
                block.vector(runner("vector"))
                block.scalar(runner("scalar"))


class Slots:

    def __init__(self, slots):
        self.slots = slots
        self.free = [[] for _ in slots]
        self.i = 0

    def acquire(self):
        j = self.i % len(self.slots)
        self.i += 1
        return j, self.slots[j], list(self.free[j])

    def release(self, j, wait):
        self.free[j] = wait if isinstance(wait, list) else [wait]


class Ctx:
    pass


def build_nc():
    nc = bass.Bass("TRN2", target_bir_lowering=False)
    P = Prog(nc)
    C = Ctx()
    C.P = P

    din = {}
    for name, shape, dt in [
        ("x", [S, D], F32), ("wq", [D, D], F32), ("wk", [D, D], F32),
        ("wv", [D, D], F32), ("bqk", [128, 16], F32), ("bvrow", [1, D], F32),
        ("wo", [D, D], F32), ("borow", [1, D], F32),
        ("w1c", [3, D, D], F32), ("w2c", [3, D, D], F32),
        ("b1c", [128, 3, 8], F32), ("b2c", [3, 16, D], F32),
        ("nodew", [D, 45], F32), ("nodeb", [1, 45], F32),
        ("idf32", [128, 128], F32), ("trimask", [128, 128], F32),
        ("onesrow", [1, 128], F32), ("onesrow_bf", [1, 128], BF16),
        ("hidiota", [128, 8], F32), ("iota16", [16, 1], F32),
        ("onescol_bf", [128, 1], BF16), ("allow", [1, 4], F32),
    ]:
        din[name] = nc.dram_tensor(name, shape, dt, kind="ExternalInput").ap()
    C.din = din
    C.out_d = nc.dram_tensor("out", [2048, D], F32, kind="ExternalOutput").ap()

    kd = "ExternalOutput" if DEBUG_SPILLS else "Internal"
    C.xnt_sp = nc.dram_tensor("xnt_sp", [NBLK, 8, 128, 512], BF16, kind=kd).ap()
    C.kt_sp = nc.dram_tensor("kt_sp", [NP, 128, S], BF16, kind=kd).ap()
    C.vp_sp = nc.dram_tensor("vp_sp", [NP, 128, 64, 130], BF16, kind=kd).ap()
    C.fkt_sp = nc.dram_tensor("fkt_sp", [8, 128, S], BF16, kind=kd).ap()
    C.fv_sp = nc.dram_tensor("fv_sp", [64, 128, D], BF16, kind=kd).ap()
    C.fqt_sp = nc.dram_tensor("fqt_sp", [8, 128, 2048], BF16, kind=kd).ap()
    C.s_sp3 = nc.dram_tensor("s_sp3", [16, 128, D], BF16, kind=kd).ap()
    C.mh_dbg = nc.dram_tensor("mh_dbg", [8, 128, 2048], BF16, kind=kd).ap() \
        if DEBUG_SPILLS else None
    C.wo_dbg = nc.dram_tensor("wo_dbg", [128, 8, 1024], BF16, kind=kd).ap() \
        if DEBUG_SPILLS else None
    C.bo_dbg = nc.dram_tensor("bo_dbg", [128, D], F32, kind=kd).ap() \
        if DEBUG_SPILLS else None

    ctx = contextlib.ExitStack()
    sbuf = lambda n, sh, dt: ctx.enter_context(nc.sbuf_tensor(n, sh, dt))
    psum = lambda n, sh: ctx.enter_context(nc.psum_tensor(n, sh, F32))

    with ctx:
        C.pS1 = psum("pS1", [128, 1024])
        C.pS2 = psum("pS2", [128, 1024])
        C.pA = psum("pA", [128, 1024])
        C.pX1 = psum("pX1", [128, 512])
        C.pX2 = psum("pX2", [128, 512])

        C.idf32 = sbuf("c_idf32", [128, 128], F32)
        C.trimask = sbuf("c_trimask", [128, 128], F32)
        C.onesP = sbuf("c_onesP", [128, 128], F32)
        C.onesbf = sbuf("c_onesbf", [1, 128], BF16)
        C.hidiota = sbuf("c_hidiota", [128, 8], F32)
        C.iota16 = sbuf("c_iota16", [16, 1], F32)
        C.onescolb = sbuf("c_onescolb", [128, 1], BF16)
        C.bqk = sbuf("c_bqk", [128, 16], F32)
        C.b1c = sbuf("c_b1c", [128, 3, 8], F32)
        C.allowrow = sbuf("c_allow", [1, 4], F32)
        C.nodew_s = sbuf("c_nodew", [128, 8, 45], F32)
        C.nodebB = sbuf("c_nodebB", [128, 45], F32)
        C.nodeb_s = sbuf("c_nodeb", [1, 45], F32)

        C.bvB = sbuf("bvB", [128, D], F32)
        C.reclB = C.bvB[0:64, 0:512]
        C.boB = sbuf("boB", [128, D], F32)
        C.allowB = sbuf("allowB", [128, 4], F32)

        ar1 = sbuf("ar1", [128, 16384], BF16)
        ar2 = sbuf("ar2", [128, 16384], BF16)
        ar3 = sbuf("ar3", [128, 16384], BF16)
        ar4 = sbuf("ar4", [128, 8320], BF16)
        ar5 = sbuf("ar5", [128, 4096], F32)

        C.QT = ar1[...].rearrange("p (c n) -> p c n", c=8)
        C.qTb = ar1[:, 0:4096].rearrange("p (c n) -> p c n", c=8)
        C.P2T = ar1[:, 4096:6144].rearrange("p (c n) -> p c n", c=4)
        C.gsb = ar1[:, 6144:7168]

        C.xnT = ar2[:, 0:8192].rearrange("p (a c n) -> p a c n", a=2, c=8)
        C.hidT = ar2[:, 8192:12288].rearrange("p (c n) -> p c n", c=8)
        C.evst = ar2[:, 12288:14336].rearrange("p (c n) -> p c n", c=4)
        C.vevst = ar2[:, 14336:15424].rearrange("p (a n) -> p a n", a=2)
        C.mhT = ar2[...].rearrange("p (c n) -> p c n", c=8)

        C.ktp = ar3[...].rearrange("p (a n) -> p a n", a=2)
        C.ktf = ar3[:, 0:8192].rearrange("p (a c n) -> p a c n", a=2, c=8)
        C.vf = ar3[:, 8192:16384].rearrange("p (a c n) -> p a c n", a=2, c=4)

        C.vaug = ar4[...].rearrange("p (c n) -> p c n", c=64)

        C.xnTf = ar5[...].rearrange("p (c n) -> p c n", c=8)
        C.A2 = ar5[...].rearrange("p (c n) -> p c n", c=4)

        C.xtile = sbuf("xtile", [128, 2, 1024], F32)
        C.xnf = sbuf("xnf", [128, 2, 1024], F32)
        C.outb = C.xnf
        C.stats = sbuf("stats", [128, 48], F32)
        C.scores = sbuf("scores", [128, 4, 3, 15], F32)
        C.leafall = sbuf("leafall", [128, 3, 64], F32)
        C.dscr = sbuf("dscr", [128, 136], F32)

        C.wbig = sbuf("wbig", [128, 2, 8, 1024], BF16)
        C.wstage = sbuf("wstage", [128, 2, 1024], F32)
        C.bvrow_s = C.wstage[0:1, 0:1, 0:1024].rearrange("p a n -> p (a n)")
        C.borow_s = C.wstage[0:1, 1:2, 0:1024].rearrange("p a n -> p (a n)")
        C.lnscr = C.wstage[0:128, 1:2, 0:1024].rearrange("p a n -> p (a n)")
        C.b2bf = ar4[0:16, 0:3072].rearrange("p (f n) -> p f n", f=3)
        C.b2f32 = C.wstage[0:16, 0:1, 0:1024].rearrange("p a n -> p (a n)")
        pts = sbuf("pts", [128, 2048], BF16)
        C.PT = pts[:, 0:2048].rearrange("p (a b n) -> p a b n", a=2, b=2)
        C.onesq = pts[0:1, 0:512]
        C.bqflat = pts[0:1, 512:1536]
        C.onesk = pts[32:33, 0:512]
        C.bkflat = pts[32:33, 512:1536]
        C.mxs = sbuf("mxs", [128, 2, 64], F32)
        C.selT = sbuf("selT", [16, 512], BF16)
        C.lrow_f = C.xnf[0:1, 0:1, 0:512].rearrange("p a n -> p (a n)")
        C.lrow_b = pts[0:1, 0:512]
        C.sc1 = sbuf("sc1", [1, 16], F32)
        C.biasq = sbuf("biasq", [128, 2, 2], F32)
        C.l2 = sbuf("l2", [128, 4], F32)
        C.fsc = sbuf("fsc", [1, 8], F32)
        C.fB = sbuf("fB", [128, 2], F32)
        C.m2t = sbuf("m2t", [128, 4], F32)
        C.sload = pts[:, 0:2048].rearrange("p (a n) -> p a n", a=2)

        for s in ("s_in", "s_w", "s_sp", "s_sp2", "s_ss", "s_out", "s_mv", "s_q",
                  "c_t", "c_v", "c_a", "c_g"):
            setattr(C, s, P.sem(s))
        C.s_x = [P.sem(f"s_x{i}") for i in range(2)]
        C.s_xnt = [P.sem(f"s_xnt{i}") for i in range(2)]
        C.s_xr = [P.sem(f"s_xr{i}") for i in range(2)]
        C.s_ev = [P.sem(f"s_ev{i}") for i in range(4)]
        C.s_mk = [P.sem(f"s_mk{i}") for i in range(2)]
        C.s_fk = [P.sem(f"s_fk{i}") for i in range(2)]
        C.s_fv = [P.sem(f"s_fv{i}") for i in range(2)]
        C.s_xl = [P.sem(f"s_xl{i}") for i in range(2)]
        C.s_sl = [P.sem(f"s_sl{i}") for i in range(2)]

        emit_setup(C)
        emit_ph0a(C)
        if STAGE >= 2:
            emit_ph0b(C)
            emit_ph0c(C)
        if STAGE >= 3:
            emit_ph1(C)
        if STAGE >= 4:
            emit_ph2(C)

        P.op("gpsimd", lambda e: e.engine_nop(),
             waits=[(C.s_out, C.s_out.n)], inc=(C.c_g, 1))
        P.emit()
    return nc


def emit_setup(C):
    P, din = C.P, C.din
    s_in, c_t, c_a = C.s_in, C.c_t, C.c_a
    for nm, dst in [("idf32", C.idf32[...]), ("trimask", C.trimask[...]),
                    ("onesrow", C.onesP[0:1, :]), ("onesrow", C.onesP[64:65, :]),
                    ("onesrow_bf", C.onesbf[...]), ("hidiota", C.hidiota[...]),
                    ("iota16", C.iota16[...]), ("onescol_bf", C.onescolb[...]),
                    ("bqk", C.bqk[...]), ("b1c", C.b1c[...]), ("allow", C.allowrow[...]),
                    ("nodeb", C.nodeb_s[...]), ("bvrow", C.bvrow_s),
                    ("borow", C.borow_s)]:
        P.dma("sync", dst, din[nm], s_in)
    P.dma("sync", C.nodew_s[...],
          din["nodew"].rearrange("(c p) n -> p c n", p=128), s_in)
    C.consts_ok = s_in.n

    def mk_bcast(rowsl, width, dst, scale=1.0):
        t = P.op("tensor", lambda e: e.matmul(
            C.pX1[0:128, 0:width], C.onesP[0:1, 0:128], rowsl,
            start=True, stop=True),
            waits=[(s_in, C.consts_ok), (c_a, c_a.n)], inc=(c_t, 1))
        P.op("scalar", lambda e: e.activation(
            out=dst, in_=C.pX1[0:128, 0:width], func=AF.Copy, scale=scale),
            waits=[(c_t, t)], inc=(c_a, 1))

    for hf in range(2):
        mk_bcast(C.bvrow_s[0:1, 512 * hf:512 * hf + 512], 512,
                 C.bvB[0:128, 512 * hf:512 * hf + 512])
        mk_bcast(C.borow_s[0:1, 512 * hf:512 * hf + 512], 512,
                 C.boB[0:128, 512 * hf:512 * hf + 512])
    mk_bcast(C.allowrow[0:1, 0:4], 4, C.allowB[0:128, 0:4])
    mk_bcast(C.nodeb_s[0:1, 0:45], 45, C.nodebB[0:128, 0:45])
    bqkT = C.selT[0:16, 384:512]
    t_bq = P.op("tensor", lambda e: e.transpose(
        C.pX1[0:16, 0:128], C.bqk[...], C.idf32[...]),
        waits=[(s_in, C.consts_ok), (c_a, c_a.n)], inc=(c_t, 1))
    a_bq = P.op("scalar", lambda e: e.activation(
        out=bqkT, in_=C.pX1[0:16, 0:128], func=AF.Copy),
        waits=[(c_t, t_bq)], inc=(c_a, 1))
    for i in range(16):
        dst = (C.bqflat[0:1, 128 * i:128 * i + 128] if i < 8 else
               C.bkflat[0:1, 128 * (i - 8):128 * (i - 8) + 128])
        P.dma("sync", dst, bqkT[i:i + 1, 0:128], s_in,
              waits=[(c_a, a_bq)] if i == 0 else [])
    C.bflat_ok = s_in.n
    P.op("gpsimd", lambda e: e.memset(C.onesq, 1.0), inc=(C.c_g, 1))
    P.op("gpsimd", lambda e: e.memset(C.onesk, 1.0), inc=(C.c_g, 1))
    C.setup_t = C.c_t.n


def emit_ph0a(C):
    P = C.P
    T, V, A, G, SY = "tensor", "vector", "scalar", "gpsimd", "sync"
    s_in, s_sp, c_t, c_v, c_a = C.s_in, C.s_sp, C.c_t, C.c_v, C.c_a
    xnf, xtile, stats = C.xnf, C.xtile, C.stats
    xnT, xnTf, scores, leafall, dscr = C.xnT, C.xnTf, C.scores, C.leafall, C.dscr
    evst, dscr = C.evst, C.dscr

    w_ok = max(load_weight_slab(C, C.din["wk"], 0),
               load_weight_slab(C, C.din["wv"], 1))
    vev4 = C.vevst[:, :, 0:520].rearrange("p a (j y) -> p a j y", j=4)
    m_on = 0
    for sl in range(2):
        P.op(G, lambda e, sl=sl: e.memset(vev4[:, sl, :, 64:65], 1.0),
             waits=[(c_a, c_a.n)], inc=(C.c_g, 1))
        m_on = P.op(G, lambda e, sl=sl: e.memset(vev4[:, sl, :, 129:130], 1.0),
                    inc=(C.c_g, 1))

    x_t = C.din["x"].rearrange("(t p) d -> t p d", p=128)
    psl = Slots([(C.pS1, 0), (C.pS1, 512), (C.pS2, 0), (C.pS2, 512)])
    esl = Slots([0, 1, 2, 3])
    xn_done = {}
    tr_done = {}
    kv_t = {}

    def load_xtile(gt):
        w = []
        if gt >= 2:
            w.append((c_v, xn_done[gt - 2]))
        if gt == 0:
            w.append((s_in, C.consts_ok))
        P.dma(SY, xtile[:, gt % 2, :], x_t[gt], C.s_x[gt % 2], waits=w)

    def mmgroup(lhs_fn, rhs_fn, n_k, width, extra0):
        j, (pt, off), frees = psl.acquire()
        for dc in range(n_k):
            w0 = (extra0 + frees) if dc == 0 else []
            lhs, rhs = lhs_fn(dc), rhs_fn(dc)
            P.op(T, lambda e, lhs=lhs, rhs=rhs, dc=dc, pt=pt, off=off,
                 width=width, n_k=n_k: e.matmul(
                pt[0:128, off:off + width], lhs, rhs,
                start=(dc == 0), stop=(dc == n_k - 1)),
                waits=w0, inc=(c_t, 1) if dc == n_k - 1 else None)
        return j, pt, off, c_t.n

    load_xtile(0)
    load_xtile(1)
    kv_ready = {}

    def emit_kv_group(b, gi):
        parb = b % 2
        if gi < 8:
            p8 = gi
            e0 = [(c_a, kv_ready[b])]
            if b == 0 and p8 == 0:
                e0 += [(c_a, w_ok), (C.c_g, m_on), (C.s_in, C.bflat_ok)]
            j, pt, off, tmm = mmgroup(
                lambda dc, p8=p8: C.wbig[:, 0, dc, 128 * p8:128 * p8 + 128]
                if dc < 8 else C.bkflat[0:1, 128 * p8:128 * p8 + 128],
                lambda dc, parb=parb: xnT[:, parb, dc, 0:512]
                if dc < 8 else C.onesk[0:1, 0:512], 9, 512, e0)
            je, es, efree = esl.acquire()
            a_ev = P.op(A, lambda e, pt=pt, off=off, es=es: e.activation(
                out=evst[:, es, :], in_=pt[0:128, off:off + 512], func=AF.Copy),
                waits=[(c_t, tmm)] + efree, inc=(c_a, 1))
            psl.release(j, (c_a, a_ev))
            P.dma(G, C.kt_sp[p8][:, 512 * b:512 * b + 512], evst[:, es, :],
                  s_sp, waits=[(c_a, a_ev)])
            esl.release(je, (s_sp, s_sp.n))
        else:
            tt, hf = (gi - 8) // 2, (gi - 8) % 2
            j, pt, off, tmm = mmgroup(
                lambda dc, tt=tt, parb=parb: xnT[:, parb, dc, 128 * tt:128 * tt + 128]
                if dc < 8 else C.onesP[0:1, 0:128],
                lambda dc, hf=hf: C.wbig[:, 1, dc, 512 * hf:512 * hf + 512]
                if dc < 8 else C.bvB[0:1, 512 * hf:512 * hf + 512], 9, 512,
                [(c_a, kv_ready[b])])
            sl = hf
            v_ev = 0
            for hhx in range(2):
                v_ev = P.op(A, lambda e, pt=pt, off=off, sl=sl, hhx=hhx: e.activation(
                    out=vev4[:, sl, :, 65 * hhx:65 * hhx + 64],
                    in_=pt[0:128, off:off + 512].rearrange(
                        "p (j h x) -> p j h x", j=4, h=2)[:, :, hhx, :],
                    func=AF.Copy),
                    waits=[(c_t, tmm), (C.c_g, m_on), (s_sp, s_sp.n)] if hhx == 0 else [],
                    inc=(c_a, 1))
            psl.release(j, (c_a, v_ev))
            cch = 4 * b + tt
            P.dma(G, C.vp_sp[4 * hf:4 * hf + 4, :, cch, :].rearrange("j p y -> p j y"),
                  C.vevst[:, sl, 0:520].rearrange("p (j y) -> p j y", j=4),
                  s_sp, waits=[(c_a, v_ev)])
        if gi == 15:
            kv_t[b] = c_t.n

    for blk in range(NBLK):
        par = blk % 2
        rout = []
        q6h = {}
        for half in range(2):
            for tt in (2 * half, 2 * half + 1):
                gt = 4 * blk + tt
                c0 = 10 * tt
                t_x = 16 * (gt // 2 + 1)
                s_x_par = C.s_x[gt % 2]
                xa = xtile[:, gt % 2, :]
                w_ln = [(s_x_par, t_x)]
                if gt == 0:
                    w_ln.append((c_t, C.setup_t))
                if gt >= 2:
                    w_ln.append((c_t, tr_done[gt - 2]))
                P.op(A, lambda e, xa=xa, c0=c0, gt=gt: e.activation(
                    out=xnf[:, gt % 2, :], in_=xa, func=AF.Square,
                    accum_out=stats[:, c0 + 1:c0 + 2]),
                    waits=w_ln, inc=(c_a, 1))
                p2 = P.op(A, lambda e, xa=xa, c0=c0, gt=gt: e.activation(
                    out=xnf[:, gt % 2, :], in_=xa, func=AF.Copy,
                    accum_out=stats[:, c0:c0 + 1]), inc=(c_a, 1))
            b0 = 20 * half
            sv = lambda j, b0=b0: stats[:, b0 + j:b0 + j + 11:10].rearrange(
                "p (a b) -> p a b", b=1)
            q1 = P.op(V, lambda e, sv=sv: e.tensor_scalar_mul(sv(2), sv(0), 1.0 / D),
                      waits=[(c_a, p2)], inc=(c_v, 1))
            q2 = P.op(V, lambda e, sv=sv: e.tensor_scalar_mul(sv(3), sv(1), 1.0 / D),
                      waits=[(c_v, q1)], inc=(c_v, 1))
            q3 = P.op(V, lambda e, sv=sv: e.tensor_tensor(sv(4), sv(2), sv(2), ALU.mult),
                      waits=[(c_v, q2)], inc=(c_v, 1))
            q4 = P.op(V, lambda e, sv=sv: e.tensor_tensor(sv(5), sv(3), sv(4), ALU.subtract),
                      waits=[(c_v, q3)], inc=(c_v, 1))
            q5 = P.op(V, lambda e, sv=sv: e.tensor_scalar_add(sv(6), sv(5), 1e-5),
                      waits=[(c_v, q4)], inc=(c_v, 1))
            a_sq = P.op(A, lambda e, sv=sv: e.activation(out=sv(7), in_=sv(6), func=AF.Sqrt),
                        waits=[(c_v, q5)], inc=(c_a, 1))
            q6h[half] = P.op(V, lambda e, sv=sv: e.reciprocal(sv(8), sv(7)),
                             waits=[(c_a, a_sq)], inc=(c_v, 1))
            for tt in (2 * half, 2 * half + 1):
                gt = 4 * blk + tt
                c0 = 10 * tt
                xa = xtile[:, gt % 2, :]
                wv_ = [(c_t, tr_done[gt - 2])] if gt >= 2 else []
                v_xn = P.op(V, lambda e, xa=xa, gt=gt, c0=c0: e.tensor_scalar(
                    out=xnf[:, gt % 2, :], in0=xa,
                    scalar1=stats[:, c0 + 2:c0 + 3], scalar2=stats[:, c0 + 8:c0 + 9],
                    op0=ALU.subtract, op1=ALU.mult),
                    waits=wv_ + [(c_v, q6h[tt // 2])], inc=(c_v, 1))
                xn_done[gt] = v_xn
                if gt + 2 < 64:
                    load_xtile(gt + 2)
                for dc in range(8):
                    j, (pt, off), frees = psl.acquire()
                    t_tr = P.op(T, lambda e, gt=gt, dc=dc, pt=pt, off=off: e.transpose(
                        pt[0:128, off:off + 128],
                        xnf[:, gt % 2, 128 * dc:128 * dc + 128], C.idf32[...]),
                        waits=[(c_v, v_xn)] + frees, inc=(c_t, 1))
                    bw = [(c_t, t_tr)]
                    if blk >= 2 and dc == 0 and tt == 0:
                        bw.append((C.s_xnt[par], C.s_xnt[par].n))
                        bw.append((c_t, kv_t[blk - 2]))
                    P.op(A, lambda e, tt=tt, dc=dc, pt=pt, off=off, par=par: e.activation(
                        out=xnT[:, par, dc, 128 * tt:128 * tt + 128],
                        in_=pt[0:128, off:off + 128], func=AF.Copy),
                        waits=bw, inc=(c_a, 1))
                    a_f = P.op(A, lambda e, tt=tt, dc=dc, pt=pt, off=off: e.activation(
                        out=xnTf[:, dc, 128 * tt:128 * tt + 128],
                        in_=pt[0:128, off:off + 128], func=AF.Copy), inc=(c_a, 1))
                    psl.release(j, (c_a, a_f))
                tr_done[gt] = c_t.n
                a_f32 = c_a.n

                col0 = 45 * tt
                for dc in range(8):
                    w0 = [(c_a, a_f32), (c_v, c_v.n)] if dc == 0 else []
                    P.op(T, lambda e, dc=dc, tt=tt, col0=col0: e.matmul(
                        C.pX2[0:128, col0:col0 + 45],
                        xnTf[:, dc, 128 * tt:128 * tt + 128],
                        C.nodew_s[:, dc, :], start=(dc == 0), stop=(dc == 7)),
                        waits=w0, inc=(c_t, 1) if dc == 7 else None)
                rout.append((tt, col0, c_t.n))
                ti = 2 * half + (tt - 2 * half)
                if blk >= 1:
                    for gi in range(4 * ti, 4 * ti + 4):
                        emit_kv_group(blk - 1, gi)

        kv_ready[blk] = c_a.n
        for tt, col0, t_r in rout:
            P.op(V, lambda e, tt=tt, col0=col0: e.tensor_tensor(
                scores[:, tt, :, :].rearrange("p f n -> p (f n)"),
                C.pX2[0:128, col0:col0 + 45], C.nodebB[0:128, 0:45], ALU.add),
                waits=[(c_t, t_r)], inc=(c_v, 1))

        scl = [scores[:, :, :, j:j + 1].rearrange("p a f b -> p (a f) b")
               for j in range(15)]

        def dr(j):
            return dscr[:, 12 * j:12 * j + 12].rearrange("p (a b) -> p a b", b=1)

        def bA(j):
            return dscr[:, 12 * j:12 * j + 12].rearrange("p (a f) -> p a f", f=3)

        b0, b1, b2, b3 = dr(0), dr(1), dr(2), dr(3)
        tmp0, tmp1 = dr(4), dr(5)
        t1, tA, tB_, tC, tD = dr(6), dr(7), dr(8), dr(9), dr(10)

        def lerp(dst, a, b, bit):
            w1 = P.op(V, lambda e, a=a, b=b: e.tensor_tensor(tmp0, b, a, ALU.subtract),
                      waits=[(c_v, c_v.n)], inc=(c_v, 1))
            w2 = P.op(V, lambda e, bit=bit: e.tensor_tensor(tmp1, tmp0, bit, ALU.mult),
                      waits=[(c_v, w1)], inc=(c_v, 1))
            P.op(V, lambda e, dst=dst, a=a: e.tensor_tensor(dst, a, tmp1, ALU.add),
                 waits=[(c_v, w2)], inc=(c_v, 1))

        P.op(V, lambda e, s0=scl[0], b0=b0: e.tensor_single_scalar(
            b0, s0, 0.0, ALU.is_ge), waits=[(c_v, c_v.n)], inc=(c_v, 1))
        lerp(t1, scl[1], scl[2], b0)
        P.op(V, lambda e, t1=t1, b1=b1: e.tensor_single_scalar(
            b1, t1, 0.0, ALU.is_ge), waits=[(c_v, c_v.n)], inc=(c_v, 1))
        lerp(tA, scl[3], scl[5], b0)
        lerp(tB_, scl[4], scl[6], b0)
        lerp(t1, tA, tB_, b1)
        P.op(V, lambda e, t1=t1, b2=b2: e.tensor_single_scalar(
            b2, t1, 0.0, ALU.is_ge), waits=[(c_v, c_v.n)], inc=(c_v, 1))
        lerp(tA, scl[7], scl[11], b0)
        lerp(tB_, scl[8], scl[12], b0)
        lerp(tC, scl[9], scl[13], b0)
        lerp(tD, scl[10], scl[14], b0)
        lerp(tA, tA, tC, b1)
        lerp(tB_, tB_, tD, b1)
        lerp(t1, tA, tB_, b2)
        P.op(V, lambda e, t1=t1, b3=b3: e.tensor_single_scalar(
            b3, t1, 0.0, ALU.is_ge), waits=[(c_v, c_v.n)], inc=(c_v, 1))
        lf = leafall[:, :, 4 * blk:4 * blk + 4].rearrange("p f a -> p a f")
        P.op(V, lambda e, lf=lf: e.tensor_scalar_mul(lf, bA(0), 8.0),
             waits=[(c_v, c_v.n)], inc=(c_v, 1))
        P.op(V, lambda e, lf=lf: e.scalar_tensor_tensor(
            out=lf, in0=bA(1), scalar=4.0, in1=lf, op0=ALU.mult, op1=ALU.add),
            waits=[(c_v, c_v.n)], inc=(c_v, 1))
        P.op(V, lambda e, lf=lf: e.scalar_tensor_tensor(
            out=lf, in0=bA(2), scalar=2.0, in1=lf, op0=ALU.mult, op1=ALU.add),
            waits=[(c_v, c_v.n)], inc=(c_v, 1))
        P.op(V, lambda e, lf=lf: e.tensor_tensor(lf, lf, bA(3), ALU.add),
             waits=[(c_v, c_v.n)], inc=(c_v, 1))

        P.dma(SY, C.xnt_sp[blk].rearrange("c p n -> p c n"), xnT[:, par, :, :],
              C.s_xnt[par], waits=[(c_a, c_a.n)])

    for gi in range(16):
        emit_kv_group(NBLK - 1, gi)
    C.ph0a_v = c_v.n
    C.ph0b_sp = s_sp.n
    C.xnt_all = [(C.s_xnt[0], C.s_xnt[0].n), (C.s_xnt[1], C.s_xnt[1].n)]


def load_weight_slab(C, src2d, wi, extra_w=()):
    P = C.P
    a = 0
    for q in range(4):
        wld = P.dma("sync", C.wstage[0:128, 0:2, 0:1024],
                    src2d.rearrange("(c p) n -> p c n", p=128)[:, 2 * q:2 * q + 2, :],
                    C.s_w, waits=[(C.c_a, C.c_a.n), (C.c_t, C.c_t.n)] + list(extra_w))
        for j in range(2):
            a = P.op("scalar", lambda e, wi=wi, q=q, j=j: e.activation(
                out=C.wbig[:, wi, 2 * q + j, 0:1024],
                in_=C.wstage[:, j, :], func=AF.Copy),
                waits=[(C.s_w, wld)], inc=(C.c_a, 1))
    return a


def emit_ph0b(C):
    P = C.P
    T, V, A, G, SY = "tensor", "vector", "scalar", "gpsimd", "sync"
    s_sp, c_t, c_v, c_a = C.s_sp, C.c_t, C.c_v, C.c_a
    xnT, QT = C.xnT, C.QT

    msl = Slots([(C.pS1, 0), (C.pS1, 512), (C.pS2, 0), (C.pS2, 512)])
    xr_done = {}

    def mmgroup(lhs_fn, rhs_fn, n_k, width, extra0):
            j, (pt, off), frees = msl.acquire()
            for dc in range(n_k):
                w0 = (extra0 + frees) if dc == 0 else []
                lhs, rhs = lhs_fn(dc), rhs_fn(dc)
                P.op(T, lambda e, lhs=lhs, rhs=rhs, dc=dc, pt=pt, off=off,
                     width=width, n_k=n_k: e.matmul(
                    pt[0:128, off:off + width], lhs, rhs,
                    start=(dc == 0), stop=(dc == n_k - 1)),
                    waits=w0, inc=(c_t, 1) if dc == n_k - 1 else None)
            return j, pt, off, c_t.n

    wq_ok = load_weight_slab(C, C.din["wq"], 0)
    for blk in range(NBLK):
        par = blk % 2
        w = [(c_t, xr_done[blk - 1])] if blk >= 1 else [(c_a, wq_ok)]
        if blk >= 2:
            w.append((c_t, xr_done[blk - 2]))
        s_xr = C.s_xr[par]
        rld = P.dma(SY, xnT[:, par, :, 0:128],
                    C.xnt_sp[blk, :, :, 0:128].rearrange("c p n -> p c n"),
                    s_xr, waits=w)
        qc = own_pos(blk)
        for p8 in range(8):
            e0 = [(s_xr, rld)] + ([(c_a, wq_ok)] if blk == 0 and p8 == 0 else [])
            j, pt, off, tmm = mmgroup(
                lambda dc, p8=p8: C.wbig[:, 0, dc, 128 * p8:128 * p8 + 128]
                if dc < 8 else C.bqflat[0:1, 128 * p8:128 * p8 + 128],
                lambda dc, par=par: xnT[:, par, dc, 0:128]
                if dc < 8 else C.onesq[0:1, 0:128], 9, 128, e0)
            a_ev = P.op(A, lambda e, p8=p8, pt=pt, off=off, qc=qc: e.activation(
                out=QT[:, p8, qc:qc + 128], in_=pt[0:128, off:off + 128],
                func=AF.Copy), waits=[(c_t, tmm)], inc=(c_a, 1))
            msl.release(j, (c_a, a_ev))
        xr_done[blk] = c_t.n


def emit_ph0c(C):
    P = C.P
    T, V, A, G, SY = "tensor", "vector", "scalar", "gpsimd", "sync"
    s_w, s_sp2, c_t, c_v, c_a = C.s_w, C.s_sp2, C.c_t, C.c_v, C.c_a
    xnT, evst, hidT, selT = C.xnT, C.evst, C.hidT, C.selT

    for f in range(3):
        b2ld = P.dma(SY, C.b2f32, C.din["b2c"][f], s_w, waits=[(c_a, c_a.n)])
        P.op(A, lambda e, f=f: e.activation(out=C.b2bf[:, f, :], in_=C.b2f32,
                                            func=AF.Copy),
             waits=[(s_w, b2ld)], inc=(c_a, 1))

    msl = Slots([(C.pS1, 0), (C.pS1, 512), (C.pS2, 0), (C.pS2, 512)])
    esl = Slots([0, 1, 2, 3])
    rld_hist = []
    prev_block_t = 0

    for f in range(3):
        load_weight_slab(C, C.din["w1c"][f], 0)
        w_ok = load_weight_slab(C, C.din["w2c"][f], 1)
        N = 128 if f == 0 else 512

        for blk in range(NBLK):
            par = blk % 2
            w = []
            if len(rld_hist) >= 2:
                w.append((c_t, rld_hist[-2][1]))
            if f == 0 and blk == 0:
                w.append((c_t, C.c_t.n))
            s_xr = C.s_xr[par]
            rld = P.dma(SY, xnT[:, par, :, :], C.xnt_sp[blk].rearrange("c p n -> p c n"),
                        s_xr, waits=w)

            for tt in range(4):
                w0 = [(c_a, c_a.n), (c_v, c_v.n)] if tt == 0 else []
                P.op(T, lambda e, f=f, blk=blk, tt=tt: e.transpose(
                    C.pX2[0:1, 128 * tt:128 * tt + 128],
                    C.leafall[:, f, 4 * blk + tt:4 * blk + tt + 1], C.idf32[...]),
                    waits=w0, inc=(c_t, 1))
            t_lt = c_t.n
            a_lf = P.op(A, lambda e: e.activation(out=C.lrow_f, in_=C.pX2[0:1, 0:512],
                                                  func=AF.Copy),
                        waits=[(c_t, t_lt)], inc=(c_a, 1))
            a_lb = P.op(A, lambda e: e.activation(out=C.lrow_b, in_=C.pX2[0:1, 0:512],
                                                  func=AF.Copy), inc=(c_a, 1))
            t_s16 = P.op(T, lambda e: e.matmul(C.pX1[0:16, 0:512], C.onesbf[0:1, 0:16],
                                               C.lrow_b, start=True, stop=True),
                         waits=[(c_a, a_lb), (c_v, c_v.n)], inc=(c_t, 1))
            v_sel = P.op(V, lambda e: e.tensor_scalar(
                out=selT[...], in0=C.pX1[0:16, 0:512], scalar1=C.iota16[...],
                scalar2=None, op0=ALU.is_equal),
                waits=[(c_t, t_s16)], inc=(c_v, 1))
            t_b128 = P.op(T, lambda e: e.matmul(C.pX2[0:128, 0:512], C.onesbf[0:1, 0:128],
                                                C.lrow_b, start=True, stop=True),
                          waits=[(c_a, a_lb), (c_v, c_v.n)], inc=(c_t, 1))

            hid_v = 0
            for hc in range(8):
                j, (pt, off), frees = msl.acquire()
                w0 = [(s_xr, rld)] + frees
                if hc == 0:
                    w0.append((c_a, w_ok))
                for dc in range(8):
                    P.op(T, lambda e, hc=hc, dc=dc, pt=pt, off=off, par=par, N=N: e.matmul(
                        pt[0:128, off:off + N], C.wbig[:, 0, dc, 128 * hc:128 * hc + 128],
                        xnT[:, par, dc, 0:N], start=(dc == 0), stop=(dc == 7)),
                        waits=w0 if dc == 0 else [], inc=(c_t, 1))
                tmm = c_t.n
                a_h = P.op(A, lambda e, hc=hc, pt=pt, off=off, f=f, N=N: e.activation(
                    out=hidT[:, hc, 0:N], in_=pt[0:128, off:off + N],
                    func=AF.Relu, bias=C.b1c[:, f, hc:hc + 1]),
                    waits=[(c_t, tmm), (c_t, prev_block_t)], inc=(c_a, 1))
                hid_v = P.op(V, lambda e, hc=hc, N=N: e.scalar_tensor_tensor(
                    out=hidT[:, hc, 0:N], in0=C.pX2[0:128, 0:N],
                    scalar=C.hidiota[:, hc:hc + 1], in1=hidT[:, hc, 0:N],
                    op0=ALU.is_equal, op1=ALU.mult),
                    waits=[(c_a, a_h), (c_t, t_b128)], inc=(c_v, 1))
                msl.release(j, (c_v, hid_v))

            if f == 0:
                qc = own_pos(blk)
                for dc in range(8):
                    j, (pt, off), frees = msl.acquire()
                    w0 = [(c_v, hid_v)] + frees
                    for hc in range(8):
                        P.op(T, lambda e, dc=dc, hc=hc, pt=pt, off=off: e.matmul(
                            pt[0:128, off:off + 128],
                            C.wbig[:, 1, hc, 128 * dc:128 * dc + 128],
                            hidT[:, hc, 0:128], start=(hc == 0), stop=False),
                            waits=w0 if hc == 0 else [], inc=(c_t, 1))
                    P.op(T, lambda e, dc=dc, f=f, pt=pt, off=off: e.matmul(
                        pt[0:128, off:off + 128],
                        C.b2bf[0:16, f, 128 * dc:128 * dc + 128],
                        selT[0:16, 0:128], start=False, stop=True),
                        waits=[(c_v, v_sel)], inc=(c_t, 1))
                    tmm = c_t.n
                    je, es, efree = esl.acquire()
                    a_ev = P.op(A, lambda e, pt=pt, off=off, es=es: e.activation(
                        out=evst[:, es, 0:128], in_=pt[0:128, off:off + 128],
                        func=AF.Copy), waits=[(c_t, tmm)] + efree, inc=(c_a, 1))
                    msl.release(j, (c_a, a_ev))
                    P.dma(G, C.fqt_sp[dc][:, qc:qc + 128], evst[:, es, 0:128],
                          s_sp2, waits=[(c_a, a_ev)])
                    esl.release(je, (s_sp2, s_sp2.n))
            elif f == 1:
                for dc in range(8):
                    j, (pt, off), frees = msl.acquire()
                    w0 = [(c_v, hid_v)] + frees
                    for hc in range(8):
                        P.op(T, lambda e, dc=dc, hc=hc, pt=pt, off=off: e.matmul(
                            pt[0:128, off:off + 512],
                            C.wbig[:, 1, hc, 128 * dc:128 * dc + 128],
                            hidT[:, hc, 0:512], start=(hc == 0), stop=False),
                            waits=w0 if hc == 0 else [], inc=(c_t, 1))
                    P.op(T, lambda e, dc=dc, f=f, pt=pt, off=off: e.matmul(
                        pt[0:128, off:off + 512],
                        C.b2bf[0:16, f, 128 * dc:128 * dc + 128],
                        selT[...], start=False, stop=True),
                        waits=[(c_v, v_sel)], inc=(c_t, 1))
                    tmm = c_t.n
                    je, es, efree = esl.acquire()
                    a_ev = P.op(A, lambda e, pt=pt, off=off, es=es: e.activation(
                        out=evst[:, es, :], in_=pt[0:128, off:off + 512], func=AF.Copy),
                        waits=[(c_t, tmm)] + efree, inc=(c_a, 1))
                    msl.release(j, (c_a, a_ev))
                    P.dma(G, C.fkt_sp[dc][:, 512 * blk:512 * blk + 512],
                          evst[:, es, :], s_sp2, waits=[(c_a, a_ev)])
                    esl.release(je, (s_sp2, s_sp2.n))
            else:
                for tt in range(4):
                    for hf in range(2):
                        j, (pt, off), frees = msl.acquire()
                        w0 = [(c_v, hid_v)] + frees
                        for hc in range(8):
                            P.op(T, lambda e, tt=tt, hf=hf, hc=hc, pt=pt, off=off: e.matmul(
                                pt[0:128, off:off + 512],
                                hidT[:, hc, 128 * tt:128 * tt + 128],
                                C.wbig[:, 1, hc, 512 * hf:512 * hf + 512],
                                start=(hc == 0), stop=False),
                                waits=w0 if hc == 0 else [], inc=(c_t, 1))
                        P.op(T, lambda e, tt=tt, hf=hf, f=f, pt=pt, off=off: e.matmul(
                            pt[0:128, off:off + 512],
                            selT[0:16, 128 * tt:128 * tt + 128],
                            C.b2bf[0:16, f, 512 * hf:512 * hf + 512],
                            start=False, stop=True),
                            waits=[(c_v, v_sel)], inc=(c_t, 1))
                        tmm = c_t.n
                        je, es, efree = esl.acquire()
                        a_ev = P.op(A, lambda e, pt=pt, off=off, es=es: e.activation(
                            out=evst[:, es, :], in_=pt[0:128, off:off + 512],
                            func=AF.Copy), waits=[(c_t, tmm)] + efree, inc=(c_a, 1))
                        msl.release(j, (c_a, a_ev))
                        P.dma(G, C.fv_sp[4 * blk + tt][:, 512 * hf:512 * hf + 512],
                              evst[:, es, :], s_sp2, waits=[(c_a, a_ev)])
                        esl.release(je, (s_sp2, s_sp2.n))
            prev_block_t = c_t.n
            rld_hist.append((rld, c_t.n))
    C.ph0c_t = c_t.n
    C.ph0c_sp2 = s_sp2.n


def emit_ph1(C):
    P = C.P
    T, V, A, G, SY = "tensor", "vector", "scalar", "gpsimd", "sync"
    s_sp, c_t, c_v, c_a, c_g = C.s_sp, C.c_t, C.c_v, C.c_a, C.c_g
    QT, ktp, vaug, PT, mxs, mhT = C.QT, C.ktp, C.vaug, C.PT, C.mxs, C.mhT
    stats, sc1, biasq, lnscr = C.stats, C.sc1, C.biasq, C.lnscr

    pair_done = {}
    pt_guard = {}
    st = {"pa_free": 0}
    pools = [Slots([(C.pS1, 0), (C.pS1, 512), (C.pX1, 0)]),
             Slots([(C.pS2, 0), (C.pS2, 512), (C.pX2, 0)])]

    def Nof(c, qb):
        over = c - (16 * qb + 3)
        return 512 if over <= 0 else 512 - 128 * ((over + 3) // 4)

    def emit_p1_chunk(c, qb, p8, kpar, kld, red_v):
        for hh in range(2):
            j, (pt, off), frees = pools[hh].acquire()
            N = Nof(c, qb)
            w0 = [(C.s_mk[kpar], kld)] + frees
            if c == 0 and p8 == 0 and qb == 0:
                w0.append((c_t, getattr(C, "ph0c_t", 0)))
            mm = P.op(T, lambda e, c=c, hh=hh, pt=pt, off=off, N=N,
                      kpar=kpar, p8=p8, qb=qb: e.matmul(
                pt[0:128, off:off + N],
                ktp[64 * hh:64 * hh + 64, kpar, 128 * c:128 * c + 128],
                QT[64 * hh:64 * hh + 64, p8, 512 * qb:512 * qb + N],
                start=True, stop=True, tile_position=(64 * hh, 0)),
                waits=w0, inc=(c_t, 1))
            red = P.op(V, lambda e, c=c, hh=hh, pt=pt, off=off, N=N: e.reduce_max(
                mxs[:, hh, c:c + 1], pt[0:128, off:off + N], axis=AX),
                waits=[(c_t, mm)], inc=(c_v, 1))
            red_v[(c, hh)] = red
            pools[hh].release(j, [(c_v, red)])

    def emit_chain(qb, Cn, red_v):
        qp = qb % 2
        j, (pt, off), frees = pools[0].acquire()
        v_m = P.op(V, lambda e, Cn=Cn: e.reduce_max(
            stats[:, 40:42], mxs[:, 0:2, 0:Cn], axis=AX),
            waits=[(c_v, red_v[(Cn - 1, 1)])], inc=(c_v, 1))
        t_tr = P.op(T, lambda e, pt=pt, off=off: e.transpose(
            pt[0:2, off:off + 128], stats[:, 40:42], C.idf32[...]),
            waits=[(c_v, v_m)] + frees, inc=(c_t, 1))
        v_s = P.op(V, lambda e, pt=pt, off=off: e.reduce_max(
            stats[0:2, 42:43], pt[0:2, off:off + 128], axis=AX),
            waits=[(c_t, t_tr)], inc=(c_v, 1))
        t_t2 = P.op(T, lambda e, pt=pt, off=off: e.transpose(
            pt[0:1, off + 128:off + 130], stats[0:2, 42:43], C.idf32[0:2, 0:2]),
            waits=[(c_v, v_s)], inc=(c_t, 1))
        a_cp = P.op(A, lambda e, pt=pt, off=off: e.activation(
            out=sc1[0:1, 0:2], in_=pt[0:1, off + 128:off + 130], func=AF.Copy),
            waits=[(c_t, t_t2)], inc=(c_a, 1))
        t_b = P.op(T, lambda e, pt=pt, off=off: e.matmul(
            pt[0:128, off + 256:off + 258], C.onesP[0:1, 0:128],
            sc1[0:1, 0:2], start=True, stop=True),
            waits=[(c_a, a_cp)], inc=(c_t, 1))
        a_b = P.op(A, lambda e, qp=qp, pt=pt, off=off: e.activation(
            out=biasq[:, qp, 0:2], in_=pt[0:128, off + 256:off + 258],
            func=AF.Copy, scale=-0.125, bias=70.0),
            waits=[(c_t, t_b)], inc=(c_a, 1))
        pools[0].release(j, [(c_a, a_b)])
        return {0: a_b, 1: a_b}

    def emit_p2_pv(c, qb, Cn, exp_a, pv_t, vld):
        for hh in range(2):
            N = Nof(c, qb)
            dep = [exp_a[(c, hh)]]
            if c == 0:
                dep.append((c_v, st["pa_free"]))
                dep.append((C.s_mv, vld))
            pv = P.op(T, lambda e, c=c, hh=hh, N=N, Cn=Cn: e.matmul(
                C.pA[0:65, 512 * hh:512 * hh + N], vaug[:, c, 65 * hh:65 * hh + 65],
                PT[:, hh, c % 2, 0:N], start=(c == 0), stop=(c == Cn - 1),
                skip_group_check=True),
                waits=dep, inc=(c_t, 1))
            pv_t[(c, hh)] = pv
            pt_guard[(hh, c % 2)] = pv

    def emit_p2_chunk(c, qb, p8, kpar, Cn, bias_a, exp_a, pv_t, vld):
        qp = qb % 2
        for hh in range(2):
            j, (pt, off), frees = pools[hh].acquire()
            N = Nof(c, qb)
            w0 = [(c_a, bias_a[hh])] + frees
            mm = P.op(T, lambda e, c=c, hh=hh, pt=pt, off=off, N=N,
                      kpar=kpar, p8=p8, qb=qb: e.matmul(
                pt[0:128, off:off + N],
                ktp[64 * hh:64 * hh + 64, kpar, 128 * c:128 * c + 128],
                QT[64 * hh:64 * hh + 64, p8, 512 * qb:512 * qb + N],
                start=True, stop=True, tile_position=(64 * hh, 0)),
                waits=w0, inc=(c_t, 1))
            extra = []
            if c % 4 == 0 and 4 * qb <= c // 4 <= 4 * qb + 3:
                dd = (4 * qb + 3) - c // 4
                v_msk = P.op(V, lambda e, pt=pt, off=off, dd=dd: e.tensor_tensor(
                    pt[0:128, off + 128 * dd:off + 128 * dd + 128],
                    pt[0:128, off + 128 * dd:off + 128 * dd + 128],
                    C.trimask[...], ALU.add),
                    waits=[(c_t, mm)], inc=(c_v, 1))
                extra = [(c_v, v_msk)]
            slotg = pt_guard.get((hh, c % 2))
            exp = P.op(A, lambda e, c=c, hh=hh, pt=pt, off=off, N=N, qp=qp: e.activation(
                out=PT[:, hh, c % 2, 0:N], in_=pt[0:128, off:off + N],
                func=AF.Exp, scale=0.125, bias=biasq[:, qp, hh:hh + 1]),
                waits=[(c_t, mm)] + extra +
                      ([(c_t, slotg)] if slotg else []), inc=(c_a, 1))
            pools[hh].release(j, [(c_a, exp)])
            dep = (c_a, exp)
            if c % 4 != 0 and 4 * qb <= c // 4 <= 4 * qb + 3:
                mmod = c % 4
                dd = (4 * qb + 3) - c // 4
                g_d = P.op(V, lambda e, c=c, hh=hh, dd=dd, mmod=mmod: e.tensor_scalar_mul(
                    PT[:, hh, c % 2, 128 * dd:128 * dd + 128],
                    PT[:, hh, c % 2, 128 * dd:128 * dd + 128],
                    C.allowB[:, mmod:mmod + 1]),
                    waits=[dep], inc=(c_v, 1))
                dep = (c_v, g_d)
            exp_a[(c, hh)] = dep
        if c >= 1:
            emit_p2_pv(c - 1, qb, Cn, exp_a, pv_t, vld)
        if c == Cn - 1:
            emit_p2_pv(c, qb, Cn, exp_a, pv_t, vld)

    def emit_norm(qb, p8, Cn, pv_t):
        for hh in range(2):
            h = 2 * p8 + hh
            j, (pt, off), frees = pools[1].acquire()
            v_l = P.op(V, lambda e, hh=hh: e.tensor_scalar_add(
                lnscr[64:65, 0:512], C.pA[64:65, 512 * hh:512 * hh + 512], 1e-30),
                waits=[(c_t, pv_t[(Cn - 1, 1)])], inc=(c_v, 1))
            v_r = P.op(V, lambda e: e.reciprocal(
                lnscr[64:65, 512:1024], lnscr[64:65, 0:512]),
                waits=[(c_v, v_l)], inc=(c_v, 1))
            t_b = P.op(T, lambda e, pt=pt, off=off: e.matmul(
                pt[0:64, off:off + 512], C.onesP[64:65, 0:64],
                lnscr[64:65, 512:1024],
                start=True, stop=True, tile_position=(64, 0)),
                waits=[(c_v, v_r)] + frees, inc=(c_t, 1))
            a_rb = P.op(A, lambda e, pt=pt, off=off: e.activation(
                out=C.reclB, in_=pt[0:64, off:off + 512], func=AF.Copy),
                waits=[(c_t, t_b)], inc=(c_a, 1))
            st["pa_free"] = P.op(V, lambda e, hh=hh, h=h, qb=qb: e.tensor_tensor(
                mhT[64 * (h % 2):64 * (h % 2) + 64, h // 2, 512 * qb:512 * qb + 512],
                C.pA[0:64, 512 * hh:512 * hh + 512], C.reclB, ALU.mult),
                waits=[(c_a, a_rb)], inc=(c_v, 1))
            pools[1].release(j, [(c_a, a_rb)])

    for p8 in range(8):
        kpar = p8 % 2
        w = [(s_sp, C.ph0b_sp)] if p8 <= 1 else []
        if p8 >= 2:
            w.append((c_t, pair_done[p8 - 2]))
        kld = P.dma(G, ktp[:, kpar, :], C.kt_sp[p8], C.s_mk[kpar], waits=w)
        wv = [(s_sp, C.ph0b_sp)]
        if p8 >= 1:
            wv.append((c_t, pair_done[p8 - 1]))
        vld = P.dma(G, C.vaug[:, 0:64, 0:130], C.vp_sp[p8], C.s_mv, waits=wv)

        prev = None
        for qb in range(4):
            Cn1 = 16 * qb + 16
            red_v = {}
            for c in range(Cn1):
                emit_p1_chunk(c, qb, p8, kpar, kld, red_v)
                if prev is not None and c < prev[1]:
                    emit_p2_chunk(c, prev[0], p8, kpar, prev[1], prev[2],
                                  prev[3], prev[4], vld)
            if prev is not None:
                emit_norm(prev[0], p8, prev[1], prev[4])
            bias_a = emit_chain(qb, Cn1, red_v)
            prev = (qb, Cn1, bias_a, {}, {})
        for c in range(prev[1]):
            emit_p2_chunk(c, prev[0], p8, kpar, prev[1], prev[2], prev[3],
                          prev[4], vld)
        emit_norm(prev[0], p8, prev[1], prev[4])
        pair_done[p8] = c_t.n

    if C.mh_dbg is not None:
        for dcq in range(8):
            P.dma(SY, C.mh_dbg[dcq], mhT[:, dcq, :], C.s_out,
                  waits=[(c_v, c_v.n)])
    C.ph1_v = c_v.n
    C.ph1_t = c_t.n


def emit_ph2(C):
    P = C.P
    T, V, A, G, SY = "tensor", "vector", "scalar", "gpsimd", "sync"
    s_w, s_sp2, s_out = C.s_w, C.s_sp2, C.s_out
    c_t, c_v, c_a, c_g = C.c_t, C.c_v, C.c_a, C.c_g
    mhT, qTb, ktf, vf = C.mhT, C.qTb, C.ktf, C.vf
    P2T, A2, l2, fsc, fB, m2t = C.P2T, C.A2, C.l2, C.fsc, C.fB, C.m2t
    stats, outb, xtile = C.stats, C.outb, C.xtile

    wo_ok = load_weight_slab(C, C.din["wo"], 0, extra_w=[(c_t, C.ph1_t)])
    if C.wo_dbg is not None:
        P.dma(SY, C.wo_dbg, C.wbig[:, 0, :, :], C.s_out, waits=[(c_a, wo_ok)])
        P.dma(SY, C.bo_dbg, C.boB[0:128, :], C.s_out, waits=[(c_a, wo_ok)])

    sst = C.qTb
    ev_half = {0: C.ph1_v, 1: C.ph1_v}
    ev_prev = C.ph1_v
    for ot in range(16):
        sj = ot % 4
        for hf in range(2):
            for dc in range(8):
                w0 = [(c_a, wo_ok), (c_v, ev_half[hf])] if dc == 0 else []
                P.op(T, lambda e, ot=ot, hf=hf, dc=dc: e.matmul(
                    C.pA[0:128, 512 * hf:512 * hf + 512],
                    mhT[:, dc, 128 * ot:128 * ot + 128],
                    C.wbig[:, 0, dc, 512 * hf:512 * hf + 512],
                    start=(dc == 0), stop=(dc == 7), skip_group_check=True),
                    waits=w0, inc=(c_t, 1) if dc == 7 else None)
            tmm = c_t.n
            w0 = [(c_t, tmm)]
            if ot >= 4 and hf == 0:
                w0.append((C.s_ss, C.s_ss.n))
            ev_half[hf] = P.op(V, lambda e, ot=ot, hf=hf, sj=sj: e.tensor_tensor(
                sst[:, 2 * sj + hf, :],
                C.pA[0:128, 512 * hf:512 * hf + 512],
                C.boB[:, 512 * hf:512 * hf + 512], ALU.add),
                waits=w0, inc=(c_v, 1))
            ev_prev = ev_half[hf]
        P.dma(G, C.s_sp3[ot], sst[:, 2 * sj:2 * sj + 2, :].rearrange("p a n -> p (a n)"),
              C.s_ss, waits=[(c_v, ev_prev)])
    wo_v = c_v.n

    slots2 = [(C.pS1, 0), (C.pS1, 512), (C.pS2, 0), (C.pS2, 512)]
    exp_prev = {}
    sup_hist = {}
    st2 = {"mrg": wo_v, "pv_done": 0}

    def emit_qk2_group(qb, sup, spar, kld, qld, mz, cc):
        gs = 16 * qb + sup
        pt, off = slots2[cc]
        w0 = [(C.s_fk[spar], kld), (C.s_q, qld)]
        if exp_prev.get(cc):
            w0.append((c_a, exp_prev[cc]))
        if gs == 0 and cc == 0:
            w0.append((c_g, mz))
            w0.append((c_v, c_v.n))
        for dc in range(8):
            P.op(T, lambda e, cc=cc, dc=dc, pt=pt, off=off, spar=spar: e.matmul(
                pt[0:128, off:off + 512],
                ktf[:, spar, dc, 128 * cc:128 * cc + 128],
                qTb[:, dc, 0:512], start=(dc == 0), stop=(dc == 7)),
                waits=w0 if dc == 0 else [], inc=(c_t, 1) if dc == 7 else None)
        tqk = c_t.n
        P.op(V, lambda e, cc=cc, pt=pt, off=off: e.reduce_max(
            m2t[:, cc:cc + 1], pt[0:128, off:off + 512], axis=AX),
            waits=[(c_t, tqk)], inc=(c_v, 1))

    def emit_chain2(sup, mz):
        v_m = P.op(V, lambda e: e.reduce_max(stats[:, 44:45], m2t[...], axis=AX),
                   inc=(c_v, 1))
        t_tr = P.op(T, lambda e: e.transpose(C.pX1[0:1, 0:128], stats[:, 44:45],
                                             C.idf32[...]),
                    waits=[(c_v, v_m), (c_v, c_v.n), (c_a, c_a.n)], inc=(c_t, 1))
        v_s = P.op(V, lambda e: e.reduce_max(fsc[0:1, 1:2], C.pX1[0:1, 0:128],
                                             axis=AX),
                   waits=[(c_t, t_tr), (c_g, mz)], inc=(c_v, 1))
        v_mx = P.op(V, lambda e: e.tensor_tensor(fsc[0:1, 5:6], fsc[0:1, 0:1],
                                                 fsc[0:1, 1:2], ALU.max),
                    waits=[(c_v, v_s)], inc=(c_v, 1))
        v_df = P.op(V, lambda e: e.tensor_tensor(fsc[0:1, 3:4], fsc[0:1, 0:1],
                                                 fsc[0:1, 5:6], ALU.subtract),
                    waits=[(c_v, v_mx)], inc=(c_v, 1))
        a_f = P.op(A, lambda e: e.activation(out=fsc[0:1, 4:5], in_=fsc[0:1, 3:4],
                                             func=AF.Exp, scale=1.0 / 32),
                   waits=[(c_v, v_df), (c_v, st2["mrg"])], inc=(c_a, 1))
        v_cp = P.op(V, lambda e: e.tensor_copy(fsc[0:1, 0:1], fsc[0:1, 5:6]),
                    waits=[(c_a, a_f)], inc=(c_v, 1))
        t_b = P.op(T, lambda e: e.matmul(C.pX1[0:128, 192:194], C.onesP[0:1, 0:128],
                                         fsc[0:1, 4:6], start=True, stop=True),
                   waits=[(c_a, a_f), (c_v, c_v.n)], inc=(c_t, 1))
        a_fB = P.op(A, lambda e: e.activation(out=fB[:, 0:1], in_=C.pX1[0:128, 192:193],
                                              func=AF.Copy),
                    waits=[(c_t, t_b), (c_v, st2["mrg"])], inc=(c_a, 1))
        a_b2 = P.op(A, lambda e: e.activation(out=fB[:, 1:2], in_=C.pX1[0:128, 193:194],
                                              func=AF.Copy, scale=-1.0 / 32),
                    waits=[(c_t, t_b)], inc=(c_a, 1))
        return a_b2

    def emit_exps2(sup, a_b2):
        for cc in range(4):
            pt, off = slots2[cc]
            exp_prev[cc] = P.op(A, lambda e, cc=cc, pt=pt, off=off: e.activation(
                out=P2T[:, cc, :], in_=pt[0:128, off:off + 512],
                func=AF.Exp, scale=1.0 / 32, bias=fB[:, 1:2]),
                waits=[(c_a, a_b2)] + ([(c_t, st2["pv_done"])] if cc == 0 else []),
                inc=(c_a, 1))
        return c_a.n

    def emit_pv2_group(qb, sup, spar, vld, exps, qt):
        lb = C.pX1 if qt % 2 == 0 else C.pX2
        w0 = [(c_a, exps), (C.s_fv[spar], vld), (c_v, st2["mrg"])]
        for cc in range(4):
            for hf in range(2):
                P.op(T, lambda e, qt=qt, cc=cc, hf=hf, spar=spar: e.matmul(
                    C.pA[0:128, 512 * hf:512 * hf + 512],
                    P2T[:, cc, 128 * qt:128 * qt + 128],
                    vf[:, spar, cc, 512 * hf:512 * hf + 512],
                    start=(cc == 0), stop=(cc == 3), skip_group_check=True),
                    waits=w0 if (cc == 0 and hf == 0) else [])
            P.op(T, lambda e, qt=qt, cc=cc, lb=lb: e.matmul(
                lb[0:128, 300 + qt:301 + qt],
                P2T[:, cc, 128 * qt:128 * qt + 128],
                C.onescolb[...], start=(cc == 0), stop=(cc == 3),
                skip_group_check=True), inc=(c_t, 1) if cc == 3 else None)
        tpv = c_t.n
        P.op(V, lambda e, qt=qt: e.scalar_tensor_tensor(
            out=A2[:, qt, :], in0=A2[:, qt, :], scalar=fB[:, 0:1],
            in1=C.pA[0:128, 0:1024], op0=ALU.mult, op1=ALU.add),
            waits=[(c_t, tpv)], inc=(c_v, 1))
        st2["mrg"] = P.op(V, lambda e, qt=qt, lb=lb: e.scalar_tensor_tensor(
            out=l2[:, qt:qt + 1], in0=l2[:, qt:qt + 1], scalar=fB[:, 0:1],
            in1=lb[0:128, 300 + qt:301 + qt], op0=ALU.mult, op1=ALU.add),
            inc=(c_v, 1))
        if qt == 3:
            st2["pv_done"] = c_t.n

    def emit_pv2(qb, sup, spar, vld, exps):
        for qt in range(4):
            emit_pv2_group(qb, sup, spar, vld, exps, qt)

    for qb in range(4):
        qld = P.dma(SY, qTb[0:128, 0:8, 0:512],
                    C.fqt_sp[:, :, 512 * qb:512 * qb + 512].rearrange("c p n -> p c n"),
                    C.s_q, waits=[(s_sp2, C.ph0c_sp2), (C.s_ss, C.s_ss.n),
                                  (c_t, c_t.n)])
        P.op(G, lambda e: e.memset(A2[0:128, 0:4, 0:1024], 0.0),
             waits=[(c_v, c_v.n), (c_a, c_a.n)], inc=(c_g, 1))
        P.op(G, lambda e: e.memset(l2[:, 0:4], 0.0), inc=(c_g, 1))
        mz = P.op(G, lambda e: e.memset(fsc[0:1, 0:1], NEGBIG), inc=(c_g, 1))

        prev2 = None
        for sup in range(16):
            gs = 16 * qb + sup
            spar = sup % 2
            w = [] if gs >= 2 else [(s_sp2, C.ph0c_sp2)]
            if gs >= 2:
                w.append((c_t, sup_hist[gs - 2]))
            kld = P.dma(G, ktf[:, spar, :, :],
                        C.fkt_sp[:, :, 512 * sup:512 * sup + 512].rearrange("c p n -> p c n"),
                        C.s_fk[spar], waits=w)
            wv2 = list(w)
            if gs >= 1:
                wv2.append((c_t, sup_hist[gs - 1]))
            vld = P.dma(G, vf[:, spar, :, :],
                        C.fv_sp[4 * sup:4 * sup + 4].rearrange("c p n -> p c n"),
                        C.s_fv[spar], waits=wv2)
            for i in range(4):
                if prev2 is not None:
                    emit_pv2_group(qb, prev2[0], prev2[1], prev2[2], prev2[3], i)
                emit_qk2_group(qb, sup, spar, kld, qld, mz, i)
            a_b2 = emit_chain2(sup, mz)
            exps = emit_exps2(sup, a_b2)
            prev2 = (sup, spar, vld, exps)
            sup_hist[gs] = c_t.n
        emit_pv2(qb, prev2[0], prev2[1], prev2[2], prev2[3])
        sup_hist[16 * qb + 15] = c_t.n
        mrg_prev = st2["mrg"]

        for qt in range(4):
            i_own = 4 * qb + (3 - qt)
            v_l = P.op(V, lambda e, qt=qt: e.tensor_scalar_add(
                stats[:, 46:47], l2[:, qt:qt + 1], 1e-30),
                waits=[(c_v, mrg_prev)], inc=(c_v, 1))
            v_r = P.op(V, lambda e: e.reciprocal(stats[:, 47:48], stats[:, 46:47]),
                       waits=[(c_v, v_l)], inc=(c_v, 1))
            a_g = P.op(A, lambda e, qt=qt: e.activation(
                out=C.gsb[0:128, 0:1024], in_=A2[:, qt, :], func=AF.Silu,
                scale=stats[:, 47:48]),
                waits=[(c_v, v_r), (c_v, c_v.n)], inc=(c_a, 1))
            xld = P.dma(SY, xtile[:, qt % 2, :],
                        C.din["x"][512 * i_own:512 * i_own + 128, :], C.s_xl[qt % 2],
                        waits=[(c_v, c_v.n)])
            sld = P.dma(SY, C.sload[:, qt % 2, :], C.s_sp3[4 * qb + qt],
                        C.s_sl[qt % 2], waits=[(c_v, c_v.n)])
            v_sg = P.op(V, lambda e, qb=qb, qt=qt: e.scalar_tensor_tensor(
                out=outb[:, qt % 2, :], in0=C.sload[:, qt % 2, :], scalar=1.0,
                in1=C.gsb[0:128, 0:1024], op0=ALU.mult, op1=ALU.mult),
                waits=[(c_a, a_g), (C.s_sl[qt % 2], sld)], inc=(c_v, 1))
            v_o = P.op(V, lambda e, qt=qt: e.tensor_tensor(
                outb[:, qt % 2, :], outb[:, qt % 2, :], xtile[:, qt % 2, :],
                ALU.add), waits=[(C.s_xl[qt % 2], xld), (c_v, v_sg)], inc=(c_v, 1))
            P.dma(SY, C.out_d[512 * qb + 128 * qt:512 * qb + 128 * qt + 128, :],
                  outb[:, qt % 2, :], s_out, waits=[(c_v, v_o)])



_NC_CACHE = {}


def kernel(**inputs):
    if "nc" not in _NC_CACHE:
        _NC_CACHE["nc"] = build_nc()
    nc = _NC_CACHE["nc"]
    in_maps = host_inputs(inputs)
    res = run_bass_kernel_spmd(nc, in_maps, core_ids=list(range(8)))
    return host_output(res.results)


if __name__ == "__main__":
    pass


# version 17
# speedup vs baseline: 1.0254x; 1.0176x over previous
import contextlib
import numpy as np
import ml_dtypes
import concourse.bass as bass
import concourse.mybir as mybir
from concourse.bass_utils import run_bass_kernel_spmd

F32 = mybir.dt.float32
BF16 = mybir.dt.bfloat16
AF = mybir.ActivationFunctionType
ALU = mybir.AluOpType
AX = mybir.AxisListType.X
BF = ml_dtypes.bfloat16

B, S, D = 2, 8192, 1024
H, NP = 16, 8
NBLK = 16
NEG = -1.0e30
NEGBIG = -3.0e38

STAGE = 4
DEBUG_SPILLS = False


def local_tile_order(k):
    order = []
    for t in range(NBLK):
        order.append(4 * t + k)
        order.extend(4 * t + j for j in range(4) if j != k)
    return order


def own_pos(i):
    qb, r = divmod(i, 4)
    return 512 * qb + 128 * (3 - r)


def host_inputs(full):
    xf = np.ascontiguousarray(np.asarray(full["L0_attn_residual"], np.float32))
    WQ = np.asarray(full["L0_attn_W_Q"], np.float32)
    WK = np.asarray(full["L0_attn_W_K"], np.float32)
    WV = np.asarray(full["L0_attn_W_V"], np.float32)
    bQ = np.asarray(full["L0_attn_b_Q"], np.float32)
    bK = np.asarray(full["L0_attn_b_K"], np.float32)
    bV = np.asarray(full["L0_attn_b_V"], np.float32)

    def pairmajor(W):
        return np.ascontiguousarray(W.transpose(1, 0, 2).reshape(D, D))

    bqk = np.zeros((128, 16), np.float32)
    for p in range(NP):
        bqk[:, p] = bQ[2 * p:2 * p + 2].reshape(128)
        bqk[:, 8 + p] = bK[2 * p:2 * p + 2].reshape(128)

    w1c = np.stack([np.asarray(full[f"L0_ffn_{f}_w1"], np.float32)
                    .transpose(1, 0, 2).reshape(D, D) for f in "qkv"])
    w2c = np.stack([np.asarray(full[f"L0_ffn_{f}_w2"], np.float32)
                    .reshape(D, D) for f in "qkv"])
    b1c = np.zeros((128, 3, 8), np.float32)
    for fi, f in enumerate("qkv"):
        b1c[:, fi, :] = np.asarray(full[f"L0_ffn_{f}_b1"], np.float32).reshape(8, 128).T
    b2c = np.stack([np.asarray(full[f"L0_ffn_{f}_b2"], np.float32) for f in "qkv"])
    nodew = np.concatenate([np.ascontiguousarray(
        np.asarray(full[f"L0_ffn_{f}_node_w"], np.float32).T) for f in "qkv"],
        axis=1)
    nodeb = np.concatenate([np.asarray(full[f"L0_ffn_{f}_node_b"], np.float32)
                            .reshape(1, 15) for f in "qkv"], axis=1)

    kk = np.arange(128)[:, None]
    rr = np.arange(128)[None, :]
    trimask = np.where(kk <= rr, 0.0, NEG).astype(np.float32)
    hidiota = np.zeros((128, 8), np.float32)
    for hc in range(8):
        hidiota[:, hc] = 2 * hc + np.arange(128) // 64

    common = dict(
        wq=pairmajor(WQ), wk=pairmajor(WK), wv=pairmajor(WV), bqk=bqk,
        bvrow=np.ascontiguousarray(bV.reshape(1, D)),
        wo=np.ascontiguousarray(np.asarray(full["L0_attn_W_O"], np.float32)),
        borow=np.ascontiguousarray(
            np.asarray(full["L0_attn_b_O"], np.float32).reshape(1, D)),
        w1c=w1c, w2c=w2c, b1c=b1c, b2c=b2c, nodew=nodew, nodeb=nodeb,
        idf32=np.eye(128, dtype=np.float32), trimask=trimask,
        onesrow=np.ones((1, 128), np.float32),
        onesrow_bf=np.ones((1, 128), BF),
        hidiota=hidiota,
        iota16=np.arange(16, dtype=np.float32).reshape(16, 1),
        onescol_bf=np.ones((128, 1), BF),
    )
    maps = []
    for c in range(8):
        b, k = c // 4, c % 4
        order = local_tile_order(k)
        x_all = np.concatenate([xf[b, 128 * g:128 * g + 128] for g in order], axis=0)
        others = [j for j in range(4) if j != k]
        allow = np.zeros((1, 4), np.float32)
        allow[0, 0] = 1.0
        for m in range(3):
            allow[0, m + 1] = 1.0 if others[m] < k else 0.0
        d = dict(common)
        d["x"] = np.ascontiguousarray(x_all)
        d["allow"] = allow
        maps.append(d)
    return maps


def host_output(results):
    out = np.zeros((B, S, D), np.float32)
    for c in range(8):
        b, k = c // 4, c % 4
        oc = np.asarray(results[c]["out"], np.float32)
        for i in range(NBLK):
            g = 4 * i + k
            out[b, 128 * g:128 * g + 128] = oc[own_pos(i):own_pos(i) + 128]
    return out



class Sem:
    __slots__ = ("h", "n", "name")

    def __init__(self, name):
        self.h = None
        self.n = 0
        self.name = name


class Prog:
    def __init__(self, nc):
        self.nc = nc
        self.ops = {k: [] for k in ("sync", "gpsimd", "tensor", "vector", "scalar")}
        self.sems = []

    def sem(self, name):
        s = Sem(name)
        self.sems.append(s)
        return s

    def op(self, eng, fn, waits=(), inc=None):
        incs = []
        ret = None
        if inc is not None:
            s, a = inc
            s.n += a
            incs.append((s, a))
            ret = s.n
        ws = []
        for w in waits:
            if w is None:
                continue
            s, v = w
            if v and v > 0:
                ws.append((s, v))
        self.ops[eng].append((fn, ws, incs))
        return ret

    def dma(self, q, out, in_, sem, waits=()):
        return self.op(q, lambda e: e.dma_start(out=out, in_=in_), waits, (sem, 16))

    def emit(self):
        nc = self.nc
        with contextlib.ExitStack() as st:
            for s in self.sems:
                s.h = st.enter_context(nc.semaphore(s.name))
            with nc.Block() as block:
                def runner(name):
                    def run(e):
                        for (fn, waits, incs) in self.ops[name]:
                            for (s, v) in waits:
                                e.wait_ge(s.h, v)
                            ins = fn(e)
                            for (s, a) in incs:
                                ins.then_inc(s.h, a)
                    return run

                block.sync(runner("sync"))
                block.gpsimd(runner("gpsimd"))
                block.tensor(runner("tensor"))
                block.vector(runner("vector"))
                block.scalar(runner("scalar"))


class Slots:

    def __init__(self, slots):
        self.slots = slots
        self.free = [[] for _ in slots]
        self.i = 0

    def acquire(self):
        j = self.i % len(self.slots)
        self.i += 1
        return j, self.slots[j], list(self.free[j])

    def release(self, j, wait):
        self.free[j] = wait if isinstance(wait, list) else [wait]


class Ctx:
    pass


def build_nc():
    nc = bass.Bass("TRN2", target_bir_lowering=False)
    P = Prog(nc)
    C = Ctx()
    C.P = P

    din = {}
    for name, shape, dt in [
        ("x", [S, D], F32), ("wq", [D, D], F32), ("wk", [D, D], F32),
        ("wv", [D, D], F32), ("bqk", [128, 16], F32), ("bvrow", [1, D], F32),
        ("wo", [D, D], F32), ("borow", [1, D], F32),
        ("w1c", [3, D, D], F32), ("w2c", [3, D, D], F32),
        ("b1c", [128, 3, 8], F32), ("b2c", [3, 16, D], F32),
        ("nodew", [D, 45], F32), ("nodeb", [1, 45], F32),
        ("idf32", [128, 128], F32), ("trimask", [128, 128], F32),
        ("onesrow", [1, 128], F32), ("onesrow_bf", [1, 128], BF16),
        ("hidiota", [128, 8], F32), ("iota16", [16, 1], F32),
        ("onescol_bf", [128, 1], BF16), ("allow", [1, 4], F32),
    ]:
        din[name] = nc.dram_tensor(name, shape, dt, kind="ExternalInput").ap()
    C.din = din
    C.out_d = nc.dram_tensor("out", [2048, D], F32, kind="ExternalOutput").ap()

    kd = "ExternalOutput" if DEBUG_SPILLS else "Internal"
    C.xnt_sp = nc.dram_tensor("xnt_sp", [NBLK, 8, 128, 512], BF16, kind=kd).ap()
    C.kt_sp = nc.dram_tensor("kt_sp", [NP, 128, S], BF16, kind=kd).ap()
    C.vp_sp = nc.dram_tensor("vp_sp", [NP, 128, 64, 130], BF16, kind=kd).ap()
    C.fkt_sp = nc.dram_tensor("fkt_sp", [8, 128, S], BF16, kind=kd).ap()
    C.fv_sp = nc.dram_tensor("fv_sp", [64, 128, D], BF16, kind=kd).ap()
    C.fqt_sp = nc.dram_tensor("fqt_sp", [8, 128, 2048], BF16, kind=kd).ap()
    C.s_sp3 = nc.dram_tensor("s_sp3", [16, 128, D], BF16, kind=kd).ap()
    C.mh_dbg = nc.dram_tensor("mh_dbg", [8, 128, 2048], BF16, kind=kd).ap() \
        if DEBUG_SPILLS else None
    C.wo_dbg = nc.dram_tensor("wo_dbg", [128, 8, 1024], BF16, kind=kd).ap() \
        if DEBUG_SPILLS else None
    C.bo_dbg = nc.dram_tensor("bo_dbg", [128, D], F32, kind=kd).ap() \
        if DEBUG_SPILLS else None

    ctx = contextlib.ExitStack()
    sbuf = lambda n, sh, dt: ctx.enter_context(nc.sbuf_tensor(n, sh, dt))
    psum = lambda n, sh: ctx.enter_context(nc.psum_tensor(n, sh, F32))

    with ctx:
        C.pS1 = psum("pS1", [128, 1024])
        C.pS2 = psum("pS2", [128, 1024])
        C.pA = psum("pA", [128, 1024])
        C.pX1 = psum("pX1", [128, 512])
        C.pX2 = psum("pX2", [128, 512])

        C.idf32 = sbuf("c_idf32", [128, 128], F32)
        C.trimask = sbuf("c_trimask", [128, 128], F32)
        C.onesP = sbuf("c_onesP", [128, 128], F32)
        C.onesbf = sbuf("c_onesbf", [1, 128], BF16)
        C.hidiota = sbuf("c_hidiota", [128, 8], F32)
        C.iota16 = sbuf("c_iota16", [16, 1], F32)
        C.onescolb = sbuf("c_onescolb", [128, 1], BF16)
        C.bqk = sbuf("c_bqk", [128, 16], F32)
        C.b1c = sbuf("c_b1c", [128, 3, 8], F32)
        C.allowrow = sbuf("c_allow", [1, 4], F32)
        C.nodew_s = sbuf("c_nodew", [128, 8, 45], F32)
        C.nodebB = sbuf("c_nodebB", [128, 45], F32)
        C.nodeb_s = sbuf("c_nodeb", [1, 45], F32)

        C.bvB = sbuf("bvB", [128, D], F32)
        C.reclB = C.bvB[0:64, 0:512]
        C.boB = sbuf("boB", [128, D], F32)
        C.allowB = sbuf("allowB", [128, 4], F32)

        ar1 = sbuf("ar1", [128, 16384], BF16)
        ar2 = sbuf("ar2", [128, 16384], BF16)
        ar3 = sbuf("ar3", [128, 16384], BF16)
        ar4 = sbuf("ar4", [128, 8320], BF16)
        ar5 = sbuf("ar5", [128, 4096], F32)

        C.QT = ar1[...].rearrange("p (c n) -> p c n", c=8)
        C.qTb = ar1[:, 0:4096].rearrange("p (c n) -> p c n", c=8)
        C.P2T = ar1[:, 4096:6144].rearrange("p (c n) -> p c n", c=4)
        C.gsb = ar1[:, 6144:7168]

        C.xnT = ar2[:, 0:8192].rearrange("p (a c n) -> p a c n", a=2, c=8)
        C.hidT = ar2[:, 8192:12288].rearrange("p (c n) -> p c n", c=8)
        C.evst = ar2[:, 12288:14336].rearrange("p (c n) -> p c n", c=4)
        C.vevst = ar2[:, 8192:10368].rearrange("p (a n) -> p a n", a=4)
        C.mhT = ar2[...].rearrange("p (c n) -> p c n", c=8)

        C.ktp = ar3[...].rearrange("p (a n) -> p a n", a=2)
        C.ktf = ar3[:, 0:8192].rearrange("p (a c n) -> p a c n", a=2, c=8)
        C.vf = ar3[:, 8192:16384].rearrange("p (a c n) -> p a c n", a=2, c=4)

        C.vaug = ar4[...].rearrange("p (c n) -> p c n", c=64)

        C.xnTf = ar5[...].rearrange("p (c n) -> p c n", c=8)
        C.A2 = ar5[...].rearrange("p (c n) -> p c n", c=4)

        C.xtile = sbuf("xtile", [128, 2, 1024], F32)
        C.xnf = sbuf("xnf", [128, 2, 1024], F32)
        C.outb = C.xnf
        C.stats = sbuf("stats", [128, 48], F32)
        C.scores = sbuf("scores", [128, 4, 3, 15], F32)
        C.leafall = sbuf("leafall", [128, 3, 64], F32)
        C.dscr = sbuf("dscr", [128, 136], F32)

        C.wbig = sbuf("wbig", [128, 2, 8, 1024], BF16)
        C.wstage = sbuf("wstage", [128, 2, 1024], F32)
        C.bvrow_s = C.wstage[0:1, 0:1, 0:1024].rearrange("p a n -> p (a n)")
        C.borow_s = C.wstage[0:1, 1:2, 0:1024].rearrange("p a n -> p (a n)")
        C.lnscr = C.wstage[0:128, 1:2, 0:1024].rearrange("p a n -> p (a n)")
        C.b2bf = ar4[0:16, 0:3072].rearrange("p (f n) -> p f n", f=3)
        C.b2f32 = C.wstage[0:16, 0:1, 0:1024].rearrange("p a n -> p (a n)")
        pts = sbuf("pts", [128, 2048], BF16)
        C.PT = pts[:, 0:2048].rearrange("p (a b n) -> p a b n", a=2, b=2)
        C.onesq = pts[0:1, 0:512]
        C.bqflat = pts[0:1, 512:1536]
        C.onesk = pts[32:33, 0:512]
        C.bkflat = pts[32:33, 512:1536]
        C.mxs = sbuf("mxs", [128, 2, 64], F32)
        C.selT = sbuf("selT", [16, 512], BF16)
        C.lrow_f = C.xnf[0:1, 0:1, 0:512].rearrange("p a n -> p (a n)")
        C.lrow_b = pts[0:1, 0:512]
        C.sc1 = sbuf("sc1", [1, 16], F32)
        C.biasq = sbuf("biasq", [128, 2, 2], F32)
        C.l2 = sbuf("l2", [128, 4], F32)
        C.fsc = sbuf("fsc", [1, 8], F32)
        C.fB = sbuf("fB", [128, 2], F32)
        C.m2t = sbuf("m2t", [128, 4], F32)
        C.sload = pts[:, 0:2048].rearrange("p (a n) -> p a n", a=2)

        for s in ("s_in", "s_w", "s_sp", "s_sp2", "s_ss", "s_out", "s_mv", "s_q",
                  "c_t", "c_v", "c_a", "c_g"):
            setattr(C, s, P.sem(s))
        C.s_x = [P.sem(f"s_x{i}") for i in range(2)]
        C.s_xnt = [P.sem(f"s_xnt{i}") for i in range(2)]
        C.s_xr = [P.sem(f"s_xr{i}") for i in range(2)]
        C.s_ev = [P.sem(f"s_ev{i}") for i in range(4)]
        C.s_vq = [P.sem(f"s_vq{i}") for i in range(4)]
        C.s_mk = [P.sem(f"s_mk{i}") for i in range(2)]
        C.s_fk = [P.sem(f"s_fk{i}") for i in range(2)]
        C.s_fv = [P.sem(f"s_fv{i}") for i in range(2)]
        C.s_xl = [P.sem(f"s_xl{i}") for i in range(2)]
        C.s_sl = [P.sem(f"s_sl{i}") for i in range(2)]

        emit_setup(C)
        emit_ph0a(C)
        if STAGE >= 2:
            emit_ph0b(C)
            emit_ph0c(C)
        if STAGE >= 3:
            emit_ph1(C)
        if STAGE >= 4:
            emit_ph2(C)

        P.op("gpsimd", lambda e: e.engine_nop(),
             waits=[(C.s_out, C.s_out.n)], inc=(C.c_g, 1))
        P.emit()
    return nc


def emit_setup(C):
    P, din = C.P, C.din
    s_in, c_t, c_a = C.s_in, C.c_t, C.c_a
    for nm, dst in [("idf32", C.idf32[...]), ("trimask", C.trimask[...]),
                    ("onesrow", C.onesP[0:1, :]), ("onesrow", C.onesP[64:65, :]),
                    ("onesrow_bf", C.onesbf[...]), ("hidiota", C.hidiota[...]),
                    ("iota16", C.iota16[...]), ("onescol_bf", C.onescolb[...]),
                    ("bqk", C.bqk[...]), ("b1c", C.b1c[...]), ("allow", C.allowrow[...]),
                    ("nodeb", C.nodeb_s[...]), ("bvrow", C.bvrow_s),
                    ("borow", C.borow_s)]:
        P.dma("sync", dst, din[nm], s_in)
    P.dma("sync", C.nodew_s[...],
          din["nodew"].rearrange("(c p) n -> p c n", p=128), s_in)
    C.consts_ok = s_in.n

    def mk_bcast(rowsl, width, dst, scale=1.0):
        t = P.op("tensor", lambda e: e.matmul(
            C.pX1[0:128, 0:width], C.onesP[0:1, 0:128], rowsl,
            start=True, stop=True),
            waits=[(s_in, C.consts_ok), (c_a, c_a.n)], inc=(c_t, 1))
        P.op("scalar", lambda e: e.activation(
            out=dst, in_=C.pX1[0:128, 0:width], func=AF.Copy, scale=scale),
            waits=[(c_t, t)], inc=(c_a, 1))

    for hf in range(2):
        mk_bcast(C.bvrow_s[0:1, 512 * hf:512 * hf + 512], 512,
                 C.bvB[0:128, 512 * hf:512 * hf + 512])
        mk_bcast(C.borow_s[0:1, 512 * hf:512 * hf + 512], 512,
                 C.boB[0:128, 512 * hf:512 * hf + 512])
    mk_bcast(C.allowrow[0:1, 0:4], 4, C.allowB[0:128, 0:4])
    mk_bcast(C.nodeb_s[0:1, 0:45], 45, C.nodebB[0:128, 0:45])
    bqkT = C.selT[0:16, 384:512]
    t_bq = P.op("tensor", lambda e: e.transpose(
        C.pX1[0:16, 0:128], C.bqk[...], C.idf32[...]),
        waits=[(s_in, C.consts_ok), (c_a, c_a.n)], inc=(c_t, 1))
    a_bq = P.op("scalar", lambda e: e.activation(
        out=bqkT, in_=C.pX1[0:16, 0:128], func=AF.Copy),
        waits=[(c_t, t_bq)], inc=(c_a, 1))
    for i in range(16):
        dst = (C.bqflat[0:1, 128 * i:128 * i + 128] if i < 8 else
               C.bkflat[0:1, 128 * (i - 8):128 * (i - 8) + 128])
        P.dma("sync", dst, bqkT[i:i + 1, 0:128], s_in,
              waits=[(c_a, a_bq)] if i == 0 else [])
    C.bflat_ok = s_in.n
    P.op("gpsimd", lambda e: e.memset(C.onesq, 1.0), inc=(C.c_g, 1))
    P.op("gpsimd", lambda e: e.memset(C.onesk, 1.0), inc=(C.c_g, 1))
    C.setup_t = C.c_t.n


def emit_ph0a(C):
    P = C.P
    T, V, A, G, SY = "tensor", "vector", "scalar", "gpsimd", "sync"
    s_in, s_sp, c_t, c_v, c_a = C.s_in, C.s_sp, C.c_t, C.c_v, C.c_a
    xnf, xtile, stats = C.xnf, C.xtile, C.stats
    xnT, xnTf, scores, leafall, dscr = C.xnT, C.xnTf, C.scores, C.leafall, C.dscr
    evst, dscr = C.evst, C.dscr

    w_ok = max(load_weight_slab(C, C.din["wk"], 0),
               load_weight_slab(C, C.din["wv"], 1))
    vev4 = C.vevst[:, :, 0:520].rearrange("p a (j y) -> p a j y", j=4)
    m_on = 0
    for sl in range(4):
        P.op(G, lambda e, sl=sl: e.memset(vev4[:, sl, :, 64:65], 1.0),
             waits=[(c_a, c_a.n)], inc=(C.c_g, 1))
        m_on = P.op(G, lambda e, sl=sl: e.memset(vev4[:, sl, :, 129:130], 1.0),
                    inc=(C.c_g, 1))

    x_t = C.din["x"].rearrange("(t p) d -> t p d", p=128)
    psl = Slots([(C.pS1, 0), (C.pS1, 512), (C.pS2, 0), (C.pS2, 512)])
    esl = Slots([0, 1, 2, 3])
    xn_done = {}
    tr_done = {}
    kv_t = {}

    def load_xtile(gt):
        w = []
        if gt >= 2:
            w.append((c_v, xn_done[gt - 2]))
        if gt == 0:
            w.append((s_in, C.consts_ok))
        P.dma(SY, xtile[:, gt % 2, :], x_t[gt], C.s_x[gt % 2], waits=w)

    def mmgroup(lhs_fn, rhs_fn, n_k, width, extra0):
        j, (pt, off), frees = psl.acquire()
        for dc in range(n_k):
            w0 = (extra0 + frees) if dc == 0 else []
            lhs, rhs = lhs_fn(dc), rhs_fn(dc)
            P.op(T, lambda e, lhs=lhs, rhs=rhs, dc=dc, pt=pt, off=off,
                 width=width, n_k=n_k: e.matmul(
                pt[0:128, off:off + width], lhs, rhs,
                start=(dc == 0), stop=(dc == n_k - 1)),
                waits=w0, inc=(c_t, 1) if dc == n_k - 1 else None)
        return j, pt, off, c_t.n

    load_xtile(0)
    load_xtile(1)
    kv_ready = {}

    def emit_kv_group(b, gi):
        parb = b % 2
        if gi < 8:
            p8 = gi
            e0 = [(c_a, kv_ready[b])]
            if b == 0 and p8 == 0:
                e0 += [(c_a, w_ok), (C.c_g, m_on), (C.s_in, C.bflat_ok)]
            j, pt, off, tmm = mmgroup(
                lambda dc, p8=p8: C.wbig[:, 0, dc, 128 * p8:128 * p8 + 128]
                if dc < 8 else C.bkflat[0:1, 128 * p8:128 * p8 + 128],
                lambda dc, parb=parb: xnT[:, parb, dc, 0:512]
                if dc < 8 else C.onesk[0:1, 0:512], 9, 512, e0)
            je, es, efree = esl.acquire()
            a_ev = P.op(A, lambda e, pt=pt, off=off, es=es: e.activation(
                out=evst[:, es, :], in_=pt[0:128, off:off + 512], func=AF.Copy),
                waits=[(c_t, tmm)] + efree, inc=(c_a, 1))
            psl.release(j, (c_a, a_ev))
            P.dma(G if gi % 2 == 0 else SY,
                  C.kt_sp[p8][:, 512 * b:512 * b + 512], evst[:, es, :],
                  C.s_ev[es], waits=[(c_a, a_ev)])
            esl.release(je, (C.s_ev[es], C.s_ev[es].n))
        else:
            tt, hf = (gi - 8) // 2, (gi - 8) % 2
            vsl = (gi - 8) % 4
            j, pt, off, tmm = mmgroup(
                lambda dc, tt=tt, parb=parb: xnT[:, parb, dc, 128 * tt:128 * tt + 128]
                if dc < 8 else C.onesP[0:1, 0:128],
                lambda dc, hf=hf: C.wbig[:, 1, dc, 512 * hf:512 * hf + 512]
                if dc < 8 else C.bvB[0:1, 512 * hf:512 * hf + 512], 9, 512,
                [(c_a, kv_ready[b])])
            sl = vsl
            v_ev = 0
            for hhx in range(2):
                v_ev = P.op(A, lambda e, pt=pt, off=off, sl=sl, hhx=hhx: e.activation(
                    out=vev4[:, sl, :, 65 * hhx:65 * hhx + 64],
                    in_=pt[0:128, off:off + 512].rearrange(
                        "p (j h x) -> p j h x", j=4, h=2)[:, :, hhx, :],
                    func=AF.Copy),
                    waits=[(c_t, tmm), (C.c_g, m_on),
                           (C.s_vq[sl], C.s_vq[sl].n)] if hhx == 0 else [],
                    inc=(c_a, 1))
            psl.release(j, (c_a, v_ev))
            cch = 4 * b + tt
            P.dma(G if gi % 2 == 0 else SY,
                  C.vp_sp[4 * hf:4 * hf + 4, :, cch, :].rearrange("j p y -> p j y"),
                  C.vevst[:, sl, 0:520].rearrange("p (j y) -> p j y", j=4),
                  C.s_vq[sl], waits=[(c_a, v_ev)])
        if gi == 15:
            kv_t[b] = c_t.n

    for blk in range(NBLK):
        par = blk % 2
        rout = []
        q6h = {}
        for half in range(2):
            for tt in (2 * half, 2 * half + 1):
                gt = 4 * blk + tt
                c0 = 10 * tt
                t_x = 16 * (gt // 2 + 1)
                s_x_par = C.s_x[gt % 2]
                xa = xtile[:, gt % 2, :]
                w_ln = [(s_x_par, t_x)]
                if gt == 0:
                    w_ln.append((c_t, C.setup_t))
                if gt >= 2:
                    w_ln.append((c_t, tr_done[gt - 2]))
                P.op(A, lambda e, xa=xa, c0=c0, gt=gt: e.activation(
                    out=xnf[:, gt % 2, :], in_=xa, func=AF.Square,
                    accum_out=stats[:, c0 + 1:c0 + 2]),
                    waits=w_ln, inc=(c_a, 1))
                p2 = P.op(A, lambda e, xa=xa, c0=c0, gt=gt: e.activation(
                    out=xnf[:, gt % 2, :], in_=xa, func=AF.Copy,
                    accum_out=stats[:, c0:c0 + 1]), inc=(c_a, 1))
            b0 = 20 * half
            sv = lambda j, b0=b0: stats[:, b0 + j:b0 + j + 11:10].rearrange(
                "p (a b) -> p a b", b=1)
            q1 = P.op(V, lambda e, sv=sv: e.tensor_scalar_mul(sv(2), sv(0), 1.0 / D),
                      waits=[(c_a, p2)], inc=(c_v, 1))
            q2 = P.op(V, lambda e, sv=sv: e.tensor_scalar_mul(sv(3), sv(1), 1.0 / D),
                      waits=[(c_v, q1)], inc=(c_v, 1))
            q3 = P.op(V, lambda e, sv=sv: e.tensor_tensor(sv(4), sv(2), sv(2), ALU.mult),
                      waits=[(c_v, q2)], inc=(c_v, 1))
            q4 = P.op(V, lambda e, sv=sv: e.tensor_tensor(sv(5), sv(3), sv(4), ALU.subtract),
                      waits=[(c_v, q3)], inc=(c_v, 1))
            q5 = P.op(V, lambda e, sv=sv: e.tensor_scalar_add(sv(6), sv(5), 1e-5),
                      waits=[(c_v, q4)], inc=(c_v, 1))
            a_sq = P.op(A, lambda e, sv=sv: e.activation(out=sv(7), in_=sv(6), func=AF.Sqrt),
                        waits=[(c_v, q5)], inc=(c_a, 1))
            q6h[half] = P.op(V, lambda e, sv=sv: e.reciprocal(sv(8), sv(7)),
                             waits=[(c_a, a_sq)], inc=(c_v, 1))
            for tt in (2 * half, 2 * half + 1):
                gt = 4 * blk + tt
                c0 = 10 * tt
                xa = xtile[:, gt % 2, :]
                wv_ = [(c_t, tr_done[gt - 2])] if gt >= 2 else []
                v_xn = P.op(V, lambda e, xa=xa, gt=gt, c0=c0: e.tensor_scalar(
                    out=xnf[:, gt % 2, :], in0=xa,
                    scalar1=stats[:, c0 + 2:c0 + 3], scalar2=stats[:, c0 + 8:c0 + 9],
                    op0=ALU.subtract, op1=ALU.mult),
                    waits=wv_ + [(c_v, q6h[tt // 2])], inc=(c_v, 1))
                xn_done[gt] = v_xn
                if gt + 2 < 64:
                    load_xtile(gt + 2)
                for dc in range(8):
                    j, (pt, off), frees = psl.acquire()
                    t_tr = P.op(T, lambda e, gt=gt, dc=dc, pt=pt, off=off: e.transpose(
                        pt[0:128, off:off + 128],
                        xnf[:, gt % 2, 128 * dc:128 * dc + 128], C.idf32[...]),
                        waits=[(c_v, v_xn)] + frees, inc=(c_t, 1))
                    bw = [(c_t, t_tr)]
                    if blk >= 2 and dc == 0 and tt == 0:
                        bw.append((C.s_xnt[par], C.s_xnt[par].n))
                        bw.append((c_t, kv_t[blk - 2]))
                    P.op(A, lambda e, tt=tt, dc=dc, pt=pt, off=off, par=par: e.activation(
                        out=xnT[:, par, dc, 128 * tt:128 * tt + 128],
                        in_=pt[0:128, off:off + 128], func=AF.Copy),
                        waits=bw, inc=(c_a, 1))
                    a_f = P.op(A, lambda e, tt=tt, dc=dc, pt=pt, off=off: e.activation(
                        out=xnTf[:, dc, 128 * tt:128 * tt + 128],
                        in_=pt[0:128, off:off + 128], func=AF.Copy), inc=(c_a, 1))
                    psl.release(j, (c_a, a_f))
                tr_done[gt] = c_t.n
                a_f32 = c_a.n

                col0 = 45 * tt
                for dc in range(8):
                    w0 = [(c_a, a_f32), (c_v, c_v.n)] if dc == 0 else []
                    P.op(T, lambda e, dc=dc, tt=tt, col0=col0: e.matmul(
                        C.pX2[0:128, col0:col0 + 45],
                        xnTf[:, dc, 128 * tt:128 * tt + 128],
                        C.nodew_s[:, dc, :], start=(dc == 0), stop=(dc == 7)),
                        waits=w0, inc=(c_t, 1) if dc == 7 else None)
                rout.append((tt, col0, c_t.n))
                ti = 2 * half + (tt - 2 * half)
                if blk >= 1:
                    for gi in range(4 * ti, 4 * ti + 4):
                        emit_kv_group(blk - 1, gi)

        kv_ready[blk] = c_a.n
        for tt, col0, t_r in rout:
            P.op(V, lambda e, tt=tt, col0=col0: e.tensor_tensor(
                scores[:, tt, :, :].rearrange("p f n -> p (f n)"),
                C.pX2[0:128, col0:col0 + 45], C.nodebB[0:128, 0:45], ALU.add),
                waits=[(c_t, t_r)], inc=(c_v, 1))

        scl = [scores[:, :, :, j:j + 1].rearrange("p a f b -> p (a f) b")
               for j in range(15)]

        def dr(j):
            return dscr[:, 12 * j:12 * j + 12].rearrange("p (a b) -> p a b", b=1)

        def bA(j):
            return dscr[:, 12 * j:12 * j + 12].rearrange("p (a f) -> p a f", f=3)

        b0, b1, b2, b3 = dr(0), dr(1), dr(2), dr(3)
        tmp0, tmp1 = dr(4), dr(5)
        t1, tA, tB_, tC, tD = dr(6), dr(7), dr(8), dr(9), dr(10)

        def lerp(dst, a, b, bit):
            w1 = P.op(V, lambda e, a=a, b=b: e.tensor_tensor(tmp0, b, a, ALU.subtract),
                      waits=[(c_v, c_v.n)], inc=(c_v, 1))
            w2 = P.op(V, lambda e, bit=bit: e.tensor_tensor(tmp1, tmp0, bit, ALU.mult),
                      waits=[(c_v, w1)], inc=(c_v, 1))
            P.op(V, lambda e, dst=dst, a=a: e.tensor_tensor(dst, a, tmp1, ALU.add),
                 waits=[(c_v, w2)], inc=(c_v, 1))

        P.op(V, lambda e, s0=scl[0], b0=b0: e.tensor_single_scalar(
            b0, s0, 0.0, ALU.is_ge), waits=[(c_v, c_v.n)], inc=(c_v, 1))
        lerp(t1, scl[1], scl[2], b0)
        P.op(V, lambda e, t1=t1, b1=b1: e.tensor_single_scalar(
            b1, t1, 0.0, ALU.is_ge), waits=[(c_v, c_v.n)], inc=(c_v, 1))
        lerp(tA, scl[3], scl[5], b0)
        lerp(tB_, scl[4], scl[6], b0)
        lerp(t1, tA, tB_, b1)
        P.op(V, lambda e, t1=t1, b2=b2: e.tensor_single_scalar(
            b2, t1, 0.0, ALU.is_ge), waits=[(c_v, c_v.n)], inc=(c_v, 1))
        lerp(tA, scl[7], scl[11], b0)
        lerp(tB_, scl[8], scl[12], b0)
        lerp(tC, scl[9], scl[13], b0)
        lerp(tD, scl[10], scl[14], b0)
        lerp(tA, tA, tC, b1)
        lerp(tB_, tB_, tD, b1)
        lerp(t1, tA, tB_, b2)
        P.op(V, lambda e, t1=t1, b3=b3: e.tensor_single_scalar(
            b3, t1, 0.0, ALU.is_ge), waits=[(c_v, c_v.n)], inc=(c_v, 1))
        lf = leafall[:, :, 4 * blk:4 * blk + 4].rearrange("p f a -> p a f")
        P.op(V, lambda e, lf=lf: e.tensor_scalar_mul(lf, bA(0), 8.0),
             waits=[(c_v, c_v.n)], inc=(c_v, 1))
        P.op(V, lambda e, lf=lf: e.scalar_tensor_tensor(
            out=lf, in0=bA(1), scalar=4.0, in1=lf, op0=ALU.mult, op1=ALU.add),
            waits=[(c_v, c_v.n)], inc=(c_v, 1))
        P.op(V, lambda e, lf=lf: e.scalar_tensor_tensor(
            out=lf, in0=bA(2), scalar=2.0, in1=lf, op0=ALU.mult, op1=ALU.add),
            waits=[(c_v, c_v.n)], inc=(c_v, 1))
        P.op(V, lambda e, lf=lf: e.tensor_tensor(lf, lf, bA(3), ALU.add),
             waits=[(c_v, c_v.n)], inc=(c_v, 1))

        P.dma(SY, C.xnt_sp[blk].rearrange("c p n -> p c n"), xnT[:, par, :, :],
              C.s_xnt[par], waits=[(c_a, c_a.n)])

    for gi in range(16):
        emit_kv_group(NBLK - 1, gi)
    C.ph0a_v = c_v.n
    C.spill_all = ([(C.s_ev[i], C.s_ev[i].n) for i in range(4)] +
                   [(C.s_vq[i], C.s_vq[i].n) for i in range(4)])
    C.xnt_all = [(C.s_xnt[0], C.s_xnt[0].n), (C.s_xnt[1], C.s_xnt[1].n)]


def load_weight_slab(C, src2d, wi, extra_w=()):
    P = C.P
    a = 0
    for q in range(4):
        wld = P.dma("sync", C.wstage[0:128, 0:2, 0:1024],
                    src2d.rearrange("(c p) n -> p c n", p=128)[:, 2 * q:2 * q + 2, :],
                    C.s_w, waits=[(C.c_a, C.c_a.n), (C.c_t, C.c_t.n)] + list(extra_w))
        for j in range(2):
            a = P.op("scalar", lambda e, wi=wi, q=q, j=j: e.activation(
                out=C.wbig[:, wi, 2 * q + j, 0:1024],
                in_=C.wstage[:, j, :], func=AF.Copy),
                waits=[(C.s_w, wld)], inc=(C.c_a, 1))
    return a


def emit_ph0b(C):
    P = C.P
    T, V, A, G, SY = "tensor", "vector", "scalar", "gpsimd", "sync"
    s_sp, c_t, c_v, c_a = C.s_sp, C.c_t, C.c_v, C.c_a
    xnT, QT = C.xnT, C.QT

    msl = Slots([(C.pS1, 0), (C.pS1, 512), (C.pS2, 0), (C.pS2, 512)])
    xr_done = {}

    def mmgroup(lhs_fn, rhs_fn, n_k, width, extra0):
            j, (pt, off), frees = msl.acquire()
            for dc in range(n_k):
                w0 = (extra0 + frees) if dc == 0 else []
                lhs, rhs = lhs_fn(dc), rhs_fn(dc)
                P.op(T, lambda e, lhs=lhs, rhs=rhs, dc=dc, pt=pt, off=off,
                     width=width, n_k=n_k: e.matmul(
                    pt[0:128, off:off + width], lhs, rhs,
                    start=(dc == 0), stop=(dc == n_k - 1)),
                    waits=w0, inc=(c_t, 1) if dc == n_k - 1 else None)
            return j, pt, off, c_t.n

    wq_ok = load_weight_slab(C, C.din["wq"], 0)
    for blk in range(NBLK):
        par = blk % 2
        w = [(c_t, xr_done[blk - 1])] if blk >= 1 else [(c_a, wq_ok)]
        if blk >= 2:
            w.append((c_t, xr_done[blk - 2]))
        s_xr = C.s_xr[par]
        rld = P.dma(SY, xnT[:, par, :, 0:128],
                    C.xnt_sp[blk, :, :, 0:128].rearrange("c p n -> p c n"),
                    s_xr, waits=w)
        qc = own_pos(blk)
        for p8 in range(8):
            e0 = [(s_xr, rld)] + ([(c_a, wq_ok)] if blk == 0 and p8 == 0 else [])
            j, pt, off, tmm = mmgroup(
                lambda dc, p8=p8: C.wbig[:, 0, dc, 128 * p8:128 * p8 + 128]
                if dc < 8 else C.bqflat[0:1, 128 * p8:128 * p8 + 128],
                lambda dc, par=par: xnT[:, par, dc, 0:128]
                if dc < 8 else C.onesq[0:1, 0:128], 9, 128, e0)
            a_ev = P.op(A, lambda e, p8=p8, pt=pt, off=off, qc=qc: e.activation(
                out=QT[:, p8, qc:qc + 128], in_=pt[0:128, off:off + 128],
                func=AF.Copy), waits=[(c_t, tmm)], inc=(c_a, 1))
            msl.release(j, (c_a, a_ev))
        xr_done[blk] = c_t.n


def emit_ph0c(C):
    P = C.P
    T, V, A, G, SY = "tensor", "vector", "scalar", "gpsimd", "sync"
    s_w, s_sp2, c_t, c_v, c_a = C.s_w, C.s_sp2, C.c_t, C.c_v, C.c_a
    xnT, evst, hidT, selT = C.xnT, C.evst, C.hidT, C.selT

    for f in range(3):
        b2ld = P.dma(SY, C.b2f32, C.din["b2c"][f], s_w, waits=[(c_a, c_a.n)])
        P.op(A, lambda e, f=f: e.activation(out=C.b2bf[:, f, :], in_=C.b2f32,
                                            func=AF.Copy),
             waits=[(s_w, b2ld)], inc=(c_a, 1))

    msl = Slots([(C.pS1, 0), (C.pS1, 512), (C.pS2, 0), (C.pS2, 512)])
    esl = Slots([0, 1, 2, 3])
    rld_hist = []
    prev_block_t = 0

    for f in range(3):
        load_weight_slab(C, C.din["w1c"][f], 0)
        w_ok = load_weight_slab(C, C.din["w2c"][f], 1)
        N = 128 if f == 0 else 512

        for blk in range(NBLK):
            par = blk % 2
            w = []
            if len(rld_hist) >= 2:
                w.append((c_t, rld_hist[-2][1]))
            if f == 0 and blk == 0:
                w.append((c_t, C.c_t.n))
            s_xr = C.s_xr[par]
            rld = P.dma(SY, xnT[:, par, :, :], C.xnt_sp[blk].rearrange("c p n -> p c n"),
                        s_xr, waits=w)

            for tt in range(4):
                w0 = [(c_a, c_a.n), (c_v, c_v.n)] if tt == 0 else []
                P.op(T, lambda e, f=f, blk=blk, tt=tt: e.transpose(
                    C.pX2[0:1, 128 * tt:128 * tt + 128],
                    C.leafall[:, f, 4 * blk + tt:4 * blk + tt + 1], C.idf32[...]),
                    waits=w0, inc=(c_t, 1))
            t_lt = c_t.n
            a_lf = P.op(A, lambda e: e.activation(out=C.lrow_f, in_=C.pX2[0:1, 0:512],
                                                  func=AF.Copy),
                        waits=[(c_t, t_lt)], inc=(c_a, 1))
            a_lb = P.op(A, lambda e: e.activation(out=C.lrow_b, in_=C.pX2[0:1, 0:512],
                                                  func=AF.Copy), inc=(c_a, 1))
            t_s16 = P.op(T, lambda e: e.matmul(C.pX1[0:16, 0:512], C.onesbf[0:1, 0:16],
                                               C.lrow_b, start=True, stop=True),
                         waits=[(c_a, a_lb), (c_v, c_v.n)], inc=(c_t, 1))
            v_sel = P.op(V, lambda e: e.tensor_scalar(
                out=selT[...], in0=C.pX1[0:16, 0:512], scalar1=C.iota16[...],
                scalar2=None, op0=ALU.is_equal),
                waits=[(c_t, t_s16)], inc=(c_v, 1))
            t_b128 = P.op(T, lambda e: e.matmul(C.pX2[0:128, 0:512], C.onesbf[0:1, 0:128],
                                                C.lrow_b, start=True, stop=True),
                          waits=[(c_a, a_lb), (c_v, c_v.n)], inc=(c_t, 1))

            hid_v = 0
            for hc in range(8):
                j, (pt, off), frees = msl.acquire()
                w0 = [(s_xr, rld)] + frees
                if hc == 0:
                    w0.append((c_a, w_ok))
                for dc in range(8):
                    P.op(T, lambda e, hc=hc, dc=dc, pt=pt, off=off, par=par, N=N: e.matmul(
                        pt[0:128, off:off + N], C.wbig[:, 0, dc, 128 * hc:128 * hc + 128],
                        xnT[:, par, dc, 0:N], start=(dc == 0), stop=(dc == 7)),
                        waits=w0 if dc == 0 else [], inc=(c_t, 1))
                tmm = c_t.n
                a_h = P.op(A, lambda e, hc=hc, pt=pt, off=off, f=f, N=N: e.activation(
                    out=hidT[:, hc, 0:N], in_=pt[0:128, off:off + N],
                    func=AF.Relu, bias=C.b1c[:, f, hc:hc + 1]),
                    waits=[(c_t, tmm), (c_t, prev_block_t)], inc=(c_a, 1))
                hid_v = P.op(V, lambda e, hc=hc, N=N: e.scalar_tensor_tensor(
                    out=hidT[:, hc, 0:N], in0=C.pX2[0:128, 0:N],
                    scalar=C.hidiota[:, hc:hc + 1], in1=hidT[:, hc, 0:N],
                    op0=ALU.is_equal, op1=ALU.mult),
                    waits=[(c_a, a_h), (c_t, t_b128)], inc=(c_v, 1))
                msl.release(j, (c_v, hid_v))

            if f == 0:
                qc = own_pos(blk)
                for dc in range(8):
                    j, (pt, off), frees = msl.acquire()
                    w0 = [(c_v, hid_v)] + frees
                    for hc in range(8):
                        P.op(T, lambda e, dc=dc, hc=hc, pt=pt, off=off: e.matmul(
                            pt[0:128, off:off + 128],
                            C.wbig[:, 1, hc, 128 * dc:128 * dc + 128],
                            hidT[:, hc, 0:128], start=(hc == 0), stop=False),
                            waits=w0 if hc == 0 else [], inc=(c_t, 1))
                    P.op(T, lambda e, dc=dc, f=f, pt=pt, off=off: e.matmul(
                        pt[0:128, off:off + 128],
                        C.b2bf[0:16, f, 128 * dc:128 * dc + 128],
                        selT[0:16, 0:128], start=False, stop=True),
                        waits=[(c_v, v_sel)], inc=(c_t, 1))
                    tmm = c_t.n
                    je, es, efree = esl.acquire()
                    a_ev = P.op(A, lambda e, pt=pt, off=off, es=es: e.activation(
                        out=evst[:, es, 0:128], in_=pt[0:128, off:off + 128],
                        func=AF.Copy), waits=[(c_t, tmm)] + efree, inc=(c_a, 1))
                    msl.release(j, (c_a, a_ev))
                    P.dma(G, C.fqt_sp[dc][:, qc:qc + 128], evst[:, es, 0:128],
                          s_sp2, waits=[(c_a, a_ev)])
                    esl.release(je, (s_sp2, s_sp2.n))
            elif f == 1:
                for dc in range(8):
                    j, (pt, off), frees = msl.acquire()
                    w0 = [(c_v, hid_v)] + frees
                    for hc in range(8):
                        P.op(T, lambda e, dc=dc, hc=hc, pt=pt, off=off: e.matmul(
                            pt[0:128, off:off + 512],
                            C.wbig[:, 1, hc, 128 * dc:128 * dc + 128],
                            hidT[:, hc, 0:512], start=(hc == 0), stop=False),
                            waits=w0 if hc == 0 else [], inc=(c_t, 1))
                    P.op(T, lambda e, dc=dc, f=f, pt=pt, off=off: e.matmul(
                        pt[0:128, off:off + 512],
                        C.b2bf[0:16, f, 128 * dc:128 * dc + 128],
                        selT[...], start=False, stop=True),
                        waits=[(c_v, v_sel)], inc=(c_t, 1))
                    tmm = c_t.n
                    je, es, efree = esl.acquire()
                    a_ev = P.op(A, lambda e, pt=pt, off=off, es=es: e.activation(
                        out=evst[:, es, :], in_=pt[0:128, off:off + 512], func=AF.Copy),
                        waits=[(c_t, tmm)] + efree, inc=(c_a, 1))
                    msl.release(j, (c_a, a_ev))
                    P.dma(G, C.fkt_sp[dc][:, 512 * blk:512 * blk + 512],
                          evst[:, es, :], s_sp2, waits=[(c_a, a_ev)])
                    esl.release(je, (s_sp2, s_sp2.n))
            else:
                for tt in range(4):
                    for hf in range(2):
                        j, (pt, off), frees = msl.acquire()
                        w0 = [(c_v, hid_v)] + frees
                        for hc in range(8):
                            P.op(T, lambda e, tt=tt, hf=hf, hc=hc, pt=pt, off=off: e.matmul(
                                pt[0:128, off:off + 512],
                                hidT[:, hc, 128 * tt:128 * tt + 128],
                                C.wbig[:, 1, hc, 512 * hf:512 * hf + 512],
                                start=(hc == 0), stop=False),
                                waits=w0 if hc == 0 else [], inc=(c_t, 1))
                        P.op(T, lambda e, tt=tt, hf=hf, f=f, pt=pt, off=off: e.matmul(
                            pt[0:128, off:off + 512],
                            selT[0:16, 128 * tt:128 * tt + 128],
                            C.b2bf[0:16, f, 512 * hf:512 * hf + 512],
                            start=False, stop=True),
                            waits=[(c_v, v_sel)], inc=(c_t, 1))
                        tmm = c_t.n
                        je, es, efree = esl.acquire()
                        a_ev = P.op(A, lambda e, pt=pt, off=off, es=es: e.activation(
                            out=evst[:, es, :], in_=pt[0:128, off:off + 512],
                            func=AF.Copy), waits=[(c_t, tmm)] + efree, inc=(c_a, 1))
                        msl.release(j, (c_a, a_ev))
                        P.dma(G, C.fv_sp[4 * blk + tt][:, 512 * hf:512 * hf + 512],
                              evst[:, es, :], s_sp2, waits=[(c_a, a_ev)])
                        esl.release(je, (s_sp2, s_sp2.n))
            prev_block_t = c_t.n
            rld_hist.append((rld, c_t.n))
    C.ph0c_t = c_t.n
    C.ph0c_sp2 = s_sp2.n


def emit_ph1(C):
    P = C.P
    T, V, A, G, SY = "tensor", "vector", "scalar", "gpsimd", "sync"
    s_sp, c_t, c_v, c_a, c_g = C.s_sp, C.c_t, C.c_v, C.c_a, C.c_g
    QT, ktp, vaug, PT, mxs, mhT = C.QT, C.ktp, C.vaug, C.PT, C.mxs, C.mhT
    stats, sc1, biasq, lnscr = C.stats, C.sc1, C.biasq, C.lnscr

    pair_done = {}
    pt_guard = {}
    st = {"pa_free": 0}
    pools = [Slots([(C.pS1, 0), (C.pS1, 512), (C.pX1, 0)]),
             Slots([(C.pS2, 0), (C.pS2, 512), (C.pX2, 0)])]

    def Nof(c, qb):
        over = c - (16 * qb + 3)
        return 512 if over <= 0 else 512 - 128 * ((over + 3) // 4)

    def emit_p1_chunk(c, qb, p8, kpar, kld, red_v):
        for hh in range(2):
            j, (pt, off), frees = pools[hh].acquire()
            N = Nof(c, qb)
            w0 = [(C.s_mk[kpar], kld)] + frees
            if c == 0 and p8 == 0 and qb == 0:
                w0.append((c_t, getattr(C, "ph0c_t", 0)))
            mm = P.op(T, lambda e, c=c, hh=hh, pt=pt, off=off, N=N,
                      kpar=kpar, p8=p8, qb=qb: e.matmul(
                pt[0:128, off:off + N],
                ktp[64 * hh:64 * hh + 64, kpar, 128 * c:128 * c + 128],
                QT[64 * hh:64 * hh + 64, p8, 512 * qb:512 * qb + N],
                start=True, stop=True, tile_position=(64 * hh, 0)),
                waits=w0, inc=(c_t, 1))
            red = P.op(V, lambda e, c=c, hh=hh, pt=pt, off=off, N=N: e.reduce_max(
                mxs[:, hh, c:c + 1], pt[0:128, off:off + N], axis=AX),
                waits=[(c_t, mm)], inc=(c_v, 1))
            red_v[(c, hh)] = red
            pools[hh].release(j, [(c_v, red)])

    def emit_chain(qb, Cn, red_v):
        qp = qb % 2
        j, (pt, off), frees = pools[0].acquire()
        v_m = P.op(V, lambda e, Cn=Cn: e.reduce_max(
            stats[:, 40:42], mxs[:, 0:2, 0:Cn], axis=AX),
            waits=[(c_v, red_v[(Cn - 1, 1)])], inc=(c_v, 1))
        t_tr = P.op(T, lambda e, pt=pt, off=off: e.transpose(
            pt[0:2, off:off + 128], stats[:, 40:42], C.idf32[...]),
            waits=[(c_v, v_m)] + frees, inc=(c_t, 1))
        v_s = P.op(V, lambda e, pt=pt, off=off: e.reduce_max(
            stats[0:2, 42:43], pt[0:2, off:off + 128], axis=AX),
            waits=[(c_t, t_tr)], inc=(c_v, 1))
        t_t2 = P.op(T, lambda e, pt=pt, off=off: e.transpose(
            pt[0:1, off + 128:off + 130], stats[0:2, 42:43], C.idf32[0:2, 0:2]),
            waits=[(c_v, v_s)], inc=(c_t, 1))
        a_cp = P.op(A, lambda e, pt=pt, off=off: e.activation(
            out=sc1[0:1, 0:2], in_=pt[0:1, off + 128:off + 130], func=AF.Copy),
            waits=[(c_t, t_t2)], inc=(c_a, 1))
        t_b = P.op(T, lambda e, pt=pt, off=off: e.matmul(
            pt[0:128, off + 256:off + 258], C.onesP[0:1, 0:128],
            sc1[0:1, 0:2], start=True, stop=True),
            waits=[(c_a, a_cp)], inc=(c_t, 1))
        a_b = P.op(A, lambda e, qp=qp, pt=pt, off=off: e.activation(
            out=biasq[:, qp, 0:2], in_=pt[0:128, off + 256:off + 258],
            func=AF.Copy, scale=-0.125, bias=70.0),
            waits=[(c_t, t_b)], inc=(c_a, 1))
        pools[0].release(j, [(c_a, a_b)])
        return {0: a_b, 1: a_b}

    def emit_p2_pv(c, qb, Cn, exp_a, pv_t, vld):
        for hh in range(2):
            N = Nof(c, qb)
            dep = [exp_a[(c, hh)]]
            if c == 0:
                dep.append((c_v, st["pa_free"]))
                dep.append((C.s_mv, vld))
            pv = P.op(T, lambda e, c=c, hh=hh, N=N, Cn=Cn: e.matmul(
                C.pA[0:65, 512 * hh:512 * hh + N], vaug[:, c, 65 * hh:65 * hh + 65],
                PT[:, hh, c % 2, 0:N], start=(c == 0), stop=(c == Cn - 1),
                skip_group_check=True),
                waits=dep, inc=(c_t, 1))
            pv_t[(c, hh)] = pv
            pt_guard[(hh, c % 2)] = pv

    def emit_p2_chunk(c, qb, p8, kpar, Cn, bias_a, exp_a, pv_t, vld):
        qp = qb % 2
        for hh in range(2):
            j, (pt, off), frees = pools[hh].acquire()
            N = Nof(c, qb)
            w0 = [(c_a, bias_a[hh])] + frees
            mm = P.op(T, lambda e, c=c, hh=hh, pt=pt, off=off, N=N,
                      kpar=kpar, p8=p8, qb=qb: e.matmul(
                pt[0:128, off:off + N],
                ktp[64 * hh:64 * hh + 64, kpar, 128 * c:128 * c + 128],
                QT[64 * hh:64 * hh + 64, p8, 512 * qb:512 * qb + N],
                start=True, stop=True, tile_position=(64 * hh, 0)),
                waits=w0, inc=(c_t, 1))
            extra = []
            if c % 4 == 0 and 4 * qb <= c // 4 <= 4 * qb + 3:
                dd = (4 * qb + 3) - c // 4
                v_msk = P.op(V, lambda e, pt=pt, off=off, dd=dd: e.tensor_tensor(
                    pt[0:128, off + 128 * dd:off + 128 * dd + 128],
                    pt[0:128, off + 128 * dd:off + 128 * dd + 128],
                    C.trimask[...], ALU.add),
                    waits=[(c_t, mm)], inc=(c_v, 1))
                extra = [(c_v, v_msk)]
            slotg = pt_guard.get((hh, c % 2))
            exp = P.op(A, lambda e, c=c, hh=hh, pt=pt, off=off, N=N, qp=qp: e.activation(
                out=PT[:, hh, c % 2, 0:N], in_=pt[0:128, off:off + N],
                func=AF.Exp, scale=0.125, bias=biasq[:, qp, hh:hh + 1]),
                waits=[(c_t, mm)] + extra +
                      ([(c_t, slotg)] if slotg else []), inc=(c_a, 1))
            pools[hh].release(j, [(c_a, exp)])
            dep = (c_a, exp)
            if c % 4 != 0 and 4 * qb <= c // 4 <= 4 * qb + 3:
                mmod = c % 4
                dd = (4 * qb + 3) - c // 4
                g_d = P.op(V, lambda e, c=c, hh=hh, dd=dd, mmod=mmod: e.tensor_scalar_mul(
                    PT[:, hh, c % 2, 128 * dd:128 * dd + 128],
                    PT[:, hh, c % 2, 128 * dd:128 * dd + 128],
                    C.allowB[:, mmod:mmod + 1]),
                    waits=[dep], inc=(c_v, 1))
                dep = (c_v, g_d)
            exp_a[(c, hh)] = dep
        if c >= 1:
            emit_p2_pv(c - 1, qb, Cn, exp_a, pv_t, vld)
        if c == Cn - 1:
            emit_p2_pv(c, qb, Cn, exp_a, pv_t, vld)

    def emit_norm(qb, p8, Cn, pv_t):
        for hh in range(2):
            h = 2 * p8 + hh
            j, (pt, off), frees = pools[1].acquire()
            v_l = P.op(V, lambda e, hh=hh: e.tensor_scalar_add(
                lnscr[64:65, 0:512], C.pA[64:65, 512 * hh:512 * hh + 512], 1e-30),
                waits=[(c_t, pv_t[(Cn - 1, 1)])], inc=(c_v, 1))
            v_r = P.op(V, lambda e: e.reciprocal(
                lnscr[64:65, 512:1024], lnscr[64:65, 0:512]),
                waits=[(c_v, v_l)], inc=(c_v, 1))
            t_b = P.op(T, lambda e, pt=pt, off=off: e.matmul(
                pt[0:64, off:off + 512], C.onesP[64:65, 0:64],
                lnscr[64:65, 512:1024],
                start=True, stop=True, tile_position=(64, 0)),
                waits=[(c_v, v_r)] + frees, inc=(c_t, 1))
            a_rb = P.op(A, lambda e, pt=pt, off=off: e.activation(
                out=C.reclB, in_=pt[0:64, off:off + 512], func=AF.Copy),
                waits=[(c_t, t_b)], inc=(c_a, 1))
            st["pa_free"] = P.op(V, lambda e, hh=hh, h=h, qb=qb: e.tensor_tensor(
                mhT[64 * (h % 2):64 * (h % 2) + 64, h // 2, 512 * qb:512 * qb + 512],
                C.pA[0:64, 512 * hh:512 * hh + 512], C.reclB, ALU.mult),
                waits=[(c_a, a_rb)], inc=(c_v, 1))
            pools[1].release(j, [(c_a, a_rb)])

    for p8 in range(8):
        kpar = p8 % 2
        w = list(C.spill_all) if p8 <= 1 else []
        if p8 >= 2:
            w.append((c_t, pair_done[p8 - 2]))
        kld = P.dma(G, ktp[:, kpar, :], C.kt_sp[p8], C.s_mk[kpar], waits=w)
        wv = list(C.spill_all) if p8 <= 1 else []
        if p8 >= 1:
            wv.append((c_t, pair_done[p8 - 1]))
        vld = P.dma(G, C.vaug[:, 0:64, 0:130], C.vp_sp[p8], C.s_mv, waits=wv)

        prev = None
        for qb in range(4):
            Cn1 = 16 * qb + 16
            red_v = {}
            for c in range(Cn1):
                emit_p1_chunk(c, qb, p8, kpar, kld, red_v)
                if prev is not None and c < prev[1]:
                    emit_p2_chunk(c, prev[0], p8, kpar, prev[1], prev[2],
                                  prev[3], prev[4], vld)
            if prev is not None:
                emit_norm(prev[0], p8, prev[1], prev[4])
            bias_a = emit_chain(qb, Cn1, red_v)
            prev = (qb, Cn1, bias_a, {}, {})
        for c in range(prev[1]):
            emit_p2_chunk(c, prev[0], p8, kpar, prev[1], prev[2], prev[3],
                          prev[4], vld)
        emit_norm(prev[0], p8, prev[1], prev[4])
        pair_done[p8] = c_t.n

    if C.mh_dbg is not None:
        for dcq in range(8):
            P.dma(SY, C.mh_dbg[dcq], mhT[:, dcq, :], C.s_out,
                  waits=[(c_v, c_v.n)])
    C.ph1_v = c_v.n
    C.ph1_t = c_t.n


def emit_ph2(C):
    P = C.P
    T, V, A, G, SY = "tensor", "vector", "scalar", "gpsimd", "sync"
    s_w, s_sp2, s_out = C.s_w, C.s_sp2, C.s_out
    c_t, c_v, c_a, c_g = C.c_t, C.c_v, C.c_a, C.c_g
    mhT, qTb, ktf, vf = C.mhT, C.qTb, C.ktf, C.vf
    P2T, A2, l2, fsc, fB, m2t = C.P2T, C.A2, C.l2, C.fsc, C.fB, C.m2t
    stats, outb, xtile = C.stats, C.outb, C.xtile

    wo_ok = load_weight_slab(C, C.din["wo"], 0, extra_w=[(c_t, C.ph1_t)])
    if C.wo_dbg is not None:
        P.dma(SY, C.wo_dbg, C.wbig[:, 0, :, :], C.s_out, waits=[(c_a, wo_ok)])
        P.dma(SY, C.bo_dbg, C.boB[0:128, :], C.s_out, waits=[(c_a, wo_ok)])

    sst = C.qTb
    ev_half = {0: C.ph1_v, 1: C.ph1_v}
    ev_prev = C.ph1_v
    for ot in range(16):
        sj = ot % 4
        for hf in range(2):
            for dc in range(8):
                w0 = [(c_a, wo_ok), (c_v, ev_half[hf])] if dc == 0 else []
                P.op(T, lambda e, ot=ot, hf=hf, dc=dc: e.matmul(
                    C.pA[0:128, 512 * hf:512 * hf + 512],
                    mhT[:, dc, 128 * ot:128 * ot + 128],
                    C.wbig[:, 0, dc, 512 * hf:512 * hf + 512],
                    start=(dc == 0), stop=(dc == 7), skip_group_check=True),
                    waits=w0, inc=(c_t, 1) if dc == 7 else None)
            tmm = c_t.n
            w0 = [(c_t, tmm)]
            if ot >= 4 and hf == 0:
                w0.append((C.s_ss, C.s_ss.n))
            ev_half[hf] = P.op(V, lambda e, ot=ot, hf=hf, sj=sj: e.tensor_tensor(
                sst[:, 2 * sj + hf, :],
                C.pA[0:128, 512 * hf:512 * hf + 512],
                C.boB[:, 512 * hf:512 * hf + 512], ALU.add),
                waits=w0, inc=(c_v, 1))
            ev_prev = ev_half[hf]
        P.dma(G, C.s_sp3[ot], sst[:, 2 * sj:2 * sj + 2, :].rearrange("p a n -> p (a n)"),
              C.s_ss, waits=[(c_v, ev_prev)])
    wo_v = c_v.n

    slots2 = [(C.pS1, 0), (C.pS1, 512), (C.pS2, 0), (C.pS2, 512)]
    exp_prev = {}
    sup_hist = {}
    st2 = {"mrg": wo_v, "pv_done": 0}

    def emit_qk2_group(qb, sup, spar, kld, qld, mz, cc):
        gs = 16 * qb + sup
        pt, off = slots2[cc]
        w0 = [(C.s_fk[spar], kld), (C.s_q, qld)]
        if exp_prev.get(cc):
            w0.append((c_a, exp_prev[cc]))
        if gs == 0 and cc == 0:
            w0.append((c_g, mz))
            w0.append((c_v, c_v.n))
        for dc in range(8):
            P.op(T, lambda e, cc=cc, dc=dc, pt=pt, off=off, spar=spar: e.matmul(
                pt[0:128, off:off + 512],
                ktf[:, spar, dc, 128 * cc:128 * cc + 128],
                qTb[:, dc, 0:512], start=(dc == 0), stop=(dc == 7)),
                waits=w0 if dc == 0 else [], inc=(c_t, 1) if dc == 7 else None)
        tqk = c_t.n
        P.op(V, lambda e, cc=cc, pt=pt, off=off: e.reduce_max(
            m2t[:, cc:cc + 1], pt[0:128, off:off + 512], axis=AX),
            waits=[(c_t, tqk)], inc=(c_v, 1))

    def emit_chain2(sup, mz):
        v_m = P.op(V, lambda e: e.reduce_max(stats[:, 44:45], m2t[...], axis=AX),
                   inc=(c_v, 1))
        t_tr = P.op(T, lambda e: e.transpose(C.pX1[0:1, 0:128], stats[:, 44:45],
                                             C.idf32[...]),
                    waits=[(c_v, v_m), (c_v, c_v.n), (c_a, c_a.n)], inc=(c_t, 1))
        v_s = P.op(V, lambda e: e.reduce_max(fsc[0:1, 1:2], C.pX1[0:1, 0:128],
                                             axis=AX),
                   waits=[(c_t, t_tr), (c_g, mz)], inc=(c_v, 1))
        v_mx = P.op(V, lambda e: e.tensor_tensor(fsc[0:1, 5:6], fsc[0:1, 0:1],
                                                 fsc[0:1, 1:2], ALU.max),
                    waits=[(c_v, v_s)], inc=(c_v, 1))
        v_df = P.op(V, lambda e: e.tensor_tensor(fsc[0:1, 3:4], fsc[0:1, 0:1],
                                                 fsc[0:1, 5:6], ALU.subtract),
                    waits=[(c_v, v_mx)], inc=(c_v, 1))
        a_f = P.op(A, lambda e: e.activation(out=fsc[0:1, 4:5], in_=fsc[0:1, 3:4],
                                             func=AF.Exp, scale=1.0 / 32),
                   waits=[(c_v, v_df), (c_v, st2["mrg"])], inc=(c_a, 1))
        v_cp = P.op(V, lambda e: e.tensor_copy(fsc[0:1, 0:1], fsc[0:1, 5:6]),
                    waits=[(c_a, a_f)], inc=(c_v, 1))
        t_b = P.op(T, lambda e: e.matmul(C.pX1[0:128, 192:194], C.onesP[0:1, 0:128],
                                         fsc[0:1, 4:6], start=True, stop=True),
                   waits=[(c_a, a_f), (c_v, c_v.n)], inc=(c_t, 1))
        a_fB = P.op(A, lambda e: e.activation(out=fB[:, 0:1], in_=C.pX1[0:128, 192:193],
                                              func=AF.Copy),
                    waits=[(c_t, t_b), (c_v, st2["mrg"])], inc=(c_a, 1))
        a_b2 = P.op(A, lambda e: e.activation(out=fB[:, 1:2], in_=C.pX1[0:128, 193:194],
                                              func=AF.Copy, scale=-1.0 / 32),
                    waits=[(c_t, t_b)], inc=(c_a, 1))
        return a_b2

    def emit_exps2(sup, a_b2):
        for cc in range(4):
            pt, off = slots2[cc]
            exp_prev[cc] = P.op(A, lambda e, cc=cc, pt=pt, off=off: e.activation(
                out=P2T[:, cc, :], in_=pt[0:128, off:off + 512],
                func=AF.Exp, scale=1.0 / 32, bias=fB[:, 1:2]),
                waits=[(c_a, a_b2)] + ([(c_t, st2["pv_done"])] if cc == 0 else []),
                inc=(c_a, 1))
        return c_a.n

    def emit_pv2_group(qb, sup, spar, vld, exps, qt):
        lb = C.pX1 if qt % 2 == 0 else C.pX2
        w0 = [(c_a, exps), (C.s_fv[spar], vld), (c_v, st2["mrg"])]
        for cc in range(4):
            for hf in range(2):
                P.op(T, lambda e, qt=qt, cc=cc, hf=hf, spar=spar: e.matmul(
                    C.pA[0:128, 512 * hf:512 * hf + 512],
                    P2T[:, cc, 128 * qt:128 * qt + 128],
                    vf[:, spar, cc, 512 * hf:512 * hf + 512],
                    start=(cc == 0), stop=(cc == 3), skip_group_check=True),
                    waits=w0 if (cc == 0 and hf == 0) else [])
            P.op(T, lambda e, qt=qt, cc=cc, lb=lb: e.matmul(
                lb[0:128, 300 + qt:301 + qt],
                P2T[:, cc, 128 * qt:128 * qt + 128],
                C.onescolb[...], start=(cc == 0), stop=(cc == 3),
                skip_group_check=True), inc=(c_t, 1) if cc == 3 else None)
        tpv = c_t.n
        P.op(V, lambda e, qt=qt: e.scalar_tensor_tensor(
            out=A2[:, qt, :], in0=A2[:, qt, :], scalar=fB[:, 0:1],
            in1=C.pA[0:128, 0:1024], op0=ALU.mult, op1=ALU.add),
            waits=[(c_t, tpv)], inc=(c_v, 1))
        st2["mrg"] = P.op(V, lambda e, qt=qt, lb=lb: e.scalar_tensor_tensor(
            out=l2[:, qt:qt + 1], in0=l2[:, qt:qt + 1], scalar=fB[:, 0:1],
            in1=lb[0:128, 300 + qt:301 + qt], op0=ALU.mult, op1=ALU.add),
            inc=(c_v, 1))
        if qt == 3:
            st2["pv_done"] = c_t.n

    def emit_pv2(qb, sup, spar, vld, exps):
        for qt in range(4):
            emit_pv2_group(qb, sup, spar, vld, exps, qt)

    for qb in range(4):
        qld = P.dma(SY, qTb[0:128, 0:8, 0:512],
                    C.fqt_sp[:, :, 512 * qb:512 * qb + 512].rearrange("c p n -> p c n"),
                    C.s_q, waits=[(s_sp2, C.ph0c_sp2), (C.s_ss, C.s_ss.n),
                                  (c_t, c_t.n)])
        P.op(G, lambda e: e.memset(A2[0:128, 0:4, 0:1024], 0.0),
             waits=[(c_v, c_v.n), (c_a, c_a.n)], inc=(c_g, 1))
        P.op(G, lambda e: e.memset(l2[:, 0:4], 0.0), inc=(c_g, 1))
        mz = P.op(G, lambda e: e.memset(fsc[0:1, 0:1], NEGBIG), inc=(c_g, 1))

        prev2 = None
        for sup in range(16):
            gs = 16 * qb + sup
            spar = sup % 2
            w = [] if gs >= 2 else [(s_sp2, C.ph0c_sp2)]
            if gs >= 2:
                w.append((c_t, sup_hist[gs - 2]))
            kld = P.dma(G, ktf[:, spar, :, :],
                        C.fkt_sp[:, :, 512 * sup:512 * sup + 512].rearrange("c p n -> p c n"),
                        C.s_fk[spar], waits=w)
            wv2 = list(w)
            if gs >= 1:
                wv2.append((c_t, sup_hist[gs - 1]))
            vld = P.dma(G, vf[:, spar, :, :],
                        C.fv_sp[4 * sup:4 * sup + 4].rearrange("c p n -> p c n"),
                        C.s_fv[spar], waits=wv2)
            for i in range(4):
                if prev2 is not None:
                    emit_pv2_group(qb, prev2[0], prev2[1], prev2[2], prev2[3], i)
                emit_qk2_group(qb, sup, spar, kld, qld, mz, i)
            a_b2 = emit_chain2(sup, mz)
            exps = emit_exps2(sup, a_b2)
            prev2 = (sup, spar, vld, exps)
            sup_hist[gs] = c_t.n
        emit_pv2(qb, prev2[0], prev2[1], prev2[2], prev2[3])
        sup_hist[16 * qb + 15] = c_t.n
        mrg_prev = st2["mrg"]

        for qt in range(4):
            i_own = 4 * qb + (3 - qt)
            v_l = P.op(V, lambda e, qt=qt: e.tensor_scalar_add(
                stats[:, 46:47], l2[:, qt:qt + 1], 1e-30),
                waits=[(c_v, mrg_prev)], inc=(c_v, 1))
            v_r = P.op(V, lambda e: e.reciprocal(stats[:, 47:48], stats[:, 46:47]),
                       waits=[(c_v, v_l)], inc=(c_v, 1))
            a_g = P.op(A, lambda e, qt=qt: e.activation(
                out=C.gsb[0:128, 0:1024], in_=A2[:, qt, :], func=AF.Silu,
                scale=stats[:, 47:48]),
                waits=[(c_v, v_r), (c_v, c_v.n)], inc=(c_a, 1))
            xld = P.dma(SY, xtile[:, qt % 2, :],
                        C.din["x"][512 * i_own:512 * i_own + 128, :], C.s_xl[qt % 2],
                        waits=[(c_v, c_v.n)])
            sld = P.dma(SY, C.sload[:, qt % 2, :], C.s_sp3[4 * qb + qt],
                        C.s_sl[qt % 2], waits=[(c_v, c_v.n)])
            v_sg = P.op(V, lambda e, qb=qb, qt=qt: e.scalar_tensor_tensor(
                out=outb[:, qt % 2, :], in0=C.sload[:, qt % 2, :], scalar=1.0,
                in1=C.gsb[0:128, 0:1024], op0=ALU.mult, op1=ALU.mult),
                waits=[(c_a, a_g), (C.s_sl[qt % 2], sld)], inc=(c_v, 1))
            v_o = P.op(V, lambda e, qt=qt: e.tensor_tensor(
                outb[:, qt % 2, :], outb[:, qt % 2, :], xtile[:, qt % 2, :],
                ALU.add), waits=[(C.s_xl[qt % 2], xld), (c_v, v_sg)], inc=(c_v, 1))
            P.dma(SY, C.out_d[512 * qb + 128 * qt:512 * qb + 128 * qt + 128, :],
                  outb[:, qt % 2, :], s_out, waits=[(c_v, v_o)])



_NC_CACHE = {}


def kernel(**inputs):
    if "nc" not in _NC_CACHE:
        _NC_CACHE["nc"] = build_nc()
    nc = _NC_CACHE["nc"]
    in_maps = host_inputs(inputs)
    res = run_bass_kernel_spmd(nc, in_maps, core_ids=list(range(8)))
    return host_output(res.results)


if __name__ == "__main__":
    pass


# version 18
# speedup vs baseline: 1.0301x; 1.0046x over previous
import contextlib
import numpy as np
import ml_dtypes
import concourse.bass as bass
import concourse.mybir as mybir
from concourse.bass_utils import run_bass_kernel_spmd

F32 = mybir.dt.float32
BF16 = mybir.dt.bfloat16
AF = mybir.ActivationFunctionType
ALU = mybir.AluOpType
AX = mybir.AxisListType.X
BF = ml_dtypes.bfloat16

B, S, D = 2, 8192, 1024
H, NP = 16, 8
NBLK = 16
NEG = -1.0e30
NEGBIG = -3.0e38

STAGE = 4
DEBUG_SPILLS = False


def local_tile_order(k):
    order = []
    for t in range(NBLK):
        order.append(4 * t + k)
        order.extend(4 * t + j for j in range(4) if j != k)
    return order


def own_pos(i):
    qb, r = divmod(i, 4)
    return 512 * qb + 128 * (3 - r)


def host_inputs(full):
    xf = np.ascontiguousarray(np.asarray(full["L0_attn_residual"], np.float32))
    WQ = np.asarray(full["L0_attn_W_Q"], np.float32)
    WK = np.asarray(full["L0_attn_W_K"], np.float32)
    WV = np.asarray(full["L0_attn_W_V"], np.float32)
    bQ = np.asarray(full["L0_attn_b_Q"], np.float32)
    bK = np.asarray(full["L0_attn_b_K"], np.float32)
    bV = np.asarray(full["L0_attn_b_V"], np.float32)

    def pairmajor(W):
        return np.ascontiguousarray(W.transpose(1, 0, 2).reshape(D, D))

    bqk = np.zeros((128, 16), np.float32)
    for p in range(NP):
        bqk[:, p] = bQ[2 * p:2 * p + 2].reshape(128)
        bqk[:, 8 + p] = bK[2 * p:2 * p + 2].reshape(128)

    w1c = np.stack([np.asarray(full[f"L0_ffn_{f}_w1"], np.float32)
                    .transpose(1, 0, 2).reshape(D, D) for f in "qkv"])
    w2c = np.stack([np.asarray(full[f"L0_ffn_{f}_w2"], np.float32)
                    .reshape(D, D) for f in "qkv"])
    b1c = np.zeros((128, 3, 8), np.float32)
    for fi, f in enumerate("qkv"):
        b1c[:, fi, :] = np.asarray(full[f"L0_ffn_{f}_b1"], np.float32).reshape(8, 128).T
    b2c = np.stack([np.asarray(full[f"L0_ffn_{f}_b2"], np.float32) for f in "qkv"])
    nodew = np.concatenate([np.ascontiguousarray(
        np.asarray(full[f"L0_ffn_{f}_node_w"], np.float32).T) for f in "qkv"],
        axis=1)
    nodeb = np.concatenate([np.asarray(full[f"L0_ffn_{f}_node_b"], np.float32)
                            .reshape(1, 15) for f in "qkv"], axis=1)

    kk = np.arange(128)[:, None]
    rr = np.arange(128)[None, :]
    trimask = np.where(kk <= rr, 0.0, NEG).astype(np.float32)
    hidiota = np.zeros((128, 8), np.float32)
    for hc in range(8):
        hidiota[:, hc] = 2 * hc + np.arange(128) // 64

    common = dict(
        wq=pairmajor(WQ), wk=pairmajor(WK), wv=pairmajor(WV), bqk=bqk,
        bvrow=np.ascontiguousarray(bV.reshape(1, D)),
        wo=np.ascontiguousarray(np.asarray(full["L0_attn_W_O"], np.float32)),
        borow=np.ascontiguousarray(
            np.asarray(full["L0_attn_b_O"], np.float32).reshape(1, D)),
        w1c=w1c, w2c=w2c, b1c=b1c, b2c=b2c, nodew=nodew, nodeb=nodeb,
        idf32=np.eye(128, dtype=np.float32), trimask=trimask,
        onesrow=np.ones((1, 128), np.float32),
        onesrow_bf=np.ones((1, 128), BF),
        hidiota=hidiota,
        iota16=np.arange(16, dtype=np.float32).reshape(16, 1),
        onescol_bf=np.ones((128, 1), BF),
    )
    maps = []
    for c in range(8):
        b, k = c // 4, c % 4
        order = local_tile_order(k)
        x_all = np.concatenate([xf[b, 128 * g:128 * g + 128] for g in order], axis=0)
        others = [j for j in range(4) if j != k]
        allow = np.zeros((1, 4), np.float32)
        allow[0, 0] = 1.0
        for m in range(3):
            allow[0, m + 1] = 1.0 if others[m] < k else 0.0
        d = dict(common)
        d["x"] = np.ascontiguousarray(x_all)
        d["allow"] = allow
        maps.append(d)
    return maps


def host_output(results):
    out = np.zeros((B, S, D), np.float32)
    for c in range(8):
        b, k = c // 4, c % 4
        oc = np.asarray(results[c]["out"], np.float32)
        for i in range(NBLK):
            g = 4 * i + k
            out[b, 128 * g:128 * g + 128] = oc[own_pos(i):own_pos(i) + 128]
    return out



class Sem:
    __slots__ = ("h", "n", "name")

    def __init__(self, name):
        self.h = None
        self.n = 0
        self.name = name


class Prog:
    def __init__(self, nc):
        self.nc = nc
        self.ops = {k: [] for k in ("sync", "gpsimd", "tensor", "vector", "scalar")}
        self.sems = []

    def sem(self, name):
        s = Sem(name)
        self.sems.append(s)
        return s

    def op(self, eng, fn, waits=(), inc=None):
        incs = []
        ret = None
        if inc is not None:
            s, a = inc
            s.n += a
            incs.append((s, a))
            ret = s.n
        ws = []
        for w in waits:
            if w is None:
                continue
            s, v = w
            if v and v > 0:
                ws.append((s, v))
        self.ops[eng].append((fn, ws, incs))
        return ret

    def dma(self, q, out, in_, sem, waits=()):
        return self.op(q, lambda e: e.dma_start(out=out, in_=in_), waits, (sem, 16))

    def emit(self):
        nc = self.nc
        with contextlib.ExitStack() as st:
            for s in self.sems:
                s.h = st.enter_context(nc.semaphore(s.name))
            with nc.Block() as block:
                def runner(name):
                    def run(e):
                        for (fn, waits, incs) in self.ops[name]:
                            for (s, v) in waits:
                                e.wait_ge(s.h, v)
                            ins = fn(e)
                            for (s, a) in incs:
                                ins.then_inc(s.h, a)
                    return run

                block.sync(runner("sync"))
                block.gpsimd(runner("gpsimd"))
                block.tensor(runner("tensor"))
                block.vector(runner("vector"))
                block.scalar(runner("scalar"))


class Slots:

    def __init__(self, slots):
        self.slots = slots
        self.free = [[] for _ in slots]
        self.i = 0

    def acquire(self):
        j = self.i % len(self.slots)
        self.i += 1
        return j, self.slots[j], list(self.free[j])

    def release(self, j, wait):
        self.free[j] = wait if isinstance(wait, list) else [wait]


class Ctx:
    pass


def build_nc():
    nc = bass.Bass("TRN2", target_bir_lowering=False)
    P = Prog(nc)
    C = Ctx()
    C.P = P

    din = {}
    for name, shape, dt in [
        ("x", [S, D], F32), ("wq", [D, D], F32), ("wk", [D, D], F32),
        ("wv", [D, D], F32), ("bqk", [128, 16], F32), ("bvrow", [1, D], F32),
        ("wo", [D, D], F32), ("borow", [1, D], F32),
        ("w1c", [3, D, D], F32), ("w2c", [3, D, D], F32),
        ("b1c", [128, 3, 8], F32), ("b2c", [3, 16, D], F32),
        ("nodew", [D, 45], F32), ("nodeb", [1, 45], F32),
        ("idf32", [128, 128], F32), ("trimask", [128, 128], F32),
        ("onesrow", [1, 128], F32), ("onesrow_bf", [1, 128], BF16),
        ("hidiota", [128, 8], F32), ("iota16", [16, 1], F32),
        ("onescol_bf", [128, 1], BF16), ("allow", [1, 4], F32),
    ]:
        din[name] = nc.dram_tensor(name, shape, dt, kind="ExternalInput").ap()
    C.din = din
    C.out_d = nc.dram_tensor("out", [2048, D], F32, kind="ExternalOutput").ap()

    kd = "ExternalOutput" if DEBUG_SPILLS else "Internal"
    C.xnt_sp = nc.dram_tensor("xnt_sp", [NBLK, 8, 128, 512], BF16, kind=kd).ap()
    C.kt_sp = nc.dram_tensor("kt_sp", [NP, 128, S], BF16, kind=kd).ap()
    C.vp_sp = nc.dram_tensor("vp_sp", [NP, 128, 64, 130], BF16, kind=kd).ap()
    C.fkt_sp = nc.dram_tensor("fkt_sp", [8, 128, S], BF16, kind=kd).ap()
    C.fv_sp = nc.dram_tensor("fv_sp", [64, 128, D], BF16, kind=kd).ap()
    C.fqt_sp = nc.dram_tensor("fqt_sp", [8, 128, 2048], BF16, kind=kd).ap()
    C.s_sp3 = nc.dram_tensor("s_sp3", [16, 128, D], BF16, kind=kd).ap()
    C.mh_dbg = nc.dram_tensor("mh_dbg", [8, 128, 2048], BF16, kind=kd).ap() \
        if DEBUG_SPILLS else None
    C.wo_dbg = nc.dram_tensor("wo_dbg", [128, 8, 1024], BF16, kind=kd).ap() \
        if DEBUG_SPILLS else None
    C.bo_dbg = nc.dram_tensor("bo_dbg", [128, D], F32, kind=kd).ap() \
        if DEBUG_SPILLS else None

    ctx = contextlib.ExitStack()
    sbuf = lambda n, sh, dt: ctx.enter_context(nc.sbuf_tensor(n, sh, dt))
    psum = lambda n, sh: ctx.enter_context(nc.psum_tensor(n, sh, F32))

    with ctx:
        C.pS1 = psum("pS1", [128, 1024])
        C.pS2 = psum("pS2", [128, 1024])
        C.pA = psum("pA", [128, 1024])
        C.pX1 = psum("pX1", [128, 512])
        C.pX2 = psum("pX2", [128, 512])

        C.idf32 = sbuf("c_idf32", [128, 128], F32)
        C.trimask = sbuf("c_trimask", [128, 128], F32)
        C.onesP = sbuf("c_onesP", [128, 128], F32)
        C.onesbf = sbuf("c_onesbf", [1, 128], BF16)
        C.hidiota = sbuf("c_hidiota", [128, 8], F32)
        C.iota16 = sbuf("c_iota16", [16, 1], F32)
        C.onescolb = sbuf("c_onescolb", [128, 1], BF16)
        C.bqk = sbuf("c_bqk", [128, 16], F32)
        C.b1c = sbuf("c_b1c", [128, 3, 8], F32)
        C.allowrow = sbuf("c_allow", [1, 4], F32)
        C.nodew_s = sbuf("c_nodew", [128, 8, 45], F32)
        C.nodebB = sbuf("c_nodebB", [128, 45], F32)
        C.nodeb_s = sbuf("c_nodeb", [1, 45], F32)

        C.bvB = sbuf("bvB", [128, D], F32)
        C.reclB = C.bvB[0:64, 0:512]
        C.boB = sbuf("boB", [128, D], F32)
        C.allowB = sbuf("allowB", [128, 4], F32)

        ar1 = sbuf("ar1", [128, 16384], BF16)
        ar2 = sbuf("ar2", [128, 16384], BF16)
        ar3 = sbuf("ar3", [128, 16384], BF16)
        ar4 = sbuf("ar4", [128, 8320], BF16)
        ar5 = sbuf("ar5", [128, 4096], F32)

        C.QT = ar1[...].rearrange("p (c n) -> p c n", c=8)
        C.qTb = ar1[:, 0:4096].rearrange("p (c n) -> p c n", c=8)
        C.P2T = ar1[:, 4096:6144].rearrange("p (c n) -> p c n", c=4)
        C.gsb = ar1[:, 6144:7168]

        C.xnT = ar2[:, 0:8192].rearrange("p (a c n) -> p a c n", a=2, c=8)
        C.hidT = ar2[:, 8192:12288].rearrange("p (c n) -> p c n", c=8)
        C.evst = ar2[:, 12288:14336].rearrange("p (c n) -> p c n", c=4)
        C.vevst = ar2[:, 8192:10368].rearrange("p (a n) -> p a n", a=4)
        C.mhT = ar2[...].rearrange("p (c n) -> p c n", c=8)

        C.ktp = ar3[...].rearrange("p (a n) -> p a n", a=2)
        C.ktf = ar3[:, 0:8192].rearrange("p (a c n) -> p a c n", a=2, c=8)
        C.vf = ar3[:, 8192:16384].rearrange("p (a c n) -> p a c n", a=2, c=4)

        C.vaug = ar4[...].rearrange("p (c n) -> p c n", c=64)

        C.xnTf = ar5[...].rearrange("p (c n) -> p c n", c=8)
        C.A2 = ar5[...].rearrange("p (c n) -> p c n", c=4)

        C.xtile = sbuf("xtile", [128, 2, 1024], F32)
        C.xnf = sbuf("xnf", [128, 2, 1024], F32)
        C.outb = C.xnf
        C.stats = sbuf("stats", [128, 48], F32)
        C.scores = sbuf("scores", [128, 4, 3, 15], F32)
        C.leafall = sbuf("leafall", [128, 3, 64], F32)
        C.dscr = sbuf("dscr", [128, 136], F32)

        C.wbig = sbuf("wbig", [128, 2, 8, 1024], BF16)
        C.wstage = sbuf("wstage", [128, 2, 1024], F32)
        C.bvrow_s = C.wstage[0:1, 0:1, 0:1024].rearrange("p a n -> p (a n)")
        C.borow_s = C.wstage[0:1, 1:2, 0:1024].rearrange("p a n -> p (a n)")
        C.lnscr = C.wstage[0:128, 1:2, 0:1024].rearrange("p a n -> p (a n)")
        C.b2bf = ar4[0:16, 0:3072].rearrange("p (f n) -> p f n", f=3)
        C.b2f32 = C.wstage[0:16, 0:1, 0:1024].rearrange("p a n -> p (a n)")
        pts = sbuf("pts", [128, 2048], BF16)
        C.PT = pts[:, 0:2048].rearrange("p (a b n) -> p a b n", a=2, b=2)
        C.onesq = pts[0:1, 0:512]
        C.bqflat = pts[0:1, 512:1536]
        C.onesk = pts[32:33, 0:512]
        C.bkflat = pts[32:33, 512:1536]
        C.mxs = sbuf("mxs", [128, 2, 64], F32)
        C.selT = sbuf("selT", [16, 512], BF16)
        C.lrow_f = C.xnf[0:1, 0:1, 0:512].rearrange("p a n -> p (a n)")
        C.lrow_b = pts[0:1, 0:512]
        C.sc1 = sbuf("sc1", [1, 16], F32)
        C.biasq = sbuf("biasq", [128, 2, 2], F32)
        C.l2 = sbuf("l2", [128, 4], F32)
        C.fsc = sbuf("fsc", [1, 8], F32)
        C.fB = sbuf("fB", [128, 2], F32)
        C.m2t = sbuf("m2t", [128, 4], F32)
        C.sload = pts[:, 0:2048].rearrange("p (a n) -> p a n", a=2)

        for s in ("s_in", "s_w", "s_sp", "s_sp2", "s_ss", "s_out", "s_mv", "s_q",
                  "c_t", "c_v", "c_a", "c_g"):
            setattr(C, s, P.sem(s))
        C.s_x = [P.sem(f"s_x{i}") for i in range(2)]
        C.s_xnt = [P.sem(f"s_xnt{i}") for i in range(2)]
        C.s_xr = [P.sem(f"s_xr{i}") for i in range(2)]
        C.s_ev = [P.sem(f"s_ev{i}") for i in range(4)]
        C.s_vq = [P.sem(f"s_vq{i}") for i in range(4)]
        C.s_mk = [P.sem(f"s_mk{i}") for i in range(2)]
        C.s_fk = [P.sem(f"s_fk{i}") for i in range(2)]
        C.s_fv = [P.sem(f"s_fv{i}") for i in range(2)]
        C.s_xl = [P.sem(f"s_xl{i}") for i in range(2)]
        C.s_sl = [P.sem(f"s_sl{i}") for i in range(2)]

        emit_setup(C)
        emit_ph0a(C)
        if STAGE >= 2:
            emit_ph0b(C)
            emit_ph0c(C)
        if STAGE >= 3:
            emit_ph1(C)
        if STAGE >= 4:
            emit_ph2(C)

        P.op("gpsimd", lambda e: e.engine_nop(),
             waits=[(C.s_out, C.s_out.n)], inc=(C.c_g, 1))
        P.emit()
    return nc


def emit_setup(C):
    P, din = C.P, C.din
    s_in, c_t, c_a = C.s_in, C.c_t, C.c_a
    for nm, dst in [("idf32", C.idf32[...]), ("trimask", C.trimask[...]),
                    ("onesrow", C.onesP[0:1, :]), ("onesrow", C.onesP[64:65, :]),
                    ("onesrow_bf", C.onesbf[...]), ("hidiota", C.hidiota[...]),
                    ("iota16", C.iota16[...]), ("onescol_bf", C.onescolb[...]),
                    ("bqk", C.bqk[...]), ("b1c", C.b1c[...]), ("allow", C.allowrow[...]),
                    ("nodeb", C.nodeb_s[...]), ("bvrow", C.bvrow_s),
                    ("borow", C.borow_s)]:
        P.dma("sync", dst, din[nm], s_in)
    P.dma("sync", C.nodew_s[...],
          din["nodew"].rearrange("(c p) n -> p c n", p=128), s_in)
    C.consts_ok = s_in.n

    def mk_bcast(rowsl, width, dst, scale=1.0):
        t = P.op("tensor", lambda e: e.matmul(
            C.pX1[0:128, 0:width], C.onesP[0:1, 0:128], rowsl,
            start=True, stop=True),
            waits=[(s_in, C.consts_ok), (c_a, c_a.n)], inc=(c_t, 1))
        P.op("scalar", lambda e: e.activation(
            out=dst, in_=C.pX1[0:128, 0:width], func=AF.Copy, scale=scale),
            waits=[(c_t, t)], inc=(c_a, 1))

    for hf in range(2):
        mk_bcast(C.bvrow_s[0:1, 512 * hf:512 * hf + 512], 512,
                 C.bvB[0:128, 512 * hf:512 * hf + 512])
        mk_bcast(C.borow_s[0:1, 512 * hf:512 * hf + 512], 512,
                 C.boB[0:128, 512 * hf:512 * hf + 512])
    mk_bcast(C.allowrow[0:1, 0:4], 4, C.allowB[0:128, 0:4])
    mk_bcast(C.nodeb_s[0:1, 0:45], 45, C.nodebB[0:128, 0:45])
    bqkT = C.selT[0:16, 384:512]
    t_bq = P.op("tensor", lambda e: e.transpose(
        C.pX1[0:16, 0:128], C.bqk[...], C.idf32[...]),
        waits=[(s_in, C.consts_ok), (c_a, c_a.n)], inc=(c_t, 1))
    a_bq = P.op("scalar", lambda e: e.activation(
        out=bqkT, in_=C.pX1[0:16, 0:128], func=AF.Copy),
        waits=[(c_t, t_bq)], inc=(c_a, 1))
    for i in range(16):
        dst = (C.bqflat[0:1, 128 * i:128 * i + 128] if i < 8 else
               C.bkflat[0:1, 128 * (i - 8):128 * (i - 8) + 128])
        P.dma("sync", dst, bqkT[i:i + 1, 0:128], s_in,
              waits=[(c_a, a_bq)] if i == 0 else [])
    C.bflat_ok = s_in.n
    P.op("gpsimd", lambda e: e.memset(C.onesq, 1.0), inc=(C.c_g, 1))
    P.op("gpsimd", lambda e: e.memset(C.onesk, 1.0), inc=(C.c_g, 1))
    C.setup_t = C.c_t.n


def emit_ph0a(C):
    P = C.P
    T, V, A, G, SY = "tensor", "vector", "scalar", "gpsimd", "sync"
    s_in, s_sp, c_t, c_v, c_a = C.s_in, C.s_sp, C.c_t, C.c_v, C.c_a
    xnf, xtile, stats = C.xnf, C.xtile, C.stats
    xnT, xnTf, scores, leafall, dscr = C.xnT, C.xnTf, C.scores, C.leafall, C.dscr
    evst, dscr = C.evst, C.dscr

    w_ok = max(load_weight_slab(C, C.din["wk"], 0),
               load_weight_slab(C, C.din["wv"], 1))
    vev4 = C.vevst[:, :, 0:520].rearrange("p a (j y) -> p a j y", j=4)
    m_on = 0
    for sl in range(4):
        P.op(G, lambda e, sl=sl: e.memset(vev4[:, sl, :, 64:65], 1.0),
             waits=[(c_a, c_a.n)], inc=(C.c_g, 1))
        m_on = P.op(G, lambda e, sl=sl: e.memset(vev4[:, sl, :, 129:130], 1.0),
                    inc=(C.c_g, 1))

    x_t = C.din["x"].rearrange("(t p) d -> t p d", p=128)
    psl = Slots([(C.pS1, 0), (C.pS1, 512), (C.pS2, 0), (C.pS2, 512)])
    esl = Slots([0, 1, 2, 3])
    xn_done = {}
    tr_done = {}
    kv_t = {}

    def load_xtile(gt):
        w = []
        if gt >= 2:
            w.append((c_v, xn_done[gt - 2]))
        if gt == 0:
            w.append((s_in, C.consts_ok))
        P.dma(SY, xtile[:, gt % 2, :], x_t[gt], C.s_x[gt % 2], waits=w)

    def mmgroup(lhs_fn, rhs_fn, n_k, width, extra0):
        j, (pt, off), frees = psl.acquire()
        for dc in range(n_k):
            w0 = (extra0 + frees) if dc == 0 else []
            lhs, rhs = lhs_fn(dc), rhs_fn(dc)
            P.op(T, lambda e, lhs=lhs, rhs=rhs, dc=dc, pt=pt, off=off,
                 width=width, n_k=n_k: e.matmul(
                pt[0:128, off:off + width], lhs, rhs,
                start=(dc == 0), stop=(dc == n_k - 1)),
                waits=w0, inc=(c_t, 1) if dc == n_k - 1 else None)
        return j, pt, off, c_t.n

    load_xtile(0)
    load_xtile(1)
    kv_ready = {}

    def emit_kv_group(b, gi):
        parb = b % 2
        if gi < 8:
            p8 = gi
            e0 = [(c_a, kv_ready[b])]
            if b == 0 and p8 == 0:
                e0 += [(c_a, w_ok), (C.c_g, m_on), (C.s_in, C.bflat_ok)]
            j, pt, off, tmm = mmgroup(
                lambda dc, p8=p8: C.wbig[:, 0, dc, 128 * p8:128 * p8 + 128]
                if dc < 8 else C.bkflat[0:1, 128 * p8:128 * p8 + 128],
                lambda dc, parb=parb: xnT[:, parb, dc, 0:512]
                if dc < 8 else C.onesk[0:1, 0:512], 9, 512, e0)
            je, es, efree = esl.acquire()
            a_ev = P.op(A, lambda e, pt=pt, off=off, es=es: e.activation(
                out=evst[:, es, :], in_=pt[0:128, off:off + 512], func=AF.Copy),
                waits=[(c_t, tmm)] + efree, inc=(c_a, 1))
            psl.release(j, (c_a, a_ev))
            P.dma(G if gi % 2 == 0 else SY,
                  C.kt_sp[p8][:, 512 * b:512 * b + 512], evst[:, es, :],
                  C.s_ev[es], waits=[(c_a, a_ev)])
            esl.release(je, (C.s_ev[es], C.s_ev[es].n))
        else:
            tt, hf = (gi - 8) // 2, (gi - 8) % 2
            vsl = (gi - 8) % 4
            j, pt, off, tmm = mmgroup(
                lambda dc, tt=tt, parb=parb: xnT[:, parb, dc, 128 * tt:128 * tt + 128]
                if dc < 8 else C.onesP[0:1, 0:128],
                lambda dc, hf=hf: C.wbig[:, 1, dc, 512 * hf:512 * hf + 512]
                if dc < 8 else C.bvB[0:1, 512 * hf:512 * hf + 512], 9, 512,
                [(c_a, kv_ready[b])])
            sl = vsl
            v_ev = 0
            for hhx in range(2):
                v_ev = P.op(A, lambda e, pt=pt, off=off, sl=sl, hhx=hhx: e.activation(
                    out=vev4[:, sl, :, 65 * hhx:65 * hhx + 64],
                    in_=pt[0:128, off:off + 512].rearrange(
                        "p (j h x) -> p j h x", j=4, h=2)[:, :, hhx, :],
                    func=AF.Copy),
                    waits=[(c_t, tmm), (C.c_g, m_on),
                           (C.s_vq[sl], C.s_vq[sl].n)] if hhx == 0 else [],
                    inc=(c_a, 1))
            psl.release(j, (c_a, v_ev))
            cch = 4 * b + tt
            P.dma(G if gi % 2 == 0 else SY,
                  C.vp_sp[4 * hf:4 * hf + 4, :, cch, :].rearrange("j p y -> p j y"),
                  C.vevst[:, sl, 0:520].rearrange("p (j y) -> p j y", j=4),
                  C.s_vq[sl], waits=[(c_a, v_ev)])
        if gi == 15:
            kv_t[b] = c_t.n

    for blk in range(NBLK):
        par = blk % 2
        rout = []
        q6h = {}
        for half in range(2):
            for tt in (2 * half, 2 * half + 1):
                gt = 4 * blk + tt
                c0 = 10 * tt
                t_x = 16 * (gt // 2 + 1)
                s_x_par = C.s_x[gt % 2]
                xa = xtile[:, gt % 2, :]
                w_ln = [(s_x_par, t_x)]
                if gt == 0:
                    w_ln.append((c_t, C.setup_t))
                if gt >= 2:
                    w_ln.append((c_t, tr_done[gt - 2]))
                P.op(A, lambda e, xa=xa, c0=c0, gt=gt: e.activation(
                    out=xnf[:, gt % 2, :], in_=xa, func=AF.Square,
                    accum_out=stats[:, c0 + 1:c0 + 2]),
                    waits=w_ln, inc=(c_a, 1))
                p2 = P.op(A, lambda e, xa=xa, c0=c0, gt=gt: e.activation(
                    out=xnf[:, gt % 2, :], in_=xa, func=AF.Copy,
                    accum_out=stats[:, c0:c0 + 1]), inc=(c_a, 1))
            b0 = 20 * half
            sv = lambda j, b0=b0: stats[:, b0 + j:b0 + j + 11:10].rearrange(
                "p (a b) -> p a b", b=1)
            q1 = P.op(V, lambda e, sv=sv: e.tensor_scalar_mul(sv(2), sv(0), 1.0 / D),
                      waits=[(c_a, p2)], inc=(c_v, 1))
            q2 = P.op(V, lambda e, sv=sv: e.tensor_scalar_mul(sv(3), sv(1), 1.0 / D),
                      waits=[(c_v, q1)], inc=(c_v, 1))
            q3 = P.op(V, lambda e, sv=sv: e.tensor_tensor(sv(4), sv(2), sv(2), ALU.mult),
                      waits=[(c_v, q2)], inc=(c_v, 1))
            q4 = P.op(V, lambda e, sv=sv: e.tensor_tensor(sv(5), sv(3), sv(4), ALU.subtract),
                      waits=[(c_v, q3)], inc=(c_v, 1))
            q5 = P.op(V, lambda e, sv=sv: e.tensor_scalar_add(sv(6), sv(5), 1e-5),
                      waits=[(c_v, q4)], inc=(c_v, 1))
            a_sq = P.op(A, lambda e, sv=sv: e.activation(out=sv(7), in_=sv(6), func=AF.Sqrt),
                        waits=[(c_v, q5)], inc=(c_a, 1))
            q6h[half] = P.op(V, lambda e, sv=sv: e.reciprocal(sv(8), sv(7)),
                             waits=[(c_a, a_sq)], inc=(c_v, 1))
            for tt in (2 * half, 2 * half + 1):
                gt = 4 * blk + tt
                c0 = 10 * tt
                xa = xtile[:, gt % 2, :]
                wv_ = [(c_t, tr_done[gt - 2])] if gt >= 2 else []
                v_xn = P.op(V, lambda e, xa=xa, gt=gt, c0=c0: e.tensor_scalar(
                    out=xnf[:, gt % 2, :], in0=xa,
                    scalar1=stats[:, c0 + 2:c0 + 3], scalar2=stats[:, c0 + 8:c0 + 9],
                    op0=ALU.subtract, op1=ALU.mult),
                    waits=wv_ + [(c_v, q6h[tt // 2])], inc=(c_v, 1))
                xn_done[gt] = v_xn
                if gt + 2 < 64:
                    load_xtile(gt + 2)
                for dc in range(8):
                    j, (pt, off), frees = psl.acquire()
                    t_tr = P.op(T, lambda e, gt=gt, dc=dc, pt=pt, off=off: e.transpose(
                        pt[0:128, off:off + 128],
                        xnf[:, gt % 2, 128 * dc:128 * dc + 128], C.idf32[...]),
                        waits=[(c_v, v_xn)] + frees, inc=(c_t, 1))
                    bw = [(c_t, t_tr)]
                    if blk >= 2 and dc == 0 and tt == 0:
                        bw.append((C.s_xnt[par], C.s_xnt[par].n))
                        bw.append((c_t, kv_t[blk - 2]))
                    P.op(A, lambda e, tt=tt, dc=dc, pt=pt, off=off, par=par: e.activation(
                        out=xnT[:, par, dc, 128 * tt:128 * tt + 128],
                        in_=pt[0:128, off:off + 128], func=AF.Copy),
                        waits=bw, inc=(c_a, 1))
                    a_f = P.op(A, lambda e, tt=tt, dc=dc, pt=pt, off=off: e.activation(
                        out=xnTf[:, dc, 128 * tt:128 * tt + 128],
                        in_=pt[0:128, off:off + 128], func=AF.Copy), inc=(c_a, 1))
                    psl.release(j, (c_a, a_f))
                tr_done[gt] = c_t.n
                a_f32 = c_a.n

                col0 = 45 * tt
                for dc in range(8):
                    w0 = [(c_a, a_f32), (c_v, c_v.n)] if dc == 0 else []
                    P.op(T, lambda e, dc=dc, tt=tt, col0=col0: e.matmul(
                        C.pX2[0:128, col0:col0 + 45],
                        xnTf[:, dc, 128 * tt:128 * tt + 128],
                        C.nodew_s[:, dc, :], start=(dc == 0), stop=(dc == 7)),
                        waits=w0, inc=(c_t, 1) if dc == 7 else None)
                rout.append((tt, col0, c_t.n))
                ti = 2 * half + (tt - 2 * half)
                if blk >= 1:
                    for gi in range(4 * ti, 4 * ti + 4):
                        emit_kv_group(blk - 1, gi)

        kv_ready[blk] = c_a.n
        for tt, col0, t_r in rout:
            P.op(V, lambda e, tt=tt, col0=col0: e.tensor_tensor(
                scores[:, tt, :, :].rearrange("p f n -> p (f n)"),
                C.pX2[0:128, col0:col0 + 45], C.nodebB[0:128, 0:45], ALU.add),
                waits=[(c_t, t_r)], inc=(c_v, 1))

        scl = [scores[:, :, :, j:j + 1].rearrange("p a f b -> p (a f) b")
               for j in range(15)]

        def dr(j):
            return dscr[:, 12 * j:12 * j + 12].rearrange("p (a b) -> p a b", b=1)

        def bA(j):
            return dscr[:, 12 * j:12 * j + 12].rearrange("p (a f) -> p a f", f=3)

        b0, b1, b2, b3 = dr(0), dr(1), dr(2), dr(3)
        tmp0, tmp1 = dr(4), dr(5)
        t1, tA, tB_, tC, tD = dr(6), dr(7), dr(8), dr(9), dr(10)

        def lerp(dst, a, b, bit):
            w1 = P.op(V, lambda e, a=a, b=b: e.tensor_tensor(tmp0, b, a, ALU.subtract),
                      waits=[(c_v, c_v.n)], inc=(c_v, 1))
            w2 = P.op(V, lambda e, bit=bit: e.tensor_tensor(tmp1, tmp0, bit, ALU.mult),
                      waits=[(c_v, w1)], inc=(c_v, 1))
            P.op(V, lambda e, dst=dst, a=a: e.tensor_tensor(dst, a, tmp1, ALU.add),
                 waits=[(c_v, w2)], inc=(c_v, 1))

        P.op(V, lambda e, s0=scl[0], b0=b0: e.tensor_single_scalar(
            b0, s0, 0.0, ALU.is_ge), waits=[(c_v, c_v.n)], inc=(c_v, 1))
        lerp(t1, scl[1], scl[2], b0)
        P.op(V, lambda e, t1=t1, b1=b1: e.tensor_single_scalar(
            b1, t1, 0.0, ALU.is_ge), waits=[(c_v, c_v.n)], inc=(c_v, 1))
        lerp(tA, scl[3], scl[5], b0)
        lerp(tB_, scl[4], scl[6], b0)
        lerp(t1, tA, tB_, b1)
        P.op(V, lambda e, t1=t1, b2=b2: e.tensor_single_scalar(
            b2, t1, 0.0, ALU.is_ge), waits=[(c_v, c_v.n)], inc=(c_v, 1))
        lerp(tA, scl[7], scl[11], b0)
        lerp(tB_, scl[8], scl[12], b0)
        lerp(tC, scl[9], scl[13], b0)
        lerp(tD, scl[10], scl[14], b0)
        lerp(tA, tA, tC, b1)
        lerp(tB_, tB_, tD, b1)
        lerp(t1, tA, tB_, b2)
        P.op(V, lambda e, t1=t1, b3=b3: e.tensor_single_scalar(
            b3, t1, 0.0, ALU.is_ge), waits=[(c_v, c_v.n)], inc=(c_v, 1))
        lf = leafall[:, :, 4 * blk:4 * blk + 4].rearrange("p f a -> p a f")
        P.op(V, lambda e, lf=lf: e.tensor_scalar_mul(lf, bA(0), 8.0),
             waits=[(c_v, c_v.n)], inc=(c_v, 1))
        P.op(V, lambda e, lf=lf: e.scalar_tensor_tensor(
            out=lf, in0=bA(1), scalar=4.0, in1=lf, op0=ALU.mult, op1=ALU.add),
            waits=[(c_v, c_v.n)], inc=(c_v, 1))
        P.op(V, lambda e, lf=lf: e.scalar_tensor_tensor(
            out=lf, in0=bA(2), scalar=2.0, in1=lf, op0=ALU.mult, op1=ALU.add),
            waits=[(c_v, c_v.n)], inc=(c_v, 1))
        P.op(V, lambda e, lf=lf: e.tensor_tensor(lf, lf, bA(3), ALU.add),
             waits=[(c_v, c_v.n)], inc=(c_v, 1))

        P.dma(SY, C.xnt_sp[blk].rearrange("c p n -> p c n"), xnT[:, par, :, :],
              C.s_xnt[par], waits=[(c_a, c_a.n)])

    for gi in range(16):
        emit_kv_group(NBLK - 1, gi)
    C.ph0a_v = c_v.n
    C.spill_all = ([(C.s_ev[i], C.s_ev[i].n) for i in range(4)] +
                   [(C.s_vq[i], C.s_vq[i].n) for i in range(4)])
    C.xnt_all = [(C.s_xnt[0], C.s_xnt[0].n), (C.s_xnt[1], C.s_xnt[1].n)]


def load_weight_slab(C, src2d, wi, extra_w=()):
    P = C.P
    a = 0
    for q in range(4):
        wld = P.dma("sync", C.wstage[0:128, 0:2, 0:1024],
                    src2d.rearrange("(c p) n -> p c n", p=128)[:, 2 * q:2 * q + 2, :],
                    C.s_w, waits=[(C.c_a, C.c_a.n), (C.c_t, C.c_t.n)] + list(extra_w))
        for j in range(2):
            a = P.op("scalar", lambda e, wi=wi, q=q, j=j: e.activation(
                out=C.wbig[:, wi, 2 * q + j, 0:1024],
                in_=C.wstage[:, j, :], func=AF.Copy),
                waits=[(C.s_w, wld)], inc=(C.c_a, 1))
    return a


def emit_ph0b(C):
    P = C.P
    T, V, A, G, SY = "tensor", "vector", "scalar", "gpsimd", "sync"
    s_sp, c_t, c_v, c_a = C.s_sp, C.c_t, C.c_v, C.c_a
    xnT, QT = C.xnT, C.QT

    msl = Slots([(C.pS1, 0), (C.pS1, 512), (C.pS2, 0), (C.pS2, 512)])
    xr_done = {}

    def mmgroup(lhs_fn, rhs_fn, n_k, width, extra0):
            j, (pt, off), frees = msl.acquire()
            for dc in range(n_k):
                w0 = (extra0 + frees) if dc == 0 else []
                lhs, rhs = lhs_fn(dc), rhs_fn(dc)
                P.op(T, lambda e, lhs=lhs, rhs=rhs, dc=dc, pt=pt, off=off,
                     width=width, n_k=n_k: e.matmul(
                    pt[0:128, off:off + width], lhs, rhs,
                    start=(dc == 0), stop=(dc == n_k - 1)),
                    waits=w0, inc=(c_t, 1) if dc == n_k - 1 else None)
            return j, pt, off, c_t.n

    wq_ok = load_weight_slab(C, C.din["wq"], 0)
    for blk in range(NBLK):
        par = blk % 2
        w = [(c_t, xr_done[blk - 1])] if blk >= 1 else [(c_a, wq_ok)]
        if blk >= 2:
            w.append((c_t, xr_done[blk - 2]))
        s_xr = C.s_xr[par]
        rld = P.dma(SY, xnT[:, par, :, 0:128],
                    C.xnt_sp[blk, :, :, 0:128].rearrange("c p n -> p c n"),
                    s_xr, waits=w)
        qc = own_pos(blk)
        for p8 in range(8):
            e0 = [(s_xr, rld)] + ([(c_a, wq_ok)] if blk == 0 and p8 == 0 else [])
            j, pt, off, tmm = mmgroup(
                lambda dc, p8=p8: C.wbig[:, 0, dc, 128 * p8:128 * p8 + 128]
                if dc < 8 else C.bqflat[0:1, 128 * p8:128 * p8 + 128],
                lambda dc, par=par: xnT[:, par, dc, 0:128]
                if dc < 8 else C.onesq[0:1, 0:128], 9, 128, e0)
            a_ev = P.op(A, lambda e, p8=p8, pt=pt, off=off, qc=qc: e.activation(
                out=QT[:, p8, qc:qc + 128], in_=pt[0:128, off:off + 128],
                func=AF.Copy), waits=[(c_t, tmm)], inc=(c_a, 1))
            msl.release(j, (c_a, a_ev))
        xr_done[blk] = c_t.n


def emit_ph0c(C):
    P = C.P
    T, V, A, G, SY = "tensor", "vector", "scalar", "gpsimd", "sync"
    s_w, s_sp2, c_t, c_v, c_a = C.s_w, C.s_sp2, C.c_t, C.c_v, C.c_a
    xnT, evst, hidT, selT = C.xnT, C.evst, C.hidT, C.selT

    for f in range(3):
        b2ld = P.dma(SY, C.b2f32, C.din["b2c"][f], s_w, waits=[(c_a, c_a.n)])
        P.op(A, lambda e, f=f: e.activation(out=C.b2bf[:, f, :], in_=C.b2f32,
                                            func=AF.Copy),
             waits=[(s_w, b2ld)], inc=(c_a, 1))

    msl = Slots([(C.pS1, 0), (C.pS1, 512), (C.pS2, 0), (C.pS2, 512)])
    esl = Slots([0, 1, 2, 3])
    rld_hist = []
    prev_block_t = 0

    for f in range(3):
        load_weight_slab(C, C.din["w1c"][f], 0)
        w_ok = load_weight_slab(C, C.din["w2c"][f], 1)
        N = 128 if f == 0 else 512

        for blk in range(NBLK):
            par = blk % 2
            w = []
            if len(rld_hist) >= 2:
                w.append((c_t, rld_hist[-2][1]))
            if f == 0 and blk == 0:
                w.append((c_t, C.c_t.n))
            s_xr = C.s_xr[par]
            rld = P.dma(SY, xnT[:, par, :, :], C.xnt_sp[blk].rearrange("c p n -> p c n"),
                        s_xr, waits=w)

            for tt in range(4):
                w0 = [(c_a, c_a.n), (c_v, c_v.n)] if tt == 0 else []
                P.op(T, lambda e, f=f, blk=blk, tt=tt: e.transpose(
                    C.pX2[0:1, 128 * tt:128 * tt + 128],
                    C.leafall[:, f, 4 * blk + tt:4 * blk + tt + 1], C.idf32[...]),
                    waits=w0, inc=(c_t, 1))
            t_lt = c_t.n
            a_lf = P.op(A, lambda e: e.activation(out=C.lrow_f, in_=C.pX2[0:1, 0:512],
                                                  func=AF.Copy),
                        waits=[(c_t, t_lt)], inc=(c_a, 1))
            a_lb = P.op(A, lambda e: e.activation(out=C.lrow_b, in_=C.pX2[0:1, 0:512],
                                                  func=AF.Copy), inc=(c_a, 1))
            t_s16 = P.op(T, lambda e: e.matmul(C.pX1[0:16, 0:512], C.onesbf[0:1, 0:16],
                                               C.lrow_b, start=True, stop=True),
                         waits=[(c_a, a_lb), (c_v, c_v.n)], inc=(c_t, 1))
            v_sel = P.op(V, lambda e: e.tensor_scalar(
                out=selT[...], in0=C.pX1[0:16, 0:512], scalar1=C.iota16[...],
                scalar2=None, op0=ALU.is_equal),
                waits=[(c_t, t_s16)], inc=(c_v, 1))
            t_b128 = P.op(T, lambda e: e.matmul(C.pX2[0:128, 0:512], C.onesbf[0:1, 0:128],
                                                C.lrow_b, start=True, stop=True),
                          waits=[(c_a, a_lb), (c_v, c_v.n)], inc=(c_t, 1))

            hid_v = 0
            for hc in range(8):
                j, (pt, off), frees = msl.acquire()
                w0 = [(s_xr, rld)] + frees
                if hc == 0:
                    w0.append((c_a, w_ok))
                for dc in range(8):
                    P.op(T, lambda e, hc=hc, dc=dc, pt=pt, off=off, par=par, N=N: e.matmul(
                        pt[0:128, off:off + N], C.wbig[:, 0, dc, 128 * hc:128 * hc + 128],
                        xnT[:, par, dc, 0:N], start=(dc == 0), stop=(dc == 7)),
                        waits=w0 if dc == 0 else [], inc=(c_t, 1))
                tmm = c_t.n
                a_h = P.op(A, lambda e, hc=hc, pt=pt, off=off, f=f, N=N: e.activation(
                    out=hidT[:, hc, 0:N], in_=pt[0:128, off:off + N],
                    func=AF.Relu, bias=C.b1c[:, f, hc:hc + 1]),
                    waits=[(c_t, tmm), (c_t, prev_block_t)], inc=(c_a, 1))
                hid_v = P.op(V, lambda e, hc=hc, N=N: e.scalar_tensor_tensor(
                    out=hidT[:, hc, 0:N], in0=C.pX2[0:128, 0:N],
                    scalar=C.hidiota[:, hc:hc + 1], in1=hidT[:, hc, 0:N],
                    op0=ALU.is_equal, op1=ALU.mult),
                    waits=[(c_a, a_h), (c_t, t_b128)], inc=(c_v, 1))
                msl.release(j, (c_v, hid_v))

            if f == 0:
                qc = own_pos(blk)
                for dc in range(8):
                    j, (pt, off), frees = msl.acquire()
                    w0 = [(c_v, hid_v)] + frees
                    for hc in range(8):
                        P.op(T, lambda e, dc=dc, hc=hc, pt=pt, off=off: e.matmul(
                            pt[0:128, off:off + 128],
                            C.wbig[:, 1, hc, 128 * dc:128 * dc + 128],
                            hidT[:, hc, 0:128], start=(hc == 0), stop=False),
                            waits=w0 if hc == 0 else [], inc=(c_t, 1))
                    P.op(T, lambda e, dc=dc, f=f, pt=pt, off=off: e.matmul(
                        pt[0:128, off:off + 128],
                        C.b2bf[0:16, f, 128 * dc:128 * dc + 128],
                        selT[0:16, 0:128], start=False, stop=True),
                        waits=[(c_v, v_sel)], inc=(c_t, 1))
                    tmm = c_t.n
                    je, es, efree = esl.acquire()
                    a_ev = P.op(A, lambda e, pt=pt, off=off, es=es: e.activation(
                        out=evst[:, es, 0:128], in_=pt[0:128, off:off + 128],
                        func=AF.Copy), waits=[(c_t, tmm)] + efree, inc=(c_a, 1))
                    msl.release(j, (c_a, a_ev))
                    P.dma(G, C.fqt_sp[dc][:, qc:qc + 128], evst[:, es, 0:128],
                          s_sp2, waits=[(c_a, a_ev)])
                    esl.release(je, (s_sp2, s_sp2.n))
            elif f == 1:
                for dc in range(8):
                    j, (pt, off), frees = msl.acquire()
                    w0 = [(c_v, hid_v)] + frees
                    for hc in range(8):
                        P.op(T, lambda e, dc=dc, hc=hc, pt=pt, off=off: e.matmul(
                            pt[0:128, off:off + 512],
                            C.wbig[:, 1, hc, 128 * dc:128 * dc + 128],
                            hidT[:, hc, 0:512], start=(hc == 0), stop=False),
                            waits=w0 if hc == 0 else [], inc=(c_t, 1))
                    P.op(T, lambda e, dc=dc, f=f, pt=pt, off=off: e.matmul(
                        pt[0:128, off:off + 512],
                        C.b2bf[0:16, f, 128 * dc:128 * dc + 128],
                        selT[...], start=False, stop=True),
                        waits=[(c_v, v_sel)], inc=(c_t, 1))
                    tmm = c_t.n
                    je, es, efree = esl.acquire()
                    a_ev = P.op(A, lambda e, pt=pt, off=off, es=es: e.activation(
                        out=evst[:, es, :], in_=pt[0:128, off:off + 512], func=AF.Copy),
                        waits=[(c_t, tmm)] + efree, inc=(c_a, 1))
                    msl.release(j, (c_a, a_ev))
                    P.dma(G, C.fkt_sp[dc][:, 512 * blk:512 * blk + 512],
                          evst[:, es, :], s_sp2, waits=[(c_a, a_ev)])
                    esl.release(je, (s_sp2, s_sp2.n))
            else:
                for tt in range(4):
                    for hf in range(2):
                        j, (pt, off), frees = msl.acquire()
                        w0 = [(c_v, hid_v)] + frees
                        for hc in range(8):
                            P.op(T, lambda e, tt=tt, hf=hf, hc=hc, pt=pt, off=off: e.matmul(
                                pt[0:128, off:off + 512],
                                hidT[:, hc, 128 * tt:128 * tt + 128],
                                C.wbig[:, 1, hc, 512 * hf:512 * hf + 512],
                                start=(hc == 0), stop=False),
                                waits=w0 if hc == 0 else [], inc=(c_t, 1))
                        P.op(T, lambda e, tt=tt, hf=hf, f=f, pt=pt, off=off: e.matmul(
                            pt[0:128, off:off + 512],
                            selT[0:16, 128 * tt:128 * tt + 128],
                            C.b2bf[0:16, f, 512 * hf:512 * hf + 512],
                            start=False, stop=True),
                            waits=[(c_v, v_sel)], inc=(c_t, 1))
                        tmm = c_t.n
                        je, es, efree = esl.acquire()
                        a_ev = P.op(A, lambda e, pt=pt, off=off, es=es: e.activation(
                            out=evst[:, es, :], in_=pt[0:128, off:off + 512],
                            func=AF.Copy), waits=[(c_t, tmm)] + efree, inc=(c_a, 1))
                        msl.release(j, (c_a, a_ev))
                        P.dma(G, C.fv_sp[4 * blk + tt][:, 512 * hf:512 * hf + 512],
                              evst[:, es, :], s_sp2, waits=[(c_a, a_ev)])
                        esl.release(je, (s_sp2, s_sp2.n))
            prev_block_t = c_t.n
            rld_hist.append((rld, c_t.n))
    C.ph0c_t = c_t.n
    C.ph0c_sp2 = s_sp2.n


def emit_ph1(C):
    P = C.P
    T, V, A, G, SY = "tensor", "vector", "scalar", "gpsimd", "sync"
    s_sp, c_t, c_v, c_a, c_g = C.s_sp, C.c_t, C.c_v, C.c_a, C.c_g
    QT, ktp, vaug, PT, mxs, mhT = C.QT, C.ktp, C.vaug, C.PT, C.mxs, C.mhT
    stats, sc1, biasq, lnscr = C.stats, C.sc1, C.biasq, C.lnscr

    pair_done = {}
    pt_guard = {}
    st = {"pa_free": 0}
    pools = [Slots([(C.pS1, 0), (C.pS1, 512), (C.pX1, 0)]),
             Slots([(C.pS2, 0), (C.pS2, 512), (C.pX2, 0)])]

    def Nof(c, qb):
        over = c - (16 * qb + 3)
        return 512 if over <= 0 else 512 - 128 * ((over + 3) // 4)

    def emit_p1_chunk(c, qb, p8, kpar, kld, red_v):
        for hh in range(2):
            j, (pt, off), frees = pools[hh].acquire()
            N = Nof(c, qb)
            w0 = [(C.s_mk[kpar], kld)] + frees
            if c == 0 and p8 == 0 and qb == 0:
                w0.append((c_t, getattr(C, "ph0c_t", 0)))
            mm = P.op(T, lambda e, c=c, hh=hh, pt=pt, off=off, N=N,
                      kpar=kpar, p8=p8, qb=qb: e.matmul(
                pt[0:128, off:off + N],
                ktp[64 * hh:64 * hh + 64, kpar, 128 * c:128 * c + 128],
                QT[64 * hh:64 * hh + 64, p8, 512 * qb:512 * qb + N],
                start=True, stop=True, tile_position=(64 * hh, 0)),
                waits=w0, inc=(c_t, 1))
            red = P.op(V, lambda e, c=c, hh=hh, pt=pt, off=off, N=N: e.reduce_max(
                mxs[:, hh, c:c + 1], pt[0:128, off:off + N], axis=AX),
                waits=[(c_t, mm)], inc=(c_v, 1))
            red_v[(c, hh)] = red
            pools[hh].release(j, [(c_v, red)])

    def emit_chain(qb, Cn, red_v):
        qp = qb % 2
        j, (pt, off), frees = pools[0].acquire()
        v_m = P.op(V, lambda e, Cn=Cn: e.reduce_max(
            stats[:, 40:42], mxs[:, 0:2, 0:Cn], axis=AX),
            waits=[(c_v, red_v[(Cn - 1, 1)])], inc=(c_v, 1))
        t_tr = P.op(T, lambda e, pt=pt, off=off: e.transpose(
            pt[0:2, off:off + 128], stats[:, 40:42], C.idf32[...]),
            waits=[(c_v, v_m)] + frees, inc=(c_t, 1))
        v_s = P.op(V, lambda e, pt=pt, off=off: e.reduce_max(
            stats[0:2, 42:43], pt[0:2, off:off + 128], axis=AX),
            waits=[(c_t, t_tr)], inc=(c_v, 1))
        t_t2 = P.op(T, lambda e, pt=pt, off=off: e.transpose(
            pt[0:1, off + 128:off + 130], stats[0:2, 42:43], C.idf32[0:2, 0:2]),
            waits=[(c_v, v_s)], inc=(c_t, 1))
        a_cp = P.op(A, lambda e, pt=pt, off=off: e.activation(
            out=sc1[0:1, 0:2], in_=pt[0:1, off + 128:off + 130], func=AF.Copy),
            waits=[(c_t, t_t2)], inc=(c_a, 1))
        t_b = P.op(T, lambda e, pt=pt, off=off: e.matmul(
            pt[0:128, off + 256:off + 258], C.onesP[0:1, 0:128],
            sc1[0:1, 0:2], start=True, stop=True),
            waits=[(c_a, a_cp)], inc=(c_t, 1))
        a_b = P.op(A, lambda e, qp=qp, pt=pt, off=off: e.activation(
            out=biasq[:, qp, 0:2], in_=pt[0:128, off + 256:off + 258],
            func=AF.Copy, scale=-0.125, bias=70.0),
            waits=[(c_t, t_b)], inc=(c_a, 1))
        pools[0].release(j, [(c_a, a_b)])
        return {0: a_b, 1: a_b}

    def emit_p2_pv(c, qb, Cn, exp_a, pv_t, vld):
        for hh in range(2):
            N = Nof(c, qb)
            dep = [exp_a[(c, hh)]]
            if c == 0:
                dep.append((c_v, st["pa_free"]))
                dep.append((C.s_mv, vld))
            pv = P.op(T, lambda e, c=c, hh=hh, N=N, Cn=Cn: e.matmul(
                C.pA[0:65, 512 * hh:512 * hh + N], vaug[:, c, 65 * hh:65 * hh + 65],
                PT[:, hh, c % 2, 0:N], start=(c == 0), stop=(c == Cn - 1),
                skip_group_check=True),
                waits=dep, inc=(c_t, 1))
            pv_t[(c, hh)] = pv
            pt_guard[(hh, c % 2)] = pv

    def emit_p2_chunk(c, qb, p8, kpar, Cn, bias_a, exp_a, pv_t, vld):
        qp = qb % 2
        for hh in range(2):
            j, (pt, off), frees = pools[hh].acquire()
            N = Nof(c, qb)
            w0 = [(c_a, bias_a[hh])] + frees
            mm = P.op(T, lambda e, c=c, hh=hh, pt=pt, off=off, N=N,
                      kpar=kpar, p8=p8, qb=qb: e.matmul(
                pt[0:128, off:off + N],
                ktp[64 * hh:64 * hh + 64, kpar, 128 * c:128 * c + 128],
                QT[64 * hh:64 * hh + 64, p8, 512 * qb:512 * qb + N],
                start=True, stop=True, tile_position=(64 * hh, 0)),
                waits=w0, inc=(c_t, 1))
            extra = []
            if c % 4 == 0 and 4 * qb <= c // 4 <= 4 * qb + 3:
                dd = (4 * qb + 3) - c // 4
                v_msk = P.op(V, lambda e, pt=pt, off=off, dd=dd: e.tensor_tensor(
                    pt[0:128, off + 128 * dd:off + 128 * dd + 128],
                    pt[0:128, off + 128 * dd:off + 128 * dd + 128],
                    C.trimask[...], ALU.add),
                    waits=[(c_t, mm)], inc=(c_v, 1))
                extra = [(c_v, v_msk)]
            slotg = pt_guard.get((hh, c % 2))
            exp = P.op(A, lambda e, c=c, hh=hh, pt=pt, off=off, N=N, qp=qp: e.activation(
                out=PT[:, hh, c % 2, 0:N], in_=pt[0:128, off:off + N],
                func=AF.Exp, scale=0.125, bias=biasq[:, qp, hh:hh + 1]),
                waits=[(c_t, mm)] + extra +
                      ([(c_t, slotg)] if slotg else []), inc=(c_a, 1))
            pools[hh].release(j, [(c_a, exp)])
            dep = (c_a, exp)
            if c % 4 != 0 and 4 * qb <= c // 4 <= 4 * qb + 3:
                mmod = c % 4
                dd = (4 * qb + 3) - c // 4
                g_d = P.op(V, lambda e, c=c, hh=hh, dd=dd, mmod=mmod: e.tensor_scalar_mul(
                    PT[:, hh, c % 2, 128 * dd:128 * dd + 128],
                    PT[:, hh, c % 2, 128 * dd:128 * dd + 128],
                    C.allowB[:, mmod:mmod + 1]),
                    waits=[dep], inc=(c_v, 1))
                dep = (c_v, g_d)
            exp_a[(c, hh)] = dep
        if c >= 1:
            emit_p2_pv(c - 1, qb, Cn, exp_a, pv_t, vld)
        if c == Cn - 1:
            emit_p2_pv(c, qb, Cn, exp_a, pv_t, vld)

    def emit_norm(qb, p8, Cn, pv_t):
        for hh in range(2):
            h = 2 * p8 + hh
            j, (pt, off), frees = pools[1].acquire()
            v_l = P.op(V, lambda e, hh=hh: e.tensor_scalar_add(
                lnscr[64:65, 0:512], C.pA[64:65, 512 * hh:512 * hh + 512], 1e-30),
                waits=[(c_t, pv_t[(Cn - 1, 1)])], inc=(c_v, 1))
            v_r = P.op(V, lambda e: e.reciprocal(
                lnscr[64:65, 512:1024], lnscr[64:65, 0:512]),
                waits=[(c_v, v_l)], inc=(c_v, 1))
            t_b = P.op(T, lambda e, pt=pt, off=off: e.matmul(
                pt[0:64, off:off + 512], C.onesP[64:65, 0:64],
                lnscr[64:65, 512:1024],
                start=True, stop=True, tile_position=(64, 0)),
                waits=[(c_v, v_r)] + frees, inc=(c_t, 1))
            a_rb = P.op(A, lambda e, pt=pt, off=off: e.activation(
                out=C.reclB, in_=pt[0:64, off:off + 512], func=AF.Copy),
                waits=[(c_t, t_b)], inc=(c_a, 1))
            st["pa_free"] = P.op(V, lambda e, hh=hh, h=h, qb=qb: e.tensor_tensor(
                mhT[64 * (h % 2):64 * (h % 2) + 64, h // 2, 512 * qb:512 * qb + 512],
                C.pA[0:64, 512 * hh:512 * hh + 512], C.reclB, ALU.mult),
                waits=[(c_a, a_rb)], inc=(c_v, 1))
            pools[1].release(j, [(c_a, a_rb)])

    for p8 in range(8):
        kpar = p8 % 2
        w = list(C.spill_all) if p8 <= 1 else []
        if p8 >= 2:
            w.append((c_t, pair_done[p8 - 2]))
        kld = P.dma(G, ktp[:, kpar, :], C.kt_sp[p8], C.s_mk[kpar], waits=w)
        wv = list(C.spill_all) if p8 <= 1 else []
        if p8 >= 1:
            wv.append((c_t, pair_done[p8 - 1]))
        vld = P.dma(G, C.vaug[:, 0:64, 0:130], C.vp_sp[p8], C.s_mv, waits=wv)

        prev = None
        for qb in range(4):
            Cn1 = 16 * qb + 16
            red_v = {}
            for c in range(Cn1):
                emit_p1_chunk(c, qb, p8, kpar, kld, red_v)
                if prev is not None and c < prev[1]:
                    emit_p2_chunk(c, prev[0], p8, kpar, prev[1], prev[2],
                                  prev[3], prev[4], vld)
            if prev is not None:
                emit_norm(prev[0], p8, prev[1], prev[4])
            bias_a = emit_chain(qb, Cn1, red_v)
            prev = (qb, Cn1, bias_a, {}, {})
        for c in range(prev[1]):
            emit_p2_chunk(c, prev[0], p8, kpar, prev[1], prev[2], prev[3],
                          prev[4], vld)
        emit_norm(prev[0], p8, prev[1], prev[4])
        pair_done[p8] = c_t.n

    if C.mh_dbg is not None:
        for dcq in range(8):
            P.dma(SY, C.mh_dbg[dcq], mhT[:, dcq, :], C.s_out,
                  waits=[(c_v, c_v.n)])
    C.ph1_v = c_v.n
    C.ph1_t = c_t.n


def emit_ph2(C):
    P = C.P
    T, V, A, G, SY = "tensor", "vector", "scalar", "gpsimd", "sync"
    s_w, s_sp2, s_out = C.s_w, C.s_sp2, C.s_out
    c_t, c_v, c_a, c_g = C.c_t, C.c_v, C.c_a, C.c_g
    mhT, qTb, ktf, vf = C.mhT, C.qTb, C.ktf, C.vf
    P2T, A2, l2, fsc, fB, m2t = C.P2T, C.A2, C.l2, C.fsc, C.fB, C.m2t
    stats, outb, xtile = C.stats, C.outb, C.xtile

    wo_ok = load_weight_slab(C, C.din["wo"], 0, extra_w=[(c_t, C.ph1_t)])
    if C.wo_dbg is not None:
        P.dma(SY, C.wo_dbg, C.wbig[:, 0, :, :], C.s_out, waits=[(c_a, wo_ok)])
        P.dma(SY, C.bo_dbg, C.boB[0:128, :], C.s_out, waits=[(c_a, wo_ok)])

    sst = C.sload

    slots2 = [(C.pS1, 0), (C.pS1, 512), (C.pS2, 0), (C.pS2, 512)]
    exp_prev = {}
    sup_hist = {}
    st2 = {"mrg": C.ph1_v, "pv_done": 0}

    def emit_ot_group(ot):
        sj = ot % 2
        ev = 0
        for hf in range(2):
            for dc in range(8):
                w0 = []
                if dc == 0 and hf == 0:
                    w0 = [(c_v, st2["mrg"])]
                    if ot == 0:
                        w0 += [(c_a, wo_ok), (c_v, C.ph1_v)]
                P.op(T, lambda e, ot=ot, hf=hf, dc=dc: e.matmul(
                    C.pA[0:128, 512 * hf:512 * hf + 512],
                    mhT[:, dc, 128 * ot:128 * ot + 128],
                    C.wbig[:, 0, dc, 512 * hf:512 * hf + 512],
                    start=(dc == 0), stop=(dc == 7), skip_group_check=True),
                    waits=w0, inc=(c_t, 1) if dc == 7 else None)
            tmm = c_t.n
            w0 = [(c_t, tmm)]
            if ot >= 2 and hf == 0:
                w0.append((C.s_ss, C.s_ss.n))
            ev = P.op(V, lambda e, hf=hf, sj=sj: e.tensor_tensor(
                sst[:, sj, 512 * hf:512 * hf + 512],
                C.pA[0:128, 512 * hf:512 * hf + 512],
                C.boB[:, 512 * hf:512 * hf + 512], ALU.add),
                waits=w0, inc=(c_v, 1))
        P.dma(G, C.s_sp3[ot], sst[:, sj, :], C.s_ss, waits=[(c_v, ev)])
        st2["mrg"] = ev

    def emit_qk2_group(qb, sup, spar, kld, qld, mz, cc):
        gs = 16 * qb + sup
        pt, off = slots2[cc]
        w0 = [(C.s_fk[spar], kld), (C.s_q, qld)]
        if exp_prev.get(cc):
            w0.append((c_a, exp_prev[cc]))
        if gs == 0 and cc == 0:
            w0.append((c_g, mz))
            w0.append((c_v, c_v.n))
        for dc in range(8):
            P.op(T, lambda e, cc=cc, dc=dc, pt=pt, off=off, spar=spar: e.matmul(
                pt[0:128, off:off + 512],
                ktf[:, spar, dc, 128 * cc:128 * cc + 128],
                qTb[:, dc, 0:512], start=(dc == 0), stop=(dc == 7)),
                waits=w0 if dc == 0 else [], inc=(c_t, 1) if dc == 7 else None)
        tqk = c_t.n
        P.op(V, lambda e, cc=cc, pt=pt, off=off: e.reduce_max(
            m2t[:, cc:cc + 1], pt[0:128, off:off + 512], axis=AX),
            waits=[(c_t, tqk)], inc=(c_v, 1))

    def emit_chain2(sup, mz):
        v_m = P.op(V, lambda e: e.reduce_max(stats[:, 44:45], m2t[...], axis=AX),
                   inc=(c_v, 1))
        t_tr = P.op(T, lambda e: e.transpose(C.pX1[0:1, 0:128], stats[:, 44:45],
                                             C.idf32[...]),
                    waits=[(c_v, v_m), (c_v, c_v.n), (c_a, c_a.n)], inc=(c_t, 1))
        v_s = P.op(V, lambda e: e.reduce_max(fsc[0:1, 1:2], C.pX1[0:1, 0:128],
                                             axis=AX),
                   waits=[(c_t, t_tr), (c_g, mz)], inc=(c_v, 1))
        v_mx = P.op(V, lambda e: e.tensor_tensor(fsc[0:1, 5:6], fsc[0:1, 0:1],
                                                 fsc[0:1, 1:2], ALU.max),
                    waits=[(c_v, v_s)], inc=(c_v, 1))
        v_df = P.op(V, lambda e: e.tensor_tensor(fsc[0:1, 3:4], fsc[0:1, 0:1],
                                                 fsc[0:1, 5:6], ALU.subtract),
                    waits=[(c_v, v_mx)], inc=(c_v, 1))
        a_f = P.op(A, lambda e: e.activation(out=fsc[0:1, 4:5], in_=fsc[0:1, 3:4],
                                             func=AF.Exp, scale=1.0 / 32),
                   waits=[(c_v, v_df), (c_v, st2["mrg"])], inc=(c_a, 1))
        v_cp = P.op(V, lambda e: e.tensor_copy(fsc[0:1, 0:1], fsc[0:1, 5:6]),
                    waits=[(c_a, a_f)], inc=(c_v, 1))
        t_b = P.op(T, lambda e: e.matmul(C.pX1[0:128, 192:194], C.onesP[0:1, 0:128],
                                         fsc[0:1, 4:6], start=True, stop=True),
                   waits=[(c_a, a_f), (c_v, c_v.n)], inc=(c_t, 1))
        a_fB = P.op(A, lambda e: e.activation(out=fB[:, 0:1], in_=C.pX1[0:128, 192:193],
                                              func=AF.Copy),
                    waits=[(c_t, t_b), (c_v, st2["mrg"])], inc=(c_a, 1))
        a_b2 = P.op(A, lambda e: e.activation(out=fB[:, 1:2], in_=C.pX1[0:128, 193:194],
                                              func=AF.Copy, scale=-1.0 / 32),
                    waits=[(c_t, t_b)], inc=(c_a, 1))
        return a_b2

    def emit_exps2(sup, a_b2):
        for cc in range(4):
            pt, off = slots2[cc]
            exp_prev[cc] = P.op(A, lambda e, cc=cc, pt=pt, off=off: e.activation(
                out=P2T[:, cc, :], in_=pt[0:128, off:off + 512],
                func=AF.Exp, scale=1.0 / 32, bias=fB[:, 1:2]),
                waits=[(c_a, a_b2)] + ([(c_t, st2["pv_done"])] if cc == 0 else []),
                inc=(c_a, 1))
        return c_a.n

    def emit_pv2_group(qb, sup, spar, vld, exps, qt):
        lb = C.pX1 if qt % 2 == 0 else C.pX2
        w0 = [(c_a, exps), (C.s_fv[spar], vld), (c_v, st2["mrg"])]
        for cc in range(4):
            for hf in range(2):
                P.op(T, lambda e, qt=qt, cc=cc, hf=hf, spar=spar: e.matmul(
                    C.pA[0:128, 512 * hf:512 * hf + 512],
                    P2T[:, cc, 128 * qt:128 * qt + 128],
                    vf[:, spar, cc, 512 * hf:512 * hf + 512],
                    start=(cc == 0), stop=(cc == 3), skip_group_check=True),
                    waits=w0 if (cc == 0 and hf == 0) else [])
            P.op(T, lambda e, qt=qt, cc=cc, lb=lb: e.matmul(
                lb[0:128, 300 + qt:301 + qt],
                P2T[:, cc, 128 * qt:128 * qt + 128],
                C.onescolb[...], start=(cc == 0), stop=(cc == 3),
                skip_group_check=True), inc=(c_t, 1) if cc == 3 else None)
        tpv = c_t.n
        P.op(V, lambda e, qt=qt: e.scalar_tensor_tensor(
            out=A2[:, qt, :], in0=A2[:, qt, :], scalar=fB[:, 0:1],
            in1=C.pA[0:128, 0:1024], op0=ALU.mult, op1=ALU.add),
            waits=[(c_t, tpv)], inc=(c_v, 1))
        st2["mrg"] = P.op(V, lambda e, qt=qt, lb=lb: e.scalar_tensor_tensor(
            out=l2[:, qt:qt + 1], in0=l2[:, qt:qt + 1], scalar=fB[:, 0:1],
            in1=lb[0:128, 300 + qt:301 + qt], op0=ALU.mult, op1=ALU.add),
            inc=(c_v, 1))
        if qt == 3:
            st2["pv_done"] = c_t.n

    def emit_pv2(qb, sup, spar, vld, exps):
        for qt in range(4):
            emit_pv2_group(qb, sup, spar, vld, exps, qt)

    def emit_gcombine(qb, mrg_prev):
        for qt in range(4):
            i_own = 4 * qb + (3 - qt)
            v_l = P.op(V, lambda e, qt=qt: e.tensor_scalar_add(
                stats[:, 46:47], l2[:, qt:qt + 1], 1e-30),
                waits=[(c_v, mrg_prev)], inc=(c_v, 1))
            v_r = P.op(V, lambda e: e.reciprocal(stats[:, 47:48], stats[:, 46:47]),
                       waits=[(c_v, v_l)], inc=(c_v, 1))
            a_g = P.op(A, lambda e, qt=qt: e.activation(
                out=C.gsb[0:128, 0:1024], in_=A2[:, qt, :], func=AF.Silu,
                scale=stats[:, 47:48]),
                waits=[(c_v, v_r), (c_v, c_v.n)], inc=(c_a, 1))
            xld = P.dma(SY, xtile[:, qt % 2, :],
                        C.din["x"][512 * i_own:512 * i_own + 128, :], C.s_xl[qt % 2],
                        waits=[(c_v, c_v.n)])
            sld = P.dma(SY, C.sload[:, qt % 2, :], C.s_sp3[4 * qb + qt],
                        C.s_sl[qt % 2], waits=[(c_v, c_v.n), (C.s_ss, C.s_ss.n)])
            v_sg = P.op(V, lambda e, qb=qb, qt=qt: e.scalar_tensor_tensor(
                out=outb[:, qt % 2, :], in0=C.sload[:, qt % 2, :], scalar=1.0,
                in1=C.gsb[0:128, 0:1024], op0=ALU.mult, op1=ALU.mult),
                waits=[(c_a, a_g), (C.s_sl[qt % 2], sld)], inc=(c_v, 1))
            v_o = P.op(V, lambda e, qt=qt: e.tensor_tensor(
                outb[:, qt % 2, :], outb[:, qt % 2, :], xtile[:, qt % 2, :],
                ALU.add), waits=[(C.s_xl[qt % 2], xld), (c_v, v_sg)], inc=(c_v, 1))
            P.dma(SY, C.out_d[512 * qb + 128 * qt:512 * qb + 128 * qt + 128, :],
                  outb[:, qt % 2, :], s_out, waits=[(c_v, v_o)])

    pend_g = None
    for qb in range(4):
        qld = P.dma(SY, qTb[0:128, 0:8, 0:512],
                    C.fqt_sp[:, :, 512 * qb:512 * qb + 512].rearrange("c p n -> p c n"),
                    C.s_q, waits=[(s_sp2, C.ph0c_sp2), (C.s_ss, C.s_ss.n),
                                  (c_t, c_t.n)])
        if pend_g is not None:
            emit_gcombine(pend_g[0], pend_g[1])
        P.op(G, lambda e: e.memset(A2[0:128, 0:4, 0:1024], 0.0),
             waits=[(c_v, c_v.n), (c_a, c_a.n)], inc=(c_g, 1))
        P.op(G, lambda e: e.memset(l2[:, 0:4], 0.0), inc=(c_g, 1))
        mz = P.op(G, lambda e: e.memset(fsc[0:1, 0:1], NEGBIG), inc=(c_g, 1))

        prev2 = None
        for sup in range(16):
            gs = 16 * qb + sup
            spar = sup % 2
            w = [] if gs >= 2 else [(s_sp2, C.ph0c_sp2)]
            if gs >= 2:
                w.append((c_t, sup_hist[gs - 2]))
            kld = P.dma(G, ktf[:, spar, :, :],
                        C.fkt_sp[:, :, 512 * sup:512 * sup + 512].rearrange("c p n -> p c n"),
                        C.s_fk[spar], waits=w)
            wv2 = list(w)
            if gs >= 1:
                wv2.append((c_t, sup_hist[gs - 1]))
            vld = P.dma(G, vf[:, spar, :, :],
                        C.fv_sp[4 * sup:4 * sup + 4].rearrange("c p n -> p c n"),
                        C.s_fv[spar], waits=wv2)
            for i in range(4):
                if prev2 is not None:
                    emit_pv2_group(qb, prev2[0], prev2[1], prev2[2], prev2[3], i)
                emit_qk2_group(qb, sup, spar, kld, qld, mz, i)
            if qb == 0:
                emit_ot_group(sup)
            a_b2 = emit_chain2(sup, mz)
            exps = emit_exps2(sup, a_b2)
            prev2 = (sup, spar, vld, exps)
            sup_hist[gs] = c_t.n
        emit_pv2(qb, prev2[0], prev2[1], prev2[2], prev2[3])
        sup_hist[16 * qb + 15] = c_t.n
        pend_g = (qb, st2["mrg"])

    emit_gcombine(pend_g[0], pend_g[1])



_NC_CACHE = {}


def kernel(**inputs):
    if "nc" not in _NC_CACHE:
        _NC_CACHE["nc"] = build_nc()
    nc = _NC_CACHE["nc"]
    in_maps = host_inputs(inputs)
    res = run_bass_kernel_spmd(nc, in_maps, core_ids=list(range(8)))
    return host_output(res.results)


if __name__ == "__main__":
    pass


# version 20
# speedup vs baseline: 1.0348x; 1.0045x over previous
import contextlib
import numpy as np
import ml_dtypes
import concourse.bass as bass
import concourse.mybir as mybir
from concourse.bass_utils import run_bass_kernel_spmd

F32 = mybir.dt.float32
BF16 = mybir.dt.bfloat16
AF = mybir.ActivationFunctionType
ALU = mybir.AluOpType
AX = mybir.AxisListType.X
BF = ml_dtypes.bfloat16

B, S, D = 2, 8192, 1024
H, NP = 16, 8
NBLK = 16
NEG = -1.0e30
NEGBIG = -3.0e38

STAGE = 4
DEBUG_SPILLS = False


def local_tile_order(k):
    order = []
    for t in range(NBLK):
        order.append(4 * t + k)
        order.extend(4 * t + j for j in range(4) if j != k)
    return order


def own_pos(i):
    qb, r = divmod(i, 4)
    return 512 * qb + 128 * (3 - r)


def host_inputs(full):
    xf = np.ascontiguousarray(np.asarray(full["L0_attn_residual"], np.float32))
    WQ = np.asarray(full["L0_attn_W_Q"], np.float32)
    WK = np.asarray(full["L0_attn_W_K"], np.float32)
    WV = np.asarray(full["L0_attn_W_V"], np.float32)
    bQ = np.asarray(full["L0_attn_b_Q"], np.float32)
    bK = np.asarray(full["L0_attn_b_K"], np.float32)
    bV = np.asarray(full["L0_attn_b_V"], np.float32)

    def pairmajor(W):
        return np.ascontiguousarray(W.transpose(1, 0, 2).reshape(D, D))

    bqk = np.zeros((128, 16), np.float32)
    for p in range(NP):
        bqk[:, p] = bQ[2 * p:2 * p + 2].reshape(128)
        bqk[:, 8 + p] = bK[2 * p:2 * p + 2].reshape(128)

    w1c = np.stack([np.asarray(full[f"L0_ffn_{f}_w1"], np.float32)
                    .transpose(1, 0, 2).reshape(D, D) for f in "qkv"])
    w2c = np.stack([np.asarray(full[f"L0_ffn_{f}_w2"], np.float32)
                    .reshape(D, D) for f in "qkv"])
    b1c = np.zeros((128, 3, 8), np.float32)
    for fi, f in enumerate("qkv"):
        b1c[:, fi, :] = np.asarray(full[f"L0_ffn_{f}_b1"], np.float32).reshape(8, 128).T
    b2c = np.stack([np.asarray(full[f"L0_ffn_{f}_b2"], np.float32) for f in "qkv"])
    nodew = np.concatenate([np.ascontiguousarray(
        np.asarray(full[f"L0_ffn_{f}_node_w"], np.float32).T) for f in "qkv"],
        axis=1)
    nodeb = np.concatenate([np.asarray(full[f"L0_ffn_{f}_node_b"], np.float32)
                            .reshape(1, 15) for f in "qkv"], axis=1)

    kk = np.arange(128)[:, None]
    rr = np.arange(128)[None, :]
    trimask = np.where(kk <= rr, 0.0, NEG).astype(np.float32)
    hidiota = np.zeros((128, 8), np.float32)
    for hc in range(8):
        hidiota[:, hc] = 2 * hc + np.arange(128) // 64

    common = dict(
        wq=pairmajor(WQ), wk=pairmajor(WK), wv=pairmajor(WV), bqk=bqk,
        bvrow=np.ascontiguousarray(bV.reshape(1, D)),
        wo=np.ascontiguousarray(np.asarray(full["L0_attn_W_O"], np.float32)),
        borow=np.ascontiguousarray(
            np.asarray(full["L0_attn_b_O"], np.float32).reshape(1, D)),
        w1c=w1c, w2c=w2c, b1c=b1c, b2c=b2c, nodew=nodew, nodeb=nodeb,
        idf32=np.eye(128, dtype=np.float32), trimask=trimask,
        onesrow=np.ones((1, 128), np.float32),
        onesrow_bf=np.ones((1, 128), BF),
        hidiota=hidiota,
        iota16=np.arange(16, dtype=np.float32).reshape(16, 1),
        onescol_bf=np.ones((128, 1), BF),
    )
    maps = []
    for c in range(8):
        b, k = c // 4, c % 4
        order = local_tile_order(k)
        x_all = np.concatenate([xf[b, 128 * g:128 * g + 128] for g in order], axis=0)
        others = [j for j in range(4) if j != k]
        allow = np.zeros((1, 4), np.float32)
        allow[0, 0] = 1.0
        for m in range(3):
            allow[0, m + 1] = 1.0 if others[m] < k else 0.0
        d = dict(common)
        d["x"] = np.ascontiguousarray(x_all)
        d["allow"] = allow
        maps.append(d)
    return maps


def host_output(results):
    out = np.zeros((B, S, D), np.float32)
    for c in range(8):
        b, k = c // 4, c % 4
        oc = np.asarray(results[c]["out"], np.float32)
        for i in range(NBLK):
            g = 4 * i + k
            out[b, 128 * g:128 * g + 128] = oc[own_pos(i):own_pos(i) + 128]
    return out



class Sem:
    __slots__ = ("h", "n", "name")

    def __init__(self, name):
        self.h = None
        self.n = 0
        self.name = name


class Prog:
    def __init__(self, nc):
        self.nc = nc
        self.ops = {k: [] for k in ("sync", "gpsimd", "tensor", "vector", "scalar")}
        self.sems = []

    def sem(self, name):
        s = Sem(name)
        self.sems.append(s)
        return s

    def op(self, eng, fn, waits=(), inc=None):
        incs = []
        ret = None
        if inc is not None:
            s, a = inc
            s.n += a
            incs.append((s, a))
            ret = s.n
        ws = []
        for w in waits:
            if w is None:
                continue
            s, v = w
            if v and v > 0:
                ws.append((s, v))
        self.ops[eng].append((fn, ws, incs))
        return ret

    def dma(self, q, out, in_, sem, waits=()):
        return self.op(q, lambda e: e.dma_start(out=out, in_=in_), waits, (sem, 16))

    def emit(self):
        nc = self.nc
        with contextlib.ExitStack() as st:
            for s in self.sems:
                s.h = st.enter_context(nc.semaphore(s.name))
            with nc.Block() as block:
                def runner(name):
                    def run(e):
                        for (fn, waits, incs) in self.ops[name]:
                            for (s, v) in waits:
                                e.wait_ge(s.h, v)
                            ins = fn(e)
                            for (s, a) in incs:
                                ins.then_inc(s.h, a)
                    return run

                block.sync(runner("sync"))
                block.gpsimd(runner("gpsimd"))
                block.tensor(runner("tensor"))
                block.vector(runner("vector"))
                block.scalar(runner("scalar"))


class Slots:

    def __init__(self, slots):
        self.slots = slots
        self.free = [[] for _ in slots]
        self.i = 0

    def acquire(self):
        j = self.i % len(self.slots)
        self.i += 1
        return j, self.slots[j], list(self.free[j])

    def release(self, j, wait):
        self.free[j] = wait if isinstance(wait, list) else [wait]


class Ctx:
    pass


def build_nc():
    nc = bass.Bass("TRN2", target_bir_lowering=False)
    P = Prog(nc)
    C = Ctx()
    C.P = P

    din = {}
    for name, shape, dt in [
        ("x", [S, D], F32), ("wq", [D, D], F32), ("wk", [D, D], F32),
        ("wv", [D, D], F32), ("bqk", [128, 16], F32), ("bvrow", [1, D], F32),
        ("wo", [D, D], F32), ("borow", [1, D], F32),
        ("w1c", [3, D, D], F32), ("w2c", [3, D, D], F32),
        ("b1c", [128, 3, 8], F32), ("b2c", [3, 16, D], F32),
        ("nodew", [D, 45], F32), ("nodeb", [1, 45], F32),
        ("idf32", [128, 128], F32), ("trimask", [128, 128], F32),
        ("onesrow", [1, 128], F32), ("onesrow_bf", [1, 128], BF16),
        ("hidiota", [128, 8], F32), ("iota16", [16, 1], F32),
        ("onescol_bf", [128, 1], BF16), ("allow", [1, 4], F32),
    ]:
        din[name] = nc.dram_tensor(name, shape, dt, kind="ExternalInput").ap()
    C.din = din
    C.out_d = nc.dram_tensor("out", [2048, D], F32, kind="ExternalOutput").ap()

    kd = "ExternalOutput" if DEBUG_SPILLS else "Internal"
    C.xnt_sp = nc.dram_tensor("xnt_sp", [NBLK, 8, 128, 512], BF16, kind=kd).ap()
    C.kt_sp = nc.dram_tensor("kt_sp", [NP, 128, S], BF16, kind=kd).ap()
    C.vp_sp = nc.dram_tensor("vp_sp", [NP, 128, 64, 130], BF16, kind=kd).ap()
    C.fkt_sp = nc.dram_tensor("fkt_sp", [8, 128, S], BF16, kind=kd).ap()
    C.fv_sp = nc.dram_tensor("fv_sp", [64, 128, D], BF16, kind=kd).ap()
    C.fqt_sp = nc.dram_tensor("fqt_sp", [8, 128, 2048], BF16, kind=kd).ap()
    C.s_sp3 = nc.dram_tensor("s_sp3", [16, 128, D], BF16, kind=kd).ap()
    C.mh_dbg = nc.dram_tensor("mh_dbg", [8, 128, 2048], BF16, kind=kd).ap() \
        if DEBUG_SPILLS else None
    C.wo_dbg = nc.dram_tensor("wo_dbg", [128, 8, 1024], BF16, kind=kd).ap() \
        if DEBUG_SPILLS else None
    C.bo_dbg = nc.dram_tensor("bo_dbg", [128, D], F32, kind=kd).ap() \
        if DEBUG_SPILLS else None

    ctx = contextlib.ExitStack()
    sbuf = lambda n, sh, dt: ctx.enter_context(nc.sbuf_tensor(n, sh, dt))
    psum = lambda n, sh: ctx.enter_context(nc.psum_tensor(n, sh, F32))

    with ctx:
        C.pS1 = psum("pS1", [128, 1024])
        C.pS2 = psum("pS2", [128, 1024])
        C.pA = psum("pA", [128, 1024])
        C.pX1 = psum("pX1", [128, 512])
        C.pX2 = psum("pX2", [128, 512])

        C.idf32 = sbuf("c_idf32", [128, 128], F32)
        C.trimask = sbuf("c_trimask", [128, 128], F32)
        C.onesP = sbuf("c_onesP", [128, 128], F32)
        C.onesbf = sbuf("c_onesbf", [1, 128], BF16)
        C.hidiota = sbuf("c_hidiota", [128, 8], F32)
        C.iota16 = sbuf("c_iota16", [16, 1], F32)
        C.onescolb = sbuf("c_onescolb", [128, 1], BF16)
        C.bqk = sbuf("c_bqk", [128, 16], F32)
        C.b1c = sbuf("c_b1c", [128, 3, 8], F32)
        C.allowrow = sbuf("c_allow", [1, 4], F32)
        C.nodew_s = sbuf("c_nodew", [128, 8, 45], F32)
        C.nodebB = sbuf("c_nodebB", [128, 45], F32)
        C.nodeb_s = sbuf("c_nodeb", [1, 45], F32)

        C.bvB = sbuf("bvB", [128, D], F32)
        C.reclB = C.bvB[0:64, 0:512]
        C.boB = sbuf("boB", [128, D], F32)
        C.allowB = sbuf("allowB", [128, 4], F32)

        ar1 = sbuf("ar1", [128, 16384], BF16)
        ar2 = sbuf("ar2", [128, 16384], BF16)
        ar3 = sbuf("ar3", [128, 16384], BF16)
        ar4 = sbuf("ar4", [128, 8320], BF16)
        ar5 = sbuf("ar5", [128, 4096], F32)

        C.QT = ar1[...].rearrange("p (c n) -> p c n", c=8)
        C.qTb = ar1[:, 0:4096].rearrange("p (c n) -> p c n", c=8)
        C.P2T = ar1[:, 4096:6144].rearrange("p (c n) -> p c n", c=4)
        C.gsb = ar1[:, 6144:7168]

        C.xnT = ar2[:, 0:8192].rearrange("p (a c n) -> p a c n", a=2, c=8)
        C.hidT = ar2[:, 8192:12288].rearrange("p (c n) -> p c n", c=8)
        C.evst = ar2[:, 12288:14336].rearrange("p (c n) -> p c n", c=4)
        C.vevst = ar2[:, 8192:10368].rearrange("p (a n) -> p a n", a=4)
        C.mhT = ar2[...].rearrange("p (c n) -> p c n", c=8)

        C.ktp = ar3[...].rearrange("p (a n) -> p a n", a=2)
        C.ktf = ar3[:, 0:8192].rearrange("p (a c n) -> p a c n", a=2, c=8)
        C.vf = ar3[:, 8192:16384].rearrange("p (a c n) -> p a c n", a=2, c=4)

        C.vaug = ar4[...].rearrange("p (c n) -> p c n", c=64)

        C.xnTf = ar5[...].rearrange("p (c n) -> p c n", c=8)
        C.A2 = ar5[...].rearrange("p (c n) -> p c n", c=4)

        C.xtile = sbuf("xtile", [128, 2, 1024], F32)
        C.xnf = sbuf("xnf", [128, 2, 1024], F32)
        C.outb = C.xnf
        C.stats = sbuf("stats", [128, 48], F32)
        C.scores = sbuf("scores", [128, 4, 3, 15], F32)
        C.leafall = sbuf("leafall", [128, 3, 64], F32)
        C.dscr = sbuf("dscr", [128, 136], F32)

        C.wbig = sbuf("wbig", [128, 2, 8, 1024], BF16)
        C.wstage = sbuf("wstage", [128, 2, 1024], F32)
        C.bvrow_s = C.wstage[0:1, 0:1, 0:1024].rearrange("p a n -> p (a n)")
        C.borow_s = C.wstage[0:1, 1:2, 0:1024].rearrange("p a n -> p (a n)")
        C.lnscr = C.wstage[0:128, 1:2, 0:1024].rearrange("p a n -> p (a n)")
        C.b2bf = ar4[0:16, 0:3072].rearrange("p (f n) -> p f n", f=3)
        C.b2f32 = C.wstage[0:16, 0:1, 0:1024].rearrange("p a n -> p (a n)")
        pts = sbuf("pts", [128, 2048], BF16)
        C.PT = pts[:, 0:2048].rearrange("p (a b n) -> p a b n", a=2, b=2)
        C.onesq = pts[0:1, 0:512]
        C.bqflat = pts[0:1, 512:1536]
        C.onesk = pts[32:33, 0:512]
        C.bkflat = pts[32:33, 512:1536]
        C.mxs = sbuf("mxs", [128, 2, 64], F32)
        C.selT = sbuf("selT", [16, 512], BF16)
        C.lrow_f = C.xnf[0:1, 0:1, 0:512].rearrange("p a n -> p (a n)")
        C.lrow_b = pts[0:1, 0:512]
        C.sc1 = sbuf("sc1", [1, 16], F32)
        C.biasq = sbuf("biasq", [128, 2, 2], F32)
        C.l2 = sbuf("l2", [128, 4], F32)
        C.fsc = sbuf("fsc", [1, 8], F32)
        C.fB = sbuf("fB", [128, 2], F32)
        C.m2t = sbuf("m2t", [128, 4], F32)
        C.sload = pts[:, 0:2048].rearrange("p (a n) -> p a n", a=2)

        for s in ("s_in", "s_w", "s_sp", "s_sp2", "s_ss", "s_out", "s_mv", "s_q",
                  "c_t", "c_v", "c_a", "c_g"):
            setattr(C, s, P.sem(s))
        C.s_x = [P.sem(f"s_x{i}") for i in range(2)]
        C.s_xnt = [P.sem(f"s_xnt{i}") for i in range(2)]
        C.s_xr = [P.sem(f"s_xr{i}") for i in range(2)]
        C.s_ev = [P.sem(f"s_ev{i}") for i in range(4)]
        C.s_vq = [P.sem(f"s_vq{i}") for i in range(4)]
        C.s_mk = [P.sem(f"s_mk{i}") for i in range(2)]
        C.s_fk = [P.sem(f"s_fk{i}") for i in range(2)]
        C.s_fv = [P.sem(f"s_fv{i}") for i in range(2)]
        C.s_xl = [P.sem(f"s_xl{i}") for i in range(2)]
        C.s_sl = [P.sem(f"s_sl{i}") for i in range(2)]

        emit_setup(C)
        emit_ph0a(C)
        if STAGE >= 2:
            emit_ph0b(C)
            emit_ph0c(C)
        if STAGE >= 3:
            emit_ph1(C)
        if STAGE >= 4:
            emit_ph2(C)

        P.op("gpsimd", lambda e: e.engine_nop(),
             waits=[(C.s_out, C.s_out.n)], inc=(C.c_g, 1))
        P.emit()
    return nc


def emit_setup(C):
    P, din = C.P, C.din
    s_in, c_t, c_a = C.s_in, C.c_t, C.c_a
    for nm, dst in [("idf32", C.idf32[...]), ("trimask", C.trimask[...]),
                    ("onesrow", C.onesP[0:1, :]), ("onesrow", C.onesP[64:65, :]),
                    ("onesrow_bf", C.onesbf[...]), ("hidiota", C.hidiota[...]),
                    ("iota16", C.iota16[...]), ("onescol_bf", C.onescolb[...]),
                    ("bqk", C.bqk[...]), ("b1c", C.b1c[...]), ("allow", C.allowrow[...]),
                    ("nodeb", C.nodeb_s[...]), ("bvrow", C.bvrow_s),
                    ("borow", C.borow_s)]:
        P.dma("sync", dst, din[nm], s_in)
    P.dma("sync", C.nodew_s[...],
          din["nodew"].rearrange("(c p) n -> p c n", p=128), s_in)
    C.consts_ok = s_in.n

    def mk_bcast(rowsl, width, dst, scale=1.0):
        t = P.op("tensor", lambda e: e.matmul(
            C.pX1[0:128, 0:width], C.onesP[0:1, 0:128], rowsl,
            start=True, stop=True),
            waits=[(s_in, C.consts_ok), (c_a, c_a.n)], inc=(c_t, 1))
        P.op("scalar", lambda e: e.activation(
            out=dst, in_=C.pX1[0:128, 0:width], func=AF.Copy, scale=scale),
            waits=[(c_t, t)], inc=(c_a, 1))

    for hf in range(2):
        mk_bcast(C.bvrow_s[0:1, 512 * hf:512 * hf + 512], 512,
                 C.bvB[0:128, 512 * hf:512 * hf + 512])
        mk_bcast(C.borow_s[0:1, 512 * hf:512 * hf + 512], 512,
                 C.boB[0:128, 512 * hf:512 * hf + 512])
    mk_bcast(C.allowrow[0:1, 0:4], 4, C.allowB[0:128, 0:4])
    mk_bcast(C.nodeb_s[0:1, 0:45], 45, C.nodebB[0:128, 0:45])
    bqkT = C.selT[0:16, 384:512]
    t_bq = P.op("tensor", lambda e: e.transpose(
        C.pX1[0:16, 0:128], C.bqk[...], C.idf32[...]),
        waits=[(s_in, C.consts_ok), (c_a, c_a.n)], inc=(c_t, 1))
    a_bq = P.op("scalar", lambda e: e.activation(
        out=bqkT, in_=C.pX1[0:16, 0:128], func=AF.Copy),
        waits=[(c_t, t_bq)], inc=(c_a, 1))
    for i in range(16):
        dst = (C.bqflat[0:1, 128 * i:128 * i + 128] if i < 8 else
               C.bkflat[0:1, 128 * (i - 8):128 * (i - 8) + 128])
        P.dma("sync", dst, bqkT[i:i + 1, 0:128], s_in,
              waits=[(c_a, a_bq)] if i == 0 else [])
    C.bflat_ok = s_in.n
    P.op("gpsimd", lambda e: e.memset(C.onesq, 1.0), inc=(C.c_g, 1))
    P.op("gpsimd", lambda e: e.memset(C.onesk, 1.0), inc=(C.c_g, 1))
    C.setup_t = C.c_t.n


def emit_ph0a(C):
    P = C.P
    T, V, A, G, SY = "tensor", "vector", "scalar", "gpsimd", "sync"
    s_in, s_sp, c_t, c_v, c_a = C.s_in, C.s_sp, C.c_t, C.c_v, C.c_a
    xnf, xtile, stats = C.xnf, C.xtile, C.stats
    xnT, xnTf, scores, leafall, dscr = C.xnT, C.xnTf, C.scores, C.leafall, C.dscr
    evst, dscr = C.evst, C.dscr

    w_ok = max(load_weight_slab(C, C.din["wk"], 0),
               load_weight_slab(C, C.din["wv"], 1))
    vev4 = C.vevst[:, :, 0:520].rearrange("p a (j y) -> p a j y", j=4)
    m_on = 0
    for sl in range(4):
        P.op(G, lambda e, sl=sl: e.memset(vev4[:, sl, :, 64:65], 1.0),
             waits=[(c_a, c_a.n)], inc=(C.c_g, 1))
        m_on = P.op(G, lambda e, sl=sl: e.memset(vev4[:, sl, :, 129:130], 1.0),
                    inc=(C.c_g, 1))

    x_t = C.din["x"].rearrange("(t p) d -> t p d", p=128)
    psl = Slots([(C.pS1, 0), (C.pS1, 512), (C.pS2, 0), (C.pS2, 512)])
    esl = Slots([0, 1, 2, 3])
    xn_done = {}
    tr_done = {}
    kv_t = {}

    def load_xtile(gt):
        w = []
        if gt >= 2:
            w.append((c_v, xn_done[gt - 2]))
        if gt == 0:
            w.append((s_in, C.consts_ok))
        P.dma(SY, xtile[:, gt % 2, :], x_t[gt], C.s_x[gt % 2], waits=w)

    def mmgroup(lhs_fn, rhs_fn, n_k, width, extra0):
        j, (pt, off), frees = psl.acquire()
        for dc in range(n_k):
            w0 = (extra0 + frees) if dc == 0 else []
            lhs, rhs = lhs_fn(dc), rhs_fn(dc)
            P.op(T, lambda e, lhs=lhs, rhs=rhs, dc=dc, pt=pt, off=off,
                 width=width, n_k=n_k: e.matmul(
                pt[0:128, off:off + width], lhs, rhs,
                start=(dc == 0), stop=(dc == n_k - 1)),
                waits=w0, inc=(c_t, 1) if dc == n_k - 1 else None)
        return j, pt, off, c_t.n

    load_xtile(0)
    load_xtile(1)
    kv_ready = {}

    def emit_kv_group(b, gi):
        parb = b % 2
        if gi < 8:
            p8 = gi
            e0 = [(c_a, kv_ready[b])]
            if b == 0 and p8 == 0:
                e0 += [(c_a, w_ok), (C.c_g, m_on), (C.s_in, C.bflat_ok)]
            j, pt, off, tmm = mmgroup(
                lambda dc, p8=p8: C.wbig[:, 0, dc, 128 * p8:128 * p8 + 128]
                if dc < 8 else C.bkflat[0:1, 128 * p8:128 * p8 + 128],
                lambda dc, parb=parb: xnT[:, parb, dc, 0:512]
                if dc < 8 else C.onesk[0:1, 0:512], 9, 512, e0)
            je, es, efree = esl.acquire()
            a_ev = P.op(A, lambda e, pt=pt, off=off, es=es: e.activation(
                out=evst[:, es, :], in_=pt[0:128, off:off + 512], func=AF.Copy),
                waits=[(c_t, tmm)] + efree, inc=(c_a, 1))
            psl.release(j, (c_a, a_ev))
            P.dma(G if gi % 2 == 0 else SY,
                  C.kt_sp[p8][:, 512 * b:512 * b + 512], evst[:, es, :],
                  C.s_ev[es], waits=[(c_a, a_ev)])
            esl.release(je, (C.s_ev[es], C.s_ev[es].n))
        else:
            tt, hf = (gi - 8) // 2, (gi - 8) % 2
            vsl = (gi - 8) % 4
            j, pt, off, tmm = mmgroup(
                lambda dc, tt=tt, parb=parb: xnT[:, parb, dc, 128 * tt:128 * tt + 128]
                if dc < 8 else C.onesP[0:1, 0:128],
                lambda dc, hf=hf: C.wbig[:, 1, dc, 512 * hf:512 * hf + 512]
                if dc < 8 else C.bvB[0:1, 512 * hf:512 * hf + 512], 9, 512,
                [(c_a, kv_ready[b])])
            sl = vsl
            v_ev = 0
            for hhx in range(2):
                v_ev = P.op(A, lambda e, pt=pt, off=off, sl=sl, hhx=hhx: e.activation(
                    out=vev4[:, sl, :, 65 * hhx:65 * hhx + 64],
                    in_=pt[0:128, off:off + 512].rearrange(
                        "p (j h x) -> p j h x", j=4, h=2)[:, :, hhx, :],
                    func=AF.Copy),
                    waits=[(c_t, tmm), (C.c_g, m_on),
                           (C.s_vq[sl], C.s_vq[sl].n)] if hhx == 0 else [],
                    inc=(c_a, 1))
            psl.release(j, (c_a, v_ev))
            cch = 4 * b + tt
            P.dma(G if gi % 2 == 0 else SY,
                  C.vp_sp[4 * hf:4 * hf + 4, :, cch, :].rearrange("j p y -> p j y"),
                  C.vevst[:, sl, 0:520].rearrange("p (j y) -> p j y", j=4),
                  C.s_vq[sl], waits=[(c_a, v_ev)])
        if gi == 15:
            kv_t[b] = c_t.n

    for blk in range(NBLK):
        par = blk % 2
        rout = []
        q6h = {}
        for half in range(2):
            for tt in (2 * half, 2 * half + 1):
                gt = 4 * blk + tt
                c0 = 10 * tt
                t_x = 16 * (gt // 2 + 1)
                s_x_par = C.s_x[gt % 2]
                xa = xtile[:, gt % 2, :]
                w_ln = [(s_x_par, t_x)]
                if gt == 0:
                    w_ln.append((c_t, C.setup_t))
                if gt >= 2:
                    w_ln.append((c_t, tr_done[gt - 2]))
                P.op(A, lambda e, xa=xa, c0=c0, gt=gt: e.activation(
                    out=xnf[:, gt % 2, :], in_=xa, func=AF.Square,
                    accum_out=stats[:, c0 + 1:c0 + 2]),
                    waits=w_ln, inc=(c_a, 1))
                p2 = P.op(A, lambda e, xa=xa, c0=c0, gt=gt: e.activation(
                    out=xnf[:, gt % 2, :], in_=xa, func=AF.Copy,
                    accum_out=stats[:, c0:c0 + 1]), inc=(c_a, 1))
            b0 = 20 * half
            sv = lambda j, b0=b0: stats[:, b0 + j:b0 + j + 11:10].rearrange(
                "p (a b) -> p a b", b=1)
            q1 = P.op(V, lambda e, sv=sv: e.tensor_scalar_mul(sv(2), sv(0), 1.0 / D),
                      waits=[(c_a, p2)], inc=(c_v, 1))
            q2 = P.op(V, lambda e, sv=sv: e.tensor_scalar_mul(sv(3), sv(1), 1.0 / D),
                      waits=[(c_v, q1)], inc=(c_v, 1))
            q3 = P.op(V, lambda e, sv=sv: e.tensor_tensor(sv(4), sv(2), sv(2), ALU.mult),
                      waits=[(c_v, q2)], inc=(c_v, 1))
            q4 = P.op(V, lambda e, sv=sv: e.tensor_tensor(sv(5), sv(3), sv(4), ALU.subtract),
                      waits=[(c_v, q3)], inc=(c_v, 1))
            q5 = P.op(V, lambda e, sv=sv: e.tensor_scalar_add(sv(6), sv(5), 1e-5),
                      waits=[(c_v, q4)], inc=(c_v, 1))
            a_sq = P.op(A, lambda e, sv=sv: e.activation(out=sv(7), in_=sv(6), func=AF.Sqrt),
                        waits=[(c_v, q5)], inc=(c_a, 1))
            q6h[half] = P.op(V, lambda e, sv=sv: e.reciprocal(sv(8), sv(7)),
                             waits=[(c_a, a_sq)], inc=(c_v, 1))
            for tt in (2 * half, 2 * half + 1):
                gt = 4 * blk + tt
                c0 = 10 * tt
                xa = xtile[:, gt % 2, :]
                wv_ = [(c_t, tr_done[gt - 2])] if gt >= 2 else []
                v_xn = P.op(V, lambda e, xa=xa, gt=gt, c0=c0: e.tensor_scalar(
                    out=xnf[:, gt % 2, :], in0=xa,
                    scalar1=stats[:, c0 + 2:c0 + 3], scalar2=stats[:, c0 + 8:c0 + 9],
                    op0=ALU.subtract, op1=ALU.mult),
                    waits=wv_ + [(c_v, q6h[tt // 2])], inc=(c_v, 1))
                xn_done[gt] = v_xn
                if gt + 2 < 64:
                    load_xtile(gt + 2)
                for dc in range(8):
                    j, (pt, off), frees = psl.acquire()
                    t_tr = P.op(T, lambda e, gt=gt, dc=dc, pt=pt, off=off: e.transpose(
                        pt[0:128, off:off + 128],
                        xnf[:, gt % 2, 128 * dc:128 * dc + 128], C.idf32[...]),
                        waits=[(c_v, v_xn)] + frees, inc=(c_t, 1))
                    bw = [(c_t, t_tr)]
                    if blk >= 2 and dc == 0 and tt == 0:
                        bw.append((C.s_xnt[par], C.s_xnt[par].n))
                        bw.append((c_t, kv_t[blk - 2]))
                    P.op(A, lambda e, tt=tt, dc=dc, pt=pt, off=off, par=par: e.activation(
                        out=xnT[:, par, dc, 128 * tt:128 * tt + 128],
                        in_=pt[0:128, off:off + 128], func=AF.Copy),
                        waits=bw, inc=(c_a, 1))
                    a_f = P.op(A, lambda e, tt=tt, dc=dc, pt=pt, off=off: e.activation(
                        out=xnTf[:, dc, 128 * tt:128 * tt + 128],
                        in_=pt[0:128, off:off + 128], func=AF.Copy), inc=(c_a, 1))
                    psl.release(j, (c_a, a_f))
                tr_done[gt] = c_t.n
                a_f32 = c_a.n

                col0 = 45 * tt
                for dc in range(8):
                    w0 = [(c_a, a_f32), (c_v, c_v.n)] if dc == 0 else []
                    P.op(T, lambda e, dc=dc, tt=tt, col0=col0: e.matmul(
                        C.pX2[0:128, col0:col0 + 45],
                        xnTf[:, dc, 128 * tt:128 * tt + 128],
                        C.nodew_s[:, dc, :], start=(dc == 0), stop=(dc == 7)),
                        waits=w0, inc=(c_t, 1) if dc == 7 else None)
                rout.append((tt, col0, c_t.n))
                ti = 2 * half + (tt - 2 * half)
                if blk >= 1:
                    for gi in range(4 * ti, 4 * ti + 4):
                        emit_kv_group(blk - 1, gi)

        kv_ready[blk] = c_a.n
        for tt, col0, t_r in rout:
            P.op(V, lambda e, tt=tt, col0=col0: e.tensor_tensor(
                scores[:, tt, :, :].rearrange("p f n -> p (f n)"),
                C.pX2[0:128, col0:col0 + 45], C.nodebB[0:128, 0:45], ALU.add),
                waits=[(c_t, t_r)], inc=(c_v, 1))

        scl = [scores[:, :, :, j:j + 1].rearrange("p a f b -> p (a f) b")
               for j in range(15)]

        def dr(j):
            return dscr[:, 12 * j:12 * j + 12].rearrange("p (a b) -> p a b", b=1)

        def bA(j):
            return dscr[:, 12 * j:12 * j + 12].rearrange("p (a f) -> p a f", f=3)

        b0, b1, b2, b3 = dr(0), dr(1), dr(2), dr(3)
        tmp0, tmp1 = dr(4), dr(5)
        t1, tA, tB_, tC, tD = dr(6), dr(7), dr(8), dr(9), dr(10)

        def lerp(dst, a, b, bit):
            w1 = P.op(V, lambda e, a=a, b=b: e.tensor_tensor(tmp0, b, a, ALU.subtract),
                      waits=[(c_v, c_v.n)], inc=(c_v, 1))
            w2 = P.op(V, lambda e, bit=bit: e.tensor_tensor(tmp1, tmp0, bit, ALU.mult),
                      waits=[(c_v, w1)], inc=(c_v, 1))
            P.op(V, lambda e, dst=dst, a=a: e.tensor_tensor(dst, a, tmp1, ALU.add),
                 waits=[(c_v, w2)], inc=(c_v, 1))

        P.op(V, lambda e, s0=scl[0], b0=b0: e.tensor_single_scalar(
            b0, s0, 0.0, ALU.is_ge), waits=[(c_v, c_v.n)], inc=(c_v, 1))
        lerp(t1, scl[1], scl[2], b0)
        P.op(V, lambda e, t1=t1, b1=b1: e.tensor_single_scalar(
            b1, t1, 0.0, ALU.is_ge), waits=[(c_v, c_v.n)], inc=(c_v, 1))
        lerp(tA, scl[3], scl[5], b0)
        lerp(tB_, scl[4], scl[6], b0)
        lerp(t1, tA, tB_, b1)
        P.op(V, lambda e, t1=t1, b2=b2: e.tensor_single_scalar(
            b2, t1, 0.0, ALU.is_ge), waits=[(c_v, c_v.n)], inc=(c_v, 1))
        lerp(tA, scl[7], scl[11], b0)
        lerp(tB_, scl[8], scl[12], b0)
        lerp(tC, scl[9], scl[13], b0)
        lerp(tD, scl[10], scl[14], b0)
        lerp(tA, tA, tC, b1)
        lerp(tB_, tB_, tD, b1)
        lerp(t1, tA, tB_, b2)
        P.op(V, lambda e, t1=t1, b3=b3: e.tensor_single_scalar(
            b3, t1, 0.0, ALU.is_ge), waits=[(c_v, c_v.n)], inc=(c_v, 1))
        lf = leafall[:, :, 4 * blk:4 * blk + 4].rearrange("p f a -> p a f")
        P.op(V, lambda e, lf=lf: e.tensor_scalar_mul(lf, bA(0), 8.0),
             waits=[(c_v, c_v.n)], inc=(c_v, 1))
        P.op(V, lambda e, lf=lf: e.scalar_tensor_tensor(
            out=lf, in0=bA(1), scalar=4.0, in1=lf, op0=ALU.mult, op1=ALU.add),
            waits=[(c_v, c_v.n)], inc=(c_v, 1))
        P.op(V, lambda e, lf=lf: e.scalar_tensor_tensor(
            out=lf, in0=bA(2), scalar=2.0, in1=lf, op0=ALU.mult, op1=ALU.add),
            waits=[(c_v, c_v.n)], inc=(c_v, 1))
        P.op(V, lambda e, lf=lf: e.tensor_tensor(lf, lf, bA(3), ALU.add),
             waits=[(c_v, c_v.n)], inc=(c_v, 1))

        P.dma(SY, C.xnt_sp[blk].rearrange("c p n -> p c n"), xnT[:, par, :, :],
              C.s_xnt[par], waits=[(c_a, c_a.n)])

    for gi in range(16):
        emit_kv_group(NBLK - 1, gi)
    C.ph0a_v = c_v.n
    C.spill_all = ([(C.s_ev[i], C.s_ev[i].n) for i in range(4)] +
                   [(C.s_vq[i], C.s_vq[i].n) for i in range(4)])
    C.xnt_all = [(C.s_xnt[0], C.s_xnt[0].n), (C.s_xnt[1], C.s_xnt[1].n)]


def load_weight_slab(C, src2d, wi, extra_w=()):
    P = C.P
    a = 0
    for q in range(4):
        wld = P.dma("sync", C.wstage[0:128, 0:2, 0:1024],
                    src2d.rearrange("(c p) n -> p c n", p=128)[:, 2 * q:2 * q + 2, :],
                    C.s_w, waits=[(C.c_a, C.c_a.n), (C.c_t, C.c_t.n)] + list(extra_w))
        for j in range(2):
            a = P.op("scalar", lambda e, wi=wi, q=q, j=j: e.activation(
                out=C.wbig[:, wi, 2 * q + j, 0:1024],
                in_=C.wstage[:, j, :], func=AF.Copy),
                waits=[(C.s_w, wld)], inc=(C.c_a, 1))
    return a


def emit_ph0b(C):
    P = C.P
    T, V, A, G, SY = "tensor", "vector", "scalar", "gpsimd", "sync"
    s_sp, c_t, c_v, c_a = C.s_sp, C.c_t, C.c_v, C.c_a
    xnT, QT = C.xnT, C.QT

    msl = Slots([(C.pS1, 0), (C.pS1, 512), (C.pS2, 0), (C.pS2, 512)])
    xr_done = {}

    def mmgroup(lhs_fn, rhs_fn, n_k, width, extra0):
            j, (pt, off), frees = msl.acquire()
            for dc in range(n_k):
                w0 = (extra0 + frees) if dc == 0 else []
                lhs, rhs = lhs_fn(dc), rhs_fn(dc)
                P.op(T, lambda e, lhs=lhs, rhs=rhs, dc=dc, pt=pt, off=off,
                     width=width, n_k=n_k: e.matmul(
                    pt[0:128, off:off + width], lhs, rhs,
                    start=(dc == 0), stop=(dc == n_k - 1)),
                    waits=w0, inc=(c_t, 1) if dc == n_k - 1 else None)
            return j, pt, off, c_t.n

    wq_ok = load_weight_slab(C, C.din["wq"], 0)
    for sb in range(4):
        par = sb % 2
        w = [(c_t, xr_done[sb - 1])] if sb >= 1 else [(c_a, wq_ok)]
        if sb >= 2:
            w.append((c_t, xr_done[sb - 2]))
        s_xr = C.s_xr[par]
        rld = 0
        for jj in range(4):
            rld = P.dma(SY, xnT[:, par, :, 128 * (3 - jj):128 * (3 - jj) + 128],
                        C.xnt_sp[4 * sb + jj, :, :, 0:128].rearrange("c p n -> p c n"),
                        s_xr, waits=w if jj == 0 else [])
        for p8 in range(8):
            e0 = [(s_xr, rld)] + ([(c_a, wq_ok)] if sb == 0 and p8 == 0 else [])
            j, pt, off, tmm = mmgroup(
                lambda dc, p8=p8: C.wbig[:, 0, dc, 128 * p8:128 * p8 + 128]
                if dc < 8 else C.bqflat[0:1, 128 * p8:128 * p8 + 128],
                lambda dc, par=par: xnT[:, par, dc, 0:512]
                if dc < 8 else C.onesq[0:1, 0:512], 9, 512, e0)
            a_ev = P.op(A, lambda e, p8=p8, pt=pt, off=off, sb=sb: e.activation(
                out=QT[:, p8, 512 * sb:512 * sb + 512], in_=pt[0:128, off:off + 512],
                func=AF.Copy), waits=[(c_t, tmm)], inc=(c_a, 1))
            msl.release(j, (c_a, a_ev))
        xr_done[sb] = c_t.n


def emit_ph0c(C):
    P = C.P
    T, V, A, G, SY = "tensor", "vector", "scalar", "gpsimd", "sync"
    s_w, s_sp2, c_t, c_v, c_a = C.s_w, C.s_sp2, C.c_t, C.c_v, C.c_a
    xnT, evst, hidT, selT = C.xnT, C.evst, C.hidT, C.selT

    for f in range(3):
        b2ld = P.dma(SY, C.b2f32, C.din["b2c"][f], s_w, waits=[(c_a, c_a.n)])
        P.op(A, lambda e, f=f: e.activation(out=C.b2bf[:, f, :], in_=C.b2f32,
                                            func=AF.Copy),
             waits=[(s_w, b2ld)], inc=(c_a, 1))

    msl = Slots([(C.pS1, 0), (C.pS1, 512), (C.pS2, 0), (C.pS2, 512)])
    esl = Slots([0, 1, 2, 3])
    rld_hist = []
    prev_block_t = 0

    for f in range(3):
        load_weight_slab(C, C.din["w1c"][f], 0)
        w_ok = load_weight_slab(C, C.din["w2c"][f], 1)
        N = 128 if f == 0 else 512

        for blk in range(NBLK):
            par = blk % 2
            w = []
            if len(rld_hist) >= 2:
                w.append((c_t, rld_hist[-2][1]))
            if f == 0 and blk == 0:
                w.append((c_t, C.c_t.n))
            s_xr = C.s_xr[par]
            rld = P.dma(SY, xnT[:, par, :, :], C.xnt_sp[blk].rearrange("c p n -> p c n"),
                        s_xr, waits=w)

            for tt in range(4):
                w0 = [(c_a, c_a.n), (c_v, c_v.n)] if tt == 0 else []
                P.op(T, lambda e, f=f, blk=blk, tt=tt: e.transpose(
                    C.pX2[0:1, 128 * tt:128 * tt + 128],
                    C.leafall[:, f, 4 * blk + tt:4 * blk + tt + 1], C.idf32[...]),
                    waits=w0, inc=(c_t, 1))
            t_lt = c_t.n
            a_lf = P.op(A, lambda e: e.activation(out=C.lrow_f, in_=C.pX2[0:1, 0:512],
                                                  func=AF.Copy),
                        waits=[(c_t, t_lt)], inc=(c_a, 1))
            a_lb = P.op(A, lambda e: e.activation(out=C.lrow_b, in_=C.pX2[0:1, 0:512],
                                                  func=AF.Copy), inc=(c_a, 1))
            t_s16 = P.op(T, lambda e: e.matmul(C.pX1[0:16, 0:512], C.onesbf[0:1, 0:16],
                                               C.lrow_b, start=True, stop=True),
                         waits=[(c_a, a_lb), (c_v, c_v.n)], inc=(c_t, 1))
            v_sel = P.op(V, lambda e: e.tensor_scalar(
                out=selT[...], in0=C.pX1[0:16, 0:512], scalar1=C.iota16[...],
                scalar2=None, op0=ALU.is_equal),
                waits=[(c_t, t_s16)], inc=(c_v, 1))
            t_b128 = P.op(T, lambda e: e.matmul(C.pX2[0:128, 0:512], C.onesbf[0:1, 0:128],
                                                C.lrow_b, start=True, stop=True),
                          waits=[(c_a, a_lb), (c_v, c_v.n)], inc=(c_t, 1))

            hid_v = 0
            for hc in range(8):
                j, (pt, off), frees = msl.acquire()
                w0 = [(s_xr, rld)] + frees
                if hc == 0:
                    w0.append((c_a, w_ok))
                for dc in range(8):
                    P.op(T, lambda e, hc=hc, dc=dc, pt=pt, off=off, par=par, N=N: e.matmul(
                        pt[0:128, off:off + N], C.wbig[:, 0, dc, 128 * hc:128 * hc + 128],
                        xnT[:, par, dc, 0:N], start=(dc == 0), stop=(dc == 7)),
                        waits=w0 if dc == 0 else [], inc=(c_t, 1))
                tmm = c_t.n
                a_h = P.op(A, lambda e, hc=hc, pt=pt, off=off, f=f, N=N: e.activation(
                    out=hidT[:, hc, 0:N], in_=pt[0:128, off:off + N],
                    func=AF.Relu, bias=C.b1c[:, f, hc:hc + 1]),
                    waits=[(c_t, tmm), (c_t, prev_block_t)], inc=(c_a, 1))
                hid_v = P.op(V, lambda e, hc=hc, N=N: e.scalar_tensor_tensor(
                    out=hidT[:, hc, 0:N], in0=C.pX2[0:128, 0:N],
                    scalar=C.hidiota[:, hc:hc + 1], in1=hidT[:, hc, 0:N],
                    op0=ALU.is_equal, op1=ALU.mult),
                    waits=[(c_a, a_h), (c_t, t_b128)], inc=(c_v, 1))
                msl.release(j, (c_v, hid_v))

            if f == 0:
                qc = own_pos(blk)
                for dc in range(8):
                    j, (pt, off), frees = msl.acquire()
                    w0 = [(c_v, hid_v)] + frees
                    for hc in range(8):
                        P.op(T, lambda e, dc=dc, hc=hc, pt=pt, off=off: e.matmul(
                            pt[0:128, off:off + 128],
                            C.wbig[:, 1, hc, 128 * dc:128 * dc + 128],
                            hidT[:, hc, 0:128], start=(hc == 0), stop=False),
                            waits=w0 if hc == 0 else [], inc=(c_t, 1))
                    P.op(T, lambda e, dc=dc, f=f, pt=pt, off=off: e.matmul(
                        pt[0:128, off:off + 128],
                        C.b2bf[0:16, f, 128 * dc:128 * dc + 128],
                        selT[0:16, 0:128], start=False, stop=True),
                        waits=[(c_v, v_sel)], inc=(c_t, 1))
                    tmm = c_t.n
                    je, es, efree = esl.acquire()
                    a_ev = P.op(A, lambda e, pt=pt, off=off, es=es: e.activation(
                        out=evst[:, es, 0:128], in_=pt[0:128, off:off + 128],
                        func=AF.Copy), waits=[(c_t, tmm)] + efree, inc=(c_a, 1))
                    msl.release(j, (c_a, a_ev))
                    P.dma(G, C.fqt_sp[dc][:, qc:qc + 128], evst[:, es, 0:128],
                          s_sp2, waits=[(c_a, a_ev)])
                    esl.release(je, (s_sp2, s_sp2.n))
            elif f == 1:
                for dc in range(8):
                    j, (pt, off), frees = msl.acquire()
                    w0 = [(c_v, hid_v)] + frees
                    for hc in range(8):
                        P.op(T, lambda e, dc=dc, hc=hc, pt=pt, off=off: e.matmul(
                            pt[0:128, off:off + 512],
                            C.wbig[:, 1, hc, 128 * dc:128 * dc + 128],
                            hidT[:, hc, 0:512], start=(hc == 0), stop=False),
                            waits=w0 if hc == 0 else [], inc=(c_t, 1))
                    P.op(T, lambda e, dc=dc, f=f, pt=pt, off=off: e.matmul(
                        pt[0:128, off:off + 512],
                        C.b2bf[0:16, f, 128 * dc:128 * dc + 128],
                        selT[...], start=False, stop=True),
                        waits=[(c_v, v_sel)], inc=(c_t, 1))
                    tmm = c_t.n
                    je, es, efree = esl.acquire()
                    a_ev = P.op(A, lambda e, pt=pt, off=off, es=es: e.activation(
                        out=evst[:, es, :], in_=pt[0:128, off:off + 512], func=AF.Copy),
                        waits=[(c_t, tmm)] + efree, inc=(c_a, 1))
                    msl.release(j, (c_a, a_ev))
                    P.dma(G, C.fkt_sp[dc][:, 512 * blk:512 * blk + 512],
                          evst[:, es, :], s_sp2, waits=[(c_a, a_ev)])
                    esl.release(je, (s_sp2, s_sp2.n))
            else:
                for tt in range(4):
                    for hf in range(2):
                        j, (pt, off), frees = msl.acquire()
                        w0 = [(c_v, hid_v)] + frees
                        for hc in range(8):
                            P.op(T, lambda e, tt=tt, hf=hf, hc=hc, pt=pt, off=off: e.matmul(
                                pt[0:128, off:off + 512],
                                hidT[:, hc, 128 * tt:128 * tt + 128],
                                C.wbig[:, 1, hc, 512 * hf:512 * hf + 512],
                                start=(hc == 0), stop=False),
                                waits=w0 if hc == 0 else [], inc=(c_t, 1))
                        P.op(T, lambda e, tt=tt, hf=hf, f=f, pt=pt, off=off: e.matmul(
                            pt[0:128, off:off + 512],
                            selT[0:16, 128 * tt:128 * tt + 128],
                            C.b2bf[0:16, f, 512 * hf:512 * hf + 512],
                            start=False, stop=True),
                            waits=[(c_v, v_sel)], inc=(c_t, 1))
                        tmm = c_t.n
                        je, es, efree = esl.acquire()
                        a_ev = P.op(A, lambda e, pt=pt, off=off, es=es: e.activation(
                            out=evst[:, es, :], in_=pt[0:128, off:off + 512],
                            func=AF.Copy), waits=[(c_t, tmm)] + efree, inc=(c_a, 1))
                        msl.release(j, (c_a, a_ev))
                        P.dma(G, C.fv_sp[4 * blk + tt][:, 512 * hf:512 * hf + 512],
                              evst[:, es, :], s_sp2, waits=[(c_a, a_ev)])
                        esl.release(je, (s_sp2, s_sp2.n))
            prev_block_t = c_t.n
            rld_hist.append((rld, c_t.n))
    C.ph0c_t = c_t.n
    C.ph0c_sp2 = s_sp2.n


def emit_ph1(C):
    P = C.P
    T, V, A, G, SY = "tensor", "vector", "scalar", "gpsimd", "sync"
    s_sp, c_t, c_v, c_a, c_g = C.s_sp, C.c_t, C.c_v, C.c_a, C.c_g
    QT, ktp, vaug, PT, mxs, mhT = C.QT, C.ktp, C.vaug, C.PT, C.mxs, C.mhT
    stats, sc1, biasq, lnscr = C.stats, C.sc1, C.biasq, C.lnscr

    pair_done = {}
    pt_guard = {}
    st = {"pa_free": 0}
    pools = [Slots([(C.pS1, 0), (C.pS1, 512), (C.pX1, 0)]),
             Slots([(C.pS2, 0), (C.pS2, 512), (C.pX2, 0)])]

    def Nof(c, qb):
        over = c - (16 * qb + 3)
        return 512 if over <= 0 else 512 - 128 * ((over + 3) // 4)

    def emit_p1_chunk(c, qb, p8, kpar, kld, red_v):
        for hh in range(2):
            j, (pt, off), frees = pools[hh].acquire()
            N = Nof(c, qb)
            w0 = [(C.s_mk[kpar], kld)] + frees
            if c == 0 and p8 == 0 and qb == 0:
                w0.append((c_t, getattr(C, "ph0c_t", 0)))
            mm = P.op(T, lambda e, c=c, hh=hh, pt=pt, off=off, N=N,
                      kpar=kpar, p8=p8, qb=qb: e.matmul(
                pt[0:128, off:off + N],
                ktp[64 * hh:64 * hh + 64, kpar, 128 * c:128 * c + 128],
                QT[64 * hh:64 * hh + 64, p8, 512 * qb:512 * qb + N],
                start=True, stop=True, tile_position=(64 * hh, 0)),
                waits=w0, inc=(c_t, 1))
            red = P.op(V, lambda e, c=c, hh=hh, pt=pt, off=off, N=N: e.reduce_max(
                mxs[:, hh, c:c + 1], pt[0:128, off:off + N], axis=AX),
                waits=[(c_t, mm)], inc=(c_v, 1))
            red_v[(c, hh)] = red
            pools[hh].release(j, [(c_v, red)])

    def emit_chain(qb, Cn, red_v):
        qp = qb % 2
        j, (pt, off), frees = pools[0].acquire()
        v_m = P.op(V, lambda e, Cn=Cn: e.reduce_max(
            stats[:, 40:42], mxs[:, 0:2, 0:Cn], axis=AX),
            waits=[(c_v, red_v[(Cn - 1, 1)])], inc=(c_v, 1))
        t_tr = P.op(T, lambda e, pt=pt, off=off: e.transpose(
            pt[0:2, off:off + 128], stats[:, 40:42], C.idf32[...]),
            waits=[(c_v, v_m)] + frees, inc=(c_t, 1))
        v_s = P.op(V, lambda e, pt=pt, off=off: e.reduce_max(
            stats[0:2, 42:43], pt[0:2, off:off + 128], axis=AX),
            waits=[(c_t, t_tr)], inc=(c_v, 1))
        t_t2 = P.op(T, lambda e, pt=pt, off=off: e.transpose(
            pt[0:1, off + 128:off + 130], stats[0:2, 42:43], C.idf32[0:2, 0:2]),
            waits=[(c_v, v_s)], inc=(c_t, 1))
        a_cp = P.op(A, lambda e, pt=pt, off=off: e.activation(
            out=sc1[0:1, 0:2], in_=pt[0:1, off + 128:off + 130], func=AF.Copy),
            waits=[(c_t, t_t2)], inc=(c_a, 1))
        t_b = P.op(T, lambda e, pt=pt, off=off: e.matmul(
            pt[0:128, off + 256:off + 258], C.onesP[0:1, 0:128],
            sc1[0:1, 0:2], start=True, stop=True),
            waits=[(c_a, a_cp)], inc=(c_t, 1))
        a_b = P.op(A, lambda e, qp=qp, pt=pt, off=off: e.activation(
            out=biasq[:, qp, 0:2], in_=pt[0:128, off + 256:off + 258],
            func=AF.Copy, scale=-0.125, bias=70.0),
            waits=[(c_t, t_b)], inc=(c_a, 1))
        pools[0].release(j, [(c_a, a_b)])
        return {0: a_b, 1: a_b}

    def emit_p2_pv(c, qb, Cn, exp_a, pv_t, vld):
        for hh in range(2):
            N = Nof(c, qb)
            dep = [exp_a[(c, hh)]]
            if c == 0:
                dep.append((c_v, st["pa_free"]))
                dep.append((C.s_mv, vld))
            pv = P.op(T, lambda e, c=c, hh=hh, N=N, Cn=Cn: e.matmul(
                C.pA[0:65, 512 * hh:512 * hh + N], vaug[:, c, 65 * hh:65 * hh + 65],
                PT[:, hh, c % 2, 0:N], start=(c == 0), stop=(c == Cn - 1),
                skip_group_check=True),
                waits=dep, inc=(c_t, 1))
            pv_t[(c, hh)] = pv
            pt_guard[(hh, c % 2)] = pv

    def emit_p2_chunk(c, qb, p8, kpar, Cn, bias_a, exp_a, pv_t, vld):
        qp = qb % 2
        for hh in range(2):
            j, (pt, off), frees = pools[hh].acquire()
            N = Nof(c, qb)
            w0 = [(c_a, bias_a[hh])] + frees
            mm = P.op(T, lambda e, c=c, hh=hh, pt=pt, off=off, N=N,
                      kpar=kpar, p8=p8, qb=qb: e.matmul(
                pt[0:128, off:off + N],
                ktp[64 * hh:64 * hh + 64, kpar, 128 * c:128 * c + 128],
                QT[64 * hh:64 * hh + 64, p8, 512 * qb:512 * qb + N],
                start=True, stop=True, tile_position=(64 * hh, 0)),
                waits=w0, inc=(c_t, 1))
            extra = []
            if c % 4 == 0 and 4 * qb <= c // 4 <= 4 * qb + 3:
                dd = (4 * qb + 3) - c // 4
                v_msk = P.op(V, lambda e, pt=pt, off=off, dd=dd: e.tensor_tensor(
                    pt[0:128, off + 128 * dd:off + 128 * dd + 128],
                    pt[0:128, off + 128 * dd:off + 128 * dd + 128],
                    C.trimask[...], ALU.add),
                    waits=[(c_t, mm)], inc=(c_v, 1))
                extra = [(c_v, v_msk)]
            slotg = pt_guard.get((hh, c % 2))
            exp = P.op(A, lambda e, c=c, hh=hh, pt=pt, off=off, N=N, qp=qp: e.activation(
                out=PT[:, hh, c % 2, 0:N], in_=pt[0:128, off:off + N],
                func=AF.Exp, scale=0.125, bias=biasq[:, qp, hh:hh + 1]),
                waits=[(c_t, mm)] + extra +
                      ([(c_t, slotg)] if slotg else []), inc=(c_a, 1))
            pools[hh].release(j, [(c_a, exp)])
            dep = (c_a, exp)
            if c % 4 != 0 and 4 * qb <= c // 4 <= 4 * qb + 3:
                mmod = c % 4
                dd = (4 * qb + 3) - c // 4
                g_d = P.op(V, lambda e, c=c, hh=hh, dd=dd, mmod=mmod: e.tensor_scalar_mul(
                    PT[:, hh, c % 2, 128 * dd:128 * dd + 128],
                    PT[:, hh, c % 2, 128 * dd:128 * dd + 128],
                    C.allowB[:, mmod:mmod + 1]),
                    waits=[dep], inc=(c_v, 1))
                dep = (c_v, g_d)
            exp_a[(c, hh)] = dep
        if c >= 1:
            emit_p2_pv(c - 1, qb, Cn, exp_a, pv_t, vld)
        if c == Cn - 1:
            emit_p2_pv(c, qb, Cn, exp_a, pv_t, vld)

    def emit_norm(qb, p8, Cn, pv_t):
        for hh in range(2):
            h = 2 * p8 + hh
            j, (pt, off), frees = pools[1].acquire()
            v_l = P.op(V, lambda e, hh=hh: e.tensor_scalar_add(
                lnscr[64:65, 0:512], C.pA[64:65, 512 * hh:512 * hh + 512], 1e-30),
                waits=[(c_t, pv_t[(Cn - 1, 1)])], inc=(c_v, 1))
            v_r = P.op(V, lambda e: e.reciprocal(
                lnscr[64:65, 512:1024], lnscr[64:65, 0:512]),
                waits=[(c_v, v_l)], inc=(c_v, 1))
            t_b = P.op(T, lambda e, pt=pt, off=off: e.matmul(
                pt[0:64, off:off + 512], C.onesP[64:65, 0:64],
                lnscr[64:65, 512:1024],
                start=True, stop=True, tile_position=(64, 0)),
                waits=[(c_v, v_r)] + frees, inc=(c_t, 1))
            a_rb = P.op(A, lambda e, pt=pt, off=off: e.activation(
                out=C.reclB, in_=pt[0:64, off:off + 512], func=AF.Copy),
                waits=[(c_t, t_b)], inc=(c_a, 1))
            st["pa_free"] = P.op(V, lambda e, hh=hh, h=h, qb=qb: e.tensor_tensor(
                mhT[64 * (h % 2):64 * (h % 2) + 64, h // 2, 512 * qb:512 * qb + 512],
                C.pA[0:64, 512 * hh:512 * hh + 512], C.reclB, ALU.mult),
                waits=[(c_a, a_rb)], inc=(c_v, 1))
            pools[1].release(j, [(c_a, a_rb)])

    for p8 in range(8):
        kpar = p8 % 2
        w = list(C.spill_all) if p8 <= 1 else []
        if p8 >= 2:
            w.append((c_t, pair_done[p8 - 2]))
        kld = P.dma(G, ktp[:, kpar, :], C.kt_sp[p8], C.s_mk[kpar], waits=w)
        wv = list(C.spill_all) if p8 <= 1 else []
        if p8 >= 1:
            wv.append((c_t, pair_done[p8 - 1]))
        vld = P.dma(G, C.vaug[:, 0:64, 0:130], C.vp_sp[p8], C.s_mv, waits=wv)

        prev = None
        for qb in range(4):
            Cn1 = 16 * qb + 16
            red_v = {}
            for c in range(Cn1):
                emit_p1_chunk(c, qb, p8, kpar, kld, red_v)
                if prev is not None and c < prev[1]:
                    emit_p2_chunk(c, prev[0], p8, kpar, prev[1], prev[2],
                                  prev[3], prev[4], vld)
            if prev is not None:
                emit_norm(prev[0], p8, prev[1], prev[4])
            bias_a = emit_chain(qb, Cn1, red_v)
            prev = (qb, Cn1, bias_a, {}, {})
        for c in range(prev[1]):
            emit_p2_chunk(c, prev[0], p8, kpar, prev[1], prev[2], prev[3],
                          prev[4], vld)
        emit_norm(prev[0], p8, prev[1], prev[4])
        pair_done[p8] = c_t.n

    if C.mh_dbg is not None:
        for dcq in range(8):
            P.dma(SY, C.mh_dbg[dcq], mhT[:, dcq, :], C.s_out,
                  waits=[(c_v, c_v.n)])
    C.ph1_v = c_v.n
    C.ph1_t = c_t.n


def emit_ph2(C):
    P = C.P
    T, V, A, G, SY = "tensor", "vector", "scalar", "gpsimd", "sync"
    s_w, s_sp2, s_out = C.s_w, C.s_sp2, C.s_out
    c_t, c_v, c_a, c_g = C.c_t, C.c_v, C.c_a, C.c_g
    mhT, qTb, ktf, vf = C.mhT, C.qTb, C.ktf, C.vf
    P2T, A2, l2, fsc, fB, m2t = C.P2T, C.A2, C.l2, C.fsc, C.fB, C.m2t
    stats, outb, xtile = C.stats, C.outb, C.xtile

    wo_ok = load_weight_slab(C, C.din["wo"], 0, extra_w=[(c_t, C.ph1_t)])
    if C.wo_dbg is not None:
        P.dma(SY, C.wo_dbg, C.wbig[:, 0, :, :], C.s_out, waits=[(c_a, wo_ok)])
        P.dma(SY, C.bo_dbg, C.boB[0:128, :], C.s_out, waits=[(c_a, wo_ok)])

    sst = C.sload

    slots2 = [(C.pS1, 0), (C.pS1, 512), (C.pS2, 0), (C.pS2, 512)]
    exp_prev = {}
    sup_hist = {}
    st2 = {"mrg": C.ph1_v, "pv_done": 0}

    def emit_ot_group(ot):
        sj = ot % 2
        ev = 0
        for hf in range(2):
            for dc in range(8):
                w0 = []
                if dc == 0 and hf == 0:
                    w0 = [(c_v, st2["mrg"])]
                    if ot == 0:
                        w0 += [(c_a, wo_ok), (c_v, C.ph1_v)]
                P.op(T, lambda e, ot=ot, hf=hf, dc=dc: e.matmul(
                    C.pA[0:128, 512 * hf:512 * hf + 512],
                    mhT[:, dc, 128 * ot:128 * ot + 128],
                    C.wbig[:, 0, dc, 512 * hf:512 * hf + 512],
                    start=(dc == 0), stop=(dc == 7), skip_group_check=True),
                    waits=w0, inc=(c_t, 1) if dc == 7 else None)
            tmm = c_t.n
            w0 = [(c_t, tmm)]
            if ot >= 2 and hf == 0:
                w0.append((C.s_ss, C.s_ss.n))
            ev = P.op(V, lambda e, hf=hf, sj=sj: e.tensor_tensor(
                sst[:, sj, 512 * hf:512 * hf + 512],
                C.pA[0:128, 512 * hf:512 * hf + 512],
                C.boB[:, 512 * hf:512 * hf + 512], ALU.add),
                waits=w0, inc=(c_v, 1))
        P.dma(G, C.s_sp3[ot], sst[:, sj, :], C.s_ss, waits=[(c_v, ev)])
        st2["mrg"] = ev

    def emit_qk2_group(qb, sup, spar, kld, qld, mz, cc):
        gs = 16 * qb + sup
        pt, off = slots2[cc]
        w0 = [(C.s_fk[spar], kld), (C.s_q, qld)]
        if exp_prev.get(cc):
            w0.append((c_a, exp_prev[cc]))
        if gs == 0 and cc == 0:
            w0.append((c_g, mz))
            w0.append((c_v, c_v.n))
        for dc in range(8):
            P.op(T, lambda e, cc=cc, dc=dc, pt=pt, off=off, spar=spar: e.matmul(
                pt[0:128, off:off + 512],
                ktf[:, spar, dc, 128 * cc:128 * cc + 128],
                qTb[:, dc, 0:512], start=(dc == 0), stop=(dc == 7)),
                waits=w0 if dc == 0 else [], inc=(c_t, 1) if dc == 7 else None)
        tqk = c_t.n
        P.op(V, lambda e, cc=cc, pt=pt, off=off: e.reduce_max(
            m2t[:, cc:cc + 1], pt[0:128, off:off + 512], axis=AX),
            waits=[(c_t, tqk)], inc=(c_v, 1))

    def emit_chain2(sup, mz):
        v_m = P.op(V, lambda e: e.reduce_max(stats[:, 44:45], m2t[...], axis=AX),
                   inc=(c_v, 1))
        t_tr = P.op(T, lambda e: e.transpose(C.pX1[0:1, 0:128], stats[:, 44:45],
                                             C.idf32[...]),
                    waits=[(c_v, v_m), (c_v, c_v.n), (c_a, c_a.n)], inc=(c_t, 1))
        v_s = P.op(V, lambda e: e.reduce_max(fsc[0:1, 1:2], C.pX1[0:1, 0:128],
                                             axis=AX),
                   waits=[(c_t, t_tr), (c_g, mz)], inc=(c_v, 1))
        v_mx = P.op(V, lambda e: e.tensor_tensor(fsc[0:1, 5:6], fsc[0:1, 0:1],
                                                 fsc[0:1, 1:2], ALU.max),
                    waits=[(c_v, v_s)], inc=(c_v, 1))
        v_df = P.op(V, lambda e: e.tensor_tensor(fsc[0:1, 3:4], fsc[0:1, 0:1],
                                                 fsc[0:1, 5:6], ALU.subtract),
                    waits=[(c_v, v_mx)], inc=(c_v, 1))
        a_f = P.op(A, lambda e: e.activation(out=fsc[0:1, 4:5], in_=fsc[0:1, 3:4],
                                             func=AF.Exp, scale=1.0 / 32),
                   waits=[(c_v, v_df), (c_v, st2["mrg"])], inc=(c_a, 1))
        v_cp = P.op(V, lambda e: e.tensor_copy(fsc[0:1, 0:1], fsc[0:1, 5:6]),
                    waits=[(c_a, a_f)], inc=(c_v, 1))
        t_b = P.op(T, lambda e: e.matmul(C.pX1[0:128, 192:194], C.onesP[0:1, 0:128],
                                         fsc[0:1, 4:6], start=True, stop=True),
                   waits=[(c_a, a_f), (c_v, c_v.n)], inc=(c_t, 1))
        a_fB = P.op(A, lambda e: e.activation(out=fB[:, 0:1], in_=C.pX1[0:128, 192:193],
                                              func=AF.Copy),
                    waits=[(c_t, t_b), (c_v, st2["mrg"])], inc=(c_a, 1))
        a_b2 = P.op(A, lambda e: e.activation(out=fB[:, 1:2], in_=C.pX1[0:128, 193:194],
                                              func=AF.Copy, scale=-1.0 / 32),
                    waits=[(c_t, t_b)], inc=(c_a, 1))
        return a_b2

    def emit_exps2(sup, a_b2):
        for cc in range(4):
            pt, off = slots2[cc]
            exp_prev[cc] = P.op(A, lambda e, cc=cc, pt=pt, off=off: e.activation(
                out=P2T[:, cc, :], in_=pt[0:128, off:off + 512],
                func=AF.Exp, scale=1.0 / 32, bias=fB[:, 1:2]),
                waits=[(c_a, a_b2)] + ([(c_t, st2["pv_done"])] if cc == 0 else []),
                inc=(c_a, 1))
        return c_a.n

    def emit_pv2_group(qb, sup, spar, vld, exps, qt):
        lb = C.pX1 if qt % 2 == 0 else C.pX2
        w0 = [(c_a, exps), (C.s_fv[spar], vld), (c_v, st2["mrg"])]
        for cc in range(4):
            for hf in range(2):
                P.op(T, lambda e, qt=qt, cc=cc, hf=hf, spar=spar: e.matmul(
                    C.pA[0:128, 512 * hf:512 * hf + 512],
                    P2T[:, cc, 128 * qt:128 * qt + 128],
                    vf[:, spar, cc, 512 * hf:512 * hf + 512],
                    start=(cc == 0), stop=(cc == 3), skip_group_check=True),
                    waits=w0 if (cc == 0 and hf == 0) else [])
            P.op(T, lambda e, qt=qt, cc=cc, lb=lb: e.matmul(
                lb[0:128, 300 + qt:301 + qt],
                P2T[:, cc, 128 * qt:128 * qt + 128],
                C.onescolb[...], start=(cc == 0), stop=(cc == 3),
                skip_group_check=True), inc=(c_t, 1) if cc == 3 else None)
        tpv = c_t.n
        P.op(V, lambda e, qt=qt: e.scalar_tensor_tensor(
            out=A2[:, qt, :], in0=A2[:, qt, :], scalar=fB[:, 0:1],
            in1=C.pA[0:128, 0:1024], op0=ALU.mult, op1=ALU.add),
            waits=[(c_t, tpv)], inc=(c_v, 1))
        st2["mrg"] = P.op(V, lambda e, qt=qt, lb=lb: e.scalar_tensor_tensor(
            out=l2[:, qt:qt + 1], in0=l2[:, qt:qt + 1], scalar=fB[:, 0:1],
            in1=lb[0:128, 300 + qt:301 + qt], op0=ALU.mult, op1=ALU.add),
            inc=(c_v, 1))
        if qt == 3:
            st2["pv_done"] = c_t.n

    def emit_pv2(qb, sup, spar, vld, exps):
        for qt in range(4):
            emit_pv2_group(qb, sup, spar, vld, exps, qt)

    def emit_gcombine(qb, mrg_prev):
        for qt in range(4):
            i_own = 4 * qb + (3 - qt)
            v_l = P.op(V, lambda e, qt=qt: e.tensor_scalar_add(
                stats[:, 46:47], l2[:, qt:qt + 1], 1e-30),
                waits=[(c_v, mrg_prev)], inc=(c_v, 1))
            v_r = P.op(V, lambda e: e.reciprocal(stats[:, 47:48], stats[:, 46:47]),
                       waits=[(c_v, v_l)], inc=(c_v, 1))
            a_g = P.op(A, lambda e, qt=qt: e.activation(
                out=C.gsb[0:128, 0:1024], in_=A2[:, qt, :], func=AF.Silu,
                scale=stats[:, 47:48]),
                waits=[(c_v, v_r), (c_v, c_v.n)], inc=(c_a, 1))
            xld = P.dma(SY, xtile[:, qt % 2, :],
                        C.din["x"][512 * i_own:512 * i_own + 128, :], C.s_xl[qt % 2],
                        waits=[(c_v, c_v.n)])
            sld = P.dma(SY, C.sload[:, qt % 2, :], C.s_sp3[4 * qb + qt],
                        C.s_sl[qt % 2], waits=[(c_v, c_v.n), (C.s_ss, C.s_ss.n)])
            v_sg = P.op(V, lambda e, qb=qb, qt=qt: e.scalar_tensor_tensor(
                out=outb[:, qt % 2, :], in0=C.sload[:, qt % 2, :], scalar=1.0,
                in1=C.gsb[0:128, 0:1024], op0=ALU.mult, op1=ALU.mult),
                waits=[(c_a, a_g), (C.s_sl[qt % 2], sld)], inc=(c_v, 1))
            v_o = P.op(V, lambda e, qt=qt: e.tensor_tensor(
                outb[:, qt % 2, :], outb[:, qt % 2, :], xtile[:, qt % 2, :],
                ALU.add), waits=[(C.s_xl[qt % 2], xld), (c_v, v_sg)], inc=(c_v, 1))
            P.dma(SY, C.out_d[512 * qb + 128 * qt:512 * qb + 128 * qt + 128, :],
                  outb[:, qt % 2, :], s_out, waits=[(c_v, v_o)])

    pend_g = None
    for qb in range(4):
        qld = P.dma(SY, qTb[0:128, 0:8, 0:512],
                    C.fqt_sp[:, :, 512 * qb:512 * qb + 512].rearrange("c p n -> p c n"),
                    C.s_q, waits=[(s_sp2, C.ph0c_sp2), (C.s_ss, C.s_ss.n),
                                  (c_t, c_t.n)])
        if pend_g is not None:
            emit_gcombine(pend_g[0], pend_g[1])
        P.op(G, lambda e: e.memset(A2[0:128, 0:4, 0:1024], 0.0),
             waits=[(c_v, c_v.n), (c_a, c_a.n)], inc=(c_g, 1))
        P.op(G, lambda e: e.memset(l2[:, 0:4], 0.0), inc=(c_g, 1))
        mz = P.op(G, lambda e: e.memset(fsc[0:1, 0:1], NEGBIG), inc=(c_g, 1))

        prev2 = None
        for sup in range(16):
            gs = 16 * qb + sup
            spar = sup % 2
            w = [] if gs >= 2 else [(s_sp2, C.ph0c_sp2)]
            if gs >= 2:
                w.append((c_t, sup_hist[gs - 2]))
            kld = P.dma(G, ktf[:, spar, :, :],
                        C.fkt_sp[:, :, 512 * sup:512 * sup + 512].rearrange("c p n -> p c n"),
                        C.s_fk[spar], waits=w)
            wv2 = list(w)
            if gs >= 1:
                wv2.append((c_t, sup_hist[gs - 1]))
            vld = P.dma(G, vf[:, spar, :, :],
                        C.fv_sp[4 * sup:4 * sup + 4].rearrange("c p n -> p c n"),
                        C.s_fv[spar], waits=wv2)
            for i in range(4):
                if prev2 is not None:
                    emit_pv2_group(qb, prev2[0], prev2[1], prev2[2], prev2[3], i)
                emit_qk2_group(qb, sup, spar, kld, qld, mz, i)
            if qb == 0:
                emit_ot_group(sup)
            a_b2 = emit_chain2(sup, mz)
            exps = emit_exps2(sup, a_b2)
            prev2 = (sup, spar, vld, exps)
            sup_hist[gs] = c_t.n
        emit_pv2(qb, prev2[0], prev2[1], prev2[2], prev2[3])
        sup_hist[16 * qb + 15] = c_t.n
        pend_g = (qb, st2["mrg"])

    emit_gcombine(pend_g[0], pend_g[1])



_NC_CACHE = {}


def kernel(**inputs):
    if "nc" not in _NC_CACHE:
        _NC_CACHE["nc"] = build_nc()
    nc = _NC_CACHE["nc"]
    in_maps = host_inputs(inputs)
    res = run_bass_kernel_spmd(nc, in_maps, core_ids=list(range(8)))
    return host_output(res.results)


if __name__ == "__main__":
    pass


# version 21 (final)
# speedup vs baseline: 1.0623x; 1.0266x over previous
import contextlib
import numpy as np
import ml_dtypes
import concourse.bass as bass
import concourse.mybir as mybir
from concourse.bass_utils import run_bass_kernel_spmd

F32 = mybir.dt.float32
BF16 = mybir.dt.bfloat16
AF = mybir.ActivationFunctionType
ALU = mybir.AluOpType
AX = mybir.AxisListType.X
BF = ml_dtypes.bfloat16

B, S, D = 2, 8192, 1024
H, NP = 16, 8
NBLK = 16
NEG = -1.0e30
NEGBIG = -3.0e38

STAGE = 4
DEBUG_SPILLS = False


def local_tile_order(k):
    order = []
    for t in range(NBLK):
        order.append(4 * t + k)
        order.extend(4 * t + j for j in range(4) if j != k)
    return order


def own_pos(i):
    qb, r = divmod(i, 4)
    return 512 * qb + 128 * (3 - r)


def host_inputs(full):
    xf = np.ascontiguousarray(np.asarray(full["L0_attn_residual"], np.float32))
    WQ = np.asarray(full["L0_attn_W_Q"], np.float32)
    WK = np.asarray(full["L0_attn_W_K"], np.float32)
    WV = np.asarray(full["L0_attn_W_V"], np.float32)
    bQ = np.asarray(full["L0_attn_b_Q"], np.float32)
    bK = np.asarray(full["L0_attn_b_K"], np.float32)
    bV = np.asarray(full["L0_attn_b_V"], np.float32)

    def pairmajor(W):
        return np.ascontiguousarray(W.transpose(1, 0, 2).reshape(D, D))

    bqk = np.zeros((128, 16), np.float32)
    for p in range(NP):
        bqk[:, p] = bQ[2 * p:2 * p + 2].reshape(128)
        bqk[:, 8 + p] = bK[2 * p:2 * p + 2].reshape(128)

    w1c = np.stack([np.asarray(full[f"L0_ffn_{f}_w1"], np.float32)
                    .transpose(1, 0, 2).reshape(D, D) for f in "qkv"])
    w2c = np.stack([np.asarray(full[f"L0_ffn_{f}_w2"], np.float32)
                    .reshape(D, D) for f in "qkv"])
    b1c = np.zeros((128, 3, 8), np.float32)
    for fi, f in enumerate("qkv"):
        b1c[:, fi, :] = np.asarray(full[f"L0_ffn_{f}_b1"], np.float32).reshape(8, 128).T
    b2c = np.stack([np.asarray(full[f"L0_ffn_{f}_b2"], np.float32) for f in "qkv"])
    nodew = np.concatenate([np.ascontiguousarray(
        np.asarray(full[f"L0_ffn_{f}_node_w"], np.float32).T) for f in "qkv"],
        axis=1)
    nodeb = np.concatenate([np.asarray(full[f"L0_ffn_{f}_node_b"], np.float32)
                            .reshape(1, 15) for f in "qkv"], axis=1)

    kk = np.arange(128)[:, None]
    rr = np.arange(128)[None, :]
    trimask = np.where(kk <= rr, 0.0, NEG).astype(np.float32)
    hidiota = np.zeros((128, 8), np.float32)
    for hc in range(8):
        hidiota[:, hc] = 2 * hc + np.arange(128) // 64

    common = dict(
        wq=pairmajor(WQ), wk=pairmajor(WK), wv=pairmajor(WV), bqk=bqk,
        bvrow=np.ascontiguousarray(bV.reshape(1, D)),
        wo=np.ascontiguousarray(np.asarray(full["L0_attn_W_O"], np.float32)),
        borow=np.ascontiguousarray(
            np.asarray(full["L0_attn_b_O"], np.float32).reshape(1, D)),
        w1c=w1c, w2c=w2c, b1c=b1c, b2c=b2c, nodew=nodew, nodeb=nodeb,
        idf32=np.eye(128, dtype=np.float32), trimask=trimask,
        onesrow=np.ones((1, 128), np.float32),
        onesrow_bf=np.ones((1, 128), BF),
        hidiota=hidiota,
        iota16=np.arange(16, dtype=np.float32).reshape(16, 1),
        onescol_bf=np.ones((128, 1), BF),
    )
    maps = []
    for c in range(8):
        b, k = c // 4, c % 4
        order = local_tile_order(k)
        x_all = np.concatenate([xf[b, 128 * g:128 * g + 128] for g in order], axis=0)
        others = [j for j in range(4) if j != k]
        allow = np.zeros((1, 4), np.float32)
        allow[0, 0] = 1.0
        for m in range(3):
            allow[0, m + 1] = 1.0 if others[m] < k else 0.0
        d = dict(common)
        d["x"] = np.ascontiguousarray(x_all)
        d["allow"] = allow
        maps.append(d)
    return maps


def host_output(results):
    out = np.zeros((B, S, D), np.float32)
    for c in range(8):
        b, k = c // 4, c % 4
        oc = np.asarray(results[c]["out"], np.float32)
        for i in range(NBLK):
            g = 4 * i + k
            out[b, 128 * g:128 * g + 128] = oc[own_pos(i):own_pos(i) + 128]
    return out



class Sem:
    __slots__ = ("h", "n", "name")

    def __init__(self, name):
        self.h = None
        self.n = 0
        self.name = name


class Prog:
    def __init__(self, nc):
        self.nc = nc
        self.ops = {k: [] for k in ("sync", "gpsimd", "tensor", "vector", "scalar")}
        self.sems = []

    def sem(self, name):
        s = Sem(name)
        self.sems.append(s)
        return s

    def op(self, eng, fn, waits=(), inc=None):
        incs = []
        ret = None
        if inc is not None:
            s, a = inc
            s.n += a
            incs.append((s, a))
            ret = s.n
        ws = []
        for w in waits:
            if w is None:
                continue
            s, v = w
            if v and v > 0:
                ws.append((s, v))
        self.ops[eng].append((fn, ws, incs))
        return ret

    def dma(self, q, out, in_, sem, waits=()):
        return self.op(q, lambda e: e.dma_start(out=out, in_=in_), waits, (sem, 16))

    def emit(self):
        nc = self.nc
        with contextlib.ExitStack() as st:
            for s in self.sems:
                s.h = st.enter_context(nc.semaphore(s.name))
            with nc.Block() as block:
                def runner(name):
                    def run(e):
                        for (fn, waits, incs) in self.ops[name]:
                            for (s, v) in waits:
                                e.wait_ge(s.h, v)
                            ins = fn(e)
                            for (s, a) in incs:
                                ins.then_inc(s.h, a)
                    return run

                block.sync(runner("sync"))
                block.gpsimd(runner("gpsimd"))
                block.tensor(runner("tensor"))
                block.vector(runner("vector"))
                block.scalar(runner("scalar"))


class Slots:

    def __init__(self, slots):
        self.slots = slots
        self.free = [[] for _ in slots]
        self.i = 0

    def acquire(self):
        j = self.i % len(self.slots)
        self.i += 1
        return j, self.slots[j], list(self.free[j])

    def release(self, j, wait):
        self.free[j] = wait if isinstance(wait, list) else [wait]


class Ctx:
    pass


def build_nc():
    nc = bass.Bass("TRN2", target_bir_lowering=False)
    P = Prog(nc)
    C = Ctx()
    C.P = P

    din = {}
    for name, shape, dt in [
        ("x", [S, D], F32), ("wq", [D, D], F32), ("wk", [D, D], F32),
        ("wv", [D, D], F32), ("bqk", [128, 16], F32), ("bvrow", [1, D], F32),
        ("wo", [D, D], F32), ("borow", [1, D], F32),
        ("w1c", [3, D, D], F32), ("w2c", [3, D, D], F32),
        ("b1c", [128, 3, 8], F32), ("b2c", [3, 16, D], F32),
        ("nodew", [D, 45], F32), ("nodeb", [1, 45], F32),
        ("idf32", [128, 128], F32), ("trimask", [128, 128], F32),
        ("onesrow", [1, 128], F32), ("onesrow_bf", [1, 128], BF16),
        ("hidiota", [128, 8], F32), ("iota16", [16, 1], F32),
        ("onescol_bf", [128, 1], BF16), ("allow", [1, 4], F32),
    ]:
        din[name] = nc.dram_tensor(name, shape, dt, kind="ExternalInput").ap()
    C.din = din
    C.out_d = nc.dram_tensor("out", [2048, D], F32, kind="ExternalOutput").ap()

    kd = "ExternalOutput" if DEBUG_SPILLS else "Internal"
    C.xnt_sp = nc.dram_tensor("xnt_sp", [NBLK, 8, 128, 512], BF16, kind=kd).ap()
    C.kt_sp = nc.dram_tensor("kt_sp", [NP, 128, S], BF16, kind=kd).ap()
    C.vp_sp = nc.dram_tensor("vp_sp", [NP, 128, 64, 130], BF16, kind=kd).ap()
    C.fkt_sp = nc.dram_tensor("fkt_sp", [8, 128, S], BF16, kind=kd).ap()
    C.fv_sp = nc.dram_tensor("fv_sp", [64, 128, D], BF16, kind=kd).ap()
    C.fqt_sp = nc.dram_tensor("fqt_sp", [8, 128, 2048], BF16, kind=kd).ap()
    C.s_sp3 = nc.dram_tensor("s_sp3", [16, 128, D], BF16, kind=kd).ap()
    C.mh_dbg = nc.dram_tensor("mh_dbg", [8, 128, 2048], BF16, kind=kd).ap() \
        if DEBUG_SPILLS else None
    C.wo_dbg = nc.dram_tensor("wo_dbg", [128, 8, 1024], BF16, kind=kd).ap() \
        if DEBUG_SPILLS else None
    C.bo_dbg = nc.dram_tensor("bo_dbg", [128, D], F32, kind=kd).ap() \
        if DEBUG_SPILLS else None

    ctx = contextlib.ExitStack()
    sbuf = lambda n, sh, dt: ctx.enter_context(nc.sbuf_tensor(n, sh, dt))
    psum = lambda n, sh: ctx.enter_context(nc.psum_tensor(n, sh, F32))

    with ctx:
        C.pS1 = psum("pS1", [128, 1024])
        C.pS2 = psum("pS2", [128, 1024])
        C.pA = psum("pA", [128, 1024])
        C.pX1 = psum("pX1", [128, 512])
        C.pX2 = psum("pX2", [128, 512])

        C.idf32 = sbuf("c_idf32", [128, 128], F32)
        C.trimask = sbuf("c_trimask", [128, 128], F32)
        C.onesP = sbuf("c_onesP", [128, 128], F32)
        C.onesbf = sbuf("c_onesbf", [1, 128], BF16)
        C.hidiota = sbuf("c_hidiota", [128, 8], F32)
        C.iota16 = sbuf("c_iota16", [16, 1], F32)
        C.onescolb = sbuf("c_onescolb", [128, 1], BF16)
        C.bqk = sbuf("c_bqk", [128, 16], F32)
        C.b1c = sbuf("c_b1c", [128, 3, 8], F32)
        C.allowrow = sbuf("c_allow", [1, 4], F32)
        C.nodew_s = sbuf("c_nodew", [128, 8, 45], F32)
        C.nodebB = sbuf("c_nodebB", [128, 45], F32)
        C.nodeb_s = sbuf("c_nodeb", [1, 45], F32)

        C.bvB = sbuf("bvB", [128, D], F32)
        C.reclB = C.bvB[0:64, 0:512]
        C.boB = sbuf("boB", [128, D], F32)
        C.allowB = sbuf("allowB", [128, 4], F32)

        ar1 = sbuf("ar1", [128, 16384], BF16)
        ar2 = sbuf("ar2", [128, 16384], BF16)
        ar3 = sbuf("ar3", [128, 16384], BF16)
        ar4 = sbuf("ar4", [128, 8320], BF16)
        ar5 = sbuf("ar5", [128, 4096], F32)

        C.QT = ar1[...].rearrange("p (c n) -> p c n", c=8)
        C.qTb = ar1[:, 0:4096].rearrange("p (c n) -> p c n", c=8)
        C.P2T = ar1[:, 4096:6144].rearrange("p (c n) -> p c n", c=4)
        C.gsb = ar1[:, 6144:7168]

        C.xnT = ar2[:, 0:8192].rearrange("p (a c n) -> p a c n", a=2, c=8)
        C.hidT = ar2[:, 8192:12288].rearrange("p (c n) -> p c n", c=8)
        C.evst = ar2[:, 12288:14336].rearrange("p (c n) -> p c n", c=4)
        C.vevst = ar2[:, 8192:10368].rearrange("p (a n) -> p a n", a=4)
        C.mhT = ar2[...].rearrange("p (c n) -> p c n", c=8)

        C.ktp = ar3[...].rearrange("p (a n) -> p a n", a=2)
        C.ktf = ar3[:, 0:8192].rearrange("p (a c n) -> p a c n", a=2, c=8)
        C.vf = ar3[:, 8192:16384].rearrange("p (a c n) -> p a c n", a=2, c=4)

        C.vaug = ar4[...].rearrange("p (c n) -> p c n", c=64)

        C.xnTf = ar5[...].rearrange("p (c n) -> p c n", c=8)
        C.A2 = ar5[...].rearrange("p (c n) -> p c n", c=4)

        C.xtile = sbuf("xtile", [128, 2, 1024], F32)
        C.xnf = sbuf("xnf", [128, 2, 1024], F32)
        C.outb = C.xnf
        C.stats = sbuf("stats", [128, 48], F32)
        C.scores = sbuf("scores", [128, 4, 3, 15], F32)
        C.leafall = sbuf("leafall", [128, 3, 64], F32)
        C.dscr = sbuf("dscr", [128, 136], F32)

        C.wbig = sbuf("wbig", [128, 2, 8, 1024], BF16)
        C.wstage = sbuf("wstage", [128, 2, 1024], F32)
        C.bvrow_s = C.wstage[0:1, 0:1, 0:1024].rearrange("p a n -> p (a n)")
        C.borow_s = C.wstage[0:1, 1:2, 0:1024].rearrange("p a n -> p (a n)")
        C.lnscr = C.wstage[0:128, 1:2, 0:1024].rearrange("p a n -> p (a n)")
        C.b2bf = ar4[0:16, 0:3072].rearrange("p (f n) -> p f n", f=3)
        C.b2f32 = C.wstage[0:16, 0:1, 0:1024].rearrange("p a n -> p (a n)")
        pts = sbuf("pts", [128, 2048], BF16)
        C.PT = pts[:, 0:2048].rearrange("p (a b n) -> p a b n", a=2, b=2)
        C.onesq = pts[0:1, 0:512]
        C.bqflat = pts[0:1, 512:1536]
        C.onesk = pts[32:33, 0:512]
        C.bkflat = pts[32:33, 512:1536]
        C.mxs = sbuf("mxs", [128, 2, 64], F32)
        C.selT = sbuf("selT", [16, 512], BF16)
        C.lrow_f = C.xnf[0:1, 0:1, 0:512].rearrange("p a n -> p (a n)")
        C.lrow_b = pts[0:1, 0:512]
        C.sc1 = sbuf("sc1", [1, 16], F32)
        C.biasq = sbuf("biasq", [128, 2, 2], F32)
        C.l2 = sbuf("l2", [128, 4], F32)
        C.fsc = sbuf("fsc", [1, 8], F32)
        C.fB = sbuf("fB", [128, 2], F32)
        C.m2t = sbuf("m2t", [128, 4], F32)
        C.sload = pts[:, 0:2048].rearrange("p (a n) -> p a n", a=2)

        for s in ("s_in", "s_w", "s_sp", "s_sp2", "s_ss", "s_out", "s_mv", "s_q",
                  "c_t", "c_v", "c_a", "c_g"):
            setattr(C, s, P.sem(s))
        C.s_x = [P.sem(f"s_x{i}") for i in range(2)]
        C.s_xnt = [P.sem(f"s_xnt{i}") for i in range(2)]
        C.s_xr = [P.sem(f"s_xr{i}") for i in range(2)]
        C.s_ev = [P.sem(f"s_ev{i}") for i in range(4)]
        C.s_vq = [P.sem(f"s_vq{i}") for i in range(4)]
        C.s_mk = [P.sem(f"s_mk{i}") for i in range(2)]
        C.s_fk = [P.sem(f"s_fk{i}") for i in range(2)]
        C.s_fv = [P.sem(f"s_fv{i}") for i in range(2)]
        C.s_xl = [P.sem(f"s_xl{i}") for i in range(2)]
        C.s_sl = [P.sem(f"s_sl{i}") for i in range(2)]

        emit_setup(C)
        emit_ph0a(C)
        if STAGE >= 2:
            emit_ph0b(C)
            emit_ph0c(C)
        if STAGE >= 3:
            emit_ph1(C)
        if STAGE >= 4:
            emit_ph2(C)

        P.op("gpsimd", lambda e: e.engine_nop(),
             waits=[(C.s_out, C.s_out.n)], inc=(C.c_g, 1))
        P.emit()
    return nc


def emit_setup(C):
    P, din = C.P, C.din
    s_in, c_t, c_a = C.s_in, C.c_t, C.c_a
    for nm, dst in [("idf32", C.idf32[...]), ("trimask", C.trimask[...]),
                    ("onesrow", C.onesP[0:1, :]), ("onesrow", C.onesP[64:65, :]),
                    ("onesrow_bf", C.onesbf[...]), ("hidiota", C.hidiota[...]),
                    ("iota16", C.iota16[...]), ("onescol_bf", C.onescolb[...]),
                    ("bqk", C.bqk[...]), ("b1c", C.b1c[...]), ("allow", C.allowrow[...]),
                    ("nodeb", C.nodeb_s[...]), ("bvrow", C.bvrow_s),
                    ("borow", C.borow_s)]:
        P.dma("sync", dst, din[nm], s_in)
    P.dma("sync", C.nodew_s[...],
          din["nodew"].rearrange("(c p) n -> p c n", p=128), s_in)
    C.consts_ok = s_in.n

    def mk_bcast(rowsl, width, dst, scale=1.0):
        t = P.op("tensor", lambda e: e.matmul(
            C.pX1[0:128, 0:width], C.onesP[0:1, 0:128], rowsl,
            start=True, stop=True),
            waits=[(s_in, C.consts_ok), (c_a, c_a.n)], inc=(c_t, 1))
        P.op("scalar", lambda e: e.activation(
            out=dst, in_=C.pX1[0:128, 0:width], func=AF.Copy, scale=scale),
            waits=[(c_t, t)], inc=(c_a, 1))

    for hf in range(2):
        mk_bcast(C.bvrow_s[0:1, 512 * hf:512 * hf + 512], 512,
                 C.bvB[0:128, 512 * hf:512 * hf + 512])
        mk_bcast(C.borow_s[0:1, 512 * hf:512 * hf + 512], 512,
                 C.boB[0:128, 512 * hf:512 * hf + 512])
    mk_bcast(C.allowrow[0:1, 0:4], 4, C.allowB[0:128, 0:4])
    mk_bcast(C.nodeb_s[0:1, 0:45], 45, C.nodebB[0:128, 0:45])
    bqkT = C.selT[0:16, 384:512]
    t_bq = P.op("tensor", lambda e: e.transpose(
        C.pX1[0:16, 0:128], C.bqk[...], C.idf32[...]),
        waits=[(s_in, C.consts_ok), (c_a, c_a.n)], inc=(c_t, 1))
    a_bq = P.op("scalar", lambda e: e.activation(
        out=bqkT, in_=C.pX1[0:16, 0:128], func=AF.Copy),
        waits=[(c_t, t_bq)], inc=(c_a, 1))
    for i in range(16):
        dst = (C.bqflat[0:1, 128 * i:128 * i + 128] if i < 8 else
               C.bkflat[0:1, 128 * (i - 8):128 * (i - 8) + 128])
        P.dma("sync", dst, bqkT[i:i + 1, 0:128], s_in,
              waits=[(c_a, a_bq)] if i == 0 else [])
    C.bflat_ok = s_in.n
    P.op("gpsimd", lambda e: e.memset(C.onesq, 1.0), inc=(C.c_g, 1))
    P.op("gpsimd", lambda e: e.memset(C.onesk, 1.0), inc=(C.c_g, 1))
    C.setup_t = C.c_t.n


def emit_ph0a(C):
    P = C.P
    T, V, A, G, SY = "tensor", "vector", "scalar", "gpsimd", "sync"
    s_in, s_sp, c_t, c_v, c_a = C.s_in, C.s_sp, C.c_t, C.c_v, C.c_a
    xnf, xtile, stats = C.xnf, C.xtile, C.stats
    xnT, xnTf, scores, leafall, dscr = C.xnT, C.xnTf, C.scores, C.leafall, C.dscr
    evst, dscr = C.evst, C.dscr

    w_ok = max(load_weight_slab(C, C.din["wk"], 0),
               load_weight_slab(C, C.din["wv"], 1))
    vev4 = C.vevst[:, :, 0:520].rearrange("p a (j y) -> p a j y", j=4)
    m_on = 0
    for sl in range(4):
        P.op(G, lambda e, sl=sl: e.memset(vev4[:, sl, :, 64:65], 1.0),
             waits=[(c_a, c_a.n)], inc=(C.c_g, 1))
        m_on = P.op(G, lambda e, sl=sl: e.memset(vev4[:, sl, :, 129:130], 1.0),
                    inc=(C.c_g, 1))

    x_t = C.din["x"].rearrange("(t p) d -> t p d", p=128)
    psl = Slots([(C.pS1, 0), (C.pS1, 512), (C.pS2, 0), (C.pS2, 512)])
    esl = Slots([0, 1, 2, 3])
    xn_done = {}
    tr_done = {}
    kv_t = {}

    def load_xtile(gt):
        w = []
        if gt >= 2:
            w.append((c_v, xn_done[gt - 2]))
        if gt == 0:
            w.append((s_in, C.consts_ok))
        P.dma(SY, xtile[:, gt % 2, :], x_t[gt], C.s_x[gt % 2], waits=w)

    def mmgroup(lhs_fn, rhs_fn, n_k, width, extra0):
        j, (pt, off), frees = psl.acquire()
        for dc in range(n_k):
            w0 = (extra0 + frees) if dc == 0 else []
            lhs, rhs = lhs_fn(dc), rhs_fn(dc)
            P.op(T, lambda e, lhs=lhs, rhs=rhs, dc=dc, pt=pt, off=off,
                 width=width, n_k=n_k: e.matmul(
                pt[0:128, off:off + width], lhs, rhs,
                start=(dc == 0), stop=(dc == n_k - 1)),
                waits=w0, inc=(c_t, 1) if dc == n_k - 1 else None)
        return j, pt, off, c_t.n

    load_xtile(0)
    load_xtile(1)
    kv_ready = {}

    def emit_kv_group(b, gi):
        parb = b % 2
        if gi < 8:
            p8 = gi
            e0 = [(c_a, kv_ready[b])]
            if b == 0 and p8 == 0:
                e0 += [(c_a, w_ok), (C.c_g, m_on), (C.s_in, C.bflat_ok)]
            j, pt, off, tmm = mmgroup(
                lambda dc, p8=p8: C.wbig[:, 0, dc, 128 * p8:128 * p8 + 128]
                if dc < 8 else C.bkflat[0:1, 128 * p8:128 * p8 + 128],
                lambda dc, parb=parb: xnT[:, parb, dc, 0:512]
                if dc < 8 else C.onesk[0:1, 0:512], 9, 512, e0)
            je, es, efree = esl.acquire()
            a_ev = P.op(A, lambda e, pt=pt, off=off, es=es: e.activation(
                out=evst[:, es, :], in_=pt[0:128, off:off + 512], func=AF.Copy),
                waits=[(c_t, tmm)] + efree, inc=(c_a, 1))
            psl.release(j, (c_a, a_ev))
            P.dma(G if gi % 2 == 0 else SY,
                  C.kt_sp[p8][:, 512 * b:512 * b + 512], evst[:, es, :],
                  C.s_ev[es], waits=[(c_a, a_ev)])
            esl.release(je, (C.s_ev[es], C.s_ev[es].n))
        else:
            tt, hf = (gi - 8) // 2, (gi - 8) % 2
            vsl = (gi - 8) % 4
            j, pt, off, tmm = mmgroup(
                lambda dc, tt=tt, parb=parb: xnT[:, parb, dc, 128 * tt:128 * tt + 128]
                if dc < 8 else C.onesP[0:1, 0:128],
                lambda dc, hf=hf: C.wbig[:, 1, dc, 512 * hf:512 * hf + 512]
                if dc < 8 else C.bvB[0:1, 512 * hf:512 * hf + 512], 9, 512,
                [(c_a, kv_ready[b])])
            sl = vsl
            v_ev = 0
            for hhx in range(2):
                v_ev = P.op(A, lambda e, pt=pt, off=off, sl=sl, hhx=hhx: e.activation(
                    out=vev4[:, sl, :, 65 * hhx:65 * hhx + 64],
                    in_=pt[0:128, off:off + 512].rearrange(
                        "p (j h x) -> p j h x", j=4, h=2)[:, :, hhx, :],
                    func=AF.Copy),
                    waits=[(c_t, tmm), (C.c_g, m_on),
                           (C.s_vq[sl], C.s_vq[sl].n)] if hhx == 0 else [],
                    inc=(c_a, 1))
            psl.release(j, (c_a, v_ev))
            cch = 4 * b + tt
            P.dma(G if gi % 2 == 0 else SY,
                  C.vp_sp[4 * hf:4 * hf + 4, :, cch, :].rearrange("j p y -> p j y"),
                  C.vevst[:, sl, 0:520].rearrange("p (j y) -> p j y", j=4),
                  C.s_vq[sl], waits=[(c_a, v_ev)])
        if gi == 15:
            kv_t[b] = c_t.n

    for blk in range(NBLK):
        par = blk % 2
        rout = []
        q6h = {}
        for half in range(2):
            for tt in (2 * half, 2 * half + 1):
                gt = 4 * blk + tt
                c0 = 10 * tt
                t_x = 16 * (gt // 2 + 1)
                s_x_par = C.s_x[gt % 2]
                xa = xtile[:, gt % 2, :]
                w_ln = [(s_x_par, t_x)]
                if gt == 0:
                    w_ln.append((c_t, C.setup_t))
                if gt >= 2:
                    w_ln.append((c_t, tr_done[gt - 2]))
                P.op(A, lambda e, xa=xa, c0=c0, gt=gt: e.activation(
                    out=xnf[:, gt % 2, :], in_=xa, func=AF.Square,
                    accum_out=stats[:, c0 + 1:c0 + 2]),
                    waits=w_ln, inc=(c_a, 1))
                p2 = P.op(A, lambda e, xa=xa, c0=c0, gt=gt: e.activation(
                    out=xnf[:, gt % 2, :], in_=xa, func=AF.Copy,
                    accum_out=stats[:, c0:c0 + 1]), inc=(c_a, 1))
            b0 = 20 * half
            sv = lambda j, b0=b0: stats[:, b0 + j:b0 + j + 11:10].rearrange(
                "p (a b) -> p a b", b=1)
            q1 = P.op(V, lambda e, sv=sv: e.tensor_scalar_mul(sv(2), sv(0), 1.0 / D),
                      waits=[(c_a, p2)], inc=(c_v, 1))
            q2 = P.op(V, lambda e, sv=sv: e.tensor_scalar_mul(sv(3), sv(1), 1.0 / D),
                      waits=[(c_v, q1)], inc=(c_v, 1))
            q3 = P.op(V, lambda e, sv=sv: e.tensor_tensor(sv(4), sv(2), sv(2), ALU.mult),
                      waits=[(c_v, q2)], inc=(c_v, 1))
            q4 = P.op(V, lambda e, sv=sv: e.tensor_tensor(sv(5), sv(3), sv(4), ALU.subtract),
                      waits=[(c_v, q3)], inc=(c_v, 1))
            q5 = P.op(V, lambda e, sv=sv: e.tensor_scalar_add(sv(6), sv(5), 1e-5),
                      waits=[(c_v, q4)], inc=(c_v, 1))
            a_sq = P.op(A, lambda e, sv=sv: e.activation(out=sv(7), in_=sv(6), func=AF.Sqrt),
                        waits=[(c_v, q5)], inc=(c_a, 1))
            q6h[half] = P.op(V, lambda e, sv=sv: e.reciprocal(sv(8), sv(7)),
                             waits=[(c_a, a_sq)], inc=(c_v, 1))
            for tt in (2 * half, 2 * half + 1):
                gt = 4 * blk + tt
                c0 = 10 * tt
                xa = xtile[:, gt % 2, :]
                wv_ = [(c_t, tr_done[gt - 2])] if gt >= 2 else []
                v_xn = P.op(V, lambda e, xa=xa, gt=gt, c0=c0: e.tensor_scalar(
                    out=xnf[:, gt % 2, :], in0=xa,
                    scalar1=stats[:, c0 + 2:c0 + 3], scalar2=stats[:, c0 + 8:c0 + 9],
                    op0=ALU.subtract, op1=ALU.mult),
                    waits=wv_ + [(c_v, q6h[tt // 2])], inc=(c_v, 1))
                xn_done[gt] = v_xn
                if gt + 2 < 64:
                    load_xtile(gt + 2)
                for dc in range(8):
                    j, (pt, off), frees = psl.acquire()
                    t_tr = P.op(T, lambda e, gt=gt, dc=dc, pt=pt, off=off: e.transpose(
                        pt[0:128, off:off + 128],
                        xnf[:, gt % 2, 128 * dc:128 * dc + 128], C.idf32[...]),
                        waits=[(c_v, v_xn)] + frees, inc=(c_t, 1))
                    bw = [(c_t, t_tr)]
                    if blk >= 2 and dc == 0 and tt == 0:
                        bw.append((C.s_xnt[par], C.s_xnt[par].n))
                        bw.append((c_t, kv_t[blk - 2]))
                    P.op(A, lambda e, tt=tt, dc=dc, pt=pt, off=off, par=par: e.activation(
                        out=xnT[:, par, dc, 128 * tt:128 * tt + 128],
                        in_=pt[0:128, off:off + 128], func=AF.Copy),
                        waits=bw, inc=(c_a, 1))
                    a_f = P.op(A, lambda e, tt=tt, dc=dc, pt=pt, off=off: e.activation(
                        out=xnTf[:, dc, 128 * tt:128 * tt + 128],
                        in_=pt[0:128, off:off + 128], func=AF.Copy), inc=(c_a, 1))
                    psl.release(j, (c_a, a_f))
                tr_done[gt] = c_t.n
                a_f32 = c_a.n

                ti = 2 * half + (tt - 2 * half)
                if blk >= 1:
                    for gi in range(4 * ti, 4 * ti + 4):
                        emit_kv_group(blk - 1, gi)
                col0 = 45 * tt
                for dc in range(8):
                    w0 = [(c_a, a_f32), (c_v, c_v.n)] if dc == 0 else []
                    P.op(T, lambda e, dc=dc, tt=tt, col0=col0: e.matmul(
                        C.pX2[0:128, col0:col0 + 45],
                        xnTf[:, dc, 128 * tt:128 * tt + 128],
                        C.nodew_s[:, dc, :], start=(dc == 0), stop=(dc == 7)),
                        waits=w0, inc=(c_t, 1) if dc == 7 else None)
                rout.append((tt, col0, c_t.n))

        kv_ready[blk] = c_a.n
        for tt, col0, t_r in rout:
            P.op(V, lambda e, tt=tt, col0=col0: e.tensor_tensor(
                scores[:, tt, :, :].rearrange("p f n -> p (f n)"),
                C.pX2[0:128, col0:col0 + 45], C.nodebB[0:128, 0:45], ALU.add),
                waits=[(c_t, t_r)], inc=(c_v, 1))

        scl = [scores[:, :, :, j:j + 1].rearrange("p a f b -> p (a f) b")
               for j in range(15)]

        def dr(j):
            return dscr[:, 12 * j:12 * j + 12].rearrange("p (a b) -> p a b", b=1)

        def bA(j):
            return dscr[:, 12 * j:12 * j + 12].rearrange("p (a f) -> p a f", f=3)

        b0, b1, b2, b3 = dr(0), dr(1), dr(2), dr(3)
        tmp0, tmp1 = dr(4), dr(5)
        t1, tA, tB_, tC, tD = dr(6), dr(7), dr(8), dr(9), dr(10)

        def lerp(dst, a, b, bit):
            w1 = P.op(V, lambda e, a=a, b=b: e.tensor_tensor(tmp0, b, a, ALU.subtract),
                      waits=[(c_v, c_v.n)], inc=(c_v, 1))
            w2 = P.op(V, lambda e, bit=bit: e.tensor_tensor(tmp1, tmp0, bit, ALU.mult),
                      waits=[(c_v, w1)], inc=(c_v, 1))
            P.op(V, lambda e, dst=dst, a=a: e.tensor_tensor(dst, a, tmp1, ALU.add),
                 waits=[(c_v, w2)], inc=(c_v, 1))

        P.op(V, lambda e, s0=scl[0], b0=b0: e.tensor_single_scalar(
            b0, s0, 0.0, ALU.is_ge), waits=[(c_v, c_v.n)], inc=(c_v, 1))
        lerp(t1, scl[1], scl[2], b0)
        P.op(V, lambda e, t1=t1, b1=b1: e.tensor_single_scalar(
            b1, t1, 0.0, ALU.is_ge), waits=[(c_v, c_v.n)], inc=(c_v, 1))
        lerp(tA, scl[3], scl[5], b0)
        lerp(tB_, scl[4], scl[6], b0)
        lerp(t1, tA, tB_, b1)
        P.op(V, lambda e, t1=t1, b2=b2: e.tensor_single_scalar(
            b2, t1, 0.0, ALU.is_ge), waits=[(c_v, c_v.n)], inc=(c_v, 1))
        lerp(tA, scl[7], scl[11], b0)
        lerp(tB_, scl[8], scl[12], b0)
        lerp(tC, scl[9], scl[13], b0)
        lerp(tD, scl[10], scl[14], b0)
        lerp(tA, tA, tC, b1)
        lerp(tB_, tB_, tD, b1)
        lerp(t1, tA, tB_, b2)
        P.op(V, lambda e, t1=t1, b3=b3: e.tensor_single_scalar(
            b3, t1, 0.0, ALU.is_ge), waits=[(c_v, c_v.n)], inc=(c_v, 1))
        lf = leafall[:, :, 4 * blk:4 * blk + 4].rearrange("p f a -> p a f")
        P.op(V, lambda e, lf=lf: e.tensor_scalar_mul(lf, bA(0), 8.0),
             waits=[(c_v, c_v.n)], inc=(c_v, 1))
        P.op(V, lambda e, lf=lf: e.scalar_tensor_tensor(
            out=lf, in0=bA(1), scalar=4.0, in1=lf, op0=ALU.mult, op1=ALU.add),
            waits=[(c_v, c_v.n)], inc=(c_v, 1))
        P.op(V, lambda e, lf=lf: e.scalar_tensor_tensor(
            out=lf, in0=bA(2), scalar=2.0, in1=lf, op0=ALU.mult, op1=ALU.add),
            waits=[(c_v, c_v.n)], inc=(c_v, 1))
        P.op(V, lambda e, lf=lf: e.tensor_tensor(lf, lf, bA(3), ALU.add),
             waits=[(c_v, c_v.n)], inc=(c_v, 1))

        P.dma(SY, C.xnt_sp[blk].rearrange("c p n -> p c n"), xnT[:, par, :, :],
              C.s_xnt[par], waits=[(c_a, c_a.n)])

    for gi in range(16):
        emit_kv_group(NBLK - 1, gi)
    C.ph0a_v = c_v.n
    C.spill_all = ([(C.s_ev[i], C.s_ev[i].n) for i in range(4)] +
                   [(C.s_vq[i], C.s_vq[i].n) for i in range(4)])
    C.xnt_all = [(C.s_xnt[0], C.s_xnt[0].n), (C.s_xnt[1], C.s_xnt[1].n)]


def load_weight_slab(C, src2d, wi, extra_w=()):
    P = C.P
    a = 0
    for q in range(4):
        wld = P.dma("sync", C.wstage[0:128, 0:2, 0:1024],
                    src2d.rearrange("(c p) n -> p c n", p=128)[:, 2 * q:2 * q + 2, :],
                    C.s_w, waits=[(C.c_a, C.c_a.n), (C.c_t, C.c_t.n)] + list(extra_w))
        for j in range(2):
            a = P.op("scalar", lambda e, wi=wi, q=q, j=j: e.activation(
                out=C.wbig[:, wi, 2 * q + j, 0:1024],
                in_=C.wstage[:, j, :], func=AF.Copy),
                waits=[(C.s_w, wld)], inc=(C.c_a, 1))
    return a


def emit_ph0b(C):
    P = C.P
    T, V, A, G, SY = "tensor", "vector", "scalar", "gpsimd", "sync"
    s_sp, c_t, c_v, c_a = C.s_sp, C.c_t, C.c_v, C.c_a
    xnT, QT = C.xnT, C.QT

    msl = Slots([(C.pS1, 0), (C.pS1, 512), (C.pS2, 0), (C.pS2, 512)])
    xr_done = {}

    def mmgroup(lhs_fn, rhs_fn, n_k, width, extra0):
            j, (pt, off), frees = msl.acquire()
            for dc in range(n_k):
                w0 = (extra0 + frees) if dc == 0 else []
                lhs, rhs = lhs_fn(dc), rhs_fn(dc)
                P.op(T, lambda e, lhs=lhs, rhs=rhs, dc=dc, pt=pt, off=off,
                     width=width, n_k=n_k: e.matmul(
                    pt[0:128, off:off + width], lhs, rhs,
                    start=(dc == 0), stop=(dc == n_k - 1)),
                    waits=w0, inc=(c_t, 1) if dc == n_k - 1 else None)
            return j, pt, off, c_t.n

    wq_ok = load_weight_slab(C, C.din["wq"], 0)
    for sb in range(4):
        par = sb % 2
        w = [(c_t, xr_done[sb - 1])] if sb >= 1 else [(c_a, wq_ok)]
        if sb >= 2:
            w.append((c_t, xr_done[sb - 2]))
        s_xr = C.s_xr[par]
        rld = 0
        for jj in range(4):
            rld = P.dma(SY, xnT[:, par, :, 128 * (3 - jj):128 * (3 - jj) + 128],
                        C.xnt_sp[4 * sb + jj, :, :, 0:128].rearrange("c p n -> p c n"),
                        s_xr, waits=w if jj == 0 else [])
        for p8 in range(8):
            e0 = [(s_xr, rld)] + ([(c_a, wq_ok)] if sb == 0 and p8 == 0 else [])
            j, pt, off, tmm = mmgroup(
                lambda dc, p8=p8: C.wbig[:, 0, dc, 128 * p8:128 * p8 + 128]
                if dc < 8 else C.bqflat[0:1, 128 * p8:128 * p8 + 128],
                lambda dc, par=par: xnT[:, par, dc, 0:512]
                if dc < 8 else C.onesq[0:1, 0:512], 9, 512, e0)
            a_ev = P.op(A, lambda e, p8=p8, pt=pt, off=off, sb=sb: e.activation(
                out=QT[:, p8, 512 * sb:512 * sb + 512], in_=pt[0:128, off:off + 512],
                func=AF.Copy), waits=[(c_t, tmm)], inc=(c_a, 1))
            msl.release(j, (c_a, a_ev))
        xr_done[sb] = c_t.n


def emit_ph0c(C):
    P = C.P
    T, V, A, G, SY = "tensor", "vector", "scalar", "gpsimd", "sync"
    s_w, s_sp2, c_t, c_v, c_a = C.s_w, C.s_sp2, C.c_t, C.c_v, C.c_a
    xnT, evst, hidT, selT = C.xnT, C.evst, C.hidT, C.selT

    for f in range(3):
        b2ld = P.dma(SY, C.b2f32, C.din["b2c"][f], s_w, waits=[(c_a, c_a.n)])
        P.op(A, lambda e, f=f: e.activation(out=C.b2bf[:, f, :], in_=C.b2f32,
                                            func=AF.Copy),
             waits=[(s_w, b2ld)], inc=(c_a, 1))

    msl = Slots([(C.pS1, 0), (C.pS1, 512), (C.pS2, 0), (C.pS2, 512)])
    esl = Slots([0, 1, 2, 3])
    rld_hist = []
    prev_block_t = 0

    for f in range(3):
        load_weight_slab(C, C.din["w1c"][f], 0)
        w_ok = load_weight_slab(C, C.din["w2c"][f], 1)
        N = 128 if f == 0 else 512

        for blk in range(NBLK):
            par = blk % 2
            w = []
            if len(rld_hist) >= 2:
                w.append((c_t, rld_hist[-2][1]))
            if f == 0 and blk == 0:
                w.append((c_t, C.c_t.n))
            s_xr = C.s_xr[par]
            rld = P.dma(SY, xnT[:, par, :, :], C.xnt_sp[blk].rearrange("c p n -> p c n"),
                        s_xr, waits=w)

            for tt in range(4):
                w0 = [(c_a, c_a.n), (c_v, c_v.n)] if tt == 0 else []
                P.op(T, lambda e, f=f, blk=blk, tt=tt: e.transpose(
                    C.pX2[0:1, 128 * tt:128 * tt + 128],
                    C.leafall[:, f, 4 * blk + tt:4 * blk + tt + 1], C.idf32[...]),
                    waits=w0, inc=(c_t, 1))
            t_lt = c_t.n
            a_lf = P.op(A, lambda e: e.activation(out=C.lrow_f, in_=C.pX2[0:1, 0:512],
                                                  func=AF.Copy),
                        waits=[(c_t, t_lt)], inc=(c_a, 1))
            a_lb = P.op(A, lambda e: e.activation(out=C.lrow_b, in_=C.pX2[0:1, 0:512],
                                                  func=AF.Copy), inc=(c_a, 1))
            t_s16 = P.op(T, lambda e: e.matmul(C.pX1[0:16, 0:512], C.onesbf[0:1, 0:16],
                                               C.lrow_b, start=True, stop=True),
                         waits=[(c_a, a_lb), (c_v, c_v.n)], inc=(c_t, 1))
            v_sel = P.op(V, lambda e: e.tensor_scalar(
                out=selT[...], in0=C.pX1[0:16, 0:512], scalar1=C.iota16[...],
                scalar2=None, op0=ALU.is_equal),
                waits=[(c_t, t_s16)], inc=(c_v, 1))
            t_b128 = P.op(T, lambda e: e.matmul(C.pX2[0:128, 0:512], C.onesbf[0:1, 0:128],
                                                C.lrow_b, start=True, stop=True),
                          waits=[(c_a, a_lb), (c_v, c_v.n)], inc=(c_t, 1))

            hid_v = 0
            for hc in range(8):
                j, (pt, off), frees = msl.acquire()
                w0 = [(s_xr, rld)] + frees
                if hc == 0:
                    w0.append((c_a, w_ok))
                for dc in range(8):
                    P.op(T, lambda e, hc=hc, dc=dc, pt=pt, off=off, par=par, N=N: e.matmul(
                        pt[0:128, off:off + N], C.wbig[:, 0, dc, 128 * hc:128 * hc + 128],
                        xnT[:, par, dc, 0:N], start=(dc == 0), stop=(dc == 7)),
                        waits=w0 if dc == 0 else [], inc=(c_t, 1))
                tmm = c_t.n
                a_h = P.op(A, lambda e, hc=hc, pt=pt, off=off, f=f, N=N: e.activation(
                    out=hidT[:, hc, 0:N], in_=pt[0:128, off:off + N],
                    func=AF.Relu, bias=C.b1c[:, f, hc:hc + 1]),
                    waits=[(c_t, tmm), (c_t, prev_block_t)], inc=(c_a, 1))
                hid_v = P.op(V, lambda e, hc=hc, N=N: e.scalar_tensor_tensor(
                    out=hidT[:, hc, 0:N], in0=C.pX2[0:128, 0:N],
                    scalar=C.hidiota[:, hc:hc + 1], in1=hidT[:, hc, 0:N],
                    op0=ALU.is_equal, op1=ALU.mult),
                    waits=[(c_a, a_h), (c_t, t_b128)], inc=(c_v, 1))
                msl.release(j, (c_v, hid_v))

            if f == 0:
                qc = own_pos(blk)
                for dc in range(8):
                    j, (pt, off), frees = msl.acquire()
                    w0 = [(c_v, hid_v)] + frees
                    for hc in range(8):
                        P.op(T, lambda e, dc=dc, hc=hc, pt=pt, off=off: e.matmul(
                            pt[0:128, off:off + 128],
                            C.wbig[:, 1, hc, 128 * dc:128 * dc + 128],
                            hidT[:, hc, 0:128], start=(hc == 0), stop=False),
                            waits=w0 if hc == 0 else [], inc=(c_t, 1))
                    P.op(T, lambda e, dc=dc, f=f, pt=pt, off=off: e.matmul(
                        pt[0:128, off:off + 128],
                        C.b2bf[0:16, f, 128 * dc:128 * dc + 128],
                        selT[0:16, 0:128], start=False, stop=True),
                        waits=[(c_v, v_sel)], inc=(c_t, 1))
                    tmm = c_t.n
                    je, es, efree = esl.acquire()
                    a_ev = P.op(A, lambda e, pt=pt, off=off, es=es: e.activation(
                        out=evst[:, es, 0:128], in_=pt[0:128, off:off + 128],
                        func=AF.Copy), waits=[(c_t, tmm)] + efree, inc=(c_a, 1))
                    msl.release(j, (c_a, a_ev))
                    P.dma(G, C.fqt_sp[dc][:, qc:qc + 128], evst[:, es, 0:128],
                          s_sp2, waits=[(c_a, a_ev)])
                    esl.release(je, (s_sp2, s_sp2.n))
            elif f == 1:
                for dc in range(8):
                    j, (pt, off), frees = msl.acquire()
                    w0 = [(c_v, hid_v)] + frees
                    for hc in range(8):
                        P.op(T, lambda e, dc=dc, hc=hc, pt=pt, off=off: e.matmul(
                            pt[0:128, off:off + 512],
                            C.wbig[:, 1, hc, 128 * dc:128 * dc + 128],
                            hidT[:, hc, 0:512], start=(hc == 0), stop=False),
                            waits=w0 if hc == 0 else [], inc=(c_t, 1))
                    P.op(T, lambda e, dc=dc, f=f, pt=pt, off=off: e.matmul(
                        pt[0:128, off:off + 512],
                        C.b2bf[0:16, f, 128 * dc:128 * dc + 128],
                        selT[...], start=False, stop=True),
                        waits=[(c_v, v_sel)], inc=(c_t, 1))
                    tmm = c_t.n
                    je, es, efree = esl.acquire()
                    a_ev = P.op(A, lambda e, pt=pt, off=off, es=es: e.activation(
                        out=evst[:, es, :], in_=pt[0:128, off:off + 512], func=AF.Copy),
                        waits=[(c_t, tmm)] + efree, inc=(c_a, 1))
                    msl.release(j, (c_a, a_ev))
                    P.dma(G, C.fkt_sp[dc][:, 512 * blk:512 * blk + 512],
                          evst[:, es, :], s_sp2, waits=[(c_a, a_ev)])
                    esl.release(je, (s_sp2, s_sp2.n))
            else:
                for tt in range(4):
                    for hf in range(2):
                        j, (pt, off), frees = msl.acquire()
                        w0 = [(c_v, hid_v)] + frees
                        for hc in range(8):
                            P.op(T, lambda e, tt=tt, hf=hf, hc=hc, pt=pt, off=off: e.matmul(
                                pt[0:128, off:off + 512],
                                hidT[:, hc, 128 * tt:128 * tt + 128],
                                C.wbig[:, 1, hc, 512 * hf:512 * hf + 512],
                                start=(hc == 0), stop=False),
                                waits=w0 if hc == 0 else [], inc=(c_t, 1))
                        P.op(T, lambda e, tt=tt, hf=hf, f=f, pt=pt, off=off: e.matmul(
                            pt[0:128, off:off + 512],
                            selT[0:16, 128 * tt:128 * tt + 128],
                            C.b2bf[0:16, f, 512 * hf:512 * hf + 512],
                            start=False, stop=True),
                            waits=[(c_v, v_sel)], inc=(c_t, 1))
                        tmm = c_t.n
                        je, es, efree = esl.acquire()
                        a_ev = P.op(A, lambda e, pt=pt, off=off, es=es: e.activation(
                            out=evst[:, es, :], in_=pt[0:128, off:off + 512],
                            func=AF.Copy), waits=[(c_t, tmm)] + efree, inc=(c_a, 1))
                        msl.release(j, (c_a, a_ev))
                        P.dma(G, C.fv_sp[4 * blk + tt][:, 512 * hf:512 * hf + 512],
                              evst[:, es, :], s_sp2, waits=[(c_a, a_ev)])
                        esl.release(je, (s_sp2, s_sp2.n))
            prev_block_t = c_t.n
            rld_hist.append((rld, c_t.n))
    C.ph0c_t = c_t.n
    C.ph0c_sp2 = s_sp2.n


def emit_ph1(C):
    P = C.P
    T, V, A, G, SY = "tensor", "vector", "scalar", "gpsimd", "sync"
    s_sp, c_t, c_v, c_a, c_g = C.s_sp, C.c_t, C.c_v, C.c_a, C.c_g
    QT, ktp, vaug, PT, mxs, mhT = C.QT, C.ktp, C.vaug, C.PT, C.mxs, C.mhT
    stats, sc1, biasq, lnscr = C.stats, C.sc1, C.biasq, C.lnscr

    pair_done = {}
    pt_guard = {}
    st = {"pa_free": 0}
    pools = [Slots([(C.pS1, 0), (C.pS1, 512), (C.pX1, 0)]),
             Slots([(C.pS2, 0), (C.pS2, 512), (C.pX2, 0)])]

    def Nof(c, qb):
        over = c - (16 * qb + 3)
        return 512 if over <= 0 else 512 - 128 * ((over + 3) // 4)

    def emit_p1_chunk(c, qb, p8, kpar, kld, red_v):
        for hh in range(2):
            j, (pt, off), frees = pools[hh].acquire()
            N = Nof(c, qb)
            w0 = [(C.s_mk[kpar], kld)] + frees
            if c == 0 and p8 == 0 and qb == 0:
                w0.append((c_t, getattr(C, "ph0c_t", 0)))
            mm = P.op(T, lambda e, c=c, hh=hh, pt=pt, off=off, N=N,
                      kpar=kpar, p8=p8, qb=qb: e.matmul(
                pt[0:128, off:off + N],
                ktp[64 * hh:64 * hh + 64, kpar, 128 * c:128 * c + 128],
                QT[64 * hh:64 * hh + 64, p8, 512 * qb:512 * qb + N],
                start=True, stop=True, tile_position=(64 * hh, 0)),
                waits=w0, inc=(c_t, 1))
            red = P.op(V, lambda e, c=c, hh=hh, pt=pt, off=off, N=N: e.reduce_max(
                mxs[:, hh, c:c + 1], pt[0:128, off:off + N], axis=AX),
                waits=[(c_t, mm)], inc=(c_v, 1))
            red_v[(c, hh)] = red
            pools[hh].release(j, [(c_v, red)])

    def emit_chain(qb, Cn, red_v):
        qp = qb % 2
        j, (pt, off), frees = pools[0].acquire()
        v_m = P.op(V, lambda e, Cn=Cn: e.reduce_max(
            stats[:, 40:42], mxs[:, 0:2, 0:Cn], axis=AX),
            waits=[(c_v, red_v[(Cn - 1, 1)])], inc=(c_v, 1))
        t_tr = P.op(T, lambda e, pt=pt, off=off: e.transpose(
            pt[0:2, off:off + 128], stats[:, 40:42], C.idf32[...]),
            waits=[(c_v, v_m)] + frees, inc=(c_t, 1))
        v_s = P.op(V, lambda e, pt=pt, off=off: e.reduce_max(
            stats[0:2, 42:43], pt[0:2, off:off + 128], axis=AX),
            waits=[(c_t, t_tr)], inc=(c_v, 1))
        t_t2 = P.op(T, lambda e, pt=pt, off=off: e.transpose(
            pt[0:1, off + 128:off + 130], stats[0:2, 42:43], C.idf32[0:2, 0:2]),
            waits=[(c_v, v_s)], inc=(c_t, 1))
        a_cp = P.op(A, lambda e, pt=pt, off=off: e.activation(
            out=sc1[0:1, 0:2], in_=pt[0:1, off + 128:off + 130], func=AF.Copy),
            waits=[(c_t, t_t2)], inc=(c_a, 1))
        t_b = P.op(T, lambda e, pt=pt, off=off: e.matmul(
            pt[0:128, off + 256:off + 258], C.onesP[0:1, 0:128],
            sc1[0:1, 0:2], start=True, stop=True),
            waits=[(c_a, a_cp)], inc=(c_t, 1))
        a_b = P.op(A, lambda e, qp=qp, pt=pt, off=off: e.activation(
            out=biasq[:, qp, 0:2], in_=pt[0:128, off + 256:off + 258],
            func=AF.Copy, scale=-0.125, bias=70.0),
            waits=[(c_t, t_b)], inc=(c_a, 1))
        pools[0].release(j, [(c_a, a_b)])
        return {0: a_b, 1: a_b}

    def emit_p2_pv(c, qb, Cn, exp_a, pv_t, vld):
        for hh in range(2):
            N = Nof(c, qb)
            dep = [exp_a[(c, hh)]]
            if c == 0:
                dep.append((c_v, st["pa_free"]))
                dep.append((C.s_mv, vld))
            pv = P.op(T, lambda e, c=c, hh=hh, N=N, Cn=Cn: e.matmul(
                C.pA[0:65, 512 * hh:512 * hh + N], vaug[:, c, 65 * hh:65 * hh + 65],
                PT[:, hh, c % 2, 0:N], start=(c == 0), stop=(c == Cn - 1),
                skip_group_check=True),
                waits=dep, inc=(c_t, 1))
            pv_t[(c, hh)] = pv
            pt_guard[(hh, c % 2)] = pv

    def emit_p2_chunk(c, qb, p8, kpar, Cn, bias_a, exp_a, pv_t, vld):
        qp = qb % 2
        for hh in range(2):
            j, (pt, off), frees = pools[hh].acquire()
            N = Nof(c, qb)
            w0 = [(c_a, bias_a[hh])] + frees
            mm = P.op(T, lambda e, c=c, hh=hh, pt=pt, off=off, N=N,
                      kpar=kpar, p8=p8, qb=qb: e.matmul(
                pt[0:128, off:off + N],
                ktp[64 * hh:64 * hh + 64, kpar, 128 * c:128 * c + 128],
                QT[64 * hh:64 * hh + 64, p8, 512 * qb:512 * qb + N],
                start=True, stop=True, tile_position=(64 * hh, 0)),
                waits=w0, inc=(c_t, 1))
            extra = []
            if c % 4 == 0 and 4 * qb <= c // 4 <= 4 * qb + 3:
                dd = (4 * qb + 3) - c // 4
                v_msk = P.op(V, lambda e, pt=pt, off=off, dd=dd: e.tensor_tensor(
                    pt[0:128, off + 128 * dd:off + 128 * dd + 128],
                    pt[0:128, off + 128 * dd:off + 128 * dd + 128],
                    C.trimask[...], ALU.add),
                    waits=[(c_t, mm)], inc=(c_v, 1))
                extra = [(c_v, v_msk)]
            slotg = pt_guard.get((hh, c % 2))
            exp = P.op(A, lambda e, c=c, hh=hh, pt=pt, off=off, N=N, qp=qp: e.activation(
                out=PT[:, hh, c % 2, 0:N], in_=pt[0:128, off:off + N],
                func=AF.Exp, scale=0.125, bias=biasq[:, qp, hh:hh + 1]),
                waits=[(c_t, mm)] + extra +
                      ([(c_t, slotg)] if slotg else []), inc=(c_a, 1))
            pools[hh].release(j, [(c_a, exp)])
            dep = (c_a, exp)
            if c % 4 != 0 and 4 * qb <= c // 4 <= 4 * qb + 3:
                mmod = c % 4
                dd = (4 * qb + 3) - c // 4
                g_d = P.op(V, lambda e, c=c, hh=hh, dd=dd, mmod=mmod: e.tensor_scalar_mul(
                    PT[:, hh, c % 2, 128 * dd:128 * dd + 128],
                    PT[:, hh, c % 2, 128 * dd:128 * dd + 128],
                    C.allowB[:, mmod:mmod + 1]),
                    waits=[dep], inc=(c_v, 1))
                dep = (c_v, g_d)
            exp_a[(c, hh)] = dep
        if c >= 1:
            emit_p2_pv(c - 1, qb, Cn, exp_a, pv_t, vld)
        if c == Cn - 1:
            emit_p2_pv(c, qb, Cn, exp_a, pv_t, vld)

    def emit_norm(qb, p8, Cn, pv_t):
        for hh in range(2):
            h = 2 * p8 + hh
            j, (pt, off), frees = pools[1].acquire()
            v_l = P.op(V, lambda e, hh=hh: e.tensor_scalar_add(
                lnscr[64:65, 0:512], C.pA[64:65, 512 * hh:512 * hh + 512], 1e-30),
                waits=[(c_t, pv_t[(Cn - 1, 1)])], inc=(c_v, 1))
            v_r = P.op(V, lambda e: e.reciprocal(
                lnscr[64:65, 512:1024], lnscr[64:65, 0:512]),
                waits=[(c_v, v_l)], inc=(c_v, 1))
            t_b = P.op(T, lambda e, pt=pt, off=off: e.matmul(
                pt[0:64, off:off + 512], C.onesP[64:65, 0:64],
                lnscr[64:65, 512:1024],
                start=True, stop=True, tile_position=(64, 0)),
                waits=[(c_v, v_r)] + frees, inc=(c_t, 1))
            a_rb = P.op(A, lambda e, pt=pt, off=off: e.activation(
                out=C.reclB, in_=pt[0:64, off:off + 512], func=AF.Copy),
                waits=[(c_t, t_b)], inc=(c_a, 1))
            st["pa_free"] = P.op(V, lambda e, hh=hh, h=h, qb=qb: e.tensor_tensor(
                mhT[64 * (h % 2):64 * (h % 2) + 64, h // 2, 512 * qb:512 * qb + 512],
                C.pA[0:64, 512 * hh:512 * hh + 512], C.reclB, ALU.mult),
                waits=[(c_a, a_rb)], inc=(c_v, 1))
            pools[1].release(j, [(c_a, a_rb)])

    for p8 in range(8):
        kpar = p8 % 2
        w = list(C.spill_all) if p8 <= 1 else []
        if p8 >= 2:
            w.append((c_t, pair_done[p8 - 2]))
        kld = P.dma(G, ktp[:, kpar, :], C.kt_sp[p8], C.s_mk[kpar], waits=w)
        wv = list(C.spill_all) if p8 <= 1 else []
        if p8 >= 1:
            wv.append((c_t, pair_done[p8 - 1]))
        vld = P.dma(G, C.vaug[:, 0:64, 0:130], C.vp_sp[p8], C.s_mv, waits=wv)

        prev = None
        for qb in range(4):
            Cn1 = 16 * qb + 16
            red_v = {}
            for c in range(Cn1):
                emit_p1_chunk(c, qb, p8, kpar, kld, red_v)
                if prev is not None and c < prev[1]:
                    emit_p2_chunk(c, prev[0], p8, kpar, prev[1], prev[2],
                                  prev[3], prev[4], vld)
            if prev is not None:
                emit_norm(prev[0], p8, prev[1], prev[4])
            bias_a = emit_chain(qb, Cn1, red_v)
            prev = (qb, Cn1, bias_a, {}, {})
        for c in range(prev[1]):
            emit_p2_chunk(c, prev[0], p8, kpar, prev[1], prev[2], prev[3],
                          prev[4], vld)
        emit_norm(prev[0], p8, prev[1], prev[4])
        pair_done[p8] = c_t.n

    if C.mh_dbg is not None:
        for dcq in range(8):
            P.dma(SY, C.mh_dbg[dcq], mhT[:, dcq, :], C.s_out,
                  waits=[(c_v, c_v.n)])
    C.ph1_v = c_v.n
    C.ph1_t = c_t.n


def emit_ph2(C):
    P = C.P
    T, V, A, G, SY = "tensor", "vector", "scalar", "gpsimd", "sync"
    s_w, s_sp2, s_out = C.s_w, C.s_sp2, C.s_out
    c_t, c_v, c_a, c_g = C.c_t, C.c_v, C.c_a, C.c_g
    mhT, qTb, ktf, vf = C.mhT, C.qTb, C.ktf, C.vf
    P2T, A2, l2, fsc, fB, m2t = C.P2T, C.A2, C.l2, C.fsc, C.fB, C.m2t
    stats, outb, xtile = C.stats, C.outb, C.xtile

    wo_ok = load_weight_slab(C, C.din["wo"], 0, extra_w=[(c_t, C.ph1_t)])
    if C.wo_dbg is not None:
        P.dma(SY, C.wo_dbg, C.wbig[:, 0, :, :], C.s_out, waits=[(c_a, wo_ok)])
        P.dma(SY, C.bo_dbg, C.boB[0:128, :], C.s_out, waits=[(c_a, wo_ok)])

    sst = C.sload

    slots2 = [(C.pS1, 0), (C.pS1, 512), (C.pS2, 0), (C.pS2, 512)]
    exp_prev = {}
    sup_hist = {}
    st2 = {"mrg": C.ph1_v, "pv_done": 0}

    def emit_ot_group(ot):
        sj = ot % 2
        ev = 0
        for hf in range(2):
            for dc in range(8):
                w0 = []
                if dc == 0 and hf == 0:
                    w0 = [(c_v, st2["mrg"])]
                    if ot == 0:
                        w0 += [(c_a, wo_ok), (c_v, C.ph1_v)]
                P.op(T, lambda e, ot=ot, hf=hf, dc=dc: e.matmul(
                    C.pA[0:128, 512 * hf:512 * hf + 512],
                    mhT[:, dc, 128 * ot:128 * ot + 128],
                    C.wbig[:, 0, dc, 512 * hf:512 * hf + 512],
                    start=(dc == 0), stop=(dc == 7), skip_group_check=True),
                    waits=w0, inc=(c_t, 1) if dc == 7 else None)
            tmm = c_t.n
            w0 = [(c_t, tmm)]
            if ot >= 2 and hf == 0:
                w0.append((C.s_ss, C.s_ss.n))
            ev = P.op(V, lambda e, hf=hf, sj=sj: e.tensor_tensor(
                sst[:, sj, 512 * hf:512 * hf + 512],
                C.pA[0:128, 512 * hf:512 * hf + 512],
                C.boB[:, 512 * hf:512 * hf + 512], ALU.add),
                waits=w0, inc=(c_v, 1))
        P.dma(G, C.s_sp3[ot], sst[:, sj, :], C.s_ss, waits=[(c_v, ev)])
        st2["mrg"] = ev

    def emit_qk2_group(qb, sup, spar, kld, qld, mz, cc):
        gs = 16 * qb + sup
        pt, off = slots2[cc]
        w0 = [(C.s_fk[spar], kld), (C.s_q, qld)]
        if exp_prev.get(cc):
            w0.append((c_a, exp_prev[cc]))
        if gs == 0 and cc == 0:
            w0.append((c_g, mz))
            w0.append((c_v, c_v.n))
        for dc in range(8):
            P.op(T, lambda e, cc=cc, dc=dc, pt=pt, off=off, spar=spar: e.matmul(
                pt[0:128, off:off + 512],
                ktf[:, spar, dc, 128 * cc:128 * cc + 128],
                qTb[:, dc, 0:512], start=(dc == 0), stop=(dc == 7)),
                waits=w0 if dc == 0 else [], inc=(c_t, 1) if dc == 7 else None)
        tqk = c_t.n
        P.op(V, lambda e, cc=cc, pt=pt, off=off: e.reduce_max(
            m2t[:, cc:cc + 1], pt[0:128, off:off + 512], axis=AX),
            waits=[(c_t, tqk)], inc=(c_v, 1))

    def emit_chain2(sup, mz):
        v_m = P.op(V, lambda e: e.reduce_max(stats[:, 44:45], m2t[...], axis=AX),
                   inc=(c_v, 1))
        t_tr = P.op(T, lambda e: e.transpose(C.pX1[0:1, 0:128], stats[:, 44:45],
                                             C.idf32[...]),
                    waits=[(c_v, v_m), (c_v, c_v.n), (c_a, c_a.n)], inc=(c_t, 1))
        v_s = P.op(V, lambda e: e.reduce_max(fsc[0:1, 1:2], C.pX1[0:1, 0:128],
                                             axis=AX),
                   waits=[(c_t, t_tr), (c_g, mz)], inc=(c_v, 1))
        v_mx = P.op(V, lambda e: e.tensor_tensor(fsc[0:1, 5:6], fsc[0:1, 0:1],
                                                 fsc[0:1, 1:2], ALU.max),
                    waits=[(c_v, v_s)], inc=(c_v, 1))
        v_df = P.op(V, lambda e: e.tensor_tensor(fsc[0:1, 3:4], fsc[0:1, 0:1],
                                                 fsc[0:1, 5:6], ALU.subtract),
                    waits=[(c_v, v_mx)], inc=(c_v, 1))
        a_f = P.op(A, lambda e: e.activation(out=fsc[0:1, 4:5], in_=fsc[0:1, 3:4],
                                             func=AF.Exp, scale=1.0 / 32),
                   waits=[(c_v, v_df), (c_v, st2["mrg"])], inc=(c_a, 1))
        v_cp = P.op(V, lambda e: e.tensor_copy(fsc[0:1, 0:1], fsc[0:1, 5:6]),
                    waits=[(c_a, a_f)], inc=(c_v, 1))
        t_b = P.op(T, lambda e: e.matmul(C.pX1[0:128, 192:194], C.onesP[0:1, 0:128],
                                         fsc[0:1, 4:6], start=True, stop=True),
                   waits=[(c_a, a_f), (c_v, c_v.n)], inc=(c_t, 1))
        a_fB = P.op(A, lambda e: e.activation(out=fB[:, 0:1], in_=C.pX1[0:128, 192:193],
                                              func=AF.Copy),
                    waits=[(c_t, t_b), (c_v, st2["mrg"])], inc=(c_a, 1))
        a_b2 = P.op(A, lambda e: e.activation(out=fB[:, 1:2], in_=C.pX1[0:128, 193:194],
                                              func=AF.Copy, scale=-1.0 / 32),
                    waits=[(c_t, t_b)], inc=(c_a, 1))
        return a_b2

    def emit_exps2(sup, a_b2):
        for cc in range(4):
            pt, off = slots2[cc]
            exp_prev[cc] = P.op(A, lambda e, cc=cc, pt=pt, off=off: e.activation(
                out=P2T[:, cc, :], in_=pt[0:128, off:off + 512],
                func=AF.Exp, scale=1.0 / 32, bias=fB[:, 1:2]),
                waits=[(c_a, a_b2)] + ([(c_t, st2["pv_done"])] if cc == 0 else []),
                inc=(c_a, 1))
        return c_a.n

    def emit_pv2_group(qb, sup, spar, vld, exps, qt):
        lb = C.pX1 if qt % 2 == 0 else C.pX2
        w0 = [(c_a, exps), (C.s_fv[spar], vld), (c_v, st2["mrg"])]
        for cc in range(4):
            for hf in range(2):
                P.op(T, lambda e, qt=qt, cc=cc, hf=hf, spar=spar: e.matmul(
                    C.pA[0:128, 512 * hf:512 * hf + 512],
                    P2T[:, cc, 128 * qt:128 * qt + 128],
                    vf[:, spar, cc, 512 * hf:512 * hf + 512],
                    start=(cc == 0), stop=(cc == 3), skip_group_check=True),
                    waits=w0 if (cc == 0 and hf == 0) else [])
            P.op(T, lambda e, qt=qt, cc=cc, lb=lb: e.matmul(
                lb[0:128, 300 + qt:301 + qt],
                P2T[:, cc, 128 * qt:128 * qt + 128],
                C.onescolb[...], start=(cc == 0), stop=(cc == 3),
                skip_group_check=True), inc=(c_t, 1) if cc == 3 else None)
        tpv = c_t.n
        P.op(V, lambda e, qt=qt: e.scalar_tensor_tensor(
            out=A2[:, qt, :], in0=A2[:, qt, :], scalar=fB[:, 0:1],
            in1=C.pA[0:128, 0:1024], op0=ALU.mult, op1=ALU.add),
            waits=[(c_t, tpv)], inc=(c_v, 1))
        st2["mrg"] = P.op(V, lambda e, qt=qt, lb=lb: e.scalar_tensor_tensor(
            out=l2[:, qt:qt + 1], in0=l2[:, qt:qt + 1], scalar=fB[:, 0:1],
            in1=lb[0:128, 300 + qt:301 + qt], op0=ALU.mult, op1=ALU.add),
            inc=(c_v, 1))
        if qt == 3:
            st2["pv_done"] = c_t.n

    def emit_pv2(qb, sup, spar, vld, exps):
        for qt in range(4):
            emit_pv2_group(qb, sup, spar, vld, exps, qt)

    def emit_gcombine(qb, mrg_prev):
        for qt in range(4):
            i_own = 4 * qb + (3 - qt)
            v_l = P.op(V, lambda e, qt=qt: e.tensor_scalar_add(
                stats[:, 46:47], l2[:, qt:qt + 1], 1e-30),
                waits=[(c_v, mrg_prev)], inc=(c_v, 1))
            v_r = P.op(V, lambda e: e.reciprocal(stats[:, 47:48], stats[:, 46:47]),
                       waits=[(c_v, v_l)], inc=(c_v, 1))
            a_g = P.op(A, lambda e, qt=qt: e.activation(
                out=C.gsb[0:128, 0:1024], in_=A2[:, qt, :], func=AF.Silu,
                scale=stats[:, 47:48]),
                waits=[(c_v, v_r), (c_v, c_v.n)], inc=(c_a, 1))
            xld = P.dma(SY, xtile[:, qt % 2, :],
                        C.din["x"][512 * i_own:512 * i_own + 128, :], C.s_xl[qt % 2],
                        waits=[(c_v, c_v.n)])
            sld = P.dma(SY, C.sload[:, qt % 2, :], C.s_sp3[4 * qb + qt],
                        C.s_sl[qt % 2], waits=[(c_v, c_v.n), (C.s_ss, C.s_ss.n)])
            v_sg = P.op(V, lambda e, qb=qb, qt=qt: e.scalar_tensor_tensor(
                out=outb[:, qt % 2, :], in0=C.sload[:, qt % 2, :], scalar=1.0,
                in1=C.gsb[0:128, 0:1024], op0=ALU.mult, op1=ALU.mult),
                waits=[(c_a, a_g), (C.s_sl[qt % 2], sld)], inc=(c_v, 1))
            v_o = P.op(V, lambda e, qt=qt: e.tensor_tensor(
                outb[:, qt % 2, :], outb[:, qt % 2, :], xtile[:, qt % 2, :],
                ALU.add), waits=[(C.s_xl[qt % 2], xld), (c_v, v_sg)], inc=(c_v, 1))
            P.dma(SY, C.out_d[512 * qb + 128 * qt:512 * qb + 128 * qt + 128, :],
                  outb[:, qt % 2, :], s_out, waits=[(c_v, v_o)])

    pend_g = None
    for qb in range(4):
        qld = P.dma(SY, qTb[0:128, 0:8, 0:512],
                    C.fqt_sp[:, :, 512 * qb:512 * qb + 512].rearrange("c p n -> p c n"),
                    C.s_q, waits=[(s_sp2, C.ph0c_sp2), (C.s_ss, C.s_ss.n),
                                  (c_t, c_t.n)])
        if pend_g is not None:
            emit_gcombine(pend_g[0], pend_g[1])
        P.op(G, lambda e: e.memset(A2[0:128, 0:4, 0:1024], 0.0),
             waits=[(c_v, c_v.n), (c_a, c_a.n)], inc=(c_g, 1))
        P.op(G, lambda e: e.memset(l2[:, 0:4], 0.0), inc=(c_g, 1))
        mz = P.op(G, lambda e: e.memset(fsc[0:1, 0:1], NEGBIG), inc=(c_g, 1))

        prev2 = None
        for sup in range(16):
            gs = 16 * qb + sup
            spar = sup % 2
            w = [] if gs >= 2 else [(s_sp2, C.ph0c_sp2)]
            if gs >= 2:
                w.append((c_t, sup_hist[gs - 2]))
            kld = P.dma(G, ktf[:, spar, :, :],
                        C.fkt_sp[:, :, 512 * sup:512 * sup + 512].rearrange("c p n -> p c n"),
                        C.s_fk[spar], waits=w)
            wv2 = list(w)
            if gs >= 1:
                wv2.append((c_t, sup_hist[gs - 1]))
            vld = P.dma(G, vf[:, spar, :, :],
                        C.fv_sp[4 * sup:4 * sup + 4].rearrange("c p n -> p c n"),
                        C.s_fv[spar], waits=wv2)
            for i in range(4):
                if prev2 is not None:
                    emit_pv2_group(qb, prev2[0], prev2[1], prev2[2], prev2[3], i)
                emit_qk2_group(qb, sup, spar, kld, qld, mz, i)
            if qb == 0:
                emit_ot_group(sup)
            a_b2 = emit_chain2(sup, mz)
            exps = emit_exps2(sup, a_b2)
            prev2 = (sup, spar, vld, exps)
            sup_hist[gs] = c_t.n
        emit_pv2(qb, prev2[0], prev2[1], prev2[2], prev2[3])
        sup_hist[16 * qb + 15] = c_t.n
        pend_g = (qb, st2["mrg"])

    emit_gcombine(pend_g[0], pend_g[1])



_NC_CACHE = {}


def kernel(**inputs):
    if "nc" not in _NC_CACHE:
        _NC_CACHE["nc"] = build_nc()
    nc = _NC_CACHE["nc"]
    in_maps = host_inputs(inputs)
    res = run_bass_kernel_spmd(nc, in_maps, core_ids=list(range(8)))
    return host_output(res.results)


if __name__ == "__main__":
    pass
